# version 05
# baseline (speedup 1.0000x reference)
import numpy as np
import ml_dtypes
from contextlib import ExitStack

import concourse.bass as bass
import concourse.mybir as mybir
from concourse.bass_utils import run_bass_kernel_spmd

F32 = mybir.dt.float32
BF16 = mybir.dt.bfloat16
AF = mybir.ActivationFunctionType
ALU = mybir.AluOpType
AX = mybir.AxisListType

B, L, D = 8, 4096, 1024
_A0_NQT = [None]
_F0_SKIP = set()
NDT = D // 128
NCHUNK = L // 512
NH, HD = 4, 256
CC = 64
NCK = L // CC
EPS_LN = 1e-5
EPS_RMS = 1e-5
TOPK = 8
NE = 64
FF = 512


class Prog:
    def __init__(self, nc, gctx):
        self.nc = nc
        self.gctx = gctx
        self.pctx = gctx
        self.block = None
        self.bar_sem = gctx.enter_context(nc.semaphore("phase_bar"))
        self.bar = 0
        self._semid = 0
        self.G = {}
        self.PS = [gctx.enter_context(nc.psum_tensor(f"psb{i}", [128, 512], F32))
                   for i in range(7)]
        self.PSBF = gctx.enter_context(nc.psum_tensor("psbf", [128, 1024], BF16))
        self.phase_sems = []
        try:
            nc.sort_free_semaphores()
        except Exception:
            pass
        nc.all_engine_barrier()
        self._ss_sem = {}
        self._ss_cnt = {}
        for k in ("dve", "act", "pool"):
            self._ss_sem[k] = gctx.enter_context(nc.semaphore(f"ss_{k}"))
            self._ss_cnt[k] = 0
        globals_max = max([self.bar_sem.num]
                          + [s.num for s in self._ss_sem.values()])
        self.clear_from = globals_max + 1

    def edge(self, inst, eng, key=None):
        k = ("dve" if eng is self.nc.vector else
             "act" if eng is self.nc.scalar else "pool")
        inst.then_inc(self._ss_sem[k], 1)
        self._ss_cnt[k] += 1
        eng.wait_ge(self._ss_sem[k], self._ss_cnt[k])

    def sb(self, name, shape, dt=F32):
        return self.pctx.enter_context(self.nc.sbuf_tensor(name, shape, dt))

    def sbg(self, name, shape, dt=F32):
        return self.gctx.enter_context(self.nc.sbuf_tensor(name, shape, dt))

    def sem(self, name, dma=False):
        self._semid += 1
        h = self.pctx.enter_context(self.nc.semaphore(f"{name}{self._semid}"))
        self.phase_sems.append(h.num if hasattr(h, "num") else h)
        return h


def ph_init(p, dr):
    nc, blk = p.nc, p.block
    G = p.G
    G["ones"] = p.sbg("g_ones", [128, 128], F32)
    G["ident"] = p.sbg("g_ident", [128, 128], F32)
    G["ident_bf"] = p.sbg("g_identbf", [128, 128], BF16)
    G["iota"] = p.sbg("g_iota", [128, 128], F32)
    G["mlow"] = p.sbg("g_mlow", [64, 64], F32)
    G["mup"] = p.sbg("g_mup", [64, 64], F32)
    G["eps_ln"] = p.sbg("g_epsln", [128, 1], F32)
    G["eps_rms"] = p.sbg("g_epsrms", [128, 1], F32)
    G["gvec"] = p.sbg("g_gvec", [128, NDT], F32)
    G["bmask"] = p.sbg("g_bmask", [128, CC], F32)
    G["ebias"] = p.sbg("g_ebias", [128, NE], F32)
    ldg = p.sem("ini_ld", dma=True)
    lds = p.sem("ini_lds", dma=True)
    gc = p.sem("ini_gc")
    vrd = p.sem("ini_v")

    @blk.gpsimd
    def _(g):
        g.memset(G["ones"][:], 1.0)
        g.memset(G["eps_ln"][:], EPS_LN)
        g.memset(G["eps_rms"][:], EPS_RMS)
        g.iota(G["iota"][:], [[1, 128]], channel_multiplier=-1,
               allow_small_or_imprecise_dtypes=True).then_inc(gc, 1)
        g.dma_start(out=G["gvec"][:], in_=dr["w_g"]).then_inc(ldg, 16)

    @blk.vector
    def _(v):
        v.wait_ge(ldg, 16)
        v.wait_ge(gc, 1)
        v.tensor_single_scalar(G["ident"][:], G["iota"][:], 0.0, ALU.is_equal)
        v.tensor_single_scalar(G["ident_bf"][:], G["iota"][:], 0.0, ALU.is_equal)
        v.tensor_single_scalar(G["mlow"][:], G["iota"][0:64, 0:64], 0.0, ALU.is_lt)
        v.tensor_single_scalar(G["mup"][:], G["iota"][0:64, 0:64], 0.0, ALU.is_gt)
        v.engine_nop().then_inc(vrd, 1)

    bm1 = p.sb("g_bm1", [1, CC], F32)
    eb1 = p.sb("g_eb1", [1, NE], F32)
    ps_b = p.PS[0]
    tdone = p.sem("ini_t")

    @blk.sync
    def _(q):
        q.dma_start(out=bm1[:], in_=dr["bmask"]).then_inc(lds, 16)
        q.dma_start(out=eb1[:], in_=dr["w_eb"]).then_inc(lds, 16)

    @blk.tensor
    def _(t):
        t.wait_ge(ldg, 16)
        t.wait_ge(gc, 1)
        t.wait_ge(lds, 32)
        t.matmul(ps_b[:, 0:CC], G["ones"][0:1, :], bm1[:], start=True, stop=True)
        t.matmul(ps_b[:, CC:CC + NE], G["ones"][0:1, :], eb1[:], start=True, stop=True
                 ).then_inc(tdone, 1)

    @blk.vector
    def _(v):
        v.wait_ge(tdone, 1)
        v.tensor_copy(G["bmask"][:], ps_b[:, 0:CC])
        v.tensor_copy(G["ebias"][:], ps_b[:, CC:CC + NE])
        v.engine_nop().then_inc(p.bar_sem, 1)
    p.bar += 1


def ph_ln(p, src_d, dst_d, aff_out=None, cent_d=None, name="ln"):
    nc, blk = p.nc, p.block
    bar0 = p.bar
    ld2 = [p.sem(name + "ld_a"), p.sem(name + "ld_b")]
    ldc = p.sem(name + "ldc")
    vsq = p.sem(name + "vsq")
    ts1 = p.sem(name + "ts1")
    ts2 = p.sem(name + "ts2")
    vrd = p.sem(name + "vrd")
    ssd = p.sem(name + "ssd")
    vxn = p.sem(name + "vxn")
    st2 = [p.sem(name + "st_a"), p.sem(name + "st_b")]
    sa2 = [p.sem(name + "sa_a"), p.sem(name + "sa_b")]
    taf = p.sem(name + "taf")
    saf = p.sem(name + "saf")
    NC2 = NCHUNK
    do_aff = aff_out is not None

    xch = [p.sb(name + f"x{i}", [128, NDT * 512], F32) for i in range(2)]
    sq = [p.sb(name + f"sq{i}", [128, NDT * 512], F32) for i in range(2)]
    mean = [p.sb(name + f"mean{i}", [128, 512], F32) for i in range(2)]
    var = [p.sb(name + f"var{i}", [128, 512], F32) for i in range(2)]
    var2 = [p.sb(name + f"var2{i}", [128, 512], F32) for i in range(2)]
    rstd = [p.sb(name + f"rstd{i}", [128, 512], F32) for i in range(2)]
    sd = [p.sb(name + f"sd{i}", [128, 512], F32) for i in range(2)]
    xn = [p.sb(name + f"xn{i}", [128, NDT * 512], BF16) for i in range(2)]
    xnf = [p.sb(name + f"xnf{i}", [128, NDT * 512], F32) for i in range(2)] if do_aff else None
    afb = [p.sb(name + f"afb{i}", [128, 4 * 64], F32) for i in range(2)] if do_aff else None
    cent = p.sb(name + "cent", [128, NDT * 64], F32) if do_aff else None
    ps_s1 = [p.PS[0], p.PS[2]]
    ps_s2 = [p.PS[1], p.PS[3]]
    ps_af = [p.PS[4], p.PS[5]] if do_aff else None

    @blk.sync
    def _(q):
        q.wait_ge(p.bar_sem, bar0)
        if do_aff:
            for dt in range(NDT):
                q.dma_start(out=cent[:, dt * 64:(dt + 1) * 64],
                            in_=cent_d[dt * 128:(dt + 1) * 128, :]).then_inc(ldc, 16)
        for c in range(NC2):
            A = c % 2
            if c >= 2:
                q.wait_ge(vxn, c - 1)
                q.wait_ge(ts1, c - 1)
            for dt in range(NDT):
                q.dma_start(out=xch[A][:, dt * 512:(dt + 1) * 512],
                            in_=src_d[dt * 128:(dt + 1) * 128, c * 512:(c + 1) * 512]
                            ).then_inc(ld2[A], 16)

    ones = p.G["ones"]

    def ld_target(c):
        return 16 * NDT * (c // 2 + 1)

    @blk.vector
    def _(v):
        v.wait_ge(p.bar_sem, bar0)
        for c in range(NC2):
            A = c % 2
            v.wait_ge(ld2[A], ld_target(c))
            if c >= 2:
                v.wait_ge(ts2, c - 1)
            for dt in range(NDT):
                sl = slice(dt * 512, (dt + 1) * 512)
                v.tensor_mul(sq[A][:, sl], xch[A][:, sl], xch[A][:, sl])
            v.engine_nop().then_inc(vsq, 1)
            if c >= 2:
                v.wait_ge(ssd, c - 1)
            v.wait_ge(ts2, c + 1)
            p.edge(v.tensor_scalar_mul(mean[A][:], ps_s1[A][:, :], 1.0 / D), v, name + "v1")
            p.edge(v.tensor_mul(var2[A][:], mean[A][:], mean[A][:]), v, name + "v2")
            v.scalar_tensor_tensor(var[A][:], ps_s2[A][:, :], 1.0 / D, var2[A][:],
                                   ALU.mult, ALU.subtract)
            v.engine_nop().then_inc(vrd, 1)
            v.wait_ge(ssd, c + 1)
            p.edge(v.reciprocal(rstd[A][:], sd[A][:]), v, name + "v3")
            per_chunk_st = NDT + (4 if do_aff else 0)
            if c >= 2:
                v.wait_ge(st2[A], 16 * NDT * (c // 2))
                if do_aff:
                    v.wait_ge(taf, c - 1)
            for dt in range(NDT):
                sl = slice(dt * 512, (dt + 1) * 512)
                ii = v.tensor_sub(sq[A][:, sl], xch[A][:, sl], mean[A][:])
            p.edge(ii, v, name + "v4")
            for dt in range(NDT):
                sl = slice(dt * 512, (dt + 1) * 512)
                v.tensor_mul(xn[A][:, sl], sq[A][:, sl], rstd[A][:])
                if do_aff:
                    v.tensor_mul(xnf[A][:, sl], sq[A][:, sl], rstd[A][:])
            v.engine_nop().then_inc(vxn, 1)

    @blk.scalar
    def _(s):
        s.wait_ge(p.bar_sem, bar0)
        for c in range(NC2):
            A = c % 2
            s.wait_ge(vrd, c + 1)
            s.activation(sd[A][:], var[A][:], AF.Sqrt, bias=p.G["eps_ln"][:, 0:1], scale=1.0
                         ).then_inc(ssd, 1)
            if do_aff:
                s.wait_ge(taf, c + 1)
                if c >= 2:
                    s.wait_ge(sa2[A], 16 * 4 * (c // 2))
                s.copy(afb[A][:], ps_af[A][:, 0:256]).then_inc(saf, 1)

    @blk.tensor
    def _(t):
        t.wait_ge(p.bar_sem, bar0)
        for c in range(NC2):
            A = c % 2
            t.wait_ge(ld2[A], ld_target(c))
            if do_aff and c == 0:
                t.wait_ge(ldc, 16 * NDT)
            if c >= 2:
                t.wait_ge(vrd, c - 1)
            for dt in range(NDT):
                mm = t.matmul(ps_s1[A][:, :], ones[:], xch[A][:, dt * 512:(dt + 1) * 512],
                              start=(dt == 0), stop=(dt == NDT - 1))
            mm.then_inc(ts1, 1)
            t.wait_ge(vsq, c + 1)
            for dt in range(NDT):
                mm = t.matmul(ps_s2[A][:, :], ones[:], sq[A][:, dt * 512:(dt + 1) * 512],
                              start=(dt == 0), stop=(dt == NDT - 1))
            mm.then_inc(ts2, 1)
            if do_aff:
                t.wait_ge(vxn, c + 1)
                if c >= 2:
                    t.wait_ge(saf, c - 1)
                for tt in range(4):
                    for dt in range(NDT):
                        mm = t.matmul(ps_af[A][:, tt * 64:tt * 64 + 64],
                                      xnf[A][:, dt * 512 + tt * 128: dt * 512 + tt * 128 + 128],
                                      cent[:, dt * 64:(dt + 1) * 64],
                                      start=(dt == 0), stop=(dt == NDT - 1))
                mm.then_inc(taf, 1)

    @blk.gpsimd
    def _(g):
        for c in range(NC2):
            A = c % 2
            g.wait_ge(vxn, c + 1)
            for dt in range(NDT):
                g.dma_start(out=dst_d[dt * 128:(dt + 1) * 128, c * 512:(c + 1) * 512],
                            in_=xn[A][:, dt * 512:(dt + 1) * 512]).then_inc(st2[A], 16)
            if do_aff:
                g.wait_ge(saf, c + 1)
                for tt in range(4):
                    g.dma_start(out=aff_out[(c * 4 + tt) * 128:(c * 4 + tt + 1) * 128, :],
                                in_=afb[A][:, tt * 64:(tt + 1) * 64]).then_inc(sa2[A], 16)
        g.wait_ge(st2[0], 16 * NDT * (NC2 // 2))
        g.wait_ge(st2[1], 16 * NDT * (NC2 // 2))
        if do_aff:
            g.wait_ge(sa2[0], 16 * 4 * (NC2 // 2))
            g.wait_ge(sa2[1], 16 * 4 * (NC2 // 2))
        g.engine_nop().then_inc(p.bar_sem, 1)
    p.bar += 1


def ph_ffn0(p, dr):
    nc, blk = p.nc, p.block
    bar0 = p.bar
    xn0d, g0d = dr["xn0d"], dr["g0d"]
    ldm = p.sem("f0ldm", dma=True)
    tmc = p.sem("f0tmc")
    vmc = p.sem("f0vmc")
    gbd = p.sem("f0gbd")
    bdq = p.sem("f0bdq", dma=True)
    wc4 = [p.sem(f"f0wcq{j}") for j in range(4)]
    xl2 = [p.sem("f0xld_a", dma=True), p.sem("f0xld_b", dma=True)]
    tz = p.sem("f0tz")
    sh = p.sem("f0sh")
    tco = p.sem("f0tco")
    vgo = p.sem("f0vgo")
    gs2 = [p.sem("f0gst_a", dma=True), p.sem("f0gst_b", dma=True)]
    sl2 = [p.sem("f0slq_a", dma=True), p.sem("f0slq_b", dma=True)]
    svn = p.sem("f0svn")
    ss2 = [p.sem("f0ssq_a", dma=True), p.sem("f0ssq_b", dma=True)]
    NQ = 4
    NSTEP = NQ * NCHUNK
    RE = "p (h q8 pgx s2 k) -> p h q8 pgx s2 k"

    BD = p.sb("f0bd", [128, 16384], BF16)

    with ExitStack() as sub:
        rm_sb = sub.enter_context(nc.sbuf_tensor("f0rm", [16, 4096], F32))
        rup_sb = sub.enter_context(nc.sbuf_tensor("f0rup", [16, 16384], F32))
        m_all = sub.enter_context(nc.sbuf_tensor("f0mall", [16, 16384], BF16))

        @blk.sync
        def _(q):
            q.wait_ge(p.bar_sem, bar0)
            q.dma_start(out=rm_sb[:],
                        in_=dr["w_rm"].rearrange("h r j i -> j (h r) i")
                        ).then_inc(ldm, 16)
            q.dma_start(out=rup_sb[:],
                        in_=dr["w_rup"].rearrange("h r j k -> j (h r) k")
                        ).then_inc(ldm, 16)

        @blk.tensor
        def _(t):
            t.wait_ge(p.bar_sem, bar0)
            t.wait_ge(ldm, 32)
            for grp in range(32):
                if grp >= 2:
                    t.wait_ge(vmc, grp - 1)
                for u in range(8):
                    hr = grp * 8 + u
                    mm = t.matmul(p.PS[grp % 2][0:16, u * 64:(u + 1) * 64],
                                  rm_sb[:, hr * 16:(hr + 1) * 16],
                                  rup_sb[:, hr * 64:(hr + 1) * 64],
                                  start=True, stop=True)
                mm.then_inc(tmc, 1)

        @blk.vector
        def _(v):
            v.wait_ge(p.bar_sem, bar0)
            for grp in range(32):
                v.wait_ge(tmc, grp + 1)
                v.tensor_copy(m_all[:, grp * 512:(grp + 1) * 512],
                              p.PS[grp % 2][0:16, :]).then_inc(vmc, 1)

        @blk.gpsimd
        def _(g):
            g.wait_ge(p.bar_sem, bar0)
            g.memset(BD[:], 0.0).then_inc(gbd, 1)
            g.wait_ge(gbd, 1)
            g.wait_ge(vmc, 32)
            for pg in range(4):
                for s in range(2):
                    base = 32 * pg + 16 * s
                    dst = BD[base:base + 16, :].rearrange(
                        RE, h=4, q8=8, pgx=4, s2=2, k=64)[:, :, :, pg, s, :]
                    srcv = m_all[0:16, :].rearrange(
                        RE, h=4, q8=8, pgx=4, s2=2, k=64)[:, :, :, pg, s, :]
                    g.dma_start(out=dst, in_=srcv).then_inc(bdq, 16)
            g.wait_ge(bdq, 16 * 8)
            g.engine_nop().then_inc(gbd, 1)
        sub.close()

    wcb = p.sb("f0wcb", [128, 32 * 1024], BF16)
    hb = [p.sb(f"f0h{i}", [128, 16384], BF16) for i in range(2)]
    xnc = [p.sb(f"f0x{i}", [128, NDT * 512], BF16) for i in range(2)]
    gtmp = p.sb("f0gt", [128, 4096], F32)

    @blk.sync
    def _(q):
        q.wait_ge(gbd, 2)
        for s in range(NSTEP):
            qq, c = divmod(s, NCHUNK)
            A = s % 2
            if s >= 2:
                q.wait_ge(tz, (s - 2) * 32 + 32)
            for dt in range(NDT):
                q.dma_start(out=xnc[A][:, dt * 512:(dt + 1) * 512],
                            in_=xn0d[dt * 128:(dt + 1) * 128, c * 512:(c + 1) * 512]
                            ).then_inc(xl2[A], 16)

    @blk.gpsimd
    def _(g):
        for qq in range(NQ):
            if qq >= 1:
                g.wait_ge(vgo, 16 * qq)
            for piece in range(4):
                g.dma_start(
                    out=wcb[:, piece * 8 * 1024:(piece + 1) * 8 * 1024],
                    in_=dr["w_wc"].rearrange("(ftg pp) d -> pp ftg d", pp=128)
                        [:, qq * 32 + piece * 8: qq * 32 + (piece + 1) * 8, :]
                    ).then_inc(wc4[piece], 16)
            for c in range(NCHUNK):
                s = qq * NCHUNK + c
                for hf in range(2):
                    g.wait_ge(vgo, 2 * s + hf + 1)
                    for i in range(4):
                        dt = hf * 4 + i
                        g.dma_start(
                            out=g0d[dt * 128:(dt + 1) * 128, c * 512:(c + 1) * 512],
                            in_=gtmp[:, hf * 2048 + i * 512: hf * 2048 + (i + 1) * 512],
                            accum_op=(ALU.add if (qq > 0 and 'acc' not in _F0_SKIP) else ALU.bypass)
                            ).then_inc(gs2[hf], 16)

    @blk.tensor
    def _(t):
        t.wait_ge(gbd, 2)
        for s in range(NSTEP):
            qq, c = divmod(s, NCHUNK)
            A = s % 2
            t.wait_ge(xl2[s % 2], 16 * NDT * (s // 2 + 1))
            t.wait_ge(vgo, 2 * s)
            for ft in range(32):
                gz = s * 32 + ft
                ftg = qq * 32 + ft
                dt_in = ft // 4
                if gz >= 2:
                    t.wait_ge(sh, gz - 1)
                t.matmul(p.PS[4 + gz % 2][:, :], BD[:, ftg * 128:(ftg + 1) * 128],
                         xnc[A][:, dt_in * 512:(dt_in + 1) * 512],
                         start=True, stop=True).then_inc(tz, 1)
                if ft >= 1:
                    fprev = ft - 1
                    t.wait_ge(sh, s * 32 + fprev + 1)
                    if fprev % 8 == 0:
                        t.wait_ge(wc4[fprev // 8], 16 * (qq + 1))
                    for dto in range(4):
                        mm = t.matmul(p.PS[dto][:, :],
                                      wcb[:, fprev * 1024 + dto * 128: fprev * 1024 + (dto + 1) * 128],
                                      hb[A][:, fprev * 512:(fprev + 1) * 512],
                                      start=(fprev == 0), stop=(fprev == 31))
            t.wait_ge(sh, s * 32 + 32)
            for dto in range(4):
                mm = t.matmul(p.PS[dto][:, :],
                              wcb[:, 31 * 1024 + dto * 128: 31 * 1024 + (dto + 1) * 128],
                              hb[A][:, 31 * 512: 32 * 512],
                              start=False, stop=True)
            mm.then_inc(tco, 1)
            t.wait_ge(vgo, 2 * s + 1)
            for ft in range(32):
                for dto in range(4):
                    mm = t.matmul(p.PS[dto][:, :],
                                  wcb[:, ft * 1024 + (4 + dto) * 128: ft * 1024 + (5 + dto) * 128],
                                  hb[A][:, ft * 512:(ft + 1) * 512],
                                  start=(ft == 0), stop=(ft == 31))
            mm.then_inc(tco, 1)

    @blk.scalar
    def _(s_):
        s_.wait_ge(p.bar_sem, bar0)
        for s in range(NSTEP):
            A = s % 2
            if s >= 2:
                s_.wait_ge(tco, 2 * (s - 1))
            for ft in range(32):
                gz = s * 32 + ft
                s_.wait_ge(tz, gz + 1)
                s_.activation(hb[A][:, ft * 512:(ft + 1) * 512],
                              p.PS[4 + gz % 2][:, :], AF.Gelu).then_inc(sh, 1)

    @blk.vector
    def _(v):
        for s in range(NSTEP):
            for hf in range(2):
                v.wait_ge(tco, 2 * s + hf + 1)
                if s >= 1:
                    v.wait_ge(gs2[hf], 16 * 4 * s)
                for i in range(4):
                    v.tensor_copy(gtmp[:, hf * 2048 + i * 512: hf * 2048 + (i + 1) * 512],
                                  p.PS[i][:, :])
                v.engine_nop().then_inc(vgo, 1)

    GSTOT = 16 * 4 * NSTEP
    if "silu" in _F0_SKIP:
        @blk.gpsimd
        def _(g):
            g.wait_ge(gs2[0], GSTOT)
            g.wait_ge(gs2[1], GSTOT)
            g.engine_nop().then_inc(p.bar_sem, 1)
        p.bar += 1
        return

    @blk.sync
    def _(q):
        q.wait_ge(gs2[0], GSTOT)
        q.wait_ge(gs2[1], GSTOT)
        for u in range(16):
            c, hf = divmod(u, 2)
            if u >= 2:
                q.wait_ge(ss2[hf], 16 * 4 * (u // 2))
            for i in range(4):
                dt = hf * 4 + i
                q.dma_start(out=gtmp[:, hf * 2048 + i * 512: hf * 2048 + (i + 1) * 512],
                            in_=g0d[dt * 128:(dt + 1) * 128, c * 512:(c + 1) * 512]
                            ).then_inc(sl2[hf], 16)

    @blk.scalar
    def _(s_):
        for u in range(16):
            c, hf = divmod(u, 2)
            s_.wait_ge(sl2[hf], 16 * 4 * (u // 2 + 1))
            s_.activation(gtmp[:, hf * 2048: hf * 2048 + 2048],
                          gtmp[:, hf * 2048: hf * 2048 + 2048], AF.Silu).then_inc(svn, 1)

    @blk.gpsimd
    def _(g):
        for u in range(16):
            c, hf = divmod(u, 2)
            g.wait_ge(svn, u + 1)
            for i in range(4):
                dt = hf * 4 + i
                g.dma_start(out=g0d[dt * 128:(dt + 1) * 128, c * 512:(c + 1) * 512],
                            in_=gtmp[:, hf * 2048 + i * 512: hf * 2048 + (i + 1) * 512]
                            ).then_inc(ss2[hf], 16)
        g.wait_ge(ss2[0], 16 * 4 * 8)
        g.wait_ge(ss2[1], 16 * 4 * 8)
        g.engine_nop().then_inc(p.bar_sem, 1)
    p.bar += 1


def ph_attn0(p, dr):
    nc, blk = p.nc, p.block
    G = p.G
    bar0 = p.bar
    xn0d, x_t, g0d, x1d = dr["xn0d"], dr["x_t"], dr["g0d"], dr["x1d"]
    pld = p.sem("a0pld", dma=True)
    xl2 = [p.sem("a0xl_a", dma=True), p.sem("a0xl_b", dma=True)]
    tpj = p.sem("a0tpj")
    vpj = p.sem("a0vpj")
    tvt = p.sem("a0tvt")
    vvt = p.sem("a0vvt")
    tL = p.sem("a0tL")
    sE = p.sem("a0sE")
    tP = p.sem("a0tP")
    vS = p.sem("a0vS")
    vR = p.sem("a0vR")
    vV = p.sem("a0vV")
    tT = p.sem("a0tT")
    vO = p.sem("a0vO")
    tY = p.sem("a0tY")
    vC = p.sem("a0vC")
    xg2 = [p.sem("a0xg_a", dma=True), p.sem("a0xg_b", dma=True)]
    so2 = [p.sem("a0so_a", dma=True), p.sem("a0so_b", dma=True)]

    NQT = (L // 128) if _A0_NQT[0] is None else _A0_NQT[0]
    pq_sb = p.sb("a0pq", [128, 8 * NDT * 128], BF16)
    pk_sb = p.sb("a0pk", [128, NDT * 128], BF16)
    pv_sb = p.sb("a0pv", [128, NDT * 128], BF16)
    po_sb = p.sb("a0po", [128, 8 * 1024], BF16)
    xnc = [p.sb(f"a0x{i}", [128, NDT * 512], BF16) for i in range(2)]
    Qs = p.sb("a0Q", [128, 8 * L], BF16)
    Ks = p.sb("a0K", [128, L], BF16)
    Vs = p.sb("a0V", [128, L], BF16)
    vtok = p.sb("a0vt", [128, 32 * 132], BF16)
    PT = [p.sb(f"a0pt{i}", [128, 32 * 128], BF16) for i in range(2)]
    rs = p.sb("a0rs", [128, 1], F32)
    onb = p.sb("a0onb", [128, 128], BF16)
    otb = [p.sb(f"a0ot{i}", [128, 128], BF16) for i in range(2)]
    xg = [p.sb(f"a0xg{i}", [128, 2 * NDT * 128], F32) for i in range(2)]
    outb = [p.sb(f"a0ob{i}", [128, NDT * 128], F32) for i in range(2)]

    @blk.gpsimd
    def _(g):
        g.wait_ge(p.bar_sem, bar0)
        g.dma_start(out=pq_sb[:],
                    in_=dr["w_pq"].rearrange("h (dtp pp) k -> pp (h dtp) k", pp=128)
                    ).then_inc(pld, 16)
        g.dma_start(out=pk_sb[:],
                    in_=dr["w_pk"].rearrange("(dtp pp) k -> pp dtp k", pp=128)
                    ).then_inc(pld, 16)
        g.dma_start(out=pv_sb[:],
                    in_=dr["w_pv"].rearrange("(dtp pp) k -> pp dtp k", pp=128)
                    ).then_inc(pld, 16)
        g.dma_start(out=po_sb[:],
                    in_=dr["w_po"].rearrange("h v dd -> v h dd")).then_inc(pld, 16)

    @blk.sync
    def _(q):
        q.wait_ge(p.bar_sem, bar0)
        for c in range(NCHUNK):
            A = c % 2
            if c >= 2:
                q.wait_ge(tpj, 10 * (c - 1))
            for dt in range(NDT):
                q.dma_start(out=xnc[A][:, dt * 512:(dt + 1) * 512],
                            in_=xn0d[dt * 128:(dt + 1) * 128, c * 512:(c + 1) * 512]
                            ).then_inc(xl2[A], 16)

    @blk.tensor
    def _(t):
        t.wait_ge(p.bar_sem, bar0)
        t.wait_ge(pld, 64)
        ng = 0
        for c in range(NCHUNK):
            A = c % 2
            t.wait_ge(xl2[A], 16 * NDT * (c // 2 + 1))
            for grp in range(10):
                if ng >= 2:
                    t.wait_ge(vpj, ng - 1)
                for dt in range(NDT):
                    if grp == 0:
                        lhs = pk_sb[:, dt * 128:(dt + 1) * 128]
                    elif grp == 1:
                        lhs = pv_sb[:, dt * 128:(dt + 1) * 128]
                    else:
                        h = grp - 2
                        lhs = pq_sb[:, (h * NDT + dt) * 128:(h * NDT + dt + 1) * 128]
                    mm = t.matmul(p.PS[ng % 2][:, :], lhs,
                                  xnc[A][:, dt * 512:(dt + 1) * 512],
                                  start=(dt == 0), stop=(dt == NDT - 1))
                mm.then_inc(tpj, 1)
                ng += 1

    @blk.vector
    def _(v):
        v.wait_ge(p.bar_sem, bar0)
        ng = 0
        for c in range(NCHUNK):
            sl = slice(c * 512, (c + 1) * 512)
            for grp in range(10):
                v.wait_ge(tpj, ng + 1)
                if grp == 0:
                    v.tensor_copy(Ks[:, sl], p.PS[ng % 2][:, :])
                elif grp == 1:
                    v.tensor_copy(Vs[:, sl], p.PS[ng % 2][:, :])
                else:
                    h = grp - 2
                    v.tensor_copy(Qs[:, h * L + c * 512: h * L + (c + 1) * 512],
                                  p.PS[ng % 2][:, :])
                v.engine_nop().then_inc(vpj, 1)
                ng += 1

    @blk.vector
    def _(v):
        v.memset(vtok[:], 1.0)
        v.engine_nop().then_inc(vvt, 1)

    @blk.tensor
    def _(t):
        t.wait_ge(vpj, 80)
        t.wait_ge(vvt, 1)
        for mt in range(32):
            if mt >= 1:
                t.wait_ge(vvt, mt + 1)
            t.transpose(p.PSBF[:, (mt % 2) * 128:(mt % 2) * 128 + 128],
                        Vs[:, mt * 128:(mt + 1) * 128],
                        G["ident_bf"][:]).then_inc(tvt, 1)

    @blk.vector
    def _(v):
        for mt in range(32):
            v.wait_ge(tvt, mt + 1)
            v.tensor_copy(vtok[:, mt * 132: mt * 132 + 128],
                          p.PSBF[:, (mt % 2) * 128:(mt % 2) * 128 + 128])
            v.engine_nop().then_inc(vvt, 1)

    @blk.sync
    def _(q):
        for qt in range(NQT):
            A = qt % 2
            if qt >= 2:
                q.wait_ge(vC, qt - 1)
            for dt in range(NDT):
                q.dma_start(out=xg[A][:, dt * 128:(dt + 1) * 128],
                            in_=x_t[dt * 128:(dt + 1) * 128, qt * 128:(qt + 1) * 128]
                            ).then_inc(xg2[A], 16)
            for dt in range(NDT):
                q.dma_start(out=xg[A][:, (NDT + dt) * 128:(NDT + dt + 1) * 128],
                            in_=g0d[dt * 128:(dt + 1) * 128, qt * 128:(qt + 1) * 128]
                            ).then_inc(xg2[A], 16)

    nL = [0]; nE = [0]; nP = [0]; nS = [0]; nT = [0]; nO = [0]; nY = [0]

    @blk.tensor
    def _(t):
        t.wait_ge(vvt, 33)
        pend = [None]

        def emit_tr(t, idx):
            qtp, hq = divmod(idx, 8)
            hpp = hq % 2
            t.wait_ge(vS, idx + 1)
            if idx >= 1:
                t.wait_ge(vO, idx)
            t.transpose(p.PSBF[:, 256 + hpp * 128: 256 + hpp * 128 + 128],
                        onb[:], G["ident_bf"][:]).then_inc(tT, 1)

        def emit_y(t, idx):
            qtp, hq = divmod(idx, 8)
            hpp = hq % 2
            t.wait_ge(vO, idx + 1)
            if hq == 0 and qtp >= 1:
                t.wait_ge(vC, qtp)
            for dt in range(NDT):
                bank = 5 + dt // 4
                col = (dt % 4) * 128
                mm = t.matmul(p.PS[bank][:, col:col + 128],
                              po_sb[:, hq * 1024 + dt * 128: hq * 1024 + (dt + 1) * 128],
                              otb[hpp][:],
                              start=(hq == 0 and dt % 4 == 0),
                              stop=(hq == 7 and dt % 4 == 3))
            mm.then_inc(tY, 1)

        for qt in range(NQT):
            for h in range(8):
                hp = h % 2
                idx = qt * 8 + h
                for rnd in range(2):
                    for mt16 in range(16):
                        mt = rnd * 16 + mt16
                        bank = mt16 // 4
                        col = (mt16 % 4) * 128
                        if nL[0] >= 16:
                            t.wait_ge(sE, (nL[0] - 16) // 4 + 1)
                        t.matmul(p.PS[bank][:, col:col + 128],
                                 Ks[:, mt * 128:(mt + 1) * 128],
                                 Qs[:, h * L + qt * 128: h * L + (qt + 1) * 128],
                                 start=True, stop=True).then_inc(tL, 1)
                        nL[0] += 1
                if pend[0] is not None:
                    emit_tr(t, pend[0])
                t.wait_ge(vS, idx)
                for mt in range(32):
                    t.wait_ge(sE, nE[0] + mt // 4 + 1)
                    mm = t.matmul(p.PS[4][:, 0:129],
                                  PT[hp][:, mt * 128:(mt + 1) * 128],
                                  vtok[:, mt * 132: mt * 132 + 129],
                                  start=(mt == 0), stop=(mt == 31))
                mm.then_inc(tP, 1)
                nE[0] += 8
                if pend[0] is not None:
                    emit_y(t, pend[0])
                pend[0] = idx
        emit_tr(t, pend[0])
        emit_y(t, pend[0])

    @blk.scalar
    def _(s_):
        cnt = 0
        for qt in range(NQT):
            for h in range(8):
                hp = h % 2
                if cnt >= 2:
                    s_.wait_ge(tP, cnt - 1)
                for b4 in range(8):
                    s_.wait_ge(tL, cnt * 32 + (b4 + 1) * 4)
                    bank = b4 % 4
                    s_.activation(PT[hp][:, b4 * 512:(b4 + 1) * 512],
                                  p.PS[bank][:, :], AF.Exp).then_inc(sE, 1)
                cnt += 1

    @blk.vector
    def _(v):
        for qt in range(NQT):
            A = qt % 2
            for h in range(8):
                hp = h % 2
                v.wait_ge(tP, qt * 8 + h + 1)
                v.reciprocal(rs[:], p.PS[4][:, 128:129]).then_inc(vR, 1)
                v.wait_ge(vR, qt * 8 + h + 1)
                v.tensor_scalar_mul(onb[:], p.PS[4][:, 0:128], rs[:, 0:1])
                v.engine_nop().then_inc(vS, 1)
                v.wait_ge(tT, qt * 8 + h + 1)
                v.tensor_copy(otb[hp][:], p.PSBF[:, 256 + hp * 128: 256 + hp * 128 + 128])
                v.engine_nop().then_inc(vO, 1)
            v.wait_ge(tY, (qt + 1) * 8)
            v.wait_ge(xg2[A], 16 * 2 * NDT * (qt // 2 + 1))
            if qt >= 2:
                v.wait_ge(so2[A], 16 * NDT * (qt // 2))
            for dt in range(NDT):
                bank = 5 + dt // 4
                col = (dt % 4) * 128
                v.scalar_tensor_tensor(outb[A][:, dt * 128:(dt + 1) * 128],
                                       p.PS[bank][:, col:col + 128], 1.0,
                                       xg[A][:, (NDT + dt) * 128:(NDT + dt + 1) * 128],
                                       ALU.mult, ALU.mult)
            v.engine_nop().then_inc(vV, 1)
            v.wait_ge(vV, qt + 1)
            for dt in range(NDT):
                v.tensor_add(outb[A][:, dt * 128:(dt + 1) * 128],
                             outb[A][:, dt * 128:(dt + 1) * 128],
                             xg[A][:, dt * 128:(dt + 1) * 128])
            v.engine_nop().then_inc(vC, 1)

    @blk.gpsimd
    def _(g):
        for qt in range(NQT):
            A = qt % 2
            g.wait_ge(vC, qt + 1)
            for dt in range(NDT):
                g.dma_start(out=x1d[dt * 128:(dt + 1) * 128, qt * 128:(qt + 1) * 128],
                            in_=outb[A][:, dt * 128:(dt + 1) * 128]).then_inc(so2[A], 16)
        g.wait_ge(so2[0], 16 * NDT * (NQT // 2))
        g.wait_ge(so2[1], 16 * NDT * (NQT // 2))
        g.engine_nop().then_inc(p.bar_sem, 1)
    p.bar += 1


def ph_qkv(p, dr):
    nc, blk = p.nc, p.block
    G = p.G
    bar0 = p.bar
    xn2d = dr["xn2d"]
    outs = {"q": dr["qnd"], "k": dr["knd"], "v": dr["vsd"]}
    wsrc = {"q": dr["w_wq"], "k": dr["w_wk"], "v": dr["w_wv"]}
    csrc = {"q": dr["w_cq"], "k": dr["w_ck"], "v": dr["w_cv"]}
    xlq = p.sem("qkxl", dma=True)
    wl = p.sem("qkwl", dma=True)
    wlb = p.sem("qkwlb", dma=True)
    cl = p.sem("qkcl", dma=True)
    tpr = p.sem("qktpr")
    vev = p.sem("qkvev")
    vcv = p.sem("qkvcv")
    vcb2 = p.sem("qkvcb2")
    ssl = p.sem("qkssl")
    vsq = p.sem("qkvsq")
    tsq = p.sem("qktsq")
    ssr = p.sem("qkssr")
    vsc = p.sem("qkvsc")
    st2 = [p.sem("qkst_a", dma=True), p.sem("qkst_b", dma=True)]
    tbe = p.sem("qktbe")
    sbe = p.sem("qksbe")

    xns = p.sb("qkxn", [128, NDT * L], BF16)
    wcur = p.sb("qkw", [128, NDT * 1024], BF16)
    cbuf = [p.sb(f"qkcb{i}", [128, 4099], F32) for i in range(2)]
    cnv = [p.sb(f"qkcv{i}", [128, L], F32) for i in range(2)]
    qh = p.sb("qkqh", [128, 2 * L], F32)
    cvw = p.sb("qkcvw", [128, 3 * NDT * 4], F32)
    wbsb = p.sb("qkwb", [128, NDT * 4], BF16)
    betab = p.sb("qkbeta", [4, L], F32)
    sdq = p.sb("qksdq", [128, 512], F32)
    rno = p.sb("qkrno", [128, 512], F32)

    PLIST = ("q", "k", "v")
    SLOTS = [("q", pr, 2) for pr in range(4)] + [("k", pr, 2) for pr in range(4)] \
            + [("v", dto, 1) for dto in range(NDT)]

    def st_total(upto, par):
        return 16 * sum(nd for j, (_, _, nd) in enumerate(SLOTS)
                        if j <= upto and j % 2 == par)

    def slot_of(pi, dto):
        P = PLIST[pi]
        if P == "v":
            return 8 + dto
        return pi * 4 + dto // 2

    @blk.sync
    def _(q):
        q.wait_ge(p.bar_sem, bar0)
        for dt in range(NDT):
            q.dma_start(out=xns[:, dt * L:(dt + 1) * L],
                        in_=xn2d[dt * 128:(dt + 1) * 128, :]).then_inc(xlq, 16)
        for pi, P in enumerate(PLIST):
            q.dma_start(out=cvw[:, pi * 32:(pi + 1) * 32],
                        in_=csrc[P].rearrange("(dtp pp) k4 -> pp dtp k4", pp=128)
                        ).then_inc(cl, 16)

    @blk.gpsimd
    def _(g):
        g.wait_ge(p.bar_sem, bar0)
        g.memset(cbuf[0][:, 0:3], 0.0)
        g.memset(cbuf[1][:, 0:3], 0.0)
        g.engine_nop().then_inc(vcv, 1)
        g.dma_start(out=wbsb[:], in_=dr["w_wb"].rearrange("(dtp pp) e -> pp dtp e", pp=128)
                    ).then_inc(wlb, 16)
        g.dma_start(out=wcur[:],
                    in_=wsrc["q"].rearrange("(dtp pp) dd -> pp dtp dd", pp=128)
                    ).then_inc(wl, 16)

    @blk.tensor
    def _(t):
        t.wait_ge(p.bar_sem, bar0)
        t.wait_ge(xlq, 16 * NDT)
        t.wait_ge(wlb, 16)
        for c in range(8):
            if c >= 1:
                t.wait_ge(sbe, c)
            for dt in range(NDT):
                mm = t.matmul(p.PS[2][0:4, :], wbsb[:, dt * 4:(dt + 1) * 4],
                              xns[:, dt * L + c * 512: dt * L + (c + 1) * 512],
                              start=(dt == 0), stop=(dt == NDT - 1))
            mm.then_inc(tbe, 1)
        ng = 0
        npair = 0
        for pi, P in enumerate(PLIST):
            t.wait_ge(wl, 16 * (pi + 1))
            for dto in range(NDT):
                for c in range(8):
                    if ng >= 2:
                        t.wait_ge(vev, ng - 1)
                    for dt in range(NDT):
                        mm = t.matmul(p.PS[ng % 2][:, :],
                                      wcur[:, dt * 1024 + dto * 128: dt * 1024 + (dto + 1) * 128],
                                      xns[:, dt * L + c * 512: dt * L + (c + 1) * 512],
                                      start=(dt == 0), stop=(dt == NDT - 1))
                    mm.then_inc(tpr, 1)
                    ng += 1
                if P != "v" and dto % 2 == 1:
                    t.wait_ge(vsq, npair + 1)
                    for c in range(8):
                        if npair * 8 + c >= 1:
                            t.wait_ge(ssr, npair * 8 + c)
                        for u in range(2):
                            mm = t.matmul(p.PS[3][:, :], G["ones"][:],
                                          cnv[u][:, c * 512:(c + 1) * 512],
                                          start=(u == 0), stop=(u == 1))
                        mm.then_inc(tsq, 1)
                    npair += 1

    @blk.vector
    def _(v):
        v.wait_ge(p.bar_sem, bar0)
        v.wait_ge(cl, 48)
        ng = 0
        npair = 0
        for pi, P in enumerate(PLIST):
            for dto in range(NDT):
                j = pi * NDT + dto
                A = dto % 2
                if ng == 0:
                    v.wait_ge(vcv, 1)
                for c in range(8):
                    v.wait_ge(tpr, ng + 1)
                    ii = v.tensor_copy(cbuf[A][:, 3 + c * 512: 3 + (c + 1) * 512],
                                       p.PS[ng % 2][:, :])
                    v.engine_nop().then_inc(vev, 1)
                    ng += 1
                p.edge(ii, v, "qkcb")
                if j >= 1:
                    v.wait_ge(ssl, j)
                if npair >= 1:
                    v.wait_ge(tsq, 8 * npair)
                wt = lambda k: cvw[:, (pi * 8 + dto) * 4 + k: (pi * 8 + dto) * 4 + k + 1]
                other, me = cnv[1 - A], cnv[A]
                p.edge(v.tensor_scalar_mul(other[:], cbuf[A][:, 0:L], wt(0)), v, "qkc0")
                p.edge(v.scalar_tensor_tensor(me[:], cbuf[A][:, 1:1 + L], wt(1), other[:],
                                              ALU.mult, ALU.add), v, "qkc1")
                p.edge(v.scalar_tensor_tensor(other[:], cbuf[A][:, 2:2 + L], wt(2), me[:],
                                              ALU.mult, ALU.add), v, "qkc2")
                v.scalar_tensor_tensor(me[:], cbuf[A][:, 3:3 + L], wt(3), other[:],
                                       ALU.mult, ALU.add).then_inc(vcv, 1)
                if P != "v" and dto % 2 == 1:
                    v.wait_ge(ssl, j + 1)
                    v.tensor_mul(cnv[0][:], qh[:, 0:L], qh[:, 0:L])
                    v.tensor_mul(cnv[1][:], qh[:, L:2 * L], qh[:, L:2 * L]
                                 ).then_inc(vsq, 1)
                    for c in range(8):
                        v.wait_ge(ssr, npair * 8 + c + 1)
                        p.edge(v.reciprocal(rno[:], sdq[:]), v, "qkrn")
                        v.tensor_mul(qh[:, c * 512:(c + 1) * 512],
                                     qh[:, c * 512:(c + 1) * 512], rno[:])
                        v.tensor_mul(qh[:, L + c * 512: L + (c + 1) * 512],
                                     qh[:, L + c * 512: L + (c + 1) * 512], rno[:]
                                     ).then_inc(vsc, 1)
                    npair += 1

    @blk.scalar
    def _(s_):
        s_.wait_ge(p.bar_sem, bar0)
        for c in range(8):
            s_.wait_ge(tbe, c + 1)
            s_.activation(betab[:, c * 512:(c + 1) * 512], p.PS[2][0:4, :],
                          AF.Sigmoid).then_inc(sbe, 1)
        npair = 0
        for pi, P in enumerate(PLIST):
            for dto in range(NDT):
                j = pi * NDT + dto
                A = dto % 2
                s_.wait_ge(vcv, 1 + j + 1)
                sl = slot_of(pi, dto)
                prev = sl - 1
                if prev >= 0 and (P == "v" or dto % 2 == 0):
                    s_.wait_ge(st2[prev % 2], st_total(prev, prev % 2))
                s_.activation(qh[:, A * L:(A + 1) * L], cnv[A][:], AF.Silu
                              ).then_inc(ssl, 1)
                if P != "v" and dto % 2 == 1:
                    for c in range(8):
                        s_.wait_ge(tsq, npair * 8 + c + 1)
                        if npair * 8 + c >= 1:
                            s_.wait_ge(vsc, npair * 8 + c)
                        s_.activation(sdq[:], p.PS[3][:, :], AF.Sqrt
                                      ).then_inc(ssr, 1)
                    npair += 1

    @blk.gpsimd
    def _(g):
        npair = 0
        for sl, (P, idx, nd) in enumerate(SLOTS):
            if sl == 4:
                g.wait_ge(tpr, 64)
                g.dma_start(out=wcur[:],
                            in_=wsrc["k"].rearrange("(dtp pp) dd -> pp dtp dd", pp=128)
                            ).then_inc(wl, 16)
            if sl == 8:
                g.wait_ge(tpr, 128)
                g.dma_start(out=wcur[:],
                            in_=wsrc["v"].rearrange("(dtp pp) dd -> pp dtp dd", pp=128)
                            ).then_inc(wl, 16)
            if P == "v":
                dto = idx
                g.wait_ge(ssl, 16 + dto + 1)
                g.dma_start(out=outs["v"][dto * 128:(dto + 1) * 128, :],
                            in_=qh[:, (dto % 2) * L:(dto % 2 + 1) * L]
                            ).then_inc(st2[sl % 2], 16)
            else:
                g.wait_ge(vsc, npair * 8 + 8)
                for u in range(2):
                    dto = idx * 2 + u
                    g.dma_start(out=outs[P][dto * 128:(dto + 1) * 128, :],
                                in_=qh[:, u * L:(u + 1) * L]
                                ).then_inc(st2[sl % 2], 16)
                npair += 1
        g.wait_ge(sbe, 8)
        g.dma_start(out=dr["betad"], in_=betab[:]).then_inc(st2[0], 16)
        g.wait_ge(st2[0], st_total(15, 0) + 16)
        g.wait_ge(st2[1], st_total(15, 1))
        g.engine_nop().then_inc(p.bar_sem, 1)
    p.bar += 1


def ph_l1core(p, dr):
    nc, blk = p.nc, p.block
    G = p.G
    bar0 = p.bar
    tsm = p.sem("lctsm")
    ssm = p.sem("lcssm")
    vsm = p.sem("lcvsm")
    gl2 = [p.sem("lcgl_a", dma=True), p.sem("lcgl_b", dma=True)]
    ws2 = [p.sem("lcws_a", dma=True), p.sem("lcws_b", dma=True)]
    hl2 = [p.sem("lchl_a", dma=True), p.sem("lchl_b", dma=True)]
    sst2 = [p.sem("lcss_a", dma=True), p.sem("lcss_b", dma=True)]
    wol = p.sem("lcwol", dma=True)

    NG = NCK // 8
    qnd, knd, vsd, betad = dr["qnd"], dr["knd"], dr["vsd"], dr["betad"]
    wtd, utd, s2d = dr["wtd"], dr["utd"], dr["s2d"]

    Qg = p.sb("lcQ", [128, 4096], F32)
    Wg_ = p.sb("lcWg", [128, 4096], F32)
    Ug_ = p.sb("lcUg", [128, 4096], F32)
    Wg = [Wg_, Wg_]
    Ug = [Ug_, Ug_]
    Rall = p.sb("lcRall", [128, 512], F32)
    Pall = p.sb("lcPall", [128, 512], F32)
    aqk = p.sb("lcaqk", [128, L], F32)
    tmpg = p.sb("lctmpg", [128, 512], F32)
    Sdall = p.sb("lcSd", [128, 512], F32)
    SdPre = p.sb("lcSdP", [128, 512], F32)
    nSdPre = p.sb("lcnSdP", [128, 512], F32)
    av = p.sb("lcav", [128, 512], F32)

    jobs = []
    tensor_order = []

    def J(t=None, s=None, v=None, deps=(), tx=(), vx=(), sx=()):
        jid = len(jobs)
        jobs.append(dict(t=t, s=s, v=v, deps=tuple(deps), tx=list(tx),
                         vx=list(vx), sx=list(sx)))
        if t is not None:
            tensor_order.append(jid)
        return jid

    sub1 = ExitStack()
    Kg = [sub1.enter_context(nc.sbuf_tensor(f"lcK{i}", [128, 4096], F32)) for i in range(2)]
    Vg = sub1.enter_context(nc.sbuf_tensor("lcV", [128, 4096], F32))
    bb = sub1.enter_context(nc.sbuf_tensor("lcbb", [128, 4 * 512], F32))
    beg = sub1.enter_context(nc.sbuf_tensor("lcbeg", [4, 512], F32))
    Kb = [sub1.enter_context(nc.sbuf_tensor(f"lcKb{i}", [128, 1024], F32)) for i in range(2)]
    Vb = [sub1.enter_context(nc.sbuf_tensor(f"lcVb{i}", [128, 1024], F32)) for i in range(2)]
    KbT = [sub1.enter_context(nc.sbuf_tensor(f"lcKbT{i}", [64, 2048], F32)) for i in range(2)]
    VbT = [sub1.enter_context(nc.sbuf_tensor(f"lcVbT{i}", [64, 2048], F32)) for i in range(2)]
    sqg = sub1.enter_context(nc.sbuf_tensor("lcsqg", [128, 1024], F32))
    ind4 = sub1.enter_context(nc.sbuf_tensor("lcind4", [4, 512], F32))
    i4s = p.sem("lci4")
    TC = [{n: sub1.enter_context(nc.sbuf_tensor(f"lc{n}{i}", [64, 128], F32)) for n in
           ("Lst", "Bst", "L2", "B2", "L4", "B4", "L8", "B8", "L16", "B16",
            "L32", "R", "R2", "TT")} for i in range(2)]


    NLD1 = 25

    def gl_val(g):
        return 16 * NLD1 * (g // 2 + 1)

    def init_jobs():
        def v_(v, bank):
            return v.engine_nop()
        return J(v=v_)

    def bb_jobs(g):
        gp = g % 2
        ids = []
        for h in range(4):
            def t_(t, bank, h=h):
                return t.matmul(bank[:, :], ind4[:, h * 128:(h + 1) * 128], beg[0:4, :],
                                start=True, stop=True)
            def v_(v, bank, h=h):
                return v.tensor_copy(bb[:, h * 512:(h + 1) * 512], bank[:, :])
            ids.append(J(t=t_, v=v_, tx=[(gl2[gp], gl_val(g)), (i4s, 2)]))
        return ids

    def prep_jobs(q, j_init):
        pp, gp = q % 2, (q // 4) % 2
        ci0 = (2 * q) % 8
        ids = []
        for dt in range(NDT):
            def v_(v, bank, dt=dt, pp=pp, gp=gp, ci0=ci0):
                v.tensor_mul(Kb[pp][:, dt * 128:(dt + 1) * 128],
                             Kg[gp][:, dt * 512 + ci0 * 64: dt * 512 + ci0 * 64 + 128],
                             bb[:, (dt // 2) * 512 + ci0 * 64: (dt // 2) * 512 + ci0 * 64 + 128])
                return v.tensor_mul(Vb[pp][:, dt * 128:(dt + 1) * 128],
                             Vg[:, dt * 512 + ci0 * 64: dt * 512 + ci0 * 64 + 128],
                             bb[:, (dt // 2) * 512 + ci0 * 64: (dt // 2) * 512 + ci0 * 64 + 128])
            ids.append(J(v=v_, vx=[(gl2[gp], gl_val(q // 4))], deps=[j_init]))
        return ids

    def agram_jobs(q, prep_ids, j_init):
        pp, gp = q % 2, (q // 4) % 2
        ci0 = (2 * q) % 8
        def tA(t, bank):
            for cc in range(2):
                for dt in range(NDT):
                    mm = t.matmul(bank[0:64, cc * 64: cc * 64 + 64],
                                  Kb[pp][:, dt * 128 + cc * 64: dt * 128 + cc * 64 + 64],
                                  Kg[gp][:, dt * 512 + (ci0 + cc) * 64: dt * 512 + (ci0 + cc) * 64 + 64],
                                  start=(dt == 0), stop=(dt == NDT - 1))
            return mm
        def vA(v, bank):
            for cc in range(2):
                ii = v.tensor_mul(TC[pp]["Lst"][:, cc * 64: cc * 64 + 64],
                                  bank[0:64, cc * 64: cc * 64 + 64], G["mlow"][:])
            return ii
        def tB(t, bank):
            for cc in range(2):
                for dt in range(NDT):
                    mm = t.matmul(bank[0:64, cc * 64: cc * 64 + 64],
                                  Kg[gp][:, dt * 512 + (ci0 + cc) * 64: dt * 512 + (ci0 + cc) * 64 + 64],
                                  Kb[pp][:, dt * 128 + cc * 64: dt * 128 + cc * 64 + 64],
                                  start=(dt == 0), stop=(dt == NDT - 1))
            return mm
        def vB(v, bank):
            for cc in range(2):
                ii = v.tensor_mul(TC[pp]["Bst"][:, cc * 64: cc * 64 + 64],
                                  bank[0:64, cc * 64: cc * 64 + 64], G["mup"][:])
            return ii
        ja = J(t=tA, v=vA, deps=list(prep_ids) + [j_init])
        jb = J(t=tB, v=vB, deps=list(prep_ids) + [j_init])
        return ja, jb

    def tr_thunks(q, prep_ids):
        pp = q % 2
        thunks = []
        for dt in range(NDT):
            for srcb, dstb in ((Kb, KbT), (Vb, VbT)):
                def mk(srcb=srcb, dstb=dstb, dt=dt, pp=pp):
                    def t_(t, bank):
                        for cc in range(2):
                            mm = t.transpose(bank[0:64, cc * 128: cc * 128 + 128],
                                             srcb[pp][:, dt * 128 + cc * 64: dt * 128 + cc * 64 + 64],
                                             G["ident"][:])
                        return mm
                    def v_(v, bank):
                        return v.tensor_copy(dstb[pp][:, dt * 256:(dt + 1) * 256],
                                             bank[0:64, 0:256])
                    return J(t=t_, v=v_, deps=prep_ids)
                thunks.append(mk)
        return thunks

    def chain_thunks(q, ja, jb, j_init):
        pp = q % 2
        C = TC[pp]
        seq = [("L2", "Bst", "Lst"), ("B2", "Lst", "Bst"),
               ("L4", "B2", "L2"), ("B4", "L2", "B2"),
               ("L8", "B4", "L4"), ("B8", "L4", "B4"),
               ("L16", "B8", "L8"), ("B16", "L8", "B8"),
               ("L32", "B16", "L16")]
        dep_of = {"Lst": ja, "Bst": jb}
        state = {}
        thunks = []
        for nm, lh, rh in seq:
            def mk(nm=nm, lh=lh, rh=rh):
                def t_(t, bank):
                    for cc in range(2):
                        mm = t.matmul(bank[0:64, cc * 64: cc * 64 + 64],
                                      C[lh][:, cc * 64: cc * 64 + 64],
                                      C[rh][:, cc * 64: cc * 64 + 64],
                                      start=True, stop=True)
                    return mm
                def v_(v, bank):
                    return v.tensor_copy(C[nm][:], bank[0:64, 0:128])
                dep_of[nm] = J(t=t_, v=v_, deps=[dep_of[lh], dep_of[rh]])
                return dep_of[nm]
            thunks.append(mk)
        def mkR0():
            def vR0(v, bank):
                for cc in range(2):
                    ii = v.tensor_sub(C["R"][:, cc * 64: cc * 64 + 64],
                                      G["ident"][0:64, 0:64],
                                      C["Bst"][:, cc * 64: cc * 64 + 64])
                return ii
            state["jr"] = J(v=vR0, deps=[jb, j_init])
            return state["jr"]
        thunks.insert(2, mkR0)
        for lk in ("L2", "L4", "L8", "L16", "L32"):
            def mk(lk=lk):
                def t_(t, bank):
                    for cc in range(2):
                        mm = t.matmul(bank[0:64, cc * 64: cc * 64 + 64],
                                      C[lk][:, cc * 64: cc * 64 + 64],
                                      C["R"][:, cc * 64: cc * 64 + 64],
                                      start=True, stop=True)
                    return mm
                def v_(v, bank):
                    ii = v.tensor_add(C["R2"][:], bank[0:64, 0:128], C["R"][:])
                    p.edge(ii, v, "lcr")
                    return v.tensor_copy(C["R"][:], C["R2"][:])
                state["jr"] = J(t=t_, v=v_, deps=[state["jr"], dep_of[lk]])
                return state["jr"]
            thunks.append(mk)
        def mkT():
            def tT(t, bank):
                for cc in range(2):
                    mm = t.matmul(bank[0:64, cc * 64: cc * 64 + 64],
                                  C["Lst"][:, cc * 64: cc * 64 + 64],
                                  C["R"][:, cc * 64: cc * 64 + 64],
                                  start=True, stop=True)
                return mm
            def vT(v, bank):
                return v.scalar_tensor_tensor(C["TT"][:], bank[0:64, 0:128], -2.0,
                                              C["R"][:], ALU.mult, ALU.add)
            state["jt"] = J(t=tT, v=vT, deps=[state["jr"], ja])
            return state["jt"]
        thunks.append(mkT)
        return thunks, (lambda: state["jt"])

    def wu_jobs(q, jt, trids):
        g = q // 4
        pp, gp = q % 2, (q // 4) % 2
        ci0 = (2 * q) % 8
        wg, ug = Wg[g % 2], Ug[g % 2]
        C = TC[pp]
        ids = []
        for dt in range(NDT):
            for which in (0, 1):
                srcT = (KbT if which == 0 else VbT)
                dst = (wg if which == 0 else ug)
                acc = (Rall if which == 0 else Pall)
                def t_(t, bank, srcT=srcT, dt=dt, pp=pp):
                    for cc in range(2):
                        mm = t.matmul(bank[:, cc * 64: cc * 64 + 64],
                                      srcT[pp][:, dt * 256 + cc * 128: dt * 256 + (cc + 1) * 128],
                                      C["TT"][:, cc * 64: cc * 64 + 64],
                                      start=True, stop=True)
                    return mm
                def v_(v, bank, dst=dst, dt=dt, ci0=ci0, acc=acc, q=q, gp=gp):
                    sl = slice(dt * 512 + ci0 * 64, dt * 512 + ci0 * 64 + 128)
                    v.tensor_copy(dst[:, sl], bank[:, 0:128])
                    for cc in range(2):
                        i = 2 * q + cc
                        ii = v.scalar_tensor_tensor(
                            bank[:, 128 + cc * 64: 128 + cc * 64 + 64],
                            Kg[gp][:, dt * 512 + (ci0 + cc) * 64: dt * 512 + (ci0 + cc) * 64 + 64],
                            1.0, bank[:, cc * 64: cc * 64 + 64], ALU.mult, ALU.mult,
                            accum_out=acc[:, dt * 64 + (i % 64): dt * 64 + (i % 64) + 1])
                    return ii
                vx = []
                if g >= 1:
                    vx.append((ws2[(g - 1) % 2], 16 * 16 * ((g - 1) // 2 + 1)))
                ids.append(J(t=t_, v=v_, deps=[jt] + list(trids), vx=vx))
        return ids

    def aqk_jobs(g, j_init):
        gp = g % 2
        ids = []
        for qq in range(4):
            ci0 = qq * 2
            def vpre(v, bank, ci0=ci0, gp=gp):
                for dt in range(NDT):
                    ii = v.tensor_mul(sqg[:, dt * 128:(dt + 1) * 128],
                                 Qg[:, dt * 512 + ci0 * 64: dt * 512 + ci0 * 64 + 128],
                                 Kg[gp][:, dt * 512 + ci0 * 64: dt * 512 + ci0 * 64 + 128])
                return ii
            jpre = J(v=vpre, vx=[(gl2[gp], gl_val(g))])
            def t_(t, bank):
                for cc in range(2):
                    for dt in range(NDT):
                        mm = t.matmul(bank[:, cc * 64: cc * 64 + 64], G["ones"][:],
                                      sqg[:, dt * 128 + cc * 64: dt * 128 + cc * 64 + 64],
                                      start=(dt == 0), stop=(dt == NDT - 1))
                return mm
            def v_(v, bank, g=g, ci0=ci0):
                for cc in range(2):
                    ii = v.tensor_mul(aqk[:, g * 512 + (ci0 + cc) * 64: g * 512 + (ci0 + cc) * 64 + 64],
                                      bank[:, cc * 64: cc * 64 + 64], G["bmask"][:])
                return ii
            ids.append(J(t=t_, v=v_, deps=[jpre, j_init]))
        return ids

    j_init = init_jobs()
    prep_last = {}
    group_wu_last = {}
    pending = None
    NPAIR = NCK // 2
    for q in range(NPAIR):
        g = q // 4
        if q % 4 == 0:
            bbids = bb_jobs(g)
            aqk_jobs(g, j_init)
        pr = prep_jobs(q, j_init)
        ja, jb = agram_jobs(q, pr, j_init)
        trth = tr_thunks(q, pr)
        chth, get_jt = chain_thunks(q, ja, jb, j_init)
        trids = []
        k1 = k2 = 0
        while k1 < len(trth) or k2 < len(chth):
            if k1 < len(trth):
                trids.append(trth[k1]())
                k1 += 1
            if k2 < len(chth):
                chth[k2]()
                k2 += 1
        if pending is not None:
            qp, jtp, trp = pending
            wids = wu_jobs(qp, jtp, trp)
            group_wu_last[qp // 4] = wids[-1]
        pending = (q, get_jt(), trids)
        prep_last[g] = max(pr)
    qp, jtp, trp = pending
    wids = wu_jobs(qp, jtp, trp)
    group_wu_last[NG - 1] = wids[-1]

    def vscan(v, bank):
        ii = v.tensor_scalar(av[:], Rall[:], -1.0, 1.0, ALU.mult, ALU.add)
        p.edge(ii, v, "lcsc")
        for dt in range(NDT):
            sl = slice(dt * 64, (dt + 1) * 64)
            ii = v.tensor_tensor_scan(Sdall[:, sl], av[:, sl], Pall[:, sl],
                                      0.0, ALU.mult, ALU.add)
        p.edge(ii, v, "lcsc2")
        ii = v.memset(SdPre[:], 0.0)
        p.edge(ii, v, "lcscm")
        for dt in range(NDT):
            ii = v.tensor_copy(SdPre[:, dt * 64 + 1: dt * 64 + 64],
                               Sdall[:, dt * 64: dt * 64 + 63])
        p.edge(ii, v, "lcsc3")
        return v.tensor_scalar_mul(nSdPre[:], SdPre[:], -1.0)
    j_scan = J(v=vscan, deps=[group_wu_last[NG - 1]])

    sub1.close()

    ub = p.sb("lcub", [128, 4096], F32)
    tmp3 = p.sb("lctmp3", [128, 4096], F32)
    ob = p.sb("lcob", [128, 4096], F32)
    onb = p.sb("lconb", [128, 4096], BF16)
    wo_bf = p.sb("lcwo", [128, NDT * 1024], BF16)
    sdq = [p.sb(f"lcsdq{i}", [128, 512], F32) for i in range(2)]
    rno = p.sb("lcrno", [128, 512], F32)
    s2st = [p.sb(f"lcs2{i}", [128, 512], F32) for i in range(2)]

    NLD3 = 24

    def hl_val(g):
        return 16 * NLD3 * (g // 2 + 1)

    s2_jobs = {}
    g3_last = {}
    rms_hist = []
    for g in range(NG):
        gp = g % 2
        def vel(v, bank, g=g, gp=gp):
            for dt in range(NDT):
                for ci in range(8):
                    i = g * 8 + ci
                    sl = slice(dt * 512 + ci * 64, dt * 512 + ci * 64 + 64)
                    v.scalar_tensor_tensor(ub[:, sl], Wg[gp][:, sl],
                                           nSdPre[:, dt * 64 + i % 64: dt * 64 + i % 64 + 1],
                                           Ug[gp][:, sl], ALU.mult, ALU.add)
            ii = v.engine_nop()
            p.edge(ii, v, "lce1")
            for dt in range(NDT):
                for ci in range(8):
                    sl = slice(dt * 512 + ci * 64, dt * 512 + ci * 64 + 64)
                    v.tensor_mul(tmp3[:, sl],
                                 aqk[:, g * 512 + ci * 64: g * 512 + ci * 64 + 64],
                                 ub[:, sl])
            ii = v.engine_nop()
            p.edge(ii, v, "lce2")
            for dt in range(NDT):
                for ci in range(8):
                    i = g * 8 + ci
                    sl = slice(dt * 512 + ci * 64, dt * 512 + ci * 64 + 64)
                    ii = v.scalar_tensor_tensor(ob[:, sl], Qg[:, sl],
                                                SdPre[:, dt * 64 + i % 64: dt * 64 + i % 64 + 1],
                                                tmp3[:, sl], ALU.mult, ALU.add)
            return ii
        j_el = J(v=vel, deps=[j_scan], vx=[(hl2[gp], hl_val(g))])
        last_rms = None
        for h in range(4):
            def vsq_(v, bank, h=h):
                v.tensor_mul(tmp3[:, (2 * h) * 512:(2 * h) * 512 + 512],
                             ob[:, (2 * h) * 512:(2 * h + 1) * 512],
                             ob[:, (2 * h) * 512:(2 * h + 1) * 512])
                return v.tensor_mul(tmp3[:, (2 * h + 1) * 512:(2 * h + 1) * 512 + 512],
                             ob[:, (2 * h + 1) * 512:(2 * h + 2) * 512],
                             ob[:, (2 * h + 1) * 512:(2 * h + 2) * 512])
            j_sq = J(v=vsq_, deps=[j_el])
            rp = len(rms_hist) % 2
            sx = []
            if len(rms_hist) >= 2:
                sx.append((vsm, rms_hist[-2] + 1))
            def t_(t, bank, h=h):
                for u in range(2):
                    mm = t.matmul(bank[:, :], G["ones"][:],
                                  tmp3[:, (2 * h + u) * 512:(2 * h + u) * 512 + 512],
                                  start=(u == 0), stop=(u == 1))
                return mm
            def s_(s_, bank, rp=rp):
                return s_.activation(sdq[rp][:], bank[:, :], AF.Sqrt,
                                     bias=G["eps_rms"][:, 0:1], scale=1.0 / HD)
            def v_(v, bank, h=h, rp=rp):
                rr = v.reciprocal(rno[:], sdq[rp][:])
                p.edge(rr, v, "lcrn")
                for u in range(2):
                    dt = 2 * h + u
                    ii = v.scalar_tensor_tensor(onb[:, dt * 512:(dt + 1) * 512],
                                                ob[:, dt * 512:(dt + 1) * 512],
                                                G["gvec"][:, dt:dt + 1], rno[:],
                                                ALU.mult, ALU.mult)
                return ii
            jr = J(t=t_, s=s_, v=v_, deps=[j_sq], sx=sx)
            rms_hist.append(jr)
            last_rms = jr
        for dto in range(NDT):
            spar = (g * NDT + dto) % 2
            vx = []
            if g * NDT + dto >= 2:
                vx.append((sst2[spar], 16 * ((g * NDT + dto) // 2)))
            def t_(t, bank, dto=dto):
                for dt in range(NDT):
                    mm = t.matmul(bank[:, :],
                                  wo_bf[:, dt * 1024 + dto * 128: dt * 1024 + (dto + 1) * 128],
                                  onb[:, dt * 512:(dt + 1) * 512],
                                  start=(dt == 0), stop=(dt == NDT - 1))
                return mm
            def v_(v, bank, spar=spar):
                return v.tensor_copy(s2st[spar][:], bank[:, :])
            jw = J(t=t_, v=v_, deps=[last_rms], tx=[(wol, 16)], vx=vx)
            s2_jobs[(g, dto)] = jw
        g3_last[g] = len(jobs) - 1

    ntj = {}
    tc = 0
    for jid, jb_ in enumerate(jobs):
        if jb_["t"] is not None:
            tc += 1
        ntj[jid] = tc
    nsj = {}
    sc = 0
    for jid, jb_ in enumerate(jobs):
        if jb_["s"] is not None:
            sc += 1
        nsj[jid] = sc

    RING = [1, 2, 3, 4, 5, 6]
    def bank(jid):
        return p.PS[RING[(ntj[jid] - 1) % len(RING)]]

    @blk.sync
    def _(q):
        q.wait_ge(p.bar_sem, bar0)
        for g in range(NG):
            gp = g % 2
            if g >= 1:
                q.wait_ge(vsm, prep_last[g - 1] + 1)
            if g >= 2:
                q.wait_ge(vsm, group_wu_last[g - 2] + 1)
            for dt in range(NDT):
                q.dma_start(out=Qg[:, dt * 512:(dt + 1) * 512],
                            in_=qnd[dt * 128:(dt + 1) * 128, g * 512:(g + 1) * 512]
                            ).then_inc(gl2[gp], 16)
            for dt in range(NDT):
                q.dma_start(out=Kg[gp][:, dt * 512:(dt + 1) * 512],
                            in_=knd[dt * 128:(dt + 1) * 128, g * 512:(g + 1) * 512]
                            ).then_inc(gl2[gp], 16)
            for dt in range(NDT):
                q.dma_start(out=Vg[:, dt * 512:(dt + 1) * 512],
                            in_=vsd[dt * 128:(dt + 1) * 128, g * 512:(g + 1) * 512]
                            ).then_inc(gl2[gp], 16)
            q.dma_start(out=beg[:], in_=betad[:, g * 512:(g + 1) * 512]
                        ).then_inc(gl2[gp], 16)
        q.wait_ge(vsm, j_scan + 1)
        q.wait_ge(ws2[0], 16 * 16 * 4)
        q.wait_ge(ws2[1], 16 * 16 * 4)
        for g in range(NG):
            gp = g % 2
            if g >= 1:
                q.wait_ge(vsm, g3_last[g - 1] + 1)
            for dt in range(NDT):
                q.dma_start(out=Qg[:, dt * 512:(dt + 1) * 512],
                            in_=qnd[dt * 128:(dt + 1) * 128, g * 512:(g + 1) * 512]
                            ).then_inc(hl2[gp], 16)
            for dt in range(NDT):
                q.dma_start(out=Wg[gp][:, dt * 512:(dt + 1) * 512],
                            in_=wtd[dt * 128:(dt + 1) * 128, g * 512:(g + 1) * 512]
                            ).then_inc(hl2[gp], 16)
            for dt in range(NDT):
                q.dma_start(out=Ug[gp][:, dt * 512:(dt + 1) * 512],
                            in_=utd[dt * 128:(dt + 1) * 128, g * 512:(g + 1) * 512]
                            ).then_inc(hl2[gp], 16)

    @blk.gpsimd
    def _(g_):
        g_.wait_ge(p.bar_sem, bar0)
        g_.iota(ind4[:], [[1, 4], [0, 128]], channel_multiplier=-1,
                allow_small_or_imprecise_dtypes=True).then_inc(i4s, 1)
        for g in range(NG):
            gp = g % 2
            g_.wait_ge(vsm, group_wu_last[g] + 1)
            for dt in range(NDT):
                g_.dma_start(out=wtd[dt * 128:(dt + 1) * 128, g * 512:(g + 1) * 512],
                             in_=Wg[gp][:, dt * 512:(dt + 1) * 512]).then_inc(ws2[gp], 16)
            for dt in range(NDT):
                g_.dma_start(out=utd[dt * 128:(dt + 1) * 128, g * 512:(g + 1) * 512],
                             in_=Ug[gp][:, dt * 512:(dt + 1) * 512]).then_inc(ws2[gp], 16)
        g_.wait_ge(vsm, j_scan + 1)
        g_.dma_start(out=wo_bf[:],
                     in_=dr["w_wo"].rearrange("(dtp pp) dd -> pp dtp dd", pp=128)
                     ).then_inc(wol, 16)
        nst = 0
        for g in range(NG):
            for dto in range(NDT):
                spar = nst % 2
                g_.wait_ge(vsm, s2_jobs[(g, dto)] + 1)
                g_.dma_start(out=s2d[dto * 128:(dto + 1) * 128, g * 512:(g + 1) * 512],
                             in_=s2st[spar][:]).then_inc(sst2[spar], 16)
                nst += 1
        g_.wait_ge(sst2[0], 16 * 32)
        g_.wait_ge(sst2[1], 16 * 32)
        g_.engine_nop().then_inc(p.bar_sem, 1)

    @blk.tensor
    def _(t):
        t.wait_ge(p.bar_sem, bar0)
        for mo, jid in enumerate(tensor_order):
            jb_ = jobs[jid]
            if mo >= len(RING):
                t.wait_ge(vsm, tensor_order[mo - len(RING)] + 1)
            if jb_["deps"]:
                t.wait_ge(vsm, max(jb_["deps"]) + 1)
            for sem, val in jb_["tx"]:
                t.wait_ge(sem, val)
            mm = jb_["t"](t, bank(jid))
            mm.then_inc(tsm, 1)

    @blk.scalar
    def _(s_):
        s_.wait_ge(p.bar_sem, bar0)
        for jid, jb_ in enumerate(jobs):
            if jb_["s"] is None:
                continue
            s_.wait_ge(tsm, ntj[jid])
            for sem, val in jb_["sx"]:
                s_.wait_ge(sem, val)
            ii = jb_["s"](s_, bank(jid))
            ii.then_inc(ssm, 1)

    @blk.vector
    def _(v):
        v.wait_ge(p.bar_sem, bar0)
        v.wait_ge(i4s, 1)
        v.tensor_single_scalar(ind4[:], ind4[:], 0.0, ALU.is_equal).then_inc(i4s, 1)
        for jid, jb_ in enumerate(jobs):
            if jb_["t"] is not None:
                v.wait_ge(tsm, ntj[jid])
            if jb_["s"] is not None:
                v.wait_ge(ssm, nsj[jid])
            if jb_["t"] is None and jb_["deps"]:
                v.wait_ge(vsm, max(jb_["deps"]) + 1)
            for sem, val in jb_["vx"]:
                v.wait_ge(sem, val)
            ii = jb_["v"](v, bank(jid) if jb_["t"] is not None else None)
            ii.then_inc(vsm, 1)
    p.bar += 1


def ph_route(p, dr):
    nc, blk = p.nc, p.block
    G = p.G
    bar0 = p.bar
    affd, cwtd = dr["affd"], dr["cwtd"]
    al2 = [p.sem("rtal_a", dma=True), p.sem("rtal_b", dma=True)]
    vbi = p.sem("rtvbi")
    vcn = p.sem("rtvcn")
    sgs = p.sem("rtsgs")
    ttr = p.sem("rtttr")
    vtr = p.sem("rtvtr")
    st2 = [p.sem("rtst_a", dma=True), p.sem("rtst_b", dma=True)]

    NT = L // 128
    afb = [p.sb(f"rtaf{i}", [128, NE], F32) for i in range(2)]
    bia = p.sb("rtbia", [128, NE], F32)
    cbuf = p.sb("rtcb", [128, NE * NE], F32)
    red = p.sb("rtred", [128, 2048], F32)
    sig = p.sb("rtsig", [128, NE], F32)
    msk = p.sb("rtmsk", [128, NE], F32)
    wsu = p.sb("rtwsu", [128, 1], F32)
    wiv = p.sb("rtwiv", [128, 1], F32)
    cwt = [p.sb(f"rtcw{i}", [128, NE], F32) for i in range(2)]
    cwT = p.sb("rtcwT", [64, 128 * 2], F32)

    @blk.sync
    def _(q):
        q.wait_ge(p.bar_sem, bar0)
        for tt in range(NT):
            A = tt % 2
            if tt >= 2:
                q.wait_ge(sgs, tt - 1)
                q.wait_ge(vcn, tt - 1)
            q.dma_start(out=afb[A][:], in_=affd[tt * 128:(tt + 1) * 128, :]
                        ).then_inc(al2[A], 16)

    @blk.vector
    def _(v):
        v.wait_ge(p.bar_sem, bar0)
        for tt in range(NT):
            A = tt % 2
            v.wait_ge(al2[A], 16 * (tt // 2 + 1))
            ii = v.tensor_add(bia[:], afb[A][:], G["ebias"][:, 0:NE])
            p.edge(ii, v, "rt1")
            for j in range(NE):
                ii = v.tensor_single_scalar(cbuf[:, j * NE:(j + 1) * NE], bia[:],
                                            bia[:, j:j + 1], ALU.is_lt)
            p.edge(ii, v, "rt2")
            width = NE
            srcbuf = cbuf
            while width > 1:
                half = width // 2
                for j in range(half):
                    ii = v.tensor_add(red[:, j * NE:(j + 1) * NE],
                                      srcbuf[:, j * NE:(j + 1) * NE],
                                      srcbuf[:, (j + half) * NE:(j + half + 1) * NE])
                p.edge(ii, v, "rt3")
                srcbuf = red
                width = half
            cnt = red[:, 0:NE]
            ii = v.tensor_single_scalar(msk[:], cnt, float(TOPK), ALU.is_lt)
            p.edge(ii, v, "rt4")
            v.wait_ge(sgs, tt + 1)
            ii = v.tensor_mul(sig[:], sig[:], msk[:])
            p.edge(ii, v, "rt5")
            ii = v.reduce_sum(wsu[:], sig[:], AX.X)
            p.edge(ii, v, "rt6")
            ii = v.tensor_scalar_add(wsu[:], wsu[:], 1e-8)
            p.edge(ii, v, "rt7")
            ii = v.reciprocal(wiv[:], wsu[:])
            p.edge(ii, v, "rt8")
            if tt >= 2:
                v.wait_ge(ttr, tt - 1)
            v.tensor_scalar_mul(cwt[A][:], sig[:], wiv[:, 0:1]).then_inc(vcn, 1)
            v.wait_ge(ttr, tt + 1)
            if tt >= 2:
                v.wait_ge(st2[A], 16 * (tt // 2))
            v.tensor_copy(cwT[:, (A) * 128:(A + 1) * 128],
                          p.PS[4 + A][0:64, 0:128]).then_inc(vtr, 1)

    @blk.scalar
    def _(s_):
        s_.wait_ge(p.bar_sem, bar0)
        for tt in range(NT):
            A = tt % 2
            s_.wait_ge(al2[A], 16 * (tt // 2 + 1))
            if tt >= 1:
                s_.wait_ge(vcn, tt)
            s_.activation(sig[:], afb[A][:], AF.Sigmoid).then_inc(sgs, 1)

    @blk.tensor
    def _(t):
        t.wait_ge(p.bar_sem, bar0)
        for tt in range(NT):
            A = tt % 2
            t.wait_ge(vcn, tt + 1)
            if tt >= 2:
                t.wait_ge(vtr, tt - 1)
            t.transpose(p.PS[4 + A][0:64, 0:128], cwt[A][:, 0:64] if False else cwt[A][:],
                        G["ident"][:]).then_inc(ttr, 1)

    @blk.gpsimd
    def _(g):
        for tt in range(NT):
            A = tt % 2
            g.wait_ge(vtr, tt + 1)
            g.dma_start(out=cwtd[:, tt * 128:(tt + 1) * 128],
                        in_=cwT[0:64, A * 128:(A + 1) * 128]).then_inc(st2[A], 16)
        g.wait_ge(st2[0], 16 * NT // 2)
        g.wait_ge(st2[1], 16 * NT // 2)
        g.engine_nop().then_inc(p.bar_sem, 1)
    p.bar += 1


def ph_moe(p, dr):
    nc, blk = p.nc, p.block
    G = p.G
    bar0 = p.bar
    xn2d, x1d, s2d, outd = dr["xn2d"], dr["x1d"], dr["s2d"], dr["outd"]
    xnl = p.sem("moxnl", dma=True)
    one = p.sem("moone", dma=True)
    wl2 = [p.sem("mowl_a", dma=True), p.sem("mowl_b", dma=True)]
    ie2 = [p.sem("moie_a"), p.sem("moie_b")]
    iv2 = [p.sem("moiv_a"), p.sem("moiv_b")]
    tup = p.sem("motup")
    ssl = p.sem("mossl")
    tcb = p.sem("motcb")
    vhs = p.sem("movhs")
    tdn = p.sem("motdn")
    tda = p.sem("motda")
    vev = p.sem("movev")
    xls2 = [p.sem("moxl_a", dma=True), p.sem("moxl_b", dma=True)]
    ssc = p.sem("mossc")
    vcb = p.sem("movcb")
    ost2 = [p.sem("moos_a", dma=True), p.sem("moos_b", dma=True)]

    NHF = 2
    HT = L // 2
    NCH = HT // 256
    GE = 2
    NGRP = NE // GE
    UPH = NCH + NGRP * NCH * GE
    DNH = NCH + NGRP * NCH

    xnh = p.sb("moxn", [128, NDT * HT], BF16)
    acc = p.sb("moacc", [128, NDT * HT], F32)
    wup = [p.sb(f"mowu{i}", [128, GE * 4096], BF16) for i in range(2)]
    wdn = [p.sb(f"mowd{i}", [128, GE * 4096], BF16) for i in range(2)]
    wsu_bf = p.sb("mowsu", [128, NDT * 512], BF16)
    wsd_bf = p.sb("mowsd", [128, 4 * 1024], BF16)
    cwTb = p.sb("mocwT", [64, L], BF16)
    inde = [p.sb(f"moind{i}", [64, 128], BF16) for i in range(2)]
    indf = [p.sb(f"moindf{i}", [64, 128], BF16) for i in range(2)]
    hb = [p.sb(f"mohb{i}", [128, 4 * 256], BF16) for i in range(2)]
    hsc = [p.sb(f"mohs{i}", [128, 4 * 256], BF16) for i in range(2)]
    xsb = [p.sb(f"moxs{i}", [128, 2 * 256], F32) for i in range(2)]
    slb = [p.sb(f"mosl{i}", [128, 2 * 256], F32) for i in range(2)]
    otb = [p.sb(f"moot{i}", [128, 2 * 256], F32) for i in range(2)]

    def up_bank(ft):
        return p.PS[4 + ft // 2], (ft % 2) * 256
    def dn_bank(dto):
        return p.PS[dto // 2], (dto % 2) * 256
    def upcount_through_group(gi):
        H, grp = divmod(gi, NGRP)
        return UPH * H + NCH + (grp + 1) * NCH * GE
    def dncount_through_group(gi):
        H, grp = divmod(gi, NGRP)
        return DNH * H + NCH + (grp + 1) * NCH

    @blk.gpsimd
    def _(g):
        g.wait_ge(p.bar_sem, bar0)
        g.dma_start(out=wsu_bf[:],
                    in_=dr["w_wsu"].rearrange("(dtp pp) f -> pp dtp f", pp=128)
                    ).then_inc(one, 16)
        g.dma_start(out=wsd_bf[:],
                    in_=dr["w_wsd"].rearrange("(ftp pp) dd -> pp ftp dd", pp=128)
                    ).then_inc(one, 16)
        g.dma_start(out=cwTb[:], in_=dr["cwtd"]).then_inc(one, 16)
        for gi in range(NHF * NGRP):
            sl = gi % 2
            grp = gi % NGRP
            if gi >= 2:
                g.wait_ge(tup, upcount_through_group(gi - 2))
                g.wait_ge(tdn, dncount_through_group(gi - 2))
            for u in range(GE):
                e = grp * GE + u
                g.dma_start(out=wup[sl][:, u * 4096:(u + 1) * 4096],
                            in_=dr["w_wup"][e].rearrange("(dtp pp) f -> pp dtp f", pp=128)
                            ).then_inc(wl2[sl], 16)
                g.dma_start(out=wdn[sl][:, u * 4096:(u + 1) * 4096],
                            in_=dr["w_wdn"][e].rearrange("(ftp pp) dd -> pp ftp dd", pp=128)
                            ).then_inc(wl2[sl], 16)
            for u in range(GE):
                e = grp * GE + u
                ei = gi * GE + u
                par = ei % 2
                if ei >= 2:
                    g.wait_ge(tcb, NCH * (ei - 1))
                    g.wait_ge(iv2[par], ei // 2)
                g.iota(indf[par][:], [[0, 128]], base=e, channel_multiplier=-1,
                       allow_small_or_imprecise_dtypes=True).then_inc(ie2[par], 1)

    @blk.gpsimd
    def _(g):
        g.wait_ge(ost2[0], 16 * 2 * 32)
        g.wait_ge(ost2[1], 16 * 2 * 32)
        g.engine_nop().then_inc(p.bar_sem, 1)

    @blk.sync
    def _(q):
        q.wait_ge(p.bar_sem, bar0)
        for H in range(NHF):
            if H >= 1:
                q.wait_ge(tup, UPH * H)
            for dt in range(NDT):
                q.dma_start(out=xnh[:, dt * HT:(dt + 1) * HT],
                            in_=xn2d[dt * 128:(dt + 1) * 128, H * HT:(H + 1) * HT]
                            ).then_inc(xnl, 16)
            for st in range(32):
                ns = H * 32 + st
                par = ns % 2
                ch, dq = divmod(st, 4)
                if ns >= 2:
                    q.wait_ge(ost2[par], 16 * 2 * (ns // 2))
                for i in range(2):
                    dt = dq * 2 + i
                    q.dma_start(out=xsb[par][:, i * 256:(i + 1) * 256],
                                in_=x1d[dt * 128:(dt + 1) * 128,
                                        H * HT + ch * 256: H * HT + (ch + 1) * 256]
                                ).then_inc(xls2[par], 16)
                for i in range(2):
                    dt = dq * 2 + i
                    q.dma_start(out=otb[par][:, i * 256:(i + 1) * 256],
                                in_=s2d[dt * 128:(dt + 1) * 128,
                                        H * HT + ch * 256: H * HT + (ch + 1) * 256]
                                ).then_inc(xls2[par], 16)

    @blk.tensor
    def _(t):
        t.wait_ge(p.bar_sem, bar0)
        t.wait_ge(one, 48)
        nup = 0
        ncb = 0
        ndn = 0
        for H in range(NHF):
            t.wait_ge(xnl, 16 * NDT * (H + 1))
            for ch in range(NCH):
                if nup >= 1:
                    t.wait_ge(ssl, nup)
                for ft in range(4):
                    bank, co = up_bank(ft)
                    for dt in range(NDT):
                        mm = t.matmul(bank[:, co:co + 256],
                                      wsu_bf[:, dt * 512 + ft * 128: dt * 512 + (ft + 1) * 128],
                                      xnh[:, dt * HT + ch * 256: dt * HT + (ch + 1) * 256],
                                      start=(dt == 0), stop=(dt == NDT - 1))
                mm.then_inc(tup, 1)
                nup += 1
                t.wait_ge(ssl, nup)
                t.wait_ge(vev, ndn)
                for dto in range(NDT):
                    bank, co = dn_bank(dto)
                    for ft in range(4):
                        mm = t.matmul(bank[:, co:co + 256],
                                      wsd_bf[:, ft * 1024 + dto * 128: ft * 1024 + (dto + 1) * 128],
                                      hb[(nup - 1) % 2][:, ft * 256:(ft + 1) * 256],
                                      start=(ft == 0 and dto % 2 == 0),
                                      stop=(ft == 3 and dto % 2 == 1))
                mm.then_inc(tdn, 1)
                ndn += 1
            pending = None

            def emit_down(t, job):
                nonlocal ndn
                sl_, u_, par_, nhs_ = job
                t.wait_ge(vhs, nhs_)
                if u_ == 0:
                    t.wait_ge(vev, ndn)
                for dto in range(NDT):
                    bank, co = dn_bank(dto)
                    for ft in range(4):
                        mm = t.matmul(bank[:, co:co + 256],
                                      wdn[sl_][:, u_ * 4096 + ft * 1024 + dto * 128:
                                              u_ * 4096 + ft * 1024 + (dto + 1) * 128],
                                      hsc[par_][:, ft * 256:(ft + 1) * 256],
                                      start=(u_ == 0 and ft == 0 and dto % 2 == 0),
                                      stop=(u_ == GE - 1 and ft == 3 and dto % 2 == 1))
                if u_ == GE - 1:
                    mm.then_inc(tdn, 1)
                    ndn += 1
                else:
                    mm.then_inc(tda, 1)

            for grp in range(NGRP):
                gi = H * NGRP + grp
                sl = gi % 2
                t.wait_ge(wl2[sl], 16 * 2 * GE * (gi // 2 + 1))
                for ch in range(NCH):
                    for u in range(GE):
                        e = grp * GE + u
                        ei = H * NE + e
                        if nup >= 1:
                            t.wait_ge(ssl, nup)
                        for ft in range(4):
                            bank, co = up_bank(ft)
                            for dt in range(NDT):
                                mm = t.matmul(bank[:, co:co + 256],
                                              wup[sl][:, u * 4096 + dt * 512 + ft * 128:
                                                      u * 4096 + dt * 512 + (ft + 1) * 128],
                                              xnh[:, dt * HT + ch * 256: dt * HT + (ch + 1) * 256],
                                              start=(dt == 0), stop=(dt == NDT - 1))
                        mm.then_inc(tup, 1)
                        nup += 1
                        if pending is not None:
                            emit_down(t, pending)
                        t.wait_ge(iv2[ei % 2], ei // 2 + 1)
                        t.wait_ge(vhs, ncb)
                        t.matmul(p.PS[6][:, 0:256], inde[ei % 2][:],
                                 cwTb[0:64, H * HT + ch * 256: H * HT + (ch + 1) * 256],
                                 start=True, stop=True).then_inc(tcb, 1)
                        ncb += 1
                        pending = (sl, u, ei % 2, ncb)
                if grp == NGRP - 1:
                    emit_down(t, pending)
                    pending = None

    @blk.scalar
    def _(s_):
        s_.wait_ge(p.bar_sem, bar0)
        nup = 0
        hb_consumer = [None, None]
        ndn_mirror = 0
        nhs = 0
        for H in range(NHF):
            for ch in range(NCH):
                par = nup % 2
                s_.wait_ge(tup, nup + 1)
                if hb_consumer[par] is not None:
                    kind, val = hb_consumer[par]
                    s_.wait_ge(tdn if kind == "tdn" else vhs, val)
                ii = None
                for half2 in range(2):
                    ii = s_.activation(hb[par][:, half2 * 512:(half2 + 1) * 512],
                                       p.PS[4 + half2][:, :], AF.Silu)
                ii.then_inc(ssl, 1)
                ndn_mirror += 1
                hb_consumer[par] = ("tdn", ndn_mirror)
                nup += 1
            for grp in range(NGRP):
                for ch in range(NCH):
                    for u in range(GE):
                        par = nup % 2
                        s_.wait_ge(tup, nup + 1)
                        if hb_consumer[par] is not None:
                            kind, val = hb_consumer[par]
                            s_.wait_ge(tdn if kind == "tdn" else vhs, val)
                        ii = None
                        for half2 in range(2):
                            ii = s_.activation(hb[par][:, half2 * 512:(half2 + 1) * 512],
                                               p.PS[4 + half2][:, :], AF.Silu)
                        ii.then_inc(ssl, 1)
                        nhs += 1
                        hb_consumer[par] = ("vhs", nhs)
                        nup += 1
                    ndn_mirror += 1
            def emit_store(ns2):
                par2 = ns2 % 2
                H2, st2_ = divmod(ns2, 32)
                ch2, dq2 = divmod(st2_, 4)
                s_.wait_ge(vcb, ns2 + 1)
                for i2 in range(2):
                    dt2 = dq2 * 2 + i2
                    s_.dma_start(out=outd[dt2 * 128:(dt2 + 1) * 128,
                                          H2 * HT + ch2 * 256: H2 * HT + (ch2 + 1) * 256],
                                 in_=otb[par2][:, i2 * 256:(i2 + 1) * 256]
                                 ).then_inc(ost2[par2], 16)
            for st in range(32):
                ns = H * 32 + st
                ch, dq = divmod(st, 4)
                if st == 0:
                    s_.wait_ge(vev, DNH * (H + 1))
                if ns >= 2:
                    s_.wait_ge(vcb, ns - 1)
                ii = None
                for i in range(2):
                    dt = dq * 2 + i
                    ii = s_.activation(slb[ns % 2][:, i * 256:(i + 1) * 256],
                                       acc[:, dt * HT + ch * 256: dt * HT + (ch + 1) * 256],
                                       AF.Silu)
                ii.then_inc(ssc, 1)
                if ns >= 1:
                    emit_store(ns - 1)
            if H == NHF - 1:
                emit_store(NHF * 32 - 1)

    @blk.vector
    def _(v):
        v.wait_ge(p.bar_sem, bar0)
        nup = 0
        ncb = 0
        nev = 0
        nhs = 0
        for H in range(NHF):
            for ch in range(NCH):
                nup += 1
                if H >= 1 and ch == 0:
                    v.wait_ge(ssc, 32 * H)
                v.wait_ge(tdn, nev + 1)
                ii = None
                for dto in range(NDT):
                    bank, co = dn_bank(dto)
                    ii = v.tensor_copy(acc[:, dto * HT + ch * 256: dto * HT + (ch + 1) * 256],
                                       bank[:, co:co + 256])
                p.edge(ii, v, "moev")
                v.engine_nop().then_inc(vev, 1)
                nev += 1
            for grp in range(NGRP):
                for ch in range(NCH):
                    for u in range(GE):
                        e = grp * GE + u
                        ei = H * NE + e
                        if ch == 0:
                            v.wait_ge(ie2[ei % 2], ei // 2 + 1)
                            v.tensor_single_scalar(inde[ei % 2][:], indf[ei % 2][:],
                                                   0.0, ALU.is_equal
                                                   ).then_inc(iv2[ei % 2], 1)
                        v.wait_ge(ssl, nup + 1)
                        v.wait_ge(tcb, ncb + 1)
                        ii_h = nhs - NE * NCH * H
                        if ii_h >= 2:
                            if ii_h % 2 == 0:
                                v.wait_ge(tda, NE * NCH * H // 2 + ii_h // 2)
                            else:
                                v.wait_ge(tdn, DNH * H + NCH + (ii_h - 1) // 2)
                        par = nup % 2
                        ii = None
                        for ft in range(4):
                            ii = v.tensor_mul(hsc[ei % 2][:, ft * 256:(ft + 1) * 256],
                                              hb[par][:, ft * 256:(ft + 1) * 256],
                                              p.PS[6][:, 0:256])
                        ii.then_inc(vhs, 1)
                        nup += 1
                        ncb += 1
                        nhs += 1
                    v.wait_ge(tdn, nev + 1)
                    ii = None
                    for dto in range(NDT):
                        bank, co = dn_bank(dto)
                        ii = v.tensor_add(acc[:, dto * HT + ch * 256: dto * HT + (ch + 1) * 256],
                                          acc[:, dto * HT + ch * 256: dto * HT + (ch + 1) * 256],
                                          bank[:, co:co + 256])
                    p.edge(ii, v, "moev")
                    v.engine_nop().then_inc(vev, 1)
                    nev += 1
            for st in range(32):
                ns = H * 32 + st
                par = ns % 2
                v.wait_ge(ssc, ns + 1)
                v.wait_ge(xls2[par], 16 * 4 * (ns // 2 + 1))
                ii = None
                for i in range(2):
                    ii = v.tensor_mul(otb[par][:, i * 256:(i + 1) * 256],
                                      otb[par][:, i * 256:(i + 1) * 256],
                                      slb[par][:, i * 256:(i + 1) * 256])
                p.edge(ii, v, "mocb")
                ii = None
                for i in range(2):
                    ii = v.tensor_add(otb[par][:, i * 256:(i + 1) * 256],
                                      otb[par][:, i * 256:(i + 1) * 256],
                                      xsb[par][:, i * 256:(i + 1) * 256])
                ii.then_inc(vcb, 1)
    p.bar += 1


def _prep_weights(inp):
    f32 = np.float32
    a = lambda x: np.ascontiguousarray(np.asarray(x, dtype=f32))
    w = {}
    w["w_pq"] = a(inp["L0_attn_proj_q"])
    w["w_pk"] = a(inp["L0_attn_proj_k"])
    w["w_pv"] = a(inp["L0_attn_proj_v"])
    w["w_po"] = a(np.transpose(np.asarray(inp["L0_attn_proj_o"]), (0, 2, 1)))
    w["w_rm"] = a(np.transpose(np.asarray(inp["L0_ffn_rail_maker"]), (0, 1, 3, 2)))
    w["w_rup"] = a(np.transpose(np.asarray(inp["L0_ffn_rail_up_proj"]), (0, 1, 3, 2)))
    w["w_wc"] = a(np.asarray(inp["L0_ffn_w_cproj"]).T)
    w["w_wq"] = a(inp["L1_attn_Wq"])
    w["w_wk"] = a(inp["L1_attn_Wk"])
    w["w_wv"] = a(inp["L1_attn_Wv"])
    w["w_wb"] = a(inp["L1_attn_Wb"])
    w["w_cq"] = a(np.asarray(inp["L1_attn_conv_q"])[:, 0, :])
    w["w_ck"] = a(np.asarray(inp["L1_attn_conv_k"])[:, 0, :])
    w["w_cv"] = a(np.asarray(inp["L1_attn_conv_v"])[:, 0, :])
    g = np.asarray(inp["L1_attn_g"], dtype=f32)
    w["w_g"] = a(np.tile(g, 4).reshape(NDT, 128).T)
    w["w_wo"] = a(inp["L1_attn_Wo"])
    w["w_cent"] = a(np.asarray(inp["L1_ffn_centroids"]).T)
    w["w_eb"] = a(np.asarray(inp["L1_ffn_expert_biases"]).reshape(1, NE))
    w["w_wsu"] = a(inp["L1_ffn_Ws_up"])
    w["w_wsd"] = a(inp["L1_ffn_Ws_down"])
    w["w_wup"] = a(inp["L1_ffn_W_up"])
    w["w_wdn"] = a(inp["L1_ffn_W_down"])
    return w


def _percore_inputs(inp, w):
    x = np.asarray(inp["L0_attn_x"], dtype=np.float32)
    maps = []
    for b in range(B):
        m = dict(w)
        m["x_t"] = np.ascontiguousarray(x[b].T)
        m["bmask"] = (np.arange(CC)[None, :] <= b).astype(np.float32)
        maps.append(m)
    return maps


_TENSORS = {
    "x_t": ((D, L), F32), "xn0d": ((D, L), BF16), "g0d": ((D, L), F32),
    "x1d": ((D, L), F32), "xn2d": ((D, L), BF16), "affd": ((L, NE), F32),
    "qnd": ((D, L), F32), "knd": ((D, L), F32), "vsd": ((D, L), F32),
    "betad": ((NH, L), F32), "s2d": ((D, L), F32), "cwtd": ((NE, L), F32),
    "wtd": ((D, L), F32), "utd": ((D, L), F32),
    "outd": ((D, L), F32),
    "w_pq": ((8, D, 128), F32), "w_pk": ((D, 128), F32), "w_pv": ((D, 128), F32),
    "w_po": ((8, 128, D), F32), "w_rm": ((4, 64, 16, 16), F32),
    "w_rup": ((4, 64, 16, 64), F32), "w_wc": ((16 * D, D), F32),
    "w_wq": ((D, D), F32), "w_wk": ((D, D), F32), "w_wv": ((D, D), F32),
    "w_wb": ((D, NH), F32), "w_cq": ((D, 4), F32), "w_ck": ((D, 4), F32),
    "w_cv": ((D, 4), F32), "w_g": ((128, NDT), F32), "w_wo": ((D, D), F32),
    "w_cent": ((D, NE), F32), "w_eb": ((1, NE), F32),
    "w_wsu": ((D, FF), F32), "w_wsd": ((FF, D), F32),
    "w_wup": ((NE, D, FF), F32), "w_wdn": ((NE, FF, D), F32),
    "bmask": ((1, CC), F32),
}

def _em_init(p, dr):
    ph_init(p, dr)

def _em_ln1(p, dr):
    ph_ln(p, dr["x_t"], dr["xn0d"], name="l1n")

def _em_ln2(p, dr):
    ph_ln(p, dr["x1d"], dr["xn2d"], aff_out=dr["affd"], cent_d=dr["w_cent"], name="l2n")

def _em_ffn0(p, dr):
    ph_ffn0(p, dr)

def _em_attn0(p, dr):
    ph_attn0(p, dr)

def _em_qkv(p, dr):
    ph_qkv(p, dr)

def _em_l1core(p, dr):
    ph_l1core(p, dr)

def _em_route(p, dr):
    ph_route(p, dr)

def _em_moe(p, dr):
    ph_moe(p, dr)

_PHASES = {
    "init": (_em_init, ["w_g", "bmask", "w_eb"], []),
    "ln1": (_em_ln1, ["x_t"], ["xn0d"]),
    "ln2": (_em_ln2, ["x1d", "w_cent"], ["xn2d", "affd"]),
    "ffn0": (_em_ffn0, ["xn0d", "w_rm", "w_rup", "w_wc"], ["g0d"]),
    "attn0": (_em_attn0, ["xn0d", "x_t", "g0d", "w_pq", "w_pk", "w_pv", "w_po"], ["x1d"]),
    "qkv": (_em_qkv, ["xn2d", "w_wq", "w_wk", "w_wv", "w_wb", "w_cq", "w_ck", "w_cv"],
            ["qnd", "knd", "vsd", "betad"]),
    "l1core": (_em_l1core, ["qnd", "knd", "vsd", "betad", "bmask", "w_g", "w_wo"],
               ["s2d", "wtd", "utd"]),
    "route": (_em_route, ["affd", "w_eb"], ["cwtd"]),
    "moe": (_em_moe, ["xn2d", "cwtd", "x1d", "s2d", "w_wsu", "w_wsd", "w_wup", "w_wdn"],
            ["outd"]),
}
_PHASE_ORDER = ["ln1", "ffn0", "attn0", "ln2", "qkv", "l1core", "route", "moe"]


def build(phase_names, ext_out):
    nc = bass.Bass("TRN2", target_bir_lowering=False)
    phase_names = ["init"] + [x for x in phase_names if x != "init"]
    produced = set()
    consumed = set()
    for ph in phase_names:
        _, ins, outs = _PHASES[ph]
        consumed.update(i for i in ins if i not in produced)
        produced.update(outs)
    dr = {}
    for name in sorted(consumed):
        shape, dt = _TENSORS[name]
        dr[name] = nc.dram_tensor(name, list(shape), dt, kind="ExternalInput").ap()
    for name in sorted(produced):
        shape, dt = _TENSORS[name]
        kind = "ExternalOutput" if name in ext_out else "Internal"
        dr[name] = nc.dram_tensor(name, list(shape), dt, kind=kind).ap()
    with ExitStack() as gctx:
        p = Prog(nc, gctx)
        for k, ph in enumerate(phase_names):
            p.phase_sems = []
            with ExitStack() as pctx:
                p.pctx = pctx
                p.block = pctx.enter_context(nc.Block())
                _PHASES[ph][0](p, dr)
                ids = list(p.phase_sems)
                assert not ids or min(ids) >= p.clear_from, (ids, p.clear_from)
            p.pctx = gctx
            if k < len(phase_names) - 1:
                nc.all_engine_barrier()
                nc.gpsimd.dma_reset(range(p.clear_from, 256))
                nc.gpsimd.sem_clear(range(p.clear_from, 256))
                nc.all_engine_barrier()
    return nc, sorted(consumed)


def run_phases(phase_names, ext_out, in_maps, core_ids=None):
    nc, needed = build(phase_names, ext_out)
    maps = [{k: m[k] for k in needed if k in m} for m in in_maps]
    res = run_bass_kernel_spmd(nc, maps, core_ids=core_ids or list(range(len(in_maps))))
    return res.results


_FULL_CHAIN = ["ln1", "ffn0", "attn0", "ln2", "qkv", "l1core", "route", "moe"]


def kernel(**inputs):
    w = _prep_weights(inputs)
    maps = _percore_inputs(inputs, w)
    nc, needed = build(_FULL_CHAIN, {"outd"})
    in_maps = [{k: m[k] for k in needed if k in m} for m in maps]
    res = run_bass_kernel_spmd(nc, in_maps, core_ids=list(range(B)))
    out = np.stack([np.asarray(r["outd"], np.float32).T for r in res.results], axis=0)
    return np.ascontiguousarray(out)


# version 06
# speedup vs baseline: 1.0488x; 1.0488x over previous
import numpy as np
import ml_dtypes
from contextlib import ExitStack

import concourse.bass as bass
import concourse.mybir as mybir
from concourse.bass_utils import run_bass_kernel_spmd

F32 = mybir.dt.float32
BF16 = mybir.dt.bfloat16
AF = mybir.ActivationFunctionType
ALU = mybir.AluOpType
AX = mybir.AxisListType

B, L, D = 8, 4096, 1024
_A0_NQT = [None]
_F0_SKIP = set()
NDT = D // 128
NCHUNK = L // 512
NH, HD = 4, 256
CC = 64
NCK = L // CC
EPS_LN = 1e-5
EPS_RMS = 1e-5
TOPK = 8
NE = 64
FF = 512


class Prog:
    def __init__(self, nc, gctx):
        self.nc = nc
        self.gctx = gctx
        self.pctx = gctx
        self.block = None
        self.bar_sem = gctx.enter_context(nc.semaphore("phase_bar"))
        self.bar = 0
        self._semid = 0
        self.G = {}
        self.PS = [gctx.enter_context(nc.psum_tensor(f"psb{i}", [128, 512], F32))
                   for i in range(7)]
        self.PSBF = gctx.enter_context(nc.psum_tensor("psbf", [128, 1024], BF16))
        self.phase_sems = []
        try:
            nc.sort_free_semaphores()
        except Exception:
            pass
        nc.all_engine_barrier()
        self._ss_sem = {}
        self._ss_cnt = {}
        for k in ("dve", "act", "pool"):
            self._ss_sem[k] = gctx.enter_context(nc.semaphore(f"ss_{k}"))
            self._ss_cnt[k] = 0
        globals_max = max([self.bar_sem.num]
                          + [s.num for s in self._ss_sem.values()])
        self.clear_from = globals_max + 1

    def edge(self, inst, eng, key=None):
        k = ("dve" if eng is self.nc.vector else
             "act" if eng is self.nc.scalar else "pool")
        inst.then_inc(self._ss_sem[k], 1)
        self._ss_cnt[k] += 1
        eng.wait_ge(self._ss_sem[k], self._ss_cnt[k])

    def sb(self, name, shape, dt=F32):
        return self.pctx.enter_context(self.nc.sbuf_tensor(name, shape, dt))

    def sbg(self, name, shape, dt=F32):
        return self.gctx.enter_context(self.nc.sbuf_tensor(name, shape, dt))

    def sem(self, name, dma=False):
        self._semid += 1
        h = self.pctx.enter_context(self.nc.semaphore(f"{name}{self._semid}"))
        self.phase_sems.append(h.num if hasattr(h, "num") else h)
        return h


def ph_init(p, dr):
    nc, blk = p.nc, p.block
    G = p.G
    G["ones"] = p.sbg("g_ones", [128, 128], F32)
    G["ident"] = p.sbg("g_ident", [128, 128], F32)
    G["ident_bf"] = p.sbg("g_identbf", [128, 128], BF16)
    G["iota"] = p.sbg("g_iota", [128, 128], F32)
    G["mlow"] = p.sbg("g_mlow", [64, 64], F32)
    G["mup"] = p.sbg("g_mup", [64, 64], F32)
    G["eps_ln"] = p.sbg("g_epsln", [128, 1], F32)
    G["eps_rms"] = p.sbg("g_epsrms", [128, 1], F32)
    G["gvec"] = p.sbg("g_gvec", [128, NDT], F32)
    G["bmask"] = p.sbg("g_bmask", [128, CC], F32)
    G["ebias"] = p.sbg("g_ebias", [128, NE], F32)
    ldg = p.sem("ini_ld", dma=True)
    lds = p.sem("ini_lds", dma=True)
    gc = p.sem("ini_gc")
    vrd = p.sem("ini_v")

    @blk.gpsimd
    def _(g):
        g.memset(G["ones"][:], 1.0)
        g.memset(G["eps_ln"][:], EPS_LN)
        g.memset(G["eps_rms"][:], EPS_RMS)
        g.iota(G["iota"][:], [[1, 128]], channel_multiplier=-1,
               allow_small_or_imprecise_dtypes=True).then_inc(gc, 1)
        g.dma_start(out=G["gvec"][:], in_=dr["w_g"]).then_inc(ldg, 16)

    @blk.vector
    def _(v):
        v.wait_ge(ldg, 16)
        v.wait_ge(gc, 1)
        v.tensor_single_scalar(G["ident"][:], G["iota"][:], 0.0, ALU.is_equal)
        v.tensor_single_scalar(G["ident_bf"][:], G["iota"][:], 0.0, ALU.is_equal)
        v.tensor_single_scalar(G["mlow"][:], G["iota"][0:64, 0:64], 0.0, ALU.is_lt)
        v.tensor_single_scalar(G["mup"][:], G["iota"][0:64, 0:64], 0.0, ALU.is_gt)
        v.engine_nop().then_inc(vrd, 1)

    bm1 = p.sb("g_bm1", [1, CC], F32)
    eb1 = p.sb("g_eb1", [1, NE], F32)
    ps_b = p.PS[0]
    tdone = p.sem("ini_t")

    @blk.sync
    def _(q):
        q.dma_start(out=bm1[:], in_=dr["bmask"]).then_inc(lds, 16)
        q.dma_start(out=eb1[:], in_=dr["w_eb"]).then_inc(lds, 16)

    @blk.tensor
    def _(t):
        t.wait_ge(ldg, 16)
        t.wait_ge(gc, 1)
        t.wait_ge(lds, 32)
        t.matmul(ps_b[:, 0:CC], G["ones"][0:1, :], bm1[:], start=True, stop=True)
        t.matmul(ps_b[:, CC:CC + NE], G["ones"][0:1, :], eb1[:], start=True, stop=True
                 ).then_inc(tdone, 1)

    @blk.vector
    def _(v):
        v.wait_ge(tdone, 1)
        v.tensor_copy(G["bmask"][:], ps_b[:, 0:CC])
        v.tensor_copy(G["ebias"][:], ps_b[:, CC:CC + NE])
        v.engine_nop().then_inc(p.bar_sem, 1)
    p.bar += 1


def ph_ln(p, src_d, dst_d, aff_out=None, cent_d=None, name="ln"):
    nc, blk = p.nc, p.block
    bar0 = p.bar
    ld2 = [p.sem(name + "ld_a"), p.sem(name + "ld_b")]
    ldc = p.sem(name + "ldc")
    vsq = p.sem(name + "vsq")
    ts1 = p.sem(name + "ts1")
    ts2 = p.sem(name + "ts2")
    vrd = p.sem(name + "vrd")
    ssd = p.sem(name + "ssd")
    vxn = p.sem(name + "vxn")
    st2 = [p.sem(name + "st_a"), p.sem(name + "st_b")]
    sa2 = [p.sem(name + "sa_a"), p.sem(name + "sa_b")]
    taf = p.sem(name + "taf")
    saf = p.sem(name + "saf")
    NC2 = NCHUNK
    do_aff = aff_out is not None

    xch = [p.sb(name + f"x{i}", [128, NDT * 512], F32) for i in range(2)]
    sq = [p.sb(name + f"sq{i}", [128, NDT * 512], F32) for i in range(2)]
    mean = [p.sb(name + f"mean{i}", [128, 512], F32) for i in range(2)]
    var = [p.sb(name + f"var{i}", [128, 512], F32) for i in range(2)]
    var2 = [p.sb(name + f"var2{i}", [128, 512], F32) for i in range(2)]
    rstd = [p.sb(name + f"rstd{i}", [128, 512], F32) for i in range(2)]
    sd = [p.sb(name + f"sd{i}", [128, 512], F32) for i in range(2)]
    xn = [p.sb(name + f"xn{i}", [128, NDT * 512], BF16) for i in range(2)]
    xnf = [p.sb(name + f"xnf{i}", [128, NDT * 512], F32) for i in range(2)] if do_aff else None
    afb = [p.sb(name + f"afb{i}", [128, 4 * 64], F32) for i in range(2)] if do_aff else None
    cent = p.sb(name + "cent", [128, NDT * 64], F32) if do_aff else None
    ps_s1 = [p.PS[0], p.PS[2]]
    ps_s2 = [p.PS[1], p.PS[3]]
    ps_af = [p.PS[4], p.PS[5]] if do_aff else None

    @blk.sync
    def _(q):
        q.wait_ge(p.bar_sem, bar0)
        if do_aff:
            for dt in range(NDT):
                q.dma_start(out=cent[:, dt * 64:(dt + 1) * 64],
                            in_=cent_d[dt * 128:(dt + 1) * 128, :]).then_inc(ldc, 16)
        for c in range(NC2):
            A = c % 2
            if c >= 2:
                q.wait_ge(vxn, c - 1)
                q.wait_ge(ts1, c - 1)
            for dt in range(NDT):
                q.dma_start(out=xch[A][:, dt * 512:(dt + 1) * 512],
                            in_=src_d[dt * 128:(dt + 1) * 128, c * 512:(c + 1) * 512]
                            ).then_inc(ld2[A], 16)

    ones = p.G["ones"]

    def ld_target(c):
        return 16 * NDT * (c // 2 + 1)

    @blk.vector
    def _(v):
        v.wait_ge(p.bar_sem, bar0)
        for c in range(NC2):
            A = c % 2
            v.wait_ge(ld2[A], ld_target(c))
            if c >= 2:
                v.wait_ge(ts2, c - 1)
            for dt in range(NDT):
                sl = slice(dt * 512, (dt + 1) * 512)
                v.tensor_mul(sq[A][:, sl], xch[A][:, sl], xch[A][:, sl])
            v.engine_nop().then_inc(vsq, 1)
            if c >= 2:
                v.wait_ge(ssd, c - 1)
            v.wait_ge(ts2, c + 1)
            p.edge(v.tensor_scalar_mul(mean[A][:], ps_s1[A][:, :], 1.0 / D), v, name + "v1")
            p.edge(v.tensor_mul(var2[A][:], mean[A][:], mean[A][:]), v, name + "v2")
            v.scalar_tensor_tensor(var[A][:], ps_s2[A][:, :], 1.0 / D, var2[A][:],
                                   ALU.mult, ALU.subtract)
            v.engine_nop().then_inc(vrd, 1)
            v.wait_ge(ssd, c + 1)
            p.edge(v.reciprocal(rstd[A][:], sd[A][:]), v, name + "v3")
            per_chunk_st = NDT + (4 if do_aff else 0)
            if c >= 2:
                v.wait_ge(st2[A], 16 * NDT * (c // 2))
                if do_aff:
                    v.wait_ge(taf, c - 1)
            for dt in range(NDT):
                sl = slice(dt * 512, (dt + 1) * 512)
                ii = v.tensor_sub(sq[A][:, sl], xch[A][:, sl], mean[A][:])
            p.edge(ii, v, name + "v4")
            for dt in range(NDT):
                sl = slice(dt * 512, (dt + 1) * 512)
                v.tensor_mul(xn[A][:, sl], sq[A][:, sl], rstd[A][:])
                if do_aff:
                    v.tensor_mul(xnf[A][:, sl], sq[A][:, sl], rstd[A][:])
            v.engine_nop().then_inc(vxn, 1)

    @blk.scalar
    def _(s):
        s.wait_ge(p.bar_sem, bar0)
        for c in range(NC2):
            A = c % 2
            s.wait_ge(vrd, c + 1)
            s.activation(sd[A][:], var[A][:], AF.Sqrt, bias=p.G["eps_ln"][:, 0:1], scale=1.0
                         ).then_inc(ssd, 1)
            if do_aff:
                s.wait_ge(taf, c + 1)
                if c >= 2:
                    s.wait_ge(sa2[A], 16 * 4 * (c // 2))
                s.copy(afb[A][:], ps_af[A][:, 0:256]).then_inc(saf, 1)

    @blk.tensor
    def _(t):
        t.wait_ge(p.bar_sem, bar0)
        for c in range(NC2):
            A = c % 2
            t.wait_ge(ld2[A], ld_target(c))
            if do_aff and c == 0:
                t.wait_ge(ldc, 16 * NDT)
            if c >= 2:
                t.wait_ge(vrd, c - 1)
            for dt in range(NDT):
                mm = t.matmul(ps_s1[A][:, :], ones[:], xch[A][:, dt * 512:(dt + 1) * 512],
                              start=(dt == 0), stop=(dt == NDT - 1))
            mm.then_inc(ts1, 1)
            t.wait_ge(vsq, c + 1)
            for dt in range(NDT):
                mm = t.matmul(ps_s2[A][:, :], ones[:], sq[A][:, dt * 512:(dt + 1) * 512],
                              start=(dt == 0), stop=(dt == NDT - 1))
            mm.then_inc(ts2, 1)
            if do_aff:
                t.wait_ge(vxn, c + 1)
                if c >= 2:
                    t.wait_ge(saf, c - 1)
                for tt in range(4):
                    for dt in range(NDT):
                        mm = t.matmul(ps_af[A][:, tt * 64:tt * 64 + 64],
                                      xnf[A][:, dt * 512 + tt * 128: dt * 512 + tt * 128 + 128],
                                      cent[:, dt * 64:(dt + 1) * 64],
                                      start=(dt == 0), stop=(dt == NDT - 1))
                mm.then_inc(taf, 1)

    @blk.gpsimd
    def _(g):
        for c in range(NC2):
            A = c % 2
            g.wait_ge(vxn, c + 1)
            for dt in range(NDT):
                g.dma_start(out=dst_d[dt * 128:(dt + 1) * 128, c * 512:(c + 1) * 512],
                            in_=xn[A][:, dt * 512:(dt + 1) * 512]).then_inc(st2[A], 16)
            if do_aff:
                g.wait_ge(saf, c + 1)
                for tt in range(4):
                    g.dma_start(out=aff_out[(c * 4 + tt) * 128:(c * 4 + tt + 1) * 128, :],
                                in_=afb[A][:, tt * 64:(tt + 1) * 64]).then_inc(sa2[A], 16)
        g.wait_ge(st2[0], 16 * NDT * (NC2 // 2))
        g.wait_ge(st2[1], 16 * NDT * (NC2 // 2))
        if do_aff:
            g.wait_ge(sa2[0], 16 * 4 * (NC2 // 2))
            g.wait_ge(sa2[1], 16 * 4 * (NC2 // 2))
        g.engine_nop().then_inc(p.bar_sem, 1)
    p.bar += 1


def ph_ffn0(p, dr):
    nc, blk = p.nc, p.block
    bar0 = p.bar
    xn0d, g0d = dr["xn0d"], dr["g0d"]
    ldm = p.sem("f0ldm", dma=True)
    tmc = p.sem("f0tmc")
    vmc = p.sem("f0vmc")
    gbd = p.sem("f0gbd")
    bdq = p.sem("f0bdq", dma=True)
    wc4 = [p.sem(f"f0wcq{j}") for j in range(4)]
    xl2 = [p.sem("f0xld_a", dma=True), p.sem("f0xld_b", dma=True)]
    tz = p.sem("f0tz")
    sh = p.sem("f0sh")
    tco = p.sem("f0tco")
    vgo = p.sem("f0vgo")
    gs2 = [p.sem("f0gst_a", dma=True), p.sem("f0gst_b", dma=True)]
    sl2 = [p.sem("f0slq_a", dma=True), p.sem("f0slq_b", dma=True)]
    svn = p.sem("f0svn")
    ss2 = [p.sem("f0ssq_a", dma=True), p.sem("f0ssq_b", dma=True)]
    NQ = 4
    NSTEP = NQ * NCHUNK
    RE = "p (h q8 pgx s2 k) -> p h q8 pgx s2 k"

    BD = p.sb("f0bd", [128, 16384], BF16)

    with ExitStack() as sub:
        rm_sb = sub.enter_context(nc.sbuf_tensor("f0rm", [16, 4096], F32))
        rup_sb = sub.enter_context(nc.sbuf_tensor("f0rup", [16, 16384], F32))
        m_all = sub.enter_context(nc.sbuf_tensor("f0mall", [16, 16384], BF16))

        @blk.sync
        def _(q):
            q.wait_ge(p.bar_sem, bar0)
            q.dma_start(out=rm_sb[:],
                        in_=dr["w_rm"].rearrange("h r j i -> j (h r) i")
                        ).then_inc(ldm, 16)
            q.dma_start(out=rup_sb[:],
                        in_=dr["w_rup"].rearrange("h r j k -> j (h r) k")
                        ).then_inc(ldm, 16)

        @blk.tensor
        def _(t):
            t.wait_ge(p.bar_sem, bar0)
            t.wait_ge(ldm, 32)
            for grp in range(32):
                if grp >= 2:
                    t.wait_ge(vmc, grp - 1)
                for u in range(8):
                    hr = grp * 8 + u
                    mm = t.matmul(p.PS[grp % 2][0:16, u * 64:(u + 1) * 64],
                                  rm_sb[:, hr * 16:(hr + 1) * 16],
                                  rup_sb[:, hr * 64:(hr + 1) * 64],
                                  start=True, stop=True)
                mm.then_inc(tmc, 1)

        @blk.vector
        def _(v):
            v.wait_ge(p.bar_sem, bar0)
            for grp in range(32):
                v.wait_ge(tmc, grp + 1)
                v.tensor_copy(m_all[:, grp * 512:(grp + 1) * 512],
                              p.PS[grp % 2][0:16, :]).then_inc(vmc, 1)

        @blk.gpsimd
        def _(g):
            g.wait_ge(p.bar_sem, bar0)
            g.memset(BD[:], 0.0).then_inc(gbd, 1)
            g.wait_ge(gbd, 1)
            g.wait_ge(vmc, 32)
            for pg in range(4):
                for s in range(2):
                    base = 32 * pg + 16 * s
                    dst = BD[base:base + 16, :].rearrange(
                        RE, h=4, q8=8, pgx=4, s2=2, k=64)[:, :, :, pg, s, :]
                    srcv = m_all[0:16, :].rearrange(
                        RE, h=4, q8=8, pgx=4, s2=2, k=64)[:, :, :, pg, s, :]
                    g.dma_start(out=dst, in_=srcv).then_inc(bdq, 16)
            g.wait_ge(bdq, 16 * 8)
            g.engine_nop().then_inc(gbd, 1)
        sub.close()

    wcb = p.sb("f0wcb", [128, 32 * 1024], BF16)
    hb = [p.sb(f"f0h{i}", [128, 16384], BF16) for i in range(2)]
    xnc = [p.sb(f"f0x{i}", [128, NDT * 512], BF16) for i in range(2)]
    gtmp = p.sb("f0gt", [128, 4096], F32)

    @blk.sync
    def _(q):
        q.wait_ge(gbd, 2)
        for s in range(NSTEP):
            qq, c = divmod(s, NCHUNK)
            A = s % 2
            if s >= 2:
                q.wait_ge(tz, (s - 2) * 32 + 32)
            for dt in range(NDT):
                q.dma_start(out=xnc[A][:, dt * 512:(dt + 1) * 512],
                            in_=xn0d[dt * 128:(dt + 1) * 128, c * 512:(c + 1) * 512]
                            ).then_inc(xl2[A], 16)

    @blk.gpsimd
    def _(g):
        for qq in range(NQ):
            if qq >= 1:
                g.wait_ge(vgo, 16 * qq)
            for piece in range(4):
                g.dma_start(
                    out=wcb[:, piece * 8 * 1024:(piece + 1) * 8 * 1024],
                    in_=dr["w_wc"].rearrange("(ftg pp) d -> pp ftg d", pp=128)
                        [:, qq * 32 + piece * 8: qq * 32 + (piece + 1) * 8, :]
                    ).then_inc(wc4[piece], 16)
            for c in range(NCHUNK):
                s = qq * NCHUNK + c
                for hf in range(2):
                    g.wait_ge(vgo, 2 * s + hf + 1)
                    for i in range(4):
                        dt = hf * 4 + i
                        g.dma_start(
                            out=g0d[dt * 128:(dt + 1) * 128, c * 512:(c + 1) * 512],
                            in_=gtmp[:, hf * 2048 + i * 512: hf * 2048 + (i + 1) * 512],
                            accum_op=(ALU.add if (qq > 0 and 'acc' not in _F0_SKIP) else ALU.bypass)
                            ).then_inc(gs2[hf], 16)

    @blk.tensor
    def _(t):
        t.wait_ge(gbd, 2)
        for s in range(NSTEP):
            qq, c = divmod(s, NCHUNK)
            A = s % 2
            t.wait_ge(xl2[s % 2], 16 * NDT * (s // 2 + 1))
            t.wait_ge(vgo, 2 * s)
            for ft in range(32):
                gz = s * 32 + ft
                ftg = qq * 32 + ft
                dt_in = ft // 4
                if gz >= 2:
                    t.wait_ge(sh, gz - 1)
                t.matmul(p.PS[4 + gz % 2][:, :], BD[:, ftg * 128:(ftg + 1) * 128],
                         xnc[A][:, dt_in * 512:(dt_in + 1) * 512],
                         start=True, stop=True).then_inc(tz, 1)
                if ft >= 1:
                    fprev = ft - 1
                    t.wait_ge(sh, s * 32 + fprev + 1)
                    if fprev % 8 == 0:
                        t.wait_ge(wc4[fprev // 8], 16 * (qq + 1))
                    for dto in range(4):
                        mm = t.matmul(p.PS[dto][:, :],
                                      wcb[:, fprev * 1024 + dto * 128: fprev * 1024 + (dto + 1) * 128],
                                      hb[A][:, fprev * 512:(fprev + 1) * 512],
                                      start=(fprev == 0), stop=(fprev == 31))
            t.wait_ge(sh, s * 32 + 32)
            for dto in range(4):
                mm = t.matmul(p.PS[dto][:, :],
                              wcb[:, 31 * 1024 + dto * 128: 31 * 1024 + (dto + 1) * 128],
                              hb[A][:, 31 * 512: 32 * 512],
                              start=False, stop=True)
            mm.then_inc(tco, 1)
            t.wait_ge(vgo, 2 * s + 1)
            for ft in range(32):
                for dto in range(4):
                    mm = t.matmul(p.PS[dto][:, :],
                                  wcb[:, ft * 1024 + (4 + dto) * 128: ft * 1024 + (5 + dto) * 128],
                                  hb[A][:, ft * 512:(ft + 1) * 512],
                                  start=(ft == 0), stop=(ft == 31))
            mm.then_inc(tco, 1)

    @blk.scalar
    def _(s_):
        s_.wait_ge(p.bar_sem, bar0)
        for s in range(NSTEP):
            A = s % 2
            if s >= 2:
                s_.wait_ge(tco, 2 * (s - 1))
            for ft in range(32):
                gz = s * 32 + ft
                s_.wait_ge(tz, gz + 1)
                s_.activation(hb[A][:, ft * 512:(ft + 1) * 512],
                              p.PS[4 + gz % 2][:, :], AF.Gelu).then_inc(sh, 1)

    @blk.vector
    def _(v):
        for s in range(NSTEP):
            for hf in range(2):
                v.wait_ge(tco, 2 * s + hf + 1)
                if s >= 1:
                    v.wait_ge(gs2[hf], 16 * 4 * s)
                for i in range(4):
                    v.tensor_copy(gtmp[:, hf * 2048 + i * 512: hf * 2048 + (i + 1) * 512],
                                  p.PS[i][:, :])
                v.engine_nop().then_inc(vgo, 1)

    GSTOT = 16 * 4 * NSTEP
    if "silu" in _F0_SKIP:
        @blk.gpsimd
        def _(g):
            g.wait_ge(gs2[0], GSTOT)
            g.wait_ge(gs2[1], GSTOT)
            g.engine_nop().then_inc(p.bar_sem, 1)
        p.bar += 1
        return

    @blk.sync
    def _(q):
        q.wait_ge(gs2[0], GSTOT)
        q.wait_ge(gs2[1], GSTOT)
        for u in range(16):
            c, hf = divmod(u, 2)
            if u >= 2:
                q.wait_ge(ss2[hf], 16 * 4 * (u // 2))
            for i in range(4):
                dt = hf * 4 + i
                q.dma_start(out=gtmp[:, hf * 2048 + i * 512: hf * 2048 + (i + 1) * 512],
                            in_=g0d[dt * 128:(dt + 1) * 128, c * 512:(c + 1) * 512]
                            ).then_inc(sl2[hf], 16)

    @blk.scalar
    def _(s_):
        for u in range(16):
            c, hf = divmod(u, 2)
            s_.wait_ge(sl2[hf], 16 * 4 * (u // 2 + 1))
            s_.activation(gtmp[:, hf * 2048: hf * 2048 + 2048],
                          gtmp[:, hf * 2048: hf * 2048 + 2048], AF.Silu).then_inc(svn, 1)

    @blk.gpsimd
    def _(g):
        for u in range(16):
            c, hf = divmod(u, 2)
            g.wait_ge(svn, u + 1)
            for i in range(4):
                dt = hf * 4 + i
                g.dma_start(out=g0d[dt * 128:(dt + 1) * 128, c * 512:(c + 1) * 512],
                            in_=gtmp[:, hf * 2048 + i * 512: hf * 2048 + (i + 1) * 512]
                            ).then_inc(ss2[hf], 16)
        g.wait_ge(ss2[0], 16 * 4 * 8)
        g.wait_ge(ss2[1], 16 * 4 * 8)
        g.engine_nop().then_inc(p.bar_sem, 1)
    p.bar += 1


def ph_attn0(p, dr):
    nc, blk = p.nc, p.block
    G = p.G
    bar0 = p.bar
    xn0d, x_t, g0d, x1d = dr["xn0d"], dr["x_t"], dr["g0d"], dr["x1d"]
    pld = p.sem("a0pld", dma=True)
    xl2 = [p.sem("a0xl_a", dma=True), p.sem("a0xl_b", dma=True)]
    tpj = p.sem("a0tpj")
    vpj = p.sem("a0vpj")
    tvt = p.sem("a0tvt")
    vvt = p.sem("a0vvt")
    tL = p.sem("a0tL")
    sE = p.sem("a0sE")
    tP = p.sem("a0tP")
    vS = p.sem("a0vS")
    vR = p.sem("a0vR")
    vV = p.sem("a0vV")
    tT = p.sem("a0tT")
    vO = p.sem("a0vO")
    tY = p.sem("a0tY")
    vC = p.sem("a0vC")
    xg2 = [p.sem("a0xg_a", dma=True), p.sem("a0xg_b", dma=True)]
    so2 = [p.sem("a0so_a", dma=True), p.sem("a0so_b", dma=True)]

    NQT = (L // 128) if _A0_NQT[0] is None else _A0_NQT[0]
    pq_sb = p.sb("a0pq", [128, 8 * NDT * 128], BF16)
    pk_sb = p.sb("a0pk", [128, NDT * 128], BF16)
    pv_sb = p.sb("a0pv", [128, NDT * 128], BF16)
    po_sb = p.sb("a0po", [128, 8 * 1024], BF16)
    xnc = [p.sb(f"a0x{i}", [128, NDT * 512], BF16) for i in range(2)]
    Qs = p.sb("a0Q", [128, 8 * L], BF16)
    Ks = p.sb("a0K", [128, L], BF16)
    Vs = p.sb("a0V", [128, L], BF16)
    vtok = p.sb("a0vt", [128, 32 * 132], BF16)
    PT = [p.sb(f"a0pt{i}", [128, 32 * 128], BF16) for i in range(2)]
    rs = p.sb("a0rs", [128, 1], F32)
    onb = p.sb("a0onb", [128, 128], BF16)
    otb = [p.sb(f"a0ot{i}", [128, 128], BF16) for i in range(2)]
    xg = [p.sb(f"a0xg{i}", [128, 2 * NDT * 128], F32) for i in range(2)]
    outb = [p.sb(f"a0ob{i}", [128, NDT * 128], F32) for i in range(2)]

    @blk.gpsimd
    def _(g):
        g.wait_ge(p.bar_sem, bar0)
        g.dma_start(out=pq_sb[:],
                    in_=dr["w_pq"].rearrange("h (dtp pp) k -> pp (h dtp) k", pp=128)
                    ).then_inc(pld, 16)
        g.dma_start(out=pk_sb[:],
                    in_=dr["w_pk"].rearrange("(dtp pp) k -> pp dtp k", pp=128)
                    ).then_inc(pld, 16)
        g.dma_start(out=pv_sb[:],
                    in_=dr["w_pv"].rearrange("(dtp pp) k -> pp dtp k", pp=128)
                    ).then_inc(pld, 16)
        g.dma_start(out=po_sb[:],
                    in_=dr["w_po"].rearrange("h v dd -> v h dd")).then_inc(pld, 16)

    @blk.sync
    def _(q):
        q.wait_ge(p.bar_sem, bar0)
        for c in range(NCHUNK):
            A = c % 2
            if c >= 2:
                q.wait_ge(tpj, 10 * (c - 1))
            for dt in range(NDT):
                q.dma_start(out=xnc[A][:, dt * 512:(dt + 1) * 512],
                            in_=xn0d[dt * 128:(dt + 1) * 128, c * 512:(c + 1) * 512]
                            ).then_inc(xl2[A], 16)

    @blk.tensor
    def _(t):
        t.wait_ge(p.bar_sem, bar0)
        t.wait_ge(pld, 64)
        ng = 0
        for c in range(NCHUNK):
            A = c % 2
            t.wait_ge(xl2[A], 16 * NDT * (c // 2 + 1))
            for grp in range(10):
                if ng >= 2:
                    t.wait_ge(vpj, ng - 1)
                for dt in range(NDT):
                    if grp == 0:
                        lhs = pk_sb[:, dt * 128:(dt + 1) * 128]
                    elif grp == 1:
                        lhs = pv_sb[:, dt * 128:(dt + 1) * 128]
                    else:
                        h = grp - 2
                        lhs = pq_sb[:, (h * NDT + dt) * 128:(h * NDT + dt + 1) * 128]
                    mm = t.matmul(p.PS[ng % 2][:, :], lhs,
                                  xnc[A][:, dt * 512:(dt + 1) * 512],
                                  start=(dt == 0), stop=(dt == NDT - 1))
                mm.then_inc(tpj, 1)
                ng += 1

    @blk.vector
    def _(v):
        v.wait_ge(p.bar_sem, bar0)
        ng = 0
        for c in range(NCHUNK):
            sl = slice(c * 512, (c + 1) * 512)
            for grp in range(10):
                v.wait_ge(tpj, ng + 1)
                if grp == 0:
                    v.tensor_copy(Ks[:, sl], p.PS[ng % 2][:, :])
                elif grp == 1:
                    v.tensor_copy(Vs[:, sl], p.PS[ng % 2][:, :])
                else:
                    h = grp - 2
                    v.tensor_copy(Qs[:, h * L + c * 512: h * L + (c + 1) * 512],
                                  p.PS[ng % 2][:, :])
                v.engine_nop().then_inc(vpj, 1)
                ng += 1

    @blk.vector
    def _(v):
        v.memset(vtok[:], 1.0)
        v.engine_nop().then_inc(vvt, 1)

    @blk.tensor
    def _(t):
        t.wait_ge(vpj, 80)
        t.wait_ge(vvt, 1)
        for mt in range(32):
            if mt >= 1:
                t.wait_ge(vvt, mt + 1)
            t.transpose(p.PSBF[:, (mt % 2) * 128:(mt % 2) * 128 + 128],
                        Vs[:, mt * 128:(mt + 1) * 128],
                        G["ident_bf"][:]).then_inc(tvt, 1)

    @blk.vector
    def _(v):
        for mt in range(32):
            v.wait_ge(tvt, mt + 1)
            v.tensor_copy(vtok[:, mt * 132: mt * 132 + 128],
                          p.PSBF[:, (mt % 2) * 128:(mt % 2) * 128 + 128])
            v.engine_nop().then_inc(vvt, 1)

    @blk.sync
    def _(q):
        for qt in range(NQT):
            A = qt % 2
            if qt >= 2:
                q.wait_ge(vC, qt - 1)
            for dt in range(NDT):
                q.dma_start(out=xg[A][:, dt * 128:(dt + 1) * 128],
                            in_=x_t[dt * 128:(dt + 1) * 128, qt * 128:(qt + 1) * 128]
                            ).then_inc(xg2[A], 16)
            for dt in range(NDT):
                q.dma_start(out=xg[A][:, (NDT + dt) * 128:(NDT + dt + 1) * 128],
                            in_=g0d[dt * 128:(dt + 1) * 128, qt * 128:(qt + 1) * 128]
                            ).then_inc(xg2[A], 16)

    nL = [0]; nE = [0]; nP = [0]; nS = [0]; nT = [0]; nO = [0]; nY = [0]

    @blk.tensor
    def _(t):
        t.wait_ge(vvt, 33)
        pend = [None]

        def emit_tr(t, idx):
            qtp, hq = divmod(idx, 8)
            hpp = hq % 2
            t.wait_ge(vS, idx + 1)
            if idx >= 1:
                t.wait_ge(vO, idx)
            t.transpose(p.PSBF[:, 256 + hpp * 128: 256 + hpp * 128 + 128],
                        onb[:], G["ident_bf"][:]).then_inc(tT, 1)

        def emit_y(t, idx):
            qtp, hq = divmod(idx, 8)
            hpp = hq % 2
            t.wait_ge(vO, idx + 1)
            if hq == 0 and qtp >= 1:
                t.wait_ge(vC, qtp)
            for dt in range(NDT):
                bank = 5 + dt // 4
                col = (dt % 4) * 128
                mm = t.matmul(p.PS[bank][:, col:col + 128],
                              po_sb[:, hq * 1024 + dt * 128: hq * 1024 + (dt + 1) * 128],
                              otb[hpp][:],
                              start=(hq == 0 and dt % 4 == 0),
                              stop=(hq == 7 and dt % 4 == 3))
            mm.then_inc(tY, 1)

        for qt in range(NQT):
            for h in range(8):
                hp = h % 2
                idx = qt * 8 + h
                for rnd in range(2):
                    for mt16 in range(16):
                        mt = rnd * 16 + mt16
                        bank = mt16 // 4
                        col = (mt16 % 4) * 128
                        if nL[0] >= 16:
                            t.wait_ge(sE, (nL[0] - 16) // 4 + 1)
                        t.matmul(p.PS[bank][:, col:col + 128],
                                 Ks[:, mt * 128:(mt + 1) * 128],
                                 Qs[:, h * L + qt * 128: h * L + (qt + 1) * 128],
                                 start=True, stop=True).then_inc(tL, 1)
                        nL[0] += 1
                if pend[0] is not None:
                    emit_tr(t, pend[0])
                t.wait_ge(vS, idx)
                for mt in range(32):
                    t.wait_ge(sE, nE[0] + mt // 4 + 1)
                    mm = t.matmul(p.PS[4][:, 0:129],
                                  PT[hp][:, mt * 128:(mt + 1) * 128],
                                  vtok[:, mt * 132: mt * 132 + 129],
                                  start=(mt == 0), stop=(mt == 31))
                mm.then_inc(tP, 1)
                nE[0] += 8
                if pend[0] is not None:
                    emit_y(t, pend[0])
                pend[0] = idx
        emit_tr(t, pend[0])
        emit_y(t, pend[0])

    @blk.scalar
    def _(s_):
        cnt = 0
        for qt in range(NQT):
            for h in range(8):
                hp = h % 2
                if cnt >= 2:
                    s_.wait_ge(tP, cnt - 1)
                for b4 in range(8):
                    s_.wait_ge(tL, cnt * 32 + (b4 + 1) * 4)
                    bank = b4 % 4
                    s_.activation(PT[hp][:, b4 * 512:(b4 + 1) * 512],
                                  p.PS[bank][:, :], AF.Exp).then_inc(sE, 1)
                cnt += 1

    @blk.vector
    def _(v):
        for qt in range(NQT):
            A = qt % 2
            for h in range(8):
                hp = h % 2
                v.wait_ge(tP, qt * 8 + h + 1)
                v.reciprocal(rs[:], p.PS[4][:, 128:129]).then_inc(vR, 1)
                v.wait_ge(vR, qt * 8 + h + 1)
                v.tensor_scalar_mul(onb[:], p.PS[4][:, 0:128], rs[:, 0:1])
                v.engine_nop().then_inc(vS, 1)
                v.wait_ge(tT, qt * 8 + h + 1)
                v.tensor_copy(otb[hp][:], p.PSBF[:, 256 + hp * 128: 256 + hp * 128 + 128])
                v.engine_nop().then_inc(vO, 1)
            v.wait_ge(tY, (qt + 1) * 8)
            v.wait_ge(xg2[A], 16 * 2 * NDT * (qt // 2 + 1))
            if qt >= 2:
                v.wait_ge(so2[A], 16 * NDT * (qt // 2))
            for dt in range(NDT):
                bank = 5 + dt // 4
                col = (dt % 4) * 128
                v.scalar_tensor_tensor(outb[A][:, dt * 128:(dt + 1) * 128],
                                       p.PS[bank][:, col:col + 128], 1.0,
                                       xg[A][:, (NDT + dt) * 128:(NDT + dt + 1) * 128],
                                       ALU.mult, ALU.mult)
            v.engine_nop().then_inc(vV, 1)
            v.wait_ge(vV, qt + 1)
            for dt in range(NDT):
                v.tensor_add(outb[A][:, dt * 128:(dt + 1) * 128],
                             outb[A][:, dt * 128:(dt + 1) * 128],
                             xg[A][:, dt * 128:(dt + 1) * 128])
            v.engine_nop().then_inc(vC, 1)

    @blk.gpsimd
    def _(g):
        for qt in range(NQT):
            A = qt % 2
            g.wait_ge(vC, qt + 1)
            for dt in range(NDT):
                g.dma_start(out=x1d[dt * 128:(dt + 1) * 128, qt * 128:(qt + 1) * 128],
                            in_=outb[A][:, dt * 128:(dt + 1) * 128]).then_inc(so2[A], 16)
        g.wait_ge(so2[0], 16 * NDT * (NQT // 2))
        g.wait_ge(so2[1], 16 * NDT * (NQT // 2))
        g.engine_nop().then_inc(p.bar_sem, 1)
    p.bar += 1


def ph_qkv(p, dr):
    nc, blk = p.nc, p.block
    G = p.G
    bar0 = p.bar
    xn2d = dr["xn2d"]
    outs = {"q": dr["qnd"], "k": dr["knd"], "v": dr["vsd"]}
    wsrc = {"q": dr["w_wq"], "k": dr["w_wk"], "v": dr["w_wv"]}
    csrc = {"q": dr["w_cq"], "k": dr["w_ck"], "v": dr["w_cv"]}
    xlq = p.sem("qkxl", dma=True)
    wl = p.sem("qkwl", dma=True)
    wlb = p.sem("qkwlb", dma=True)
    cl = p.sem("qkcl", dma=True)
    tpr = p.sem("qktpr")
    vev = p.sem("qkvev")
    vcv = p.sem("qkvcv")
    vcb2 = p.sem("qkvcb2")
    ssl = p.sem("qkssl")
    vsq = p.sem("qkvsq")
    tsq = p.sem("qktsq")
    ssr = p.sem("qkssr")
    vsc = p.sem("qkvsc")
    st2 = [p.sem("qkst_a", dma=True), p.sem("qkst_b", dma=True)]
    tbe = p.sem("qktbe")
    sbe = p.sem("qksbe")

    xns = p.sb("qkxn", [128, NDT * L], BF16)
    wcur = p.sb("qkw", [128, NDT * 1024], BF16)
    cbuf = [p.sb(f"qkcb{i}", [128, 4099], F32) for i in range(2)]
    cnv = [p.sb(f"qkcv{i}", [128, L], F32) for i in range(2)]
    qh = p.sb("qkqh", [128, 2 * L], F32)
    cvw = p.sb("qkcvw", [128, 3 * NDT * 4], F32)
    wbsb = p.sb("qkwb", [128, NDT * 4], BF16)
    betab = p.sb("qkbeta", [4, L], F32)
    sdq = p.sb("qksdq", [128, 512], F32)
    rno = p.sb("qkrno", [128, 512], F32)

    PLIST = ("q", "k", "v")
    SLOTS = [("q", pr, 2) for pr in range(4)] + [("k", pr, 2) for pr in range(4)] \
            + [("v", dto, 1) for dto in range(NDT)]

    def st_total(upto, par):
        return 16 * sum(nd for j, (_, _, nd) in enumerate(SLOTS)
                        if j <= upto and j % 2 == par)

    def slot_of(pi, dto):
        P = PLIST[pi]
        if P == "v":
            return 8 + dto
        return pi * 4 + dto // 2

    @blk.sync
    def _(q):
        q.wait_ge(p.bar_sem, bar0)
        for dt in range(NDT):
            q.dma_start(out=xns[:, dt * L:(dt + 1) * L],
                        in_=xn2d[dt * 128:(dt + 1) * 128, :]).then_inc(xlq, 16)
        for pi, P in enumerate(PLIST):
            q.dma_start(out=cvw[:, pi * 32:(pi + 1) * 32],
                        in_=csrc[P].rearrange("(dtp pp) k4 -> pp dtp k4", pp=128)
                        ).then_inc(cl, 16)

    @blk.gpsimd
    def _(g):
        g.wait_ge(p.bar_sem, bar0)
        g.memset(cbuf[0][:, 0:3], 0.0)
        g.memset(cbuf[1][:, 0:3], 0.0)
        g.engine_nop().then_inc(vcv, 1)
        g.dma_start(out=wbsb[:], in_=dr["w_wb"].rearrange("(dtp pp) e -> pp dtp e", pp=128)
                    ).then_inc(wlb, 16)
        g.dma_start(out=wcur[:],
                    in_=wsrc["q"].rearrange("(dtp pp) dd -> pp dtp dd", pp=128)
                    ).then_inc(wl, 16)

    @blk.tensor
    def _(t):
        t.wait_ge(p.bar_sem, bar0)
        t.wait_ge(xlq, 16 * NDT)
        t.wait_ge(wlb, 16)
        for c in range(8):
            if c >= 1:
                t.wait_ge(sbe, c)
            for dt in range(NDT):
                mm = t.matmul(p.PS[2][0:4, :], wbsb[:, dt * 4:(dt + 1) * 4],
                              xns[:, dt * L + c * 512: dt * L + (c + 1) * 512],
                              start=(dt == 0), stop=(dt == NDT - 1))
            mm.then_inc(tbe, 1)
        ng = 0
        npair = 0
        for pi, P in enumerate(PLIST):
            t.wait_ge(wl, 16 * (pi + 1))
            for dto in range(NDT):
                for c in range(8):
                    if ng >= 2:
                        t.wait_ge(vev, ng - 1)
                    for dt in range(NDT):
                        mm = t.matmul(p.PS[ng % 2][:, :],
                                      wcur[:, dt * 1024 + dto * 128: dt * 1024 + (dto + 1) * 128],
                                      xns[:, dt * L + c * 512: dt * L + (c + 1) * 512],
                                      start=(dt == 0), stop=(dt == NDT - 1))
                    mm.then_inc(tpr, 1)
                    ng += 1
                if P != "v" and dto % 2 == 1:
                    t.wait_ge(vsq, npair + 1)
                    for c in range(8):
                        if npair * 8 + c >= 1:
                            t.wait_ge(ssr, npair * 8 + c)
                        for u in range(2):
                            mm = t.matmul(p.PS[3][:, :], G["ones"][:],
                                          cnv[u][:, c * 512:(c + 1) * 512],
                                          start=(u == 0), stop=(u == 1))
                        mm.then_inc(tsq, 1)
                    npair += 1

    @blk.vector
    def _(v):
        v.wait_ge(p.bar_sem, bar0)
        v.wait_ge(cl, 48)
        ng = 0
        npair = 0
        for pi, P in enumerate(PLIST):
            for dto in range(NDT):
                j = pi * NDT + dto
                A = dto % 2
                if ng == 0:
                    v.wait_ge(vcv, 1)
                for c in range(8):
                    v.wait_ge(tpr, ng + 1)
                    ii = v.tensor_copy(cbuf[A][:, 3 + c * 512: 3 + (c + 1) * 512],
                                       p.PS[ng % 2][:, :])
                    v.engine_nop().then_inc(vev, 1)
                    ng += 1
                p.edge(ii, v, "qkcb")
                if j >= 1:
                    v.wait_ge(ssl, j)
                if npair >= 1:
                    v.wait_ge(tsq, 8 * npair)
                wt = lambda k: cvw[:, (pi * 8 + dto) * 4 + k: (pi * 8 + dto) * 4 + k + 1]
                other, me = cnv[1 - A], cnv[A]
                p.edge(v.tensor_scalar_mul(other[:], cbuf[A][:, 0:L], wt(0)), v, "qkc0")
                p.edge(v.scalar_tensor_tensor(me[:], cbuf[A][:, 1:1 + L], wt(1), other[:],
                                              ALU.mult, ALU.add), v, "qkc1")
                p.edge(v.scalar_tensor_tensor(other[:], cbuf[A][:, 2:2 + L], wt(2), me[:],
                                              ALU.mult, ALU.add), v, "qkc2")
                v.scalar_tensor_tensor(me[:], cbuf[A][:, 3:3 + L], wt(3), other[:],
                                       ALU.mult, ALU.add).then_inc(vcv, 1)
                if P != "v" and dto % 2 == 1:
                    v.wait_ge(ssl, j + 1)
                    v.tensor_mul(cnv[0][:], qh[:, 0:L], qh[:, 0:L])
                    v.tensor_mul(cnv[1][:], qh[:, L:2 * L], qh[:, L:2 * L]
                                 ).then_inc(vsq, 1)
                    for c in range(8):
                        v.wait_ge(ssr, npair * 8 + c + 1)
                        p.edge(v.reciprocal(rno[:], sdq[:]), v, "qkrn")
                        v.tensor_mul(qh[:, c * 512:(c + 1) * 512],
                                     qh[:, c * 512:(c + 1) * 512], rno[:])
                        v.tensor_mul(qh[:, L + c * 512: L + (c + 1) * 512],
                                     qh[:, L + c * 512: L + (c + 1) * 512], rno[:]
                                     ).then_inc(vsc, 1)
                    npair += 1

    @blk.scalar
    def _(s_):
        s_.wait_ge(p.bar_sem, bar0)
        for c in range(8):
            s_.wait_ge(tbe, c + 1)
            s_.activation(betab[:, c * 512:(c + 1) * 512], p.PS[2][0:4, :],
                          AF.Sigmoid).then_inc(sbe, 1)
        npair = 0
        for pi, P in enumerate(PLIST):
            for dto in range(NDT):
                j = pi * NDT + dto
                A = dto % 2
                s_.wait_ge(vcv, 1 + j + 1)
                sl = slot_of(pi, dto)
                prev = sl - 1
                if prev >= 0 and (P == "v" or dto % 2 == 0):
                    s_.wait_ge(st2[prev % 2], st_total(prev, prev % 2))
                s_.activation(qh[:, A * L:(A + 1) * L], cnv[A][:], AF.Silu
                              ).then_inc(ssl, 1)
                if P != "v" and dto % 2 == 1:
                    for c in range(8):
                        s_.wait_ge(tsq, npair * 8 + c + 1)
                        if npair * 8 + c >= 1:
                            s_.wait_ge(vsc, npair * 8 + c)
                        s_.activation(sdq[:], p.PS[3][:, :], AF.Sqrt
                                      ).then_inc(ssr, 1)
                    npair += 1

    @blk.gpsimd
    def _(g):
        npair = 0
        for sl, (P, idx, nd) in enumerate(SLOTS):
            if sl == 4:
                g.wait_ge(tpr, 64)
                g.dma_start(out=wcur[:],
                            in_=wsrc["k"].rearrange("(dtp pp) dd -> pp dtp dd", pp=128)
                            ).then_inc(wl, 16)
            if sl == 8:
                g.wait_ge(tpr, 128)
                g.dma_start(out=wcur[:],
                            in_=wsrc["v"].rearrange("(dtp pp) dd -> pp dtp dd", pp=128)
                            ).then_inc(wl, 16)
            if P == "v":
                dto = idx
                g.wait_ge(ssl, 16 + dto + 1)
                g.dma_start(out=outs["v"][dto * 128:(dto + 1) * 128, :],
                            in_=qh[:, (dto % 2) * L:(dto % 2 + 1) * L]
                            ).then_inc(st2[sl % 2], 16)
            else:
                g.wait_ge(vsc, npair * 8 + 8)
                for u in range(2):
                    dto = idx * 2 + u
                    g.dma_start(out=outs[P][dto * 128:(dto + 1) * 128, :],
                                in_=qh[:, u * L:(u + 1) * L]
                                ).then_inc(st2[sl % 2], 16)
                npair += 1
        g.wait_ge(sbe, 8)
        g.dma_start(out=dr["betad"], in_=betab[:]).then_inc(st2[0], 16)
        g.wait_ge(st2[0], st_total(15, 0) + 16)
        g.wait_ge(st2[1], st_total(15, 1))
        g.engine_nop().then_inc(p.bar_sem, 1)
    p.bar += 1


def ph_l1core(p, dr):
    nc, blk = p.nc, p.block
    G = p.G
    bar0 = p.bar
    tsm = p.sem("lctsm")
    ssm = p.sem("lcssm")
    vsm = p.sem("lcvsm")
    gl2 = [p.sem("lcgl_a", dma=True), p.sem("lcgl_b", dma=True)]
    ws2 = [p.sem("lcws_a", dma=True), p.sem("lcws_b", dma=True)]
    hl2 = [p.sem("lchl_a", dma=True), p.sem("lchl_b", dma=True)]
    sst2 = [p.sem("lcss_a", dma=True), p.sem("lcss_b", dma=True)]
    wol = p.sem("lcwol", dma=True)

    NG = NCK // 8
    qnd, knd, vsd, betad = dr["qnd"], dr["knd"], dr["vsd"], dr["betad"]
    wtd, utd, s2d = dr["wtd"], dr["utd"], dr["s2d"]

    Qg = p.sb("lcQ", [128, 4096], F32)
    Wg_ = p.sb("lcWg", [128, 4096], F32)
    Ug_ = p.sb("lcUg", [128, 4096], F32)
    Wg = [Wg_, Wg_]
    Ug = [Ug_, Ug_]
    Rall = p.sb("lcRall", [128, 512], F32)
    Pall = p.sb("lcPall", [128, 512], F32)
    aqk = p.sb("lcaqk", [128, L], F32)
    tmpg = p.sb("lctmpg", [128, 512], F32)
    Sdall = p.sb("lcSd", [128, 512], F32)
    SdPre = p.sb("lcSdP", [128, 512], F32)
    nSdPre = p.sb("lcnSdP", [128, 512], F32)
    av = p.sb("lcav", [128, 512], F32)

    jobs = []
    tensor_order = []

    def J(t=None, s=None, v=None, deps=(), tx=(), vx=(), sx=()):
        jid = len(jobs)
        jobs.append(dict(t=t, s=s, v=v, deps=tuple(deps), tx=list(tx),
                         vx=list(vx), sx=list(sx)))
        if t is not None:
            tensor_order.append(jid)
        return jid

    sub1 = ExitStack()
    Kg = [sub1.enter_context(nc.sbuf_tensor(f"lcK{i}", [128, 4096], F32)) for i in range(2)]
    Vg = sub1.enter_context(nc.sbuf_tensor("lcV", [128, 4096], F32))
    bb = sub1.enter_context(nc.sbuf_tensor("lcbb", [128, 4 * 512], F32))
    beg = sub1.enter_context(nc.sbuf_tensor("lcbeg", [4, 512], F32))
    Kb = [sub1.enter_context(nc.sbuf_tensor(f"lcKb{i}", [128, 1024], F32)) for i in range(2)]
    Vb = [sub1.enter_context(nc.sbuf_tensor(f"lcVb{i}", [128, 1024], F32)) for i in range(2)]
    KbT = [sub1.enter_context(nc.sbuf_tensor(f"lcKbT{i}", [64, 2048], F32)) for i in range(2)]
    VbT = [sub1.enter_context(nc.sbuf_tensor(f"lcVbT{i}", [64, 2048], F32)) for i in range(2)]
    sqg = sub1.enter_context(nc.sbuf_tensor("lcsqg", [128, 1024], F32))
    ind4 = sub1.enter_context(nc.sbuf_tensor("lcind4", [4, 512], F32))
    i4s = p.sem("lci4")
    TC = [{n: sub1.enter_context(nc.sbuf_tensor(f"lc{n}{i}", [64, 128], F32)) for n in
           ("Lst", "Bst", "L2", "B2", "L4", "B4", "L8", "B8", "L16", "B16",
            "L32", "R", "R2", "TT")} for i in range(2)]


    NLD1 = 25

    def gl_val(g):
        return 16 * NLD1 * (g // 2 + 1)

    def init_jobs():
        def v_(v, bank):
            return v.engine_nop()
        return J(v=v_)

    def bb_jobs(g):
        gp = g % 2
        ids = []
        for h in range(4):
            def t_(t, bank, h=h):
                return t.matmul(bank[:, :], ind4[:, h * 128:(h + 1) * 128], beg[0:4, :],
                                start=True, stop=True)
            def v_(v, bank, h=h):
                return v.tensor_copy(bb[:, h * 512:(h + 1) * 512], bank[:, :])
            ids.append(J(t=t_, v=v_, tx=[(gl2[gp], gl_val(g)), (i4s, 2)]))
        return ids

    def prep_jobs(q, j_init):
        pp, gp = q % 2, (q // 4) % 2
        ci0 = (2 * q) % 8
        ids = []
        for dt in range(NDT):
            def v_(v, bank, dt=dt, pp=pp, gp=gp, ci0=ci0):
                v.tensor_mul(Kb[pp][:, dt * 128:(dt + 1) * 128],
                             Kg[gp][:, dt * 512 + ci0 * 64: dt * 512 + ci0 * 64 + 128],
                             bb[:, (dt // 2) * 512 + ci0 * 64: (dt // 2) * 512 + ci0 * 64 + 128])
                return v.tensor_mul(Vb[pp][:, dt * 128:(dt + 1) * 128],
                             Vg[:, dt * 512 + ci0 * 64: dt * 512 + ci0 * 64 + 128],
                             bb[:, (dt // 2) * 512 + ci0 * 64: (dt // 2) * 512 + ci0 * 64 + 128])
            ids.append(J(v=v_, vx=[(gl2[gp], gl_val(q // 4))], deps=[j_init]))
        return ids

    def agram_jobs(q, prep_ids, j_init):
        pp, gp = q % 2, (q // 4) % 2
        ci0 = (2 * q) % 8
        def tA(t, bank):
            for cc in range(2):
                for dt in range(NDT):
                    mm = t.matmul(bank[0:64, cc * 64: cc * 64 + 64],
                                  Kb[pp][:, dt * 128 + cc * 64: dt * 128 + cc * 64 + 64],
                                  Kg[gp][:, dt * 512 + (ci0 + cc) * 64: dt * 512 + (ci0 + cc) * 64 + 64],
                                  start=(dt == 0), stop=(dt == NDT - 1))
            return mm
        def vA(v, bank):
            for cc in range(2):
                ii = v.tensor_mul(TC[pp]["Lst"][:, cc * 64: cc * 64 + 64],
                                  bank[0:64, cc * 64: cc * 64 + 64], G["mlow"][:])
            return ii
        def tB(t, bank):
            for cc in range(2):
                for dt in range(NDT):
                    mm = t.matmul(bank[0:64, cc * 64: cc * 64 + 64],
                                  Kg[gp][:, dt * 512 + (ci0 + cc) * 64: dt * 512 + (ci0 + cc) * 64 + 64],
                                  Kb[pp][:, dt * 128 + cc * 64: dt * 128 + cc * 64 + 64],
                                  start=(dt == 0), stop=(dt == NDT - 1))
            return mm
        def vB(v, bank):
            for cc in range(2):
                ii = v.tensor_mul(TC[pp]["Bst"][:, cc * 64: cc * 64 + 64],
                                  bank[0:64, cc * 64: cc * 64 + 64], G["mup"][:])
            return ii
        ja = J(t=tA, v=vA, deps=list(prep_ids) + [j_init])
        jb = J(t=tB, v=vB, deps=list(prep_ids) + [j_init])
        return ja, jb

    def tr_thunks(q, prep_ids):
        pp = q % 2
        thunks = []
        for dt in range(NDT):
            for srcb, dstb in ((Kb, KbT), (Vb, VbT)):
                def mk(srcb=srcb, dstb=dstb, dt=dt, pp=pp):
                    def t_(t, bank):
                        for cc in range(2):
                            mm = t.transpose(bank[0:64, cc * 128: cc * 128 + 128],
                                             srcb[pp][:, dt * 128 + cc * 64: dt * 128 + cc * 64 + 64],
                                             G["ident"][:])
                        return mm
                    def v_(v, bank):
                        return v.tensor_copy(dstb[pp][:, dt * 256:(dt + 1) * 256],
                                             bank[0:64, 0:256])
                    return J(t=t_, v=v_, deps=prep_ids)
                thunks.append(mk)
        return thunks

    def chain_thunks(q, ja, jb, j_init):
        pp = q % 2
        C = TC[pp]
        seq = [("L2", "Bst", "Lst"), ("B2", "Lst", "Bst"),
               ("L4", "B2", "L2"), ("B4", "L2", "B2"),
               ("L8", "B4", "L4"), ("B8", "L4", "B4"),
               ("L16", "B8", "L8"), ("B16", "L8", "B8"),
               ("L32", "B16", "L16")]
        dep_of = {"Lst": ja, "Bst": jb}
        state = {}
        thunks = []
        for nm, lh, rh in seq:
            def mk(nm=nm, lh=lh, rh=rh):
                def t_(t, bank):
                    for cc in range(2):
                        mm = t.matmul(bank[0:64, cc * 64: cc * 64 + 64],
                                      C[lh][:, cc * 64: cc * 64 + 64],
                                      C[rh][:, cc * 64: cc * 64 + 64],
                                      start=True, stop=True)
                    return mm
                def v_(v, bank):
                    return v.tensor_copy(C[nm][:], bank[0:64, 0:128])
                dep_of[nm] = J(t=t_, v=v_, deps=[dep_of[lh], dep_of[rh]])
                return dep_of[nm]
            thunks.append(mk)
        def mkR0():
            def vR0(v, bank):
                for cc in range(2):
                    ii = v.tensor_sub(C["R"][:, cc * 64: cc * 64 + 64],
                                      G["ident"][0:64, 0:64],
                                      C["Bst"][:, cc * 64: cc * 64 + 64])
                return ii
            state["jr"] = J(v=vR0, deps=[jb, j_init])
            return state["jr"]
        thunks.insert(2, mkR0)
        for lk in ("L2", "L4", "L8", "L16", "L32"):
            def mk(lk=lk):
                def t_(t, bank):
                    for cc in range(2):
                        mm = t.matmul(bank[0:64, cc * 64: cc * 64 + 64],
                                      C[lk][:, cc * 64: cc * 64 + 64],
                                      C["R"][:, cc * 64: cc * 64 + 64],
                                      start=True, stop=True)
                    return mm
                def v_(v, bank):
                    ii = v.tensor_add(C["R2"][:], bank[0:64, 0:128], C["R"][:])
                    p.edge(ii, v, "lcr")
                    return v.tensor_copy(C["R"][:], C["R2"][:])
                state["jr"] = J(t=t_, v=v_, deps=[state["jr"], dep_of[lk]])
                return state["jr"]
            thunks.append(mk)
        def mkT():
            def tT(t, bank):
                for cc in range(2):
                    mm = t.matmul(bank[0:64, cc * 64: cc * 64 + 64],
                                  C["Lst"][:, cc * 64: cc * 64 + 64],
                                  C["R"][:, cc * 64: cc * 64 + 64],
                                  start=True, stop=True)
                return mm
            def vT(v, bank):
                return v.scalar_tensor_tensor(C["TT"][:], bank[0:64, 0:128], -2.0,
                                              C["R"][:], ALU.mult, ALU.add)
            state["jt"] = J(t=tT, v=vT, deps=[state["jr"], ja])
            return state["jt"]
        thunks.append(mkT)
        return thunks, (lambda: state["jt"])

    def wu_jobs(q, jt, trids):
        g = q // 4
        pp, gp = q % 2, (q // 4) % 2
        ci0 = (2 * q) % 8
        wg, ug = Wg[g % 2], Ug[g % 2]
        C = TC[pp]
        ids = []
        for dt in range(NDT):
            for which in (0, 1):
                srcT = (KbT if which == 0 else VbT)
                dst = (wg if which == 0 else ug)
                acc = (Rall if which == 0 else Pall)
                def t_(t, bank, srcT=srcT, dt=dt, pp=pp):
                    for cc in range(2):
                        mm = t.matmul(bank[:, cc * 64: cc * 64 + 64],
                                      srcT[pp][:, dt * 256 + cc * 128: dt * 256 + (cc + 1) * 128],
                                      C["TT"][:, cc * 64: cc * 64 + 64],
                                      start=True, stop=True)
                    return mm
                def v_(v, bank, dst=dst, dt=dt, ci0=ci0, acc=acc, q=q, gp=gp):
                    sl = slice(dt * 512 + ci0 * 64, dt * 512 + ci0 * 64 + 128)
                    v.tensor_copy(dst[:, sl], bank[:, 0:128])
                    for cc in range(2):
                        i = 2 * q + cc
                        ii = v.scalar_tensor_tensor(
                            bank[:, 128 + cc * 64: 128 + cc * 64 + 64],
                            Kg[gp][:, dt * 512 + (ci0 + cc) * 64: dt * 512 + (ci0 + cc) * 64 + 64],
                            1.0, bank[:, cc * 64: cc * 64 + 64], ALU.mult, ALU.mult,
                            accum_out=acc[:, dt * 64 + (i % 64): dt * 64 + (i % 64) + 1])
                    return ii
                vx = []
                if g >= 1:
                    vx.append((ws2[(g - 1) % 2], 16 * 16 * ((g - 1) // 2 + 1)))
                ids.append(J(t=t_, v=v_, deps=[jt] + list(trids), vx=vx))
        return ids

    def aqk_jobs(g, j_init):
        gp = g % 2
        ids = []
        for qq in range(4):
            ci0 = qq * 2
            def vpre(v, bank, ci0=ci0, gp=gp):
                for dt in range(NDT):
                    ii = v.tensor_mul(sqg[:, dt * 128:(dt + 1) * 128],
                                 Qg[:, dt * 512 + ci0 * 64: dt * 512 + ci0 * 64 + 128],
                                 Kg[gp][:, dt * 512 + ci0 * 64: dt * 512 + ci0 * 64 + 128])
                return ii
            jpre = J(v=vpre, vx=[(gl2[gp], gl_val(g))])
            def t_(t, bank):
                for cc in range(2):
                    for dt in range(NDT):
                        mm = t.matmul(bank[:, cc * 64: cc * 64 + 64], G["ones"][:],
                                      sqg[:, dt * 128 + cc * 64: dt * 128 + cc * 64 + 64],
                                      start=(dt == 0), stop=(dt == NDT - 1))
                return mm
            def v_(v, bank, g=g, ci0=ci0):
                for cc in range(2):
                    ii = v.tensor_mul(aqk[:, g * 512 + (ci0 + cc) * 64: g * 512 + (ci0 + cc) * 64 + 64],
                                      bank[:, cc * 64: cc * 64 + 64], G["bmask"][:])
                return ii
            ids.append(J(t=t_, v=v_, deps=[jpre, j_init]))
        return ids

    j_init = init_jobs()
    prep_last = {}
    group_wu_last = {}
    pending = None
    NPAIR = NCK // 2
    for q in range(NPAIR):
        g = q // 4
        if q % 4 == 0:
            bbids = bb_jobs(g)
            aqk_jobs(g, j_init)
        pr = prep_jobs(q, j_init)
        ja, jb = agram_jobs(q, pr, j_init)
        trth = tr_thunks(q, pr)
        chth, get_jt = chain_thunks(q, ja, jb, j_init)
        trids = []
        k1 = k2 = 0
        while k1 < len(trth) or k2 < len(chth):
            if k1 < len(trth):
                trids.append(trth[k1]())
                k1 += 1
            if k2 < len(chth):
                chth[k2]()
                k2 += 1
        if pending is not None:
            qp, jtp, trp = pending
            wids = wu_jobs(qp, jtp, trp)
            group_wu_last[qp // 4] = wids[-1]
        pending = (q, get_jt(), trids)
        prep_last[g] = max(pr)
    qp, jtp, trp = pending
    wids = wu_jobs(qp, jtp, trp)
    group_wu_last[NG - 1] = wids[-1]

    def vscan(v, bank):
        ii = v.tensor_scalar(av[:], Rall[:], -1.0, 1.0, ALU.mult, ALU.add)
        p.edge(ii, v, "lcsc")
        for dt in range(NDT):
            sl = slice(dt * 64, (dt + 1) * 64)
            ii = v.tensor_tensor_scan(Sdall[:, sl], av[:, sl], Pall[:, sl],
                                      0.0, ALU.mult, ALU.add)
        p.edge(ii, v, "lcsc2")
        ii = v.memset(SdPre[:], 0.0)
        p.edge(ii, v, "lcscm")
        for dt in range(NDT):
            ii = v.tensor_copy(SdPre[:, dt * 64 + 1: dt * 64 + 64],
                               Sdall[:, dt * 64: dt * 64 + 63])
        p.edge(ii, v, "lcsc3")
        return v.tensor_scalar_mul(nSdPre[:], SdPre[:], -1.0)
    j_scan = J(v=vscan, deps=[group_wu_last[NG - 1]])

    sub1.close()

    ub = p.sb("lcub", [128, 4096], F32)
    tmp3 = p.sb("lctmp3", [128, 4096], F32)
    ob = p.sb("lcob", [128, 4096], F32)
    onb = p.sb("lconb", [128, 4096], BF16)
    wo_bf = p.sb("lcwo", [128, NDT * 1024], BF16)
    sdq = [p.sb(f"lcsdq{i}", [128, 512], F32) for i in range(2)]
    rno = p.sb("lcrno", [128, 512], F32)
    s2st = [p.sb(f"lcs2{i}", [128, 512], F32) for i in range(2)]

    NLD3 = 24

    def hl_val(g):
        return 16 * NLD3 * (g // 2 + 1)

    s2_jobs = {}
    g3_last = {}
    rms_hist = []
    for g in range(NG):
        gp = g % 2
        def vel(v, bank, g=g, gp=gp):
            for dt in range(NDT):
                for ci in range(8):
                    i = g * 8 + ci
                    sl = slice(dt * 512 + ci * 64, dt * 512 + ci * 64 + 64)
                    v.scalar_tensor_tensor(ub[:, sl], Wg[gp][:, sl],
                                           nSdPre[:, dt * 64 + i % 64: dt * 64 + i % 64 + 1],
                                           Ug[gp][:, sl], ALU.mult, ALU.add)
            ii = v.engine_nop()
            p.edge(ii, v, "lce1")
            for dt in range(NDT):
                for ci in range(8):
                    sl = slice(dt * 512 + ci * 64, dt * 512 + ci * 64 + 64)
                    v.tensor_mul(tmp3[:, sl],
                                 aqk[:, g * 512 + ci * 64: g * 512 + ci * 64 + 64],
                                 ub[:, sl])
            ii = v.engine_nop()
            p.edge(ii, v, "lce2")
            for dt in range(NDT):
                for ci in range(8):
                    i = g * 8 + ci
                    sl = slice(dt * 512 + ci * 64, dt * 512 + ci * 64 + 64)
                    ii = v.scalar_tensor_tensor(ob[:, sl], Qg[:, sl],
                                                SdPre[:, dt * 64 + i % 64: dt * 64 + i % 64 + 1],
                                                tmp3[:, sl], ALU.mult, ALU.add)
            return ii
        j_el = J(v=vel, deps=[j_scan], vx=[(hl2[gp], hl_val(g))])
        last_rms = None
        for h in range(4):
            def vsq_(v, bank, h=h):
                v.tensor_mul(tmp3[:, (2 * h) * 512:(2 * h) * 512 + 512],
                             ob[:, (2 * h) * 512:(2 * h + 1) * 512],
                             ob[:, (2 * h) * 512:(2 * h + 1) * 512])
                return v.tensor_mul(tmp3[:, (2 * h + 1) * 512:(2 * h + 1) * 512 + 512],
                             ob[:, (2 * h + 1) * 512:(2 * h + 2) * 512],
                             ob[:, (2 * h + 1) * 512:(2 * h + 2) * 512])
            j_sq = J(v=vsq_, deps=[j_el])
            rp = len(rms_hist) % 2
            sx = []
            if len(rms_hist) >= 2:
                sx.append((vsm, rms_hist[-2] + 1))
            def t_(t, bank, h=h):
                for u in range(2):
                    mm = t.matmul(bank[:, :], G["ones"][:],
                                  tmp3[:, (2 * h + u) * 512:(2 * h + u) * 512 + 512],
                                  start=(u == 0), stop=(u == 1))
                return mm
            def s_(s_, bank, rp=rp):
                return s_.activation(sdq[rp][:], bank[:, :], AF.Sqrt,
                                     bias=G["eps_rms"][:, 0:1], scale=1.0 / HD)
            def v_(v, bank, h=h, rp=rp):
                rr = v.reciprocal(rno[:], sdq[rp][:])
                p.edge(rr, v, "lcrn")
                for u in range(2):
                    dt = 2 * h + u
                    ii = v.scalar_tensor_tensor(onb[:, dt * 512:(dt + 1) * 512],
                                                ob[:, dt * 512:(dt + 1) * 512],
                                                G["gvec"][:, dt:dt + 1], rno[:],
                                                ALU.mult, ALU.mult)
                return ii
            jr = J(t=t_, s=s_, v=v_, deps=[j_sq], sx=sx)
            rms_hist.append(jr)
            last_rms = jr
        for dto in range(NDT):
            spar = (g * NDT + dto) % 2
            vx = []
            if g * NDT + dto >= 2:
                vx.append((sst2[spar], 16 * ((g * NDT + dto) // 2)))
            def t_(t, bank, dto=dto):
                for dt in range(NDT):
                    mm = t.matmul(bank[:, :],
                                  wo_bf[:, dt * 1024 + dto * 128: dt * 1024 + (dto + 1) * 128],
                                  onb[:, dt * 512:(dt + 1) * 512],
                                  start=(dt == 0), stop=(dt == NDT - 1))
                return mm
            def v_(v, bank, spar=spar):
                return v.tensor_copy(s2st[spar][:], bank[:, :])
            jw = J(t=t_, v=v_, deps=[last_rms], tx=[(wol, 16)], vx=vx)
            s2_jobs[(g, dto)] = jw
        g3_last[g] = len(jobs) - 1

    ntj = {}
    tc = 0
    for jid, jb_ in enumerate(jobs):
        if jb_["t"] is not None:
            tc += 1
        ntj[jid] = tc
    nsj = {}
    sc = 0
    for jid, jb_ in enumerate(jobs):
        if jb_["s"] is not None:
            sc += 1
        nsj[jid] = sc

    RING = [1, 2, 3, 4, 5, 6]
    def bank(jid):
        return p.PS[RING[(ntj[jid] - 1) % len(RING)]]

    @blk.sync
    def _(q):
        q.wait_ge(p.bar_sem, bar0)
        for g in range(NG):
            gp = g % 2
            if g >= 1:
                q.wait_ge(vsm, prep_last[g - 1] + 1)
            if g >= 2:
                q.wait_ge(vsm, group_wu_last[g - 2] + 1)
            for dt in range(NDT):
                q.dma_start(out=Qg[:, dt * 512:(dt + 1) * 512],
                            in_=qnd[dt * 128:(dt + 1) * 128, g * 512:(g + 1) * 512]
                            ).then_inc(gl2[gp], 16)
            for dt in range(NDT):
                q.dma_start(out=Kg[gp][:, dt * 512:(dt + 1) * 512],
                            in_=knd[dt * 128:(dt + 1) * 128, g * 512:(g + 1) * 512]
                            ).then_inc(gl2[gp], 16)
            for dt in range(NDT):
                q.dma_start(out=Vg[:, dt * 512:(dt + 1) * 512],
                            in_=vsd[dt * 128:(dt + 1) * 128, g * 512:(g + 1) * 512]
                            ).then_inc(gl2[gp], 16)
            q.dma_start(out=beg[:], in_=betad[:, g * 512:(g + 1) * 512]
                        ).then_inc(gl2[gp], 16)
        q.wait_ge(vsm, j_scan + 1)
        q.wait_ge(ws2[0], 16 * 16 * 4)
        q.wait_ge(ws2[1], 16 * 16 * 4)
        for g in range(NG):
            gp = g % 2
            if g >= 1:
                q.wait_ge(vsm, g3_last[g - 1] + 1)
            for dt in range(NDT):
                q.dma_start(out=Qg[:, dt * 512:(dt + 1) * 512],
                            in_=qnd[dt * 128:(dt + 1) * 128, g * 512:(g + 1) * 512]
                            ).then_inc(hl2[gp], 16)
            for dt in range(NDT):
                q.dma_start(out=Wg[gp][:, dt * 512:(dt + 1) * 512],
                            in_=wtd[dt * 128:(dt + 1) * 128, g * 512:(g + 1) * 512]
                            ).then_inc(hl2[gp], 16)
            for dt in range(NDT):
                q.dma_start(out=Ug[gp][:, dt * 512:(dt + 1) * 512],
                            in_=utd[dt * 128:(dt + 1) * 128, g * 512:(g + 1) * 512]
                            ).then_inc(hl2[gp], 16)

    @blk.gpsimd
    def _(g_):
        g_.wait_ge(p.bar_sem, bar0)
        g_.iota(ind4[:], [[1, 4], [0, 128]], channel_multiplier=-1,
                allow_small_or_imprecise_dtypes=True).then_inc(i4s, 1)
        for g in range(NG):
            gp = g % 2
            g_.wait_ge(vsm, group_wu_last[g] + 1)
            for dt in range(NDT):
                g_.dma_start(out=wtd[dt * 128:(dt + 1) * 128, g * 512:(g + 1) * 512],
                             in_=Wg[gp][:, dt * 512:(dt + 1) * 512]).then_inc(ws2[gp], 16)
            for dt in range(NDT):
                g_.dma_start(out=utd[dt * 128:(dt + 1) * 128, g * 512:(g + 1) * 512],
                             in_=Ug[gp][:, dt * 512:(dt + 1) * 512]).then_inc(ws2[gp], 16)
        g_.wait_ge(vsm, j_scan + 1)
        g_.dma_start(out=wo_bf[:],
                     in_=dr["w_wo"].rearrange("(dtp pp) dd -> pp dtp dd", pp=128)
                     ).then_inc(wol, 16)
        nst = 0
        for g in range(NG):
            for dto in range(NDT):
                spar = nst % 2
                g_.wait_ge(vsm, s2_jobs[(g, dto)] + 1)
                g_.dma_start(out=s2d[dto * 128:(dto + 1) * 128, g * 512:(g + 1) * 512],
                             in_=s2st[spar][:]).then_inc(sst2[spar], 16)
                nst += 1
        g_.wait_ge(sst2[0], 16 * 32)
        g_.wait_ge(sst2[1], 16 * 32)
        g_.engine_nop().then_inc(p.bar_sem, 1)

    @blk.tensor
    def _(t):
        t.wait_ge(p.bar_sem, bar0)
        for mo, jid in enumerate(tensor_order):
            jb_ = jobs[jid]
            if mo >= len(RING):
                t.wait_ge(vsm, tensor_order[mo - len(RING)] + 1)
            if jb_["deps"]:
                t.wait_ge(vsm, max(jb_["deps"]) + 1)
            for sem, val in jb_["tx"]:
                t.wait_ge(sem, val)
            mm = jb_["t"](t, bank(jid))
            mm.then_inc(tsm, 1)

    @blk.scalar
    def _(s_):
        s_.wait_ge(p.bar_sem, bar0)
        for jid, jb_ in enumerate(jobs):
            if jb_["s"] is None:
                continue
            s_.wait_ge(tsm, ntj[jid])
            for sem, val in jb_["sx"]:
                s_.wait_ge(sem, val)
            ii = jb_["s"](s_, bank(jid))
            ii.then_inc(ssm, 1)

    @blk.vector
    def _(v):
        v.wait_ge(p.bar_sem, bar0)
        v.wait_ge(i4s, 1)
        v.tensor_single_scalar(ind4[:], ind4[:], 0.0, ALU.is_equal).then_inc(i4s, 1)
        for jid, jb_ in enumerate(jobs):
            if jb_["t"] is not None:
                v.wait_ge(tsm, ntj[jid])
            if jb_["s"] is not None:
                v.wait_ge(ssm, nsj[jid])
            if jb_["t"] is None and jb_["deps"]:
                v.wait_ge(vsm, max(jb_["deps"]) + 1)
            for sem, val in jb_["vx"]:
                v.wait_ge(sem, val)
            ii = jb_["v"](v, bank(jid) if jb_["t"] is not None else None)
            ii.then_inc(vsm, 1)
    p.bar += 1


def ph_route(p, dr):
    nc, blk = p.nc, p.block
    G = p.G
    bar0 = p.bar
    affd, cwtd = dr["affd"], dr["cwtd"]
    al2 = [p.sem("rtal_a", dma=True), p.sem("rtal_b", dma=True)]
    vbi = p.sem("rtvbi")
    vcn = p.sem("rtvcn")
    sgs = p.sem("rtsgs")
    ttr = p.sem("rtttr")
    vtr = p.sem("rtvtr")
    st2 = [p.sem("rtst_a", dma=True), p.sem("rtst_b", dma=True)]

    NT = L // 128
    afb = [p.sb(f"rtaf{i}", [128, NE], F32) for i in range(2)]
    bia = p.sb("rtbia", [128, NE], F32)
    cbuf = p.sb("rtcb", [128, NE * NE], F32)
    red = p.sb("rtred", [128, 2048], F32)
    sig = p.sb("rtsig", [128, NE], F32)
    msk = p.sb("rtmsk", [128, NE], F32)
    wsu = p.sb("rtwsu", [128, 1], F32)
    wiv = p.sb("rtwiv", [128, 1], F32)
    cwt = [p.sb(f"rtcw{i}", [128, NE], F32) for i in range(2)]
    cwT = p.sb("rtcwT", [64, 128 * 2], F32)

    @blk.sync
    def _(q):
        q.wait_ge(p.bar_sem, bar0)
        for tt in range(NT):
            A = tt % 2
            if tt >= 2:
                q.wait_ge(sgs, tt - 1)
                q.wait_ge(vcn, tt - 1)
            q.dma_start(out=afb[A][:], in_=affd[tt * 128:(tt + 1) * 128, :]
                        ).then_inc(al2[A], 16)

    @blk.vector
    def _(v):
        v.wait_ge(p.bar_sem, bar0)
        for tt in range(NT):
            A = tt % 2
            v.wait_ge(al2[A], 16 * (tt // 2 + 1))
            ii = v.tensor_add(bia[:], afb[A][:], G["ebias"][:, 0:NE])
            p.edge(ii, v, "rt1")
            for j in range(NE):
                ii = v.tensor_single_scalar(cbuf[:, j * NE:(j + 1) * NE], bia[:],
                                            bia[:, j:j + 1], ALU.is_lt)
            p.edge(ii, v, "rt2")
            width = NE
            srcbuf = cbuf
            while width > 1:
                half = width // 2
                for j in range(half):
                    ii = v.tensor_add(red[:, j * NE:(j + 1) * NE],
                                      srcbuf[:, j * NE:(j + 1) * NE],
                                      srcbuf[:, (j + half) * NE:(j + half + 1) * NE])
                p.edge(ii, v, "rt3")
                srcbuf = red
                width = half
            cnt = red[:, 0:NE]
            ii = v.tensor_single_scalar(msk[:], cnt, float(TOPK), ALU.is_lt)
            p.edge(ii, v, "rt4")
            v.wait_ge(sgs, tt + 1)
            ii = v.tensor_mul(sig[:], sig[:], msk[:])
            p.edge(ii, v, "rt5")
            ii = v.reduce_sum(wsu[:], sig[:], AX.X)
            p.edge(ii, v, "rt6")
            ii = v.tensor_scalar_add(wsu[:], wsu[:], 1e-8)
            p.edge(ii, v, "rt7")
            ii = v.reciprocal(wiv[:], wsu[:])
            p.edge(ii, v, "rt8")
            if tt >= 2:
                v.wait_ge(ttr, tt - 1)
            v.tensor_scalar_mul(cwt[A][:], sig[:], wiv[:, 0:1]).then_inc(vcn, 1)
            v.wait_ge(ttr, tt + 1)
            if tt >= 2:
                v.wait_ge(st2[A], 16 * (tt // 2))
            v.tensor_copy(cwT[:, (A) * 128:(A + 1) * 128],
                          p.PS[4 + A][0:64, 0:128]).then_inc(vtr, 1)

    @blk.scalar
    def _(s_):
        s_.wait_ge(p.bar_sem, bar0)
        for tt in range(NT):
            A = tt % 2
            s_.wait_ge(al2[A], 16 * (tt // 2 + 1))
            if tt >= 1:
                s_.wait_ge(vcn, tt)
            s_.activation(sig[:], afb[A][:], AF.Sigmoid).then_inc(sgs, 1)

    @blk.tensor
    def _(t):
        t.wait_ge(p.bar_sem, bar0)
        for tt in range(NT):
            A = tt % 2
            t.wait_ge(vcn, tt + 1)
            if tt >= 2:
                t.wait_ge(vtr, tt - 1)
            t.transpose(p.PS[4 + A][0:64, 0:128], cwt[A][:, 0:64] if False else cwt[A][:],
                        G["ident"][:]).then_inc(ttr, 1)

    @blk.gpsimd
    def _(g):
        for tt in range(NT):
            A = tt % 2
            g.wait_ge(vtr, tt + 1)
            g.dma_start(out=cwtd[:, tt * 128:(tt + 1) * 128],
                        in_=cwT[0:64, A * 128:(A + 1) * 128]).then_inc(st2[A], 16)
        g.wait_ge(st2[0], 16 * NT // 2)
        g.wait_ge(st2[1], 16 * NT // 2)
        g.engine_nop().then_inc(p.bar_sem, 1)
    p.bar += 1


def ph_moe(p, dr):
    nc, blk = p.nc, p.block
    G = p.G
    bar0 = p.bar
    xn2d, x1d, s2d, outd = dr["xn2d"], dr["x1d"], dr["s2d"], dr["outd"]
    xnl = p.sem("moxnl", dma=True)
    one = p.sem("moone", dma=True)
    wl2 = [p.sem("mowl_a", dma=True), p.sem("mowl_b", dma=True)]
    ie4 = [p.sem(f"moie_{i}") for i in range(4)]
    iv4 = [p.sem(f"moiv_{i}") for i in range(4)]
    tup = p.sem("motup")
    ssl = p.sem("mossl")
    tcb = p.sem("motcb")
    vhs = p.sem("movhs")
    tdn = p.sem("motdn")
    tda = p.sem("motda")
    vev = p.sem("movev")
    xls2 = [p.sem("moxl_a", dma=True), p.sem("moxl_b", dma=True)]
    ssc = p.sem("mossc")
    vcb = p.sem("movcb")
    ost2 = [p.sem("moos_a", dma=True), p.sem("moos_b", dma=True)]

    NHF = 2
    HT = L // 2
    NCH = HT // 256
    GE = 2
    NGRP = NE // GE
    UPH = NCH + NGRP * NCH * GE
    DNH = NCH + NGRP * NCH

    xnh = p.sb("moxn", [128, NDT * HT], BF16)
    acc = p.sb("moacc", [128, NDT * HT], F32)
    wup = [p.sb(f"mowu{i}", [128, GE * 4096], BF16) for i in range(2)]
    wdn = [p.sb(f"mowd{i}", [128, GE * 4096], BF16) for i in range(2)]
    wsu_bf = p.sb("mowsu", [128, NDT * 512], BF16)
    wsd_bf = p.sb("mowsd", [128, 4 * 1024], BF16)
    cwTb = p.sb("mocwT", [64, L], BF16)
    inde = [p.sb(f"moind{i}", [64, 128], BF16) for i in range(4)]
    hb = [p.sb(f"mohb{i}", [128, 4 * 256], BF16) for i in range(2)]
    hsc = [p.sb(f"mohs{i}", [128, 4 * 256], BF16) for i in range(2)]
    xsb = [p.sb(f"moxs{i}", [128, 2 * 256], F32) for i in range(2)]
    slb = [p.sb(f"mosl{i}", [128, 2 * 256], F32) for i in range(2)]
    otb = [p.sb(f"moot{i}", [128, 2 * 256], F32) for i in range(2)]

    def up_bank(ft):
        return p.PS[4 + ft // 2], (ft % 2) * 256
    def dn_bank(dto):
        return p.PS[dto // 2], (dto % 2) * 256
    def upcount_through_group(gi):
        H, grp = divmod(gi, NGRP)
        return UPH * H + NCH + (grp + 1) * NCH * GE
    def dncount_through_group(gi):
        H, grp = divmod(gi, NGRP)
        return DNH * H + NCH + (grp + 1) * NCH

    @blk.gpsimd
    def _(g):
        g.wait_ge(p.bar_sem, bar0)
        g.dma_start(out=wsu_bf[:],
                    in_=dr["w_wsu"].rearrange("(dtp pp) f -> pp dtp f", pp=128)
                    ).then_inc(one, 16)
        g.dma_start(out=wsd_bf[:],
                    in_=dr["w_wsd"].rearrange("(ftp pp) dd -> pp ftp dd", pp=128)
                    ).then_inc(one, 16)
        g.dma_start(out=cwTb[:], in_=dr["cwtd"]).then_inc(one, 16)
        for gi in range(NHF * NGRP):
            sl = gi % 2
            grp = gi % NGRP
            if gi >= 2:
                g.wait_ge(tup, upcount_through_group(gi - 2))
                g.wait_ge(tdn, dncount_through_group(gi - 2))
            for u in range(GE):
                e = grp * GE + u
                g.dma_start(out=wup[sl][:, u * 4096:(u + 1) * 4096],
                            in_=dr["w_wup"][e].rearrange("(dtp pp) f -> pp dtp f", pp=128)
                            ).then_inc(wl2[sl], 16)
                g.dma_start(out=wdn[sl][:, u * 4096:(u + 1) * 4096],
                            in_=dr["w_wdn"][e].rearrange("(ftp pp) dd -> pp ftp dd", pp=128)
                            ).then_inc(wl2[sl], 16)
            for u in range(GE):
                e = grp * GE + u
                ei = gi * GE + u
                par = ei % 4
                if ei >= 4:
                    g.wait_ge(tcb, NCH * GE * (ei // 2 - 1))
                    g.wait_ge(iv4[par], ei // 4)
                g.iota(inde[par][:], [[0, 128]], base=e, channel_multiplier=-1,
                       allow_small_or_imprecise_dtypes=True).then_inc(ie4[par], 1)

    @blk.gpsimd
    def _(g):
        g.wait_ge(ost2[0], 16 * 2 * 32)
        g.wait_ge(ost2[1], 16 * 2 * 32)
        g.engine_nop().then_inc(p.bar_sem, 1)

    @blk.sync
    def _(q):
        q.wait_ge(p.bar_sem, bar0)
        for H in range(NHF):
            if H >= 1:
                q.wait_ge(tup, UPH * H)
            for dt in range(NDT):
                q.dma_start(out=xnh[:, dt * HT:(dt + 1) * HT],
                            in_=xn2d[dt * 128:(dt + 1) * 128, H * HT:(H + 1) * HT]
                            ).then_inc(xnl, 16)
            for st in range(32):
                ns = H * 32 + st
                par = ns % 2
                ch, dq = divmod(st, 4)
                if ns >= 2:
                    q.wait_ge(ost2[par], 16 * 2 * (ns // 2))
                for i in range(2):
                    dt = dq * 2 + i
                    q.dma_start(out=xsb[par][:, i * 256:(i + 1) * 256],
                                in_=x1d[dt * 128:(dt + 1) * 128,
                                        H * HT + ch * 256: H * HT + (ch + 1) * 256]
                                ).then_inc(xls2[par], 16)
                for i in range(2):
                    dt = dq * 2 + i
                    q.dma_start(out=otb[par][:, i * 256:(i + 1) * 256],
                                in_=s2d[dt * 128:(dt + 1) * 128,
                                        H * HT + ch * 256: H * HT + (ch + 1) * 256]
                                ).then_inc(xls2[par], 16)

    @blk.tensor
    def _(t):
        t.wait_ge(p.bar_sem, bar0)
        t.wait_ge(one, 48)
        nup = 0
        ncb = 0
        ndn = 0
        for H in range(NHF):
            t.wait_ge(xnl, 16 * NDT * (H + 1))
            for ch in range(NCH):
                if nup >= 1:
                    t.wait_ge(ssl, nup)
                for ft in range(4):
                    bank, co = up_bank(ft)
                    for dt in range(NDT):
                        mm = t.matmul(bank[:, co:co + 256],
                                      wsu_bf[:, dt * 512 + ft * 128: dt * 512 + (ft + 1) * 128],
                                      xnh[:, dt * HT + ch * 256: dt * HT + (ch + 1) * 256],
                                      start=(dt == 0), stop=(dt == NDT - 1))
                mm.then_inc(tup, 1)
                nup += 1
                t.wait_ge(ssl, nup)
                t.wait_ge(vev, ndn)
                for dto in range(NDT):
                    bank, co = dn_bank(dto)
                    for ft in range(4):
                        mm = t.matmul(bank[:, co:co + 256],
                                      wsd_bf[:, ft * 1024 + dto * 128: ft * 1024 + (dto + 1) * 128],
                                      hb[(nup - 1) % 2][:, ft * 256:(ft + 1) * 256],
                                      start=(ft == 0 and dto % 2 == 0),
                                      stop=(ft == 3 and dto % 2 == 1))
                mm.then_inc(tdn, 1)
                ndn += 1
            pending = None

            def emit_down(t, job):
                nonlocal ndn
                sl_, u_, par_, nhs_ = job
                t.wait_ge(vhs, nhs_)
                if u_ == 0:
                    t.wait_ge(vev, ndn)
                for dto in range(NDT):
                    bank, co = dn_bank(dto)
                    for ft in range(4):
                        mm = t.matmul(bank[:, co:co + 256],
                                      wdn[sl_][:, u_ * 4096 + ft * 1024 + dto * 128:
                                              u_ * 4096 + ft * 1024 + (dto + 1) * 128],
                                      hsc[par_][:, ft * 256:(ft + 1) * 256],
                                      start=(u_ == 0 and ft == 0 and dto % 2 == 0),
                                      stop=(u_ == GE - 1 and ft == 3 and dto % 2 == 1))
                if u_ == GE - 1:
                    mm.then_inc(tdn, 1)
                    ndn += 1
                else:
                    mm.then_inc(tda, 1)

            for grp in range(NGRP):
                gi = H * NGRP + grp
                sl = gi % 2
                t.wait_ge(wl2[sl], 16 * 2 * GE * (gi // 2 + 1))
                for ch in range(NCH):
                    for u in range(GE):
                        e = grp * GE + u
                        ei = H * NE + e
                        if nup >= 1:
                            t.wait_ge(ssl, nup)
                        for ft in range(4):
                            bank, co = up_bank(ft)
                            for dt in range(NDT):
                                mm = t.matmul(bank[:, co:co + 256],
                                              wup[sl][:, u * 4096 + dt * 512 + ft * 128:
                                                      u * 4096 + dt * 512 + (ft + 1) * 128],
                                              xnh[:, dt * HT + ch * 256: dt * HT + (ch + 1) * 256],
                                              start=(dt == 0), stop=(dt == NDT - 1))
                        mm.then_inc(tup, 1)
                        nup += 1
                        if pending is not None:
                            emit_down(t, pending)
                        t.wait_ge(iv4[ei % 4], ei // 4 + 1)
                        t.wait_ge(vhs, ncb)
                        t.matmul(p.PS[6][:, 0:256], inde[ei % 4][:],
                                 cwTb[0:64, H * HT + ch * 256: H * HT + (ch + 1) * 256],
                                 start=True, stop=True).then_inc(tcb, 1)
                        ncb += 1
                        pending = (sl, u, ei % 2, ncb)
                if grp == NGRP - 1:
                    emit_down(t, pending)
                    pending = None

    SW = 3

    @blk.scalar
    def _(s_):
        s_.wait_ge(p.bar_sem, bar0)
        nup = 0
        hb_consumer = [None, None]
        ndn_mirror = 0
        nhs = 0

        def emit_store(ns2):
            par2 = ns2 % 2
            H2, st2_ = divmod(ns2, 32)
            ch2, dq2 = divmod(st2_, 4)
            s_.wait_ge(vcb, ns2 + 1)
            for i2 in range(2):
                dt2 = dq2 * 2 + i2
                s_.dma_start(out=outd[dt2 * 128:(dt2 + 1) * 128,
                                      H2 * HT + ch2 * 256: H2 * HT + (ch2 + 1) * 256],
                             in_=otb[par2][:, i2 * 256:(i2 + 1) * 256]
                             ).then_inc(ost2[par2], 16)

        def comb_step(ns):
            H2, st2 = divmod(ns, 32)
            ch2, dq2 = divmod(st2, 4)
            s_.wait_ge(vev, DNH * H2 + NCH + (NGRP - 1) * NCH + ch2 + 1)
            if ns >= 2:
                s_.wait_ge(vcb, ns - 1)
            ii = None
            for i in range(2):
                dt = dq2 * 2 + i
                ii = s_.activation(slb[ns % 2][:, i * 256:(i + 1) * 256],
                                   acc[:, dt * HT + ch2 * 256: dt * HT + (ch2 + 1) * 256],
                                   AF.Silu)
            ii.then_inc(ssc, 1)
            if ns >= 2:
                emit_store(ns - 2)

        for H in range(NHF):
            for ch in range(NCH):
                par = nup % 2
                s_.wait_ge(tup, nup + 1)
                if hb_consumer[par] is not None:
                    kind, val = hb_consumer[par]
                    s_.wait_ge(tdn if kind == "tdn" else vhs, val)
                ii = None
                for half2 in range(2):
                    ii = s_.activation(hb[par][:, half2 * 512:(half2 + 1) * 512],
                                       p.PS[4 + half2][:, :], AF.Silu)
                ii.then_inc(ssl, 1)
                ndn_mirror += 1
                hb_consumer[par] = ("tdn", ndn_mirror)
                nup += 1
            for grp in range(NGRP):
                for ch in range(NCH):
                    for u in range(GE):
                        par = nup % 2
                        s_.wait_ge(tup, nup + 1)
                        if hb_consumer[par] is not None:
                            kind, val = hb_consumer[par]
                            s_.wait_ge(tdn if kind == "tdn" else vhs, val)
                        ii = None
                        for half2 in range(2):
                            ii = s_.activation(hb[par][:, half2 * 512:(half2 + 1) * 512],
                                               p.PS[4 + half2][:, :], AF.Silu)
                        ii.then_inc(ssl, 1)
                        nhs += 1
                        hb_consumer[par] = ("vhs", nhs)
                        nup += 1
                    ndn_mirror += 1
                    if grp == NGRP - 1 and ch >= SW:
                        hi = 32 if ch == NCH - 1 else 4 * (ch - SW + 1)
                        for st in range(4 * (ch - SW), hi):
                            comb_step(H * 32 + st)
        emit_store(NHF * 32 - 2)
        emit_store(NHF * 32 - 1)

    @blk.vector
    def _(v):
        v.wait_ge(p.bar_sem, bar0)
        nup = 0
        ncb = 0
        nev = 0
        nhs = 0

        def comb_mul(ns):
            par = ns % 2
            v.wait_ge(ssc, ns + 1)
            v.wait_ge(xls2[par], 16 * 4 * (ns // 2 + 1))
            ii = None
            for i in range(2):
                ii = v.tensor_mul(otb[par][:, i * 256:(i + 1) * 256],
                                  otb[par][:, i * 256:(i + 1) * 256],
                                  slb[par][:, i * 256:(i + 1) * 256])
            p.edge(ii, v, "mocb")
            ii = None
            for i in range(2):
                ii = v.tensor_add(otb[par][:, i * 256:(i + 1) * 256],
                                  otb[par][:, i * 256:(i + 1) * 256],
                                  xsb[par][:, i * 256:(i + 1) * 256])
            ii.then_inc(vcb, 1)

        def is_eq_group(gin):
            for u in range(GE):
                ei = gin * GE + u
                par = ei % 4
                if ei >= 4:
                    v.wait_ge(tcb, NCH * GE * (ei // 2 - 1))
                v.wait_ge(ie4[par], ei // 4 + 1)
                v.tensor_single_scalar(inde[par][:], inde[par][:],
                                       0.0, ALU.is_equal).then_inc(iv4[par], 1)

        is_eq_group(0)
        for H in range(NHF):
            for ch in range(NCH):
                nup += 1
                if H >= 1:
                    v.wait_ge(ssc, 32 * (H - 1) + 4 * (ch + 1))
                v.wait_ge(tdn, nev + 1)
                ii = None
                for dto in range(NDT):
                    bank, co = dn_bank(dto)
                    ii = v.tensor_copy(acc[:, dto * HT + ch * 256: dto * HT + (ch + 1) * 256],
                                       bank[:, co:co + 256])
                p.edge(ii, v, "moev")
                v.engine_nop().then_inc(vev, 1)
                nev += 1
            for grp in range(NGRP):
                gi = H * NGRP + grp
                for ch in range(NCH):
                    for u in range(GE):
                        e = grp * GE + u
                        ei = H * NE + e
                        v.wait_ge(ssl, nup + 1)
                        v.wait_ge(tcb, ncb + 1)
                        ii_h = nhs - NE * NCH * H
                        if ii_h >= 2:
                            if ii_h % 2 == 0:
                                v.wait_ge(tda, NE * NCH * H // 2 + ii_h // 2)
                            else:
                                v.wait_ge(tdn, DNH * H + NCH + (ii_h - 1) // 2)
                        par = nup % 2
                        ii = None
                        for ft in range(4):
                            ii = v.tensor_mul(hsc[ei % 2][:, ft * 256:(ft + 1) * 256],
                                              hb[par][:, ft * 256:(ft + 1) * 256],
                                              p.PS[6][:, 0:256])
                        ii.then_inc(vhs, 1)
                        nup += 1
                        ncb += 1
                        nhs += 1
                    v.wait_ge(tdn, nev + 1)
                    ii = None
                    for dto in range(NDT):
                        bank, co = dn_bank(dto)
                        ii = v.tensor_add(acc[:, dto * HT + ch * 256: dto * HT + (ch + 1) * 256],
                                          acc[:, dto * HT + ch * 256: dto * HT + (ch + 1) * 256],
                                          bank[:, co:co + 256])
                    p.edge(ii, v, "moev")
                    v.engine_nop().then_inc(vev, 1)
                    nev += 1
                    if ch == NCH - 2 and gi + 1 < NHF * NGRP:
                        is_eq_group(gi + 1)
                    if grp == NGRP - 1 and ch >= SW:
                        hi = 32 if ch == NCH - 1 else 4 * (ch - SW + 1)
                        for st in range(4 * (ch - SW), hi):
                            comb_mul(H * 32 + st)
    p.bar += 1


def _prep_weights(inp):
    f32 = np.float32
    a = lambda x: np.ascontiguousarray(np.asarray(x, dtype=f32))
    w = {}
    w["w_pq"] = a(inp["L0_attn_proj_q"])
    w["w_pk"] = a(inp["L0_attn_proj_k"])
    w["w_pv"] = a(inp["L0_attn_proj_v"])
    w["w_po"] = a(np.transpose(np.asarray(inp["L0_attn_proj_o"]), (0, 2, 1)))
    w["w_rm"] = a(np.transpose(np.asarray(inp["L0_ffn_rail_maker"]), (0, 1, 3, 2)))
    w["w_rup"] = a(np.transpose(np.asarray(inp["L0_ffn_rail_up_proj"]), (0, 1, 3, 2)))
    w["w_wc"] = a(np.asarray(inp["L0_ffn_w_cproj"]).T)
    w["w_wq"] = a(inp["L1_attn_Wq"])
    w["w_wk"] = a(inp["L1_attn_Wk"])
    w["w_wv"] = a(inp["L1_attn_Wv"])
    w["w_wb"] = a(inp["L1_attn_Wb"])
    w["w_cq"] = a(np.asarray(inp["L1_attn_conv_q"])[:, 0, :])
    w["w_ck"] = a(np.asarray(inp["L1_attn_conv_k"])[:, 0, :])
    w["w_cv"] = a(np.asarray(inp["L1_attn_conv_v"])[:, 0, :])
    g = np.asarray(inp["L1_attn_g"], dtype=f32)
    w["w_g"] = a(np.tile(g, 4).reshape(NDT, 128).T)
    w["w_wo"] = a(inp["L1_attn_Wo"])
    w["w_cent"] = a(np.asarray(inp["L1_ffn_centroids"]).T)
    w["w_eb"] = a(np.asarray(inp["L1_ffn_expert_biases"]).reshape(1, NE))
    w["w_wsu"] = a(inp["L1_ffn_Ws_up"])
    w["w_wsd"] = a(inp["L1_ffn_Ws_down"])
    w["w_wup"] = a(inp["L1_ffn_W_up"])
    w["w_wdn"] = a(inp["L1_ffn_W_down"])
    return w


def _percore_inputs(inp, w):
    x = np.asarray(inp["L0_attn_x"], dtype=np.float32)
    maps = []
    for b in range(B):
        m = dict(w)
        m["x_t"] = np.ascontiguousarray(x[b].T)
        m["bmask"] = (np.arange(CC)[None, :] <= b).astype(np.float32)
        maps.append(m)
    return maps


_TENSORS = {
    "x_t": ((D, L), F32), "xn0d": ((D, L), BF16), "g0d": ((D, L), F32),
    "x1d": ((D, L), F32), "xn2d": ((D, L), BF16), "affd": ((L, NE), F32),
    "qnd": ((D, L), F32), "knd": ((D, L), F32), "vsd": ((D, L), F32),
    "betad": ((NH, L), F32), "s2d": ((D, L), F32), "cwtd": ((NE, L), F32),
    "wtd": ((D, L), F32), "utd": ((D, L), F32),
    "outd": ((D, L), F32),
    "w_pq": ((8, D, 128), F32), "w_pk": ((D, 128), F32), "w_pv": ((D, 128), F32),
    "w_po": ((8, 128, D), F32), "w_rm": ((4, 64, 16, 16), F32),
    "w_rup": ((4, 64, 16, 64), F32), "w_wc": ((16 * D, D), F32),
    "w_wq": ((D, D), F32), "w_wk": ((D, D), F32), "w_wv": ((D, D), F32),
    "w_wb": ((D, NH), F32), "w_cq": ((D, 4), F32), "w_ck": ((D, 4), F32),
    "w_cv": ((D, 4), F32), "w_g": ((128, NDT), F32), "w_wo": ((D, D), F32),
    "w_cent": ((D, NE), F32), "w_eb": ((1, NE), F32),
    "w_wsu": ((D, FF), F32), "w_wsd": ((FF, D), F32),
    "w_wup": ((NE, D, FF), F32), "w_wdn": ((NE, FF, D), F32),
    "bmask": ((1, CC), F32),
}

def _em_init(p, dr):
    ph_init(p, dr)

def _em_ln1(p, dr):
    ph_ln(p, dr["x_t"], dr["xn0d"], name="l1n")

def _em_ln2(p, dr):
    ph_ln(p, dr["x1d"], dr["xn2d"], aff_out=dr["affd"], cent_d=dr["w_cent"], name="l2n")

def _em_ffn0(p, dr):
    ph_ffn0(p, dr)

def _em_attn0(p, dr):
    ph_attn0(p, dr)

def _em_qkv(p, dr):
    ph_qkv(p, dr)

def _em_l1core(p, dr):
    ph_l1core(p, dr)

def _em_route(p, dr):
    ph_route(p, dr)

def _em_moe(p, dr):
    ph_moe(p, dr)

_PHASES = {
    "init": (_em_init, ["w_g", "bmask", "w_eb"], []),
    "ln1": (_em_ln1, ["x_t"], ["xn0d"]),
    "ln2": (_em_ln2, ["x1d", "w_cent"], ["xn2d", "affd"]),
    "ffn0": (_em_ffn0, ["xn0d", "w_rm", "w_rup", "w_wc"], ["g0d"]),
    "attn0": (_em_attn0, ["xn0d", "x_t", "g0d", "w_pq", "w_pk", "w_pv", "w_po"], ["x1d"]),
    "qkv": (_em_qkv, ["xn2d", "w_wq", "w_wk", "w_wv", "w_wb", "w_cq", "w_ck", "w_cv"],
            ["qnd", "knd", "vsd", "betad"]),
    "l1core": (_em_l1core, ["qnd", "knd", "vsd", "betad", "bmask", "w_g", "w_wo"],
               ["s2d", "wtd", "utd"]),
    "route": (_em_route, ["affd", "w_eb"], ["cwtd"]),
    "moe": (_em_moe, ["xn2d", "cwtd", "x1d", "s2d", "w_wsu", "w_wsd", "w_wup", "w_wdn"],
            ["outd"]),
}
_PHASE_ORDER = ["ln1", "ffn0", "attn0", "ln2", "qkv", "l1core", "route", "moe"]


def build(phase_names, ext_out):
    nc = bass.Bass("TRN2", target_bir_lowering=False)
    phase_names = ["init"] + [x for x in phase_names if x != "init"]
    produced = set()
    consumed = set()
    for ph in phase_names:
        _, ins, outs = _PHASES[ph]
        consumed.update(i for i in ins if i not in produced)
        produced.update(outs)
    dr = {}
    for name in sorted(consumed):
        shape, dt = _TENSORS[name]
        dr[name] = nc.dram_tensor(name, list(shape), dt, kind="ExternalInput").ap()
    for name in sorted(produced):
        shape, dt = _TENSORS[name]
        kind = "ExternalOutput" if name in ext_out else "Internal"
        dr[name] = nc.dram_tensor(name, list(shape), dt, kind=kind).ap()
    with ExitStack() as gctx:
        p = Prog(nc, gctx)
        for k, ph in enumerate(phase_names):
            p.phase_sems = []
            with ExitStack() as pctx:
                p.pctx = pctx
                p.block = pctx.enter_context(nc.Block())
                _PHASES[ph][0](p, dr)
                ids = list(p.phase_sems)
                assert not ids or min(ids) >= p.clear_from, (ids, p.clear_from)
            p.pctx = gctx
            if k < len(phase_names) - 1:
                nc.all_engine_barrier()
                nc.gpsimd.dma_reset(range(p.clear_from, 256))
                nc.gpsimd.sem_clear(range(p.clear_from, 256))
                nc.all_engine_barrier()
    return nc, sorted(consumed)


def run_phases(phase_names, ext_out, in_maps, core_ids=None):
    nc, needed = build(phase_names, ext_out)
    maps = [{k: m[k] for k in needed if k in m} for m in in_maps]
    import os as _os
    res = run_bass_kernel_spmd(nc, maps, core_ids=core_ids or list(range(len(in_maps))),
                               trace=bool(_os.environ.get("DEVTRACE")))
    return res.results


_FULL_CHAIN = ["ln1", "ffn0", "attn0", "ln2", "qkv", "l1core", "route", "moe"]


def kernel(**inputs):
    w = _prep_weights(inputs)
    maps = _percore_inputs(inputs, w)
    nc, needed = build(_FULL_CHAIN, {"outd"})
    in_maps = [{k: m[k] for k in needed if k in m} for m in maps]
    res = run_bass_kernel_spmd(nc, in_maps, core_ids=list(range(B)))
    out = np.stack([np.asarray(r["outd"], np.float32).T for r in res.results], axis=0)
    return np.ascontiguousarray(out)


# version 07
# speedup vs baseline: 1.0980x; 1.0469x over previous
import numpy as np
import ml_dtypes
from contextlib import ExitStack

import concourse.bass as bass
import concourse.mybir as mybir
from concourse.bass_utils import run_bass_kernel_spmd

F32 = mybir.dt.float32
BF16 = mybir.dt.bfloat16
AF = mybir.ActivationFunctionType
ALU = mybir.AluOpType
AX = mybir.AxisListType

B, L, D = 8, 4096, 1024
_A0_NQT = [None]
_F0_SKIP = set()
NDT = D // 128
NCHUNK = L // 512
NH, HD = 4, 256
CC = 64
NCK = L // CC
EPS_LN = 1e-5
EPS_RMS = 1e-5
TOPK = 8
NE = 64
FF = 512


class Prog:
    def __init__(self, nc, gctx):
        self.nc = nc
        self.gctx = gctx
        self.pctx = gctx
        self.block = None
        self.bar_sem = gctx.enter_context(nc.semaphore("phase_bar"))
        self.bar = 0
        self._semid = 0
        self.G = {}
        self.PS = [gctx.enter_context(nc.psum_tensor(f"psb{i}", [128, 512], F32))
                   for i in range(7)]
        self.PSBF = gctx.enter_context(nc.psum_tensor("psbf", [128, 1024], BF16))
        self.phase_sems = []
        try:
            nc.sort_free_semaphores()
        except Exception:
            pass
        nc.all_engine_barrier()
        self._ss_sem = {}
        self._ss_cnt = {}
        for k in ("dve", "act", "pool"):
            self._ss_sem[k] = gctx.enter_context(nc.semaphore(f"ss_{k}"))
            self._ss_cnt[k] = 0
        globals_max = max([self.bar_sem.num]
                          + [s.num for s in self._ss_sem.values()])
        self.clear_from = globals_max + 1

    def edge(self, inst, eng, key=None):
        k = ("dve" if eng is self.nc.vector else
             "act" if eng is self.nc.scalar else "pool")
        inst.then_inc(self._ss_sem[k], 1)
        self._ss_cnt[k] += 1
        eng.wait_ge(self._ss_sem[k], self._ss_cnt[k])

    def sb(self, name, shape, dt=F32):
        return self.pctx.enter_context(self.nc.sbuf_tensor(name, shape, dt))

    def sbg(self, name, shape, dt=F32):
        return self.gctx.enter_context(self.nc.sbuf_tensor(name, shape, dt))

    def sem(self, name, dma=False):
        self._semid += 1
        h = self.pctx.enter_context(self.nc.semaphore(f"{name}{self._semid}"))
        self.phase_sems.append(h.num if hasattr(h, "num") else h)
        return h


def ph_init(p, dr):
    nc, blk = p.nc, p.block
    G = p.G
    G["ones"] = p.sbg("g_ones", [128, 128], F32)
    G["ident"] = p.sbg("g_ident", [128, 128], F32)
    G["ident_bf"] = p.sbg("g_identbf", [128, 128], BF16)
    G["iota"] = p.sbg("g_iota", [128, 128], F32)
    G["mlow"] = p.sbg("g_mlow", [64, 64], F32)
    G["mup"] = p.sbg("g_mup", [64, 64], F32)
    G["eps_ln"] = p.sbg("g_epsln", [128, 1], F32)
    G["eps_rms"] = p.sbg("g_epsrms", [128, 1], F32)
    G["gvec"] = p.sbg("g_gvec", [128, NDT], F32)
    G["bmask"] = p.sbg("g_bmask", [128, CC], F32)
    G["ebias"] = p.sbg("g_ebias", [128, NE], F32)
    ldg = p.sem("ini_ld", dma=True)
    lds = p.sem("ini_lds", dma=True)
    gc = p.sem("ini_gc")
    vrd = p.sem("ini_v")

    @blk.gpsimd
    def _(g):
        g.memset(G["ones"][:], 1.0)
        g.memset(G["eps_ln"][:], EPS_LN)
        g.memset(G["eps_rms"][:], EPS_RMS)
        g.iota(G["iota"][:], [[1, 128]], channel_multiplier=-1,
               allow_small_or_imprecise_dtypes=True).then_inc(gc, 1)
        g.dma_start(out=G["gvec"][:], in_=dr["w_g"]).then_inc(ldg, 16)

    @blk.vector
    def _(v):
        v.wait_ge(ldg, 16)
        v.wait_ge(gc, 1)
        v.tensor_single_scalar(G["ident"][:], G["iota"][:], 0.0, ALU.is_equal)
        v.tensor_single_scalar(G["ident_bf"][:], G["iota"][:], 0.0, ALU.is_equal)
        v.tensor_single_scalar(G["mlow"][:], G["iota"][0:64, 0:64], 0.0, ALU.is_lt)
        v.tensor_single_scalar(G["mup"][:], G["iota"][0:64, 0:64], 0.0, ALU.is_gt)
        v.engine_nop().then_inc(vrd, 1)

    bm1 = p.sb("g_bm1", [1, CC], F32)
    eb1 = p.sb("g_eb1", [1, NE], F32)
    ps_b = p.PS[0]
    tdone = p.sem("ini_t")

    @blk.sync
    def _(q):
        q.dma_start(out=bm1[:], in_=dr["bmask"]).then_inc(lds, 16)
        q.dma_start(out=eb1[:], in_=dr["w_eb"]).then_inc(lds, 16)

    @blk.tensor
    def _(t):
        t.wait_ge(ldg, 16)
        t.wait_ge(gc, 1)
        t.wait_ge(lds, 32)
        t.matmul(ps_b[:, 0:CC], G["ones"][0:1, :], bm1[:], start=True, stop=True)
        t.matmul(ps_b[:, CC:CC + NE], G["ones"][0:1, :], eb1[:], start=True, stop=True
                 ).then_inc(tdone, 1)

    @blk.vector
    def _(v):
        v.wait_ge(tdone, 1)
        v.tensor_copy(G["bmask"][:], ps_b[:, 0:CC])
        v.tensor_copy(G["ebias"][:], ps_b[:, CC:CC + NE])
        v.engine_nop().then_inc(p.bar_sem, 1)
    p.bar += 1


def ph_ln(p, src_d, dst_d, aff_out=None, cent_d=None, name="ln"):
    nc, blk = p.nc, p.block
    bar0 = p.bar
    ld2 = [p.sem(name + "ld_a"), p.sem(name + "ld_b")]
    ldc = p.sem(name + "ldc")
    vsq = p.sem(name + "vsq")
    ts1 = p.sem(name + "ts1")
    ts2 = p.sem(name + "ts2")
    vrd = p.sem(name + "vrd")
    ssd = p.sem(name + "ssd")
    vxn = p.sem(name + "vxn")
    st2 = [p.sem(name + "st_a"), p.sem(name + "st_b")]
    sa2 = [p.sem(name + "sa_a"), p.sem(name + "sa_b")]
    taf = p.sem(name + "taf")
    saf = p.sem(name + "saf")
    NC2 = NCHUNK
    do_aff = aff_out is not None

    xch = [p.sb(name + f"x{i}", [128, NDT * 512], F32) for i in range(2)]
    sq = [p.sb(name + f"sq{i}", [128, NDT * 512], F32) for i in range(2)]
    mean = [p.sb(name + f"mean{i}", [128, 512], F32) for i in range(2)]
    var = [p.sb(name + f"var{i}", [128, 512], F32) for i in range(2)]
    var2 = [p.sb(name + f"var2{i}", [128, 512], F32) for i in range(2)]
    rstd = [p.sb(name + f"rstd{i}", [128, 512], F32) for i in range(2)]
    sd = [p.sb(name + f"sd{i}", [128, 512], F32) for i in range(2)]
    xn = [p.sb(name + f"xn{i}", [128, NDT * 512], BF16) for i in range(2)]
    xnf = [p.sb(name + f"xnf{i}", [128, NDT * 512], F32) for i in range(2)] if do_aff else None
    afb = [p.sb(name + f"afb{i}", [128, 4 * 64], F32) for i in range(2)] if do_aff else None
    cent = p.sb(name + "cent", [128, NDT * 64], F32) if do_aff else None
    ps_s1 = [p.PS[0], p.PS[2]]
    ps_s2 = [p.PS[1], p.PS[3]]
    ps_af = [p.PS[4], p.PS[5]] if do_aff else None

    @blk.sync
    def _(q):
        q.wait_ge(p.bar_sem, bar0)
        if do_aff:
            for dt in range(NDT):
                q.dma_start(out=cent[:, dt * 64:(dt + 1) * 64],
                            in_=cent_d[dt * 128:(dt + 1) * 128, :]).then_inc(ldc, 16)
        for c in range(NC2):
            A = c % 2
            if c >= 2:
                q.wait_ge(vxn, c - 1)
                q.wait_ge(ts1, c - 1)
            for dt in range(NDT):
                q.dma_start(out=xch[A][:, dt * 512:(dt + 1) * 512],
                            in_=src_d[dt * 128:(dt + 1) * 128, c * 512:(c + 1) * 512]
                            ).then_inc(ld2[A], 16)

    ones = p.G["ones"]

    def ld_target(c):
        return 16 * NDT * (c // 2 + 1)

    @blk.vector
    def _(v):
        v.wait_ge(p.bar_sem, bar0)
        for c in range(NC2):
            A = c % 2
            v.wait_ge(ld2[A], ld_target(c))
            if c >= 2:
                v.wait_ge(ts2, c - 1)
            for dt in range(NDT):
                sl = slice(dt * 512, (dt + 1) * 512)
                v.tensor_mul(sq[A][:, sl], xch[A][:, sl], xch[A][:, sl])
            v.engine_nop().then_inc(vsq, 1)
            if c >= 2:
                v.wait_ge(ssd, c - 1)
            v.wait_ge(ts2, c + 1)
            p.edge(v.tensor_scalar_mul(mean[A][:], ps_s1[A][:, :], 1.0 / D), v, name + "v1")
            p.edge(v.tensor_mul(var2[A][:], mean[A][:], mean[A][:]), v, name + "v2")
            v.scalar_tensor_tensor(var[A][:], ps_s2[A][:, :], 1.0 / D, var2[A][:],
                                   ALU.mult, ALU.subtract)
            v.engine_nop().then_inc(vrd, 1)
            v.wait_ge(ssd, c + 1)
            p.edge(v.reciprocal(rstd[A][:], sd[A][:]), v, name + "v3")
            per_chunk_st = NDT + (4 if do_aff else 0)
            if c >= 2:
                v.wait_ge(st2[A], 16 * NDT * (c // 2))
                if do_aff:
                    v.wait_ge(taf, c - 1)
            for dt in range(NDT):
                sl = slice(dt * 512, (dt + 1) * 512)
                ii = v.tensor_sub(sq[A][:, sl], xch[A][:, sl], mean[A][:])
            p.edge(ii, v, name + "v4")
            for dt in range(NDT):
                sl = slice(dt * 512, (dt + 1) * 512)
                v.tensor_mul(xn[A][:, sl], sq[A][:, sl], rstd[A][:])
                if do_aff:
                    v.tensor_mul(xnf[A][:, sl], sq[A][:, sl], rstd[A][:])
            v.engine_nop().then_inc(vxn, 1)

    @blk.scalar
    def _(s):
        s.wait_ge(p.bar_sem, bar0)
        for c in range(NC2):
            A = c % 2
            s.wait_ge(vrd, c + 1)
            s.activation(sd[A][:], var[A][:], AF.Sqrt, bias=p.G["eps_ln"][:, 0:1], scale=1.0
                         ).then_inc(ssd, 1)
            if do_aff:
                s.wait_ge(taf, c + 1)
                if c >= 2:
                    s.wait_ge(sa2[A], 16 * 4 * (c // 2))
                s.copy(afb[A][:], ps_af[A][:, 0:256]).then_inc(saf, 1)

    @blk.tensor
    def _(t):
        t.wait_ge(p.bar_sem, bar0)
        for c in range(NC2):
            A = c % 2
            t.wait_ge(ld2[A], ld_target(c))
            if do_aff and c == 0:
                t.wait_ge(ldc, 16 * NDT)
            if c >= 2:
                t.wait_ge(vrd, c - 1)
            for dt in range(NDT):
                mm = t.matmul(ps_s1[A][:, :], ones[:], xch[A][:, dt * 512:(dt + 1) * 512],
                              start=(dt == 0), stop=(dt == NDT - 1))
            mm.then_inc(ts1, 1)
            t.wait_ge(vsq, c + 1)
            for dt in range(NDT):
                mm = t.matmul(ps_s2[A][:, :], ones[:], sq[A][:, dt * 512:(dt + 1) * 512],
                              start=(dt == 0), stop=(dt == NDT - 1))
            mm.then_inc(ts2, 1)
            if do_aff:
                t.wait_ge(vxn, c + 1)
                if c >= 2:
                    t.wait_ge(saf, c - 1)
                for tt in range(4):
                    for dt in range(NDT):
                        mm = t.matmul(ps_af[A][:, tt * 64:tt * 64 + 64],
                                      xnf[A][:, dt * 512 + tt * 128: dt * 512 + tt * 128 + 128],
                                      cent[:, dt * 64:(dt + 1) * 64],
                                      start=(dt == 0), stop=(dt == NDT - 1))
                mm.then_inc(taf, 1)

    @blk.gpsimd
    def _(g):
        for c in range(NC2):
            A = c % 2
            g.wait_ge(vxn, c + 1)
            for dt in range(NDT):
                g.dma_start(out=dst_d[dt * 128:(dt + 1) * 128, c * 512:(c + 1) * 512],
                            in_=xn[A][:, dt * 512:(dt + 1) * 512]).then_inc(st2[A], 16)
            if do_aff:
                g.wait_ge(saf, c + 1)
                for tt in range(4):
                    g.dma_start(out=aff_out[(c * 4 + tt) * 128:(c * 4 + tt + 1) * 128, :],
                                in_=afb[A][:, tt * 64:(tt + 1) * 64]).then_inc(sa2[A], 16)
        g.wait_ge(st2[0], 16 * NDT * (NC2 // 2))
        g.wait_ge(st2[1], 16 * NDT * (NC2 // 2))
        if do_aff:
            g.wait_ge(sa2[0], 16 * 4 * (NC2 // 2))
            g.wait_ge(sa2[1], 16 * 4 * (NC2 // 2))
        g.engine_nop().then_inc(p.bar_sem, 1)
    p.bar += 1


def ph_ffn0(p, dr):
    nc, blk = p.nc, p.block
    bar0 = p.bar
    xn0d, g0d = dr["xn0d"], dr["g0d"]
    ldm = p.sem("f0ldm", dma=True)
    tmc = p.sem("f0tmc")
    vmc = p.sem("f0vmc")
    gbd = p.sem("f0gbd")
    bdq = p.sem("f0bdq", dma=True)
    wc4 = [p.sem(f"f0wcq{j}") for j in range(4)]
    xl2 = [p.sem("f0xld_a", dma=True), p.sem("f0xld_b", dma=True)]
    tz = p.sem("f0tz")
    sh = p.sem("f0sh")
    tco = p.sem("f0tco")
    vgo = p.sem("f0vgo")
    gs2 = [p.sem("f0gst_a", dma=True), p.sem("f0gst_b", dma=True)]
    sl2 = [p.sem("f0slq_a", dma=True), p.sem("f0slq_b", dma=True)]
    svn = p.sem("f0svn")
    ss2 = [p.sem("f0ssq_a", dma=True), p.sem("f0ssq_b", dma=True)]
    NQ = 4
    NSTEP = NQ * NCHUNK
    RE = "p (h q8 pgx s2 k) -> p h q8 pgx s2 k"

    BD = p.sb("f0bd", [128, 16384], BF16)

    with ExitStack() as sub:
        rm_sb = sub.enter_context(nc.sbuf_tensor("f0rm", [16, 4096], F32))
        rup_sb = sub.enter_context(nc.sbuf_tensor("f0rup", [16, 16384], F32))
        m_all = sub.enter_context(nc.sbuf_tensor("f0mall", [16, 16384], BF16))

        @blk.sync
        def _(q):
            q.wait_ge(p.bar_sem, bar0)
            q.dma_start(out=rm_sb[:],
                        in_=dr["w_rm"].rearrange("h r j i -> j (h r) i")
                        ).then_inc(ldm, 16)
            q.dma_start(out=rup_sb[:],
                        in_=dr["w_rup"].rearrange("h r j k -> j (h r) k")
                        ).then_inc(ldm, 16)

        @blk.tensor
        def _(t):
            t.wait_ge(p.bar_sem, bar0)
            t.wait_ge(ldm, 32)
            for grp in range(32):
                if grp >= 2:
                    t.wait_ge(vmc, grp - 1)
                for u in range(8):
                    hr = grp * 8 + u
                    mm = t.matmul(p.PS[grp % 2][0:16, u * 64:(u + 1) * 64],
                                  rm_sb[:, hr * 16:(hr + 1) * 16],
                                  rup_sb[:, hr * 64:(hr + 1) * 64],
                                  start=True, stop=True)
                mm.then_inc(tmc, 1)

        @blk.vector
        def _(v):
            v.wait_ge(p.bar_sem, bar0)
            for grp in range(32):
                v.wait_ge(tmc, grp + 1)
                v.tensor_copy(m_all[:, grp * 512:(grp + 1) * 512],
                              p.PS[grp % 2][0:16, :]).then_inc(vmc, 1)

        @blk.gpsimd
        def _(g):
            g.wait_ge(p.bar_sem, bar0)
            g.memset(BD[:], 0.0).then_inc(gbd, 1)
            g.wait_ge(gbd, 1)
            g.wait_ge(vmc, 32)
            for pg in range(4):
                for s in range(2):
                    base = 32 * pg + 16 * s
                    dst = BD[base:base + 16, :].rearrange(
                        RE, h=4, q8=8, pgx=4, s2=2, k=64)[:, :, :, pg, s, :]
                    srcv = m_all[0:16, :].rearrange(
                        RE, h=4, q8=8, pgx=4, s2=2, k=64)[:, :, :, pg, s, :]
                    g.dma_start(out=dst, in_=srcv).then_inc(bdq, 16)
            g.wait_ge(bdq, 16 * 8)
            g.engine_nop().then_inc(gbd, 1)
        sub.close()

    wcb = p.sb("f0wcb", [128, 32 * 1024], BF16)
    hb = [p.sb(f"f0h{i}", [128, 16384], BF16) for i in range(2)]
    xnc = [p.sb(f"f0x{i}", [128, NDT * 512], BF16) for i in range(2)]
    gtmp = p.sb("f0gt", [128, 4096], F32)

    @blk.sync
    def _(q):
        q.wait_ge(gbd, 2)
        for s in range(NSTEP):
            qq, c = divmod(s, NCHUNK)
            A = s % 2
            if s >= 2:
                q.wait_ge(tz, (s - 2) * 32 + 32)
            for dt in range(NDT):
                q.dma_start(out=xnc[A][:, dt * 512:(dt + 1) * 512],
                            in_=xn0d[dt * 128:(dt + 1) * 128, c * 512:(c + 1) * 512]
                            ).then_inc(xl2[A], 16)

    @blk.gpsimd
    def _(g):
        for qq in range(NQ):
            if qq >= 1:
                g.wait_ge(vgo, 16 * qq)
            for piece in range(4):
                g.dma_start(
                    out=wcb[:, piece * 8 * 1024:(piece + 1) * 8 * 1024],
                    in_=dr["w_wc"].rearrange("(ftg pp) d -> pp ftg d", pp=128)
                        [:, qq * 32 + piece * 8: qq * 32 + (piece + 1) * 8, :]
                    ).then_inc(wc4[piece], 16)
            for c in range(NCHUNK):
                s = qq * NCHUNK + c
                for hf in range(2):
                    g.wait_ge(vgo, 2 * s + hf + 1)
                    for i in range(4):
                        dt = hf * 4 + i
                        g.dma_start(
                            out=g0d[dt * 128:(dt + 1) * 128, c * 512:(c + 1) * 512],
                            in_=gtmp[:, hf * 2048 + i * 512: hf * 2048 + (i + 1) * 512],
                            accum_op=(ALU.add if (qq > 0 and 'acc' not in _F0_SKIP) else ALU.bypass)
                            ).then_inc(gs2[hf], 16)

    @blk.tensor
    def _(t):
        t.wait_ge(gbd, 2)
        for s in range(NSTEP):
            qq, c = divmod(s, NCHUNK)
            A = s % 2
            t.wait_ge(xl2[s % 2], 16 * NDT * (s // 2 + 1))
            t.wait_ge(vgo, 2 * s)
            for ft in range(32):
                gz = s * 32 + ft
                ftg = qq * 32 + ft
                dt_in = ft // 4
                if gz >= 2:
                    t.wait_ge(sh, gz - 1)
                t.matmul(p.PS[4 + gz % 2][:, :], BD[:, ftg * 128:(ftg + 1) * 128],
                         xnc[A][:, dt_in * 512:(dt_in + 1) * 512],
                         start=True, stop=True).then_inc(tz, 1)
                if ft >= 1:
                    fprev = ft - 1
                    t.wait_ge(sh, s * 32 + fprev + 1)
                    if fprev % 8 == 0:
                        t.wait_ge(wc4[fprev // 8], 16 * (qq + 1))
                    for dto in range(4):
                        mm = t.matmul(p.PS[dto][:, :],
                                      wcb[:, fprev * 1024 + dto * 128: fprev * 1024 + (dto + 1) * 128],
                                      hb[A][:, fprev * 512:(fprev + 1) * 512],
                                      start=(fprev == 0), stop=(fprev == 31))
            t.wait_ge(sh, s * 32 + 32)
            for dto in range(4):
                mm = t.matmul(p.PS[dto][:, :],
                              wcb[:, 31 * 1024 + dto * 128: 31 * 1024 + (dto + 1) * 128],
                              hb[A][:, 31 * 512: 32 * 512],
                              start=False, stop=True)
            mm.then_inc(tco, 1)
            t.wait_ge(vgo, 2 * s + 1)
            for ft in range(32):
                for dto in range(4):
                    mm = t.matmul(p.PS[dto][:, :],
                                  wcb[:, ft * 1024 + (4 + dto) * 128: ft * 1024 + (5 + dto) * 128],
                                  hb[A][:, ft * 512:(ft + 1) * 512],
                                  start=(ft == 0), stop=(ft == 31))
            mm.then_inc(tco, 1)

    @blk.scalar
    def _(s_):
        s_.wait_ge(p.bar_sem, bar0)
        for s in range(NSTEP):
            A = s % 2
            if s >= 2:
                s_.wait_ge(tco, 2 * (s - 1))
            for ft in range(32):
                gz = s * 32 + ft
                s_.wait_ge(tz, gz + 1)
                s_.activation(hb[A][:, ft * 512:(ft + 1) * 512],
                              p.PS[4 + gz % 2][:, :], AF.Gelu).then_inc(sh, 1)

    @blk.vector
    def _(v):
        for s in range(NSTEP):
            for hf in range(2):
                v.wait_ge(tco, 2 * s + hf + 1)
                if s >= 1:
                    v.wait_ge(gs2[hf], 16 * 4 * s)
                for i in range(4):
                    v.tensor_copy(gtmp[:, hf * 2048 + i * 512: hf * 2048 + (i + 1) * 512],
                                  p.PS[i][:, :])
                v.engine_nop().then_inc(vgo, 1)

    GSTOT = 16 * 4 * NSTEP
    if "silu" in _F0_SKIP:
        @blk.gpsimd
        def _(g):
            g.wait_ge(gs2[0], GSTOT)
            g.wait_ge(gs2[1], GSTOT)
            g.engine_nop().then_inc(p.bar_sem, 1)
        p.bar += 1
        return

    @blk.sync
    def _(q):
        q.wait_ge(gs2[0], GSTOT)
        q.wait_ge(gs2[1], GSTOT)
        for u in range(16):
            c, hf = divmod(u, 2)
            if u >= 2:
                q.wait_ge(ss2[hf], 16 * 4 * (u // 2))
            for i in range(4):
                dt = hf * 4 + i
                q.dma_start(out=gtmp[:, hf * 2048 + i * 512: hf * 2048 + (i + 1) * 512],
                            in_=g0d[dt * 128:(dt + 1) * 128, c * 512:(c + 1) * 512]
                            ).then_inc(sl2[hf], 16)

    @blk.scalar
    def _(s_):
        for u in range(16):
            c, hf = divmod(u, 2)
            s_.wait_ge(sl2[hf], 16 * 4 * (u // 2 + 1))
            s_.activation(gtmp[:, hf * 2048: hf * 2048 + 2048],
                          gtmp[:, hf * 2048: hf * 2048 + 2048], AF.Silu).then_inc(svn, 1)

    @blk.gpsimd
    def _(g):
        for u in range(16):
            c, hf = divmod(u, 2)
            g.wait_ge(svn, u + 1)
            for i in range(4):
                dt = hf * 4 + i
                g.dma_start(out=g0d[dt * 128:(dt + 1) * 128, c * 512:(c + 1) * 512],
                            in_=gtmp[:, hf * 2048 + i * 512: hf * 2048 + (i + 1) * 512]
                            ).then_inc(ss2[hf], 16)
        g.wait_ge(ss2[0], 16 * 4 * 8)
        g.wait_ge(ss2[1], 16 * 4 * 8)
        g.engine_nop().then_inc(p.bar_sem, 1)
    p.bar += 1


def ph_attn0(p, dr):
    nc, blk = p.nc, p.block
    G = p.G
    bar0 = p.bar
    xn0d, x_t, g0d, x1d = dr["xn0d"], dr["x_t"], dr["g0d"], dr["x1d"]
    pld = p.sem("a0pld", dma=True)
    xl2 = [p.sem("a0xl_a", dma=True), p.sem("a0xl_b", dma=True)]
    tpj = p.sem("a0tpj")
    vpj = p.sem("a0vpj")
    tvt = p.sem("a0tvt")
    vvt = p.sem("a0vvt")
    tL = p.sem("a0tL")
    sE = p.sem("a0sE")
    tP = p.sem("a0tP")
    vS = p.sem("a0vS")
    vR = p.sem("a0vR")
    vV = p.sem("a0vV")
    tT = p.sem("a0tT")
    vO = p.sem("a0vO")
    tY = p.sem("a0tY")
    vC = p.sem("a0vC")
    xg2 = [p.sem("a0xg_a", dma=True), p.sem("a0xg_b", dma=True)]
    so2 = [p.sem("a0so_a", dma=True), p.sem("a0so_b", dma=True)]

    NQT = (L // 128) if _A0_NQT[0] is None else _A0_NQT[0]
    pq_sb = p.sb("a0pq", [128, 8 * NDT * 128], BF16)
    pk_sb = p.sb("a0pk", [128, NDT * 128], BF16)
    pv_sb = p.sb("a0pv", [128, NDT * 128], BF16)
    po_sb = p.sb("a0po", [128, 8 * 1024], BF16)
    xnc = [p.sb(f"a0x{i}", [128, NDT * 512], BF16) for i in range(2)]
    Qs = p.sb("a0Q", [128, 8 * L], BF16)
    Ks = p.sb("a0K", [128, L], BF16)
    Vs = p.sb("a0V", [128, L], BF16)
    vtok = p.sb("a0vt", [128, 32 * 132], BF16)
    PT = [p.sb(f"a0pt{i}", [128, 32 * 128], BF16) for i in range(2)]
    rs = p.sb("a0rs", [128, 1], F32)
    onb = p.sb("a0onb", [128, 128], BF16)
    otb = [p.sb(f"a0ot{i}", [128, 128], BF16) for i in range(2)]
    xg = [p.sb(f"a0xg{i}", [128, 2 * NDT * 128], F32) for i in range(2)]
    outb = [p.sb(f"a0ob{i}", [128, NDT * 128], F32) for i in range(2)]

    @blk.gpsimd
    def _(g):
        g.wait_ge(p.bar_sem, bar0)
        g.dma_start(out=pq_sb[:],
                    in_=dr["w_pq"].rearrange("h (dtp pp) k -> pp (h dtp) k", pp=128)
                    ).then_inc(pld, 16)
        g.dma_start(out=pk_sb[:],
                    in_=dr["w_pk"].rearrange("(dtp pp) k -> pp dtp k", pp=128)
                    ).then_inc(pld, 16)
        g.dma_start(out=pv_sb[:],
                    in_=dr["w_pv"].rearrange("(dtp pp) k -> pp dtp k", pp=128)
                    ).then_inc(pld, 16)
        g.dma_start(out=po_sb[:],
                    in_=dr["w_po"].rearrange("h v dd -> v h dd")).then_inc(pld, 16)

    @blk.sync
    def _(q):
        q.wait_ge(p.bar_sem, bar0)
        for c in range(NCHUNK):
            A = c % 2
            if c >= 2:
                q.wait_ge(tpj, 10 * (c - 1))
            for dt in range(NDT):
                q.dma_start(out=xnc[A][:, dt * 512:(dt + 1) * 512],
                            in_=xn0d[dt * 128:(dt + 1) * 128, c * 512:(c + 1) * 512]
                            ).then_inc(xl2[A], 16)

    @blk.tensor
    def _(t):
        t.wait_ge(p.bar_sem, bar0)
        t.wait_ge(pld, 64)
        ng = 0
        for c in range(NCHUNK):
            A = c % 2
            t.wait_ge(xl2[A], 16 * NDT * (c // 2 + 1))
            for grp in range(10):
                if ng >= 2:
                    t.wait_ge(vpj, ng - 1)
                for dt in range(NDT):
                    if grp == 0:
                        lhs = pk_sb[:, dt * 128:(dt + 1) * 128]
                    elif grp == 1:
                        lhs = pv_sb[:, dt * 128:(dt + 1) * 128]
                    else:
                        h = grp - 2
                        lhs = pq_sb[:, (h * NDT + dt) * 128:(h * NDT + dt + 1) * 128]
                    mm = t.matmul(p.PS[ng % 2][:, :], lhs,
                                  xnc[A][:, dt * 512:(dt + 1) * 512],
                                  start=(dt == 0), stop=(dt == NDT - 1))
                mm.then_inc(tpj, 1)
                ng += 1

    @blk.vector
    def _(v):
        v.wait_ge(p.bar_sem, bar0)
        ng = 0
        for c in range(NCHUNK):
            sl = slice(c * 512, (c + 1) * 512)
            for grp in range(10):
                v.wait_ge(tpj, ng + 1)
                if grp == 0:
                    v.tensor_copy(Ks[:, sl], p.PS[ng % 2][:, :])
                elif grp == 1:
                    v.tensor_copy(Vs[:, sl], p.PS[ng % 2][:, :])
                else:
                    h = grp - 2
                    v.tensor_copy(Qs[:, h * L + c * 512: h * L + (c + 1) * 512],
                                  p.PS[ng % 2][:, :])
                v.engine_nop().then_inc(vpj, 1)
                ng += 1

    @blk.vector
    def _(v):
        v.memset(vtok[:], 1.0)
        v.engine_nop().then_inc(vvt, 1)

    @blk.tensor
    def _(t):
        t.wait_ge(vpj, 80)
        t.wait_ge(vvt, 1)
        for mt in range(32):
            if mt >= 1:
                t.wait_ge(vvt, mt + 1)
            t.transpose(p.PSBF[:, (mt % 2) * 128:(mt % 2) * 128 + 128],
                        Vs[:, mt * 128:(mt + 1) * 128],
                        G["ident_bf"][:]).then_inc(tvt, 1)

    @blk.vector
    def _(v):
        for mt in range(32):
            v.wait_ge(tvt, mt + 1)
            v.tensor_copy(vtok[:, mt * 132: mt * 132 + 128],
                          p.PSBF[:, (mt % 2) * 128:(mt % 2) * 128 + 128])
            v.engine_nop().then_inc(vvt, 1)

    @blk.sync
    def _(q):
        for qt in range(NQT):
            A = qt % 2
            if qt >= 2:
                q.wait_ge(vC, qt - 1)
            for dt in range(NDT):
                q.dma_start(out=xg[A][:, dt * 128:(dt + 1) * 128],
                            in_=x_t[dt * 128:(dt + 1) * 128, qt * 128:(qt + 1) * 128]
                            ).then_inc(xg2[A], 16)
            for dt in range(NDT):
                q.dma_start(out=xg[A][:, (NDT + dt) * 128:(NDT + dt + 1) * 128],
                            in_=g0d[dt * 128:(dt + 1) * 128, qt * 128:(qt + 1) * 128]
                            ).then_inc(xg2[A], 16)

    nL = [0]; nE = [0]; nP = [0]; nS = [0]; nT = [0]; nO = [0]; nY = [0]

    @blk.tensor
    def _(t):
        t.wait_ge(vvt, 33)
        pend = [None]

        def emit_tr(t, idx):
            t.wait_ge(vS, idx + 1)
            if idx >= 2:
                t.wait_ge(vO, idx - 1)
            if idx % 2 == 0:
                t.transpose(p.PSBF[:, 256:384], onb[:],
                            G["ident_bf"][:]).then_inc(tT, 1)
            else:
                t.matmul(p.PS[3][:, 0:128], onb[:], G["ident_bf"][:],
                         start=True, stop=True).then_inc(tT, 1)

        def emit_y(t, idx):
            qtp, hq = divmod(idx, 8)
            hpp = hq % 2
            t.wait_ge(vO, idx + 1)
            if hq == 0 and qtp >= 1:
                t.wait_ge(vC, qtp)
            for dt in range(NDT):
                bank = 5 + dt // 4
                col = (dt % 4) * 128
                mm = t.matmul(p.PS[bank][:, col:col + 128],
                              po_sb[:, hq * 1024 + dt * 128: hq * 1024 + (dt + 1) * 128],
                              otb[hpp][:],
                              start=(hq == 0 and dt % 4 == 0),
                              stop=(hq == 7 and dt % 4 == 3))
            mm.then_inc(tY, 1)

        for qt in range(NQT):
            for h in range(8):
                hp = h % 2
                idx = qt * 8 + h
                for mt in range(32):
                    slot = nL[0] % 12
                    bank = slot // 4
                    col = (slot % 4) * 128
                    if slot % 4 == 0 and nL[0] >= 9:
                        t.wait_ge(sE, (nL[0] + 3 - 12) // 4 + 1)
                    t.matmul(p.PS[bank][:, col:col + 128],
                             Ks[:, mt * 128:(mt + 1) * 128],
                             Qs[:, h * L + qt * 128: h * L + (qt + 1) * 128],
                             start=True, stop=True).then_inc(tL, 1)
                    nL[0] += 1
                if pend[0] is not None:
                    emit_tr(t, pend[0])
                t.wait_ge(vS, idx)
                for mt in range(32):
                    if mt % 4 == 0:
                        t.wait_ge(sE, nE[0] + mt // 4 + 1)
                    mm = t.matmul(p.PS[4][:, 0:129],
                                  PT[hp][:, mt * 128:(mt + 1) * 128],
                                  vtok[:, mt * 132: mt * 132 + 129],
                                  start=(mt == 0), stop=(mt == 31))
                mm.then_inc(tP, 1)
                nE[0] += 8
                if pend[0] is not None:
                    emit_y(t, pend[0])
                pend[0] = idx
        emit_tr(t, pend[0])
        emit_y(t, pend[0])

    @blk.scalar
    def _(s_):
        cnt = 0
        for qt in range(NQT):
            for h in range(8):
                hp = h % 2
                if cnt >= 2:
                    s_.wait_ge(tP, cnt - 1)
                for b4 in range(8):
                    s_.wait_ge(tL, cnt * 32 + (b4 + 1) * 4)
                    bank = (cnt * 8 + b4) % 3
                    s_.activation(PT[hp][:, b4 * 512:(b4 + 1) * 512],
                                  p.PS[bank][:, :], AF.Exp).then_inc(sE, 1)
                cnt += 1

    @blk.vector
    def _(v):
        for qt in range(NQT):
            A = qt % 2
            for h in range(8):
                hp = h % 2
                v.wait_ge(tP, qt * 8 + h + 1)
                v.reciprocal(rs[:], p.PS[4][:, 128:129]).then_inc(vR, 1)
                v.wait_ge(vR, qt * 8 + h + 1)
                v.tensor_scalar_mul(onb[:], p.PS[4][:, 0:128], rs[:, 0:1])
                v.engine_nop().then_inc(vS, 1)
                v.wait_ge(tT, qt * 8 + h + 1)
                idx = qt * 8 + h
                if idx % 2 == 0:
                    v.tensor_copy(otb[hp][:], p.PSBF[:, 256:384])
                else:
                    v.tensor_copy(otb[hp][:], p.PS[3][:, 0:128])
                v.engine_nop().then_inc(vO, 1)
            v.wait_ge(tY, (qt + 1) * 8)
            v.wait_ge(xg2[A], 16 * 2 * NDT * (qt // 2 + 1))
            if qt >= 2:
                v.wait_ge(so2[A], 16 * NDT * (qt // 2))
            for dt in range(NDT):
                bank = 5 + dt // 4
                col = (dt % 4) * 128
                v.scalar_tensor_tensor(outb[A][:, dt * 128:(dt + 1) * 128],
                                       p.PS[bank][:, col:col + 128], 1.0,
                                       xg[A][:, (NDT + dt) * 128:(NDT + dt + 1) * 128],
                                       ALU.mult, ALU.mult)
            v.engine_nop().then_inc(vV, 1)
            v.wait_ge(vV, qt + 1)
            for dt in range(NDT):
                v.tensor_add(outb[A][:, dt * 128:(dt + 1) * 128],
                             outb[A][:, dt * 128:(dt + 1) * 128],
                             xg[A][:, dt * 128:(dt + 1) * 128])
            v.engine_nop().then_inc(vC, 1)

    @blk.gpsimd
    def _(g):
        for qt in range(NQT):
            A = qt % 2
            g.wait_ge(vC, qt + 1)
            for dt in range(NDT):
                g.dma_start(out=x1d[dt * 128:(dt + 1) * 128, qt * 128:(qt + 1) * 128],
                            in_=outb[A][:, dt * 128:(dt + 1) * 128]).then_inc(so2[A], 16)
        g.wait_ge(so2[0], 16 * NDT * (NQT // 2))
        g.wait_ge(so2[1], 16 * NDT * (NQT // 2))
        g.engine_nop().then_inc(p.bar_sem, 1)
    p.bar += 1


def ph_qkv(p, dr):
    nc, blk = p.nc, p.block
    G = p.G
    bar0 = p.bar
    xn2d = dr["xn2d"]
    outs = {"q": dr["qnd"], "k": dr["knd"], "v": dr["vsd"]}
    wsrc = {"q": dr["w_wq"], "k": dr["w_wk"], "v": dr["w_wv"]}
    csrc = {"q": dr["w_cq"], "k": dr["w_ck"], "v": dr["w_cv"]}
    xlq = p.sem("qkxl", dma=True)
    wl = p.sem("qkwl", dma=True)
    wlb = p.sem("qkwlb", dma=True)
    cl = p.sem("qkcl", dma=True)
    tpr = p.sem("qktpr")
    vev = p.sem("qkvev")
    vcv = p.sem("qkvcv")
    vcb2 = p.sem("qkvcb2")
    ssl = p.sem("qkssl")
    vsq = p.sem("qkvsq")
    tsq = p.sem("qktsq")
    ssr = p.sem("qkssr")
    vsc = p.sem("qkvsc")
    st2 = [p.sem("qkst_a", dma=True), p.sem("qkst_b", dma=True)]
    tbe = p.sem("qktbe")
    sbe = p.sem("qksbe")

    xns = p.sb("qkxn", [128, NDT * L], BF16)
    wcur = p.sb("qkw", [128, NDT * 1024], BF16)
    cbuf = [p.sb(f"qkcb{i}", [128, 4099], F32) for i in range(2)]
    cnv = [p.sb(f"qkcv{i}", [128, L], F32) for i in range(2)]
    qh = p.sb("qkqh", [128, 2 * L], F32)
    cvw = p.sb("qkcvw", [128, 3 * NDT * 4], F32)
    wbsb = p.sb("qkwb", [128, NDT * 4], BF16)
    betab = p.sb("qkbeta", [4, L], F32)
    sdq = p.sb("qksdq", [128, 512], F32)
    rno = p.sb("qkrno", [128, 512], F32)

    PLIST = ("q", "k", "v")
    SLOTS = [("q", pr, 2) for pr in range(4)] + [("k", pr, 2) for pr in range(4)] \
            + [("v", dto, 1) for dto in range(NDT)]

    def st_total(upto, par):
        return 16 * sum(nd for j, (_, _, nd) in enumerate(SLOTS)
                        if j <= upto and j % 2 == par)

    def slot_of(pi, dto):
        P = PLIST[pi]
        if P == "v":
            return 8 + dto
        return pi * 4 + dto // 2

    @blk.sync
    def _(q):
        q.wait_ge(p.bar_sem, bar0)
        for dt in range(NDT):
            q.dma_start(out=xns[:, dt * L:(dt + 1) * L],
                        in_=xn2d[dt * 128:(dt + 1) * 128, :]).then_inc(xlq, 16)
        for pi, P in enumerate(PLIST):
            q.dma_start(out=cvw[:, pi * 32:(pi + 1) * 32],
                        in_=csrc[P].rearrange("(dtp pp) k4 -> pp dtp k4", pp=128)
                        ).then_inc(cl, 16)

    @blk.gpsimd
    def _(g):
        g.wait_ge(p.bar_sem, bar0)
        g.memset(cbuf[0][:, 0:3], 0.0)
        g.memset(cbuf[1][:, 0:3], 0.0)
        g.engine_nop().then_inc(vcv, 1)
        g.dma_start(out=wbsb[:], in_=dr["w_wb"].rearrange("(dtp pp) e -> pp dtp e", pp=128)
                    ).then_inc(wlb, 16)
        g.dma_start(out=wcur[:],
                    in_=wsrc["q"].rearrange("(dtp pp) dd -> pp dtp dd", pp=128)
                    ).then_inc(wl, 16)

    @blk.tensor
    def _(t):
        t.wait_ge(p.bar_sem, bar0)
        t.wait_ge(xlq, 16 * NDT)
        t.wait_ge(wlb, 16)
        for c in range(8):
            if c >= 1:
                t.wait_ge(sbe, c)
            for dt in range(NDT):
                mm = t.matmul(p.PS[2][0:4, :], wbsb[:, dt * 4:(dt + 1) * 4],
                              xns[:, dt * L + c * 512: dt * L + (c + 1) * 512],
                              start=(dt == 0), stop=(dt == NDT - 1))
            mm.then_inc(tbe, 1)
        ng = 0
        npair = 0
        for pi, P in enumerate(PLIST):
            t.wait_ge(wl, 16 * (pi + 1))
            for dto in range(NDT):
                for c in range(8):
                    if ng >= 2:
                        t.wait_ge(vev, ng - 1)
                    for dt in range(NDT):
                        mm = t.matmul(p.PS[ng % 2][:, :],
                                      wcur[:, dt * 1024 + dto * 128: dt * 1024 + (dto + 1) * 128],
                                      xns[:, dt * L + c * 512: dt * L + (c + 1) * 512],
                                      start=(dt == 0), stop=(dt == NDT - 1))
                    mm.then_inc(tpr, 1)
                    ng += 1
                if P != "v" and dto % 2 == 1:
                    t.wait_ge(vsq, npair + 1)
                    for c in range(8):
                        if npair * 8 + c >= 1:
                            t.wait_ge(ssr, npair * 8 + c)
                        for u in range(2):
                            mm = t.matmul(p.PS[3][:, :], G["ones"][:],
                                          cnv[u][:, c * 512:(c + 1) * 512],
                                          start=(u == 0), stop=(u == 1))
                        mm.then_inc(tsq, 1)
                    npair += 1

    @blk.vector
    def _(v):
        v.wait_ge(p.bar_sem, bar0)
        v.wait_ge(cl, 48)
        ng = 0
        npair = 0
        for pi, P in enumerate(PLIST):
            for dto in range(NDT):
                j = pi * NDT + dto
                A = dto % 2
                if ng == 0:
                    v.wait_ge(vcv, 1)
                for c in range(8):
                    v.wait_ge(tpr, ng + 1)
                    ii = v.tensor_copy(cbuf[A][:, 3 + c * 512: 3 + (c + 1) * 512],
                                       p.PS[ng % 2][:, :])
                    v.engine_nop().then_inc(vev, 1)
                    ng += 1
                p.edge(ii, v, "qkcb")
                if j >= 1:
                    v.wait_ge(ssl, j)
                if npair >= 1:
                    v.wait_ge(tsq, 8 * npair)
                wt = lambda k: cvw[:, (pi * 8 + dto) * 4 + k: (pi * 8 + dto) * 4 + k + 1]
                other, me = cnv[1 - A], cnv[A]
                p.edge(v.tensor_scalar_mul(other[:], cbuf[A][:, 0:L], wt(0)), v, "qkc0")
                p.edge(v.scalar_tensor_tensor(me[:], cbuf[A][:, 1:1 + L], wt(1), other[:],
                                              ALU.mult, ALU.add), v, "qkc1")
                p.edge(v.scalar_tensor_tensor(other[:], cbuf[A][:, 2:2 + L], wt(2), me[:],
                                              ALU.mult, ALU.add), v, "qkc2")
                v.scalar_tensor_tensor(me[:], cbuf[A][:, 3:3 + L], wt(3), other[:],
                                       ALU.mult, ALU.add).then_inc(vcv, 1)
                if P != "v" and dto % 2 == 1:
                    v.wait_ge(ssl, j + 1)
                    v.tensor_mul(cnv[0][:], qh[:, 0:L], qh[:, 0:L])
                    v.tensor_mul(cnv[1][:], qh[:, L:2 * L], qh[:, L:2 * L]
                                 ).then_inc(vsq, 1)
                    for c in range(8):
                        v.wait_ge(ssr, npair * 8 + c + 1)
                        p.edge(v.reciprocal(rno[:], sdq[:]), v, "qkrn")
                        v.tensor_mul(qh[:, c * 512:(c + 1) * 512],
                                     qh[:, c * 512:(c + 1) * 512], rno[:])
                        v.tensor_mul(qh[:, L + c * 512: L + (c + 1) * 512],
                                     qh[:, L + c * 512: L + (c + 1) * 512], rno[:]
                                     ).then_inc(vsc, 1)
                    npair += 1

    @blk.scalar
    def _(s_):
        s_.wait_ge(p.bar_sem, bar0)
        for c in range(8):
            s_.wait_ge(tbe, c + 1)
            s_.activation(betab[:, c * 512:(c + 1) * 512], p.PS[2][0:4, :],
                          AF.Sigmoid).then_inc(sbe, 1)
        npair = 0
        for pi, P in enumerate(PLIST):
            for dto in range(NDT):
                j = pi * NDT + dto
                A = dto % 2
                s_.wait_ge(vcv, 1 + j + 1)
                sl = slot_of(pi, dto)
                prev = sl - 1
                if prev >= 0 and (P == "v" or dto % 2 == 0):
                    s_.wait_ge(st2[prev % 2], st_total(prev, prev % 2))
                s_.activation(qh[:, A * L:(A + 1) * L], cnv[A][:], AF.Silu
                              ).then_inc(ssl, 1)
                if P != "v" and dto % 2 == 1:
                    for c in range(8):
                        s_.wait_ge(tsq, npair * 8 + c + 1)
                        if npair * 8 + c >= 1:
                            s_.wait_ge(vsc, npair * 8 + c)
                        s_.activation(sdq[:], p.PS[3][:, :], AF.Sqrt
                                      ).then_inc(ssr, 1)
                    npair += 1

    @blk.gpsimd
    def _(g):
        npair = 0
        for sl, (P, idx, nd) in enumerate(SLOTS):
            if sl == 4:
                g.wait_ge(tpr, 64)
                g.dma_start(out=wcur[:],
                            in_=wsrc["k"].rearrange("(dtp pp) dd -> pp dtp dd", pp=128)
                            ).then_inc(wl, 16)
            if sl == 8:
                g.wait_ge(tpr, 128)
                g.dma_start(out=wcur[:],
                            in_=wsrc["v"].rearrange("(dtp pp) dd -> pp dtp dd", pp=128)
                            ).then_inc(wl, 16)
            if P == "v":
                dto = idx
                g.wait_ge(ssl, 16 + dto + 1)
                g.dma_start(out=outs["v"][dto * 128:(dto + 1) * 128, :],
                            in_=qh[:, (dto % 2) * L:(dto % 2 + 1) * L]
                            ).then_inc(st2[sl % 2], 16)
            else:
                g.wait_ge(vsc, npair * 8 + 8)
                for u in range(2):
                    dto = idx * 2 + u
                    g.dma_start(out=outs[P][dto * 128:(dto + 1) * 128, :],
                                in_=qh[:, u * L:(u + 1) * L]
                                ).then_inc(st2[sl % 2], 16)
                npair += 1
        g.wait_ge(sbe, 8)
        g.dma_start(out=dr["betad"], in_=betab[:]).then_inc(st2[0], 16)
        g.wait_ge(st2[0], st_total(15, 0) + 16)
        g.wait_ge(st2[1], st_total(15, 1))
        g.engine_nop().then_inc(p.bar_sem, 1)
    p.bar += 1


def ph_l1core(p, dr):
    nc, blk = p.nc, p.block
    G = p.G
    bar0 = p.bar
    tsm = p.sem("lctsm")
    ssm = p.sem("lcssm")
    vsm = p.sem("lcvsm")
    gl2 = [p.sem("lcgl_a", dma=True), p.sem("lcgl_b", dma=True)]
    ws2 = [p.sem("lcws_a", dma=True), p.sem("lcws_b", dma=True)]
    hl2 = [p.sem("lchl_a", dma=True), p.sem("lchl_b", dma=True)]
    sst2 = [p.sem("lcss_a", dma=True), p.sem("lcss_b", dma=True)]
    wol = p.sem("lcwol", dma=True)

    NG = NCK // 8
    qnd, knd, vsd, betad = dr["qnd"], dr["knd"], dr["vsd"], dr["betad"]
    wtd, utd, s2d = dr["wtd"], dr["utd"], dr["s2d"]

    Qg = p.sb("lcQ", [128, 4096], F32)
    Wg_ = p.sb("lcWg", [128, 4096], F32)
    Ug_ = p.sb("lcUg", [128, 4096], F32)
    Wg = [Wg_, Wg_]
    Ug = [Ug_, Ug_]
    Rall = p.sb("lcRall", [128, 512], F32)
    Pall = p.sb("lcPall", [128, 512], F32)
    aqk = p.sb("lcaqk", [128, L], F32)
    tmpg = p.sb("lctmpg", [128, 512], F32)
    Sdall = p.sb("lcSd", [128, 512], F32)
    SdPre = p.sb("lcSdP", [128, 512], F32)
    nSdPre = p.sb("lcnSdP", [128, 512], F32)
    av = p.sb("lcav", [128, 512], F32)

    jobs = []
    tensor_order = []

    def J(t=None, s=None, v=None, deps=(), tx=(), vx=(), sx=()):
        jid = len(jobs)
        jobs.append(dict(t=t, s=s, v=v, deps=tuple(deps), tx=list(tx),
                         vx=list(vx), sx=list(sx)))
        if t is not None:
            tensor_order.append(jid)
        return jid

    sub1 = ExitStack()
    Kg = [sub1.enter_context(nc.sbuf_tensor(f"lcK{i}", [128, 4096], F32)) for i in range(2)]
    Vg = sub1.enter_context(nc.sbuf_tensor("lcV", [128, 4096], F32))
    bb = sub1.enter_context(nc.sbuf_tensor("lcbb", [128, 4 * 512], F32))
    beg = sub1.enter_context(nc.sbuf_tensor("lcbeg", [4, 512], F32))
    Kb = [sub1.enter_context(nc.sbuf_tensor(f"lcKb{i}", [128, 1024], F32)) for i in range(2)]
    Vb = [sub1.enter_context(nc.sbuf_tensor(f"lcVb{i}", [128, 1024], F32)) for i in range(2)]
    KbT = [sub1.enter_context(nc.sbuf_tensor(f"lcKbT{i}", [64, 2048], F32)) for i in range(2)]
    VbT = [sub1.enter_context(nc.sbuf_tensor(f"lcVbT{i}", [64, 2048], F32)) for i in range(2)]
    sqg = sub1.enter_context(nc.sbuf_tensor("lcsqg", [128, 1024], F32))
    ind4 = sub1.enter_context(nc.sbuf_tensor("lcind4", [4, 512], F32))
    i4s = p.sem("lci4")
    TC = [{n: sub1.enter_context(nc.sbuf_tensor(f"lc{n}{i}", [64, 128], F32)) for n in
           ("Lst", "Bst", "L2", "B2", "L4", "B4", "L8", "B8", "L16", "B16",
            "L32", "R", "R2", "TT")} for i in range(2)]


    NLD1 = 25

    def gl_val(g):
        return 16 * NLD1 * (g // 2 + 1)

    def init_jobs():
        def v_(v, bank):
            return v.engine_nop()
        return J(v=v_)

    def bb_jobs(g):
        gp = g % 2
        ids = []
        for h in range(4):
            def t_(t, bank, h=h):
                return t.matmul(bank[:, :], ind4[:, h * 128:(h + 1) * 128], beg[0:4, :],
                                start=True, stop=True)
            def v_(v, bank, h=h):
                return v.tensor_copy(bb[:, h * 512:(h + 1) * 512], bank[:, :])
            ids.append(J(t=t_, v=v_, tx=[(gl2[gp], gl_val(g)), (i4s, 2)]))
        return ids

    def prep_jobs(q, j_init):
        pp, gp = q % 2, (q // 4) % 2
        ci0 = (2 * q) % 8
        ids = []
        for dt in range(NDT):
            def v_(v, bank, dt=dt, pp=pp, gp=gp, ci0=ci0):
                v.tensor_mul(Kb[pp][:, dt * 128:(dt + 1) * 128],
                             Kg[gp][:, dt * 512 + ci0 * 64: dt * 512 + ci0 * 64 + 128],
                             bb[:, (dt // 2) * 512 + ci0 * 64: (dt // 2) * 512 + ci0 * 64 + 128])
                return v.tensor_mul(Vb[pp][:, dt * 128:(dt + 1) * 128],
                             Vg[:, dt * 512 + ci0 * 64: dt * 512 + ci0 * 64 + 128],
                             bb[:, (dt // 2) * 512 + ci0 * 64: (dt // 2) * 512 + ci0 * 64 + 128])
            ids.append(J(v=v_, vx=[(gl2[gp], gl_val(q // 4))], deps=[j_init]))
        return ids

    def agram_jobs(q, prep_ids, j_init):
        pp, gp = q % 2, (q // 4) % 2
        ci0 = (2 * q) % 8
        def tA(t, bank):
            for cc in range(2):
                for dt in range(NDT):
                    mm = t.matmul(bank[0:64, cc * 64: cc * 64 + 64],
                                  Kb[pp][:, dt * 128 + cc * 64: dt * 128 + cc * 64 + 64],
                                  Kg[gp][:, dt * 512 + (ci0 + cc) * 64: dt * 512 + (ci0 + cc) * 64 + 64],
                                  start=(dt == 0), stop=(dt == NDT - 1))
            return mm
        def vA(v, bank):
            for cc in range(2):
                ii = v.tensor_mul(TC[pp]["Lst"][:, cc * 64: cc * 64 + 64],
                                  bank[0:64, cc * 64: cc * 64 + 64], G["mlow"][:])
            return ii
        def tB(t, bank):
            for cc in range(2):
                for dt in range(NDT):
                    mm = t.matmul(bank[0:64, cc * 64: cc * 64 + 64],
                                  Kg[gp][:, dt * 512 + (ci0 + cc) * 64: dt * 512 + (ci0 + cc) * 64 + 64],
                                  Kb[pp][:, dt * 128 + cc * 64: dt * 128 + cc * 64 + 64],
                                  start=(dt == 0), stop=(dt == NDT - 1))
            return mm
        def vB(v, bank):
            for cc in range(2):
                ii = v.tensor_mul(TC[pp]["Bst"][:, cc * 64: cc * 64 + 64],
                                  bank[0:64, cc * 64: cc * 64 + 64], G["mup"][:])
            return ii
        ja = J(t=tA, v=vA, deps=list(prep_ids) + [j_init])
        jb = J(t=tB, v=vB, deps=list(prep_ids) + [j_init])
        return ja, jb

    def tr_thunks(q, prep_ids):
        pp = q % 2
        thunks = []
        for dt in range(NDT):
            for srcb, dstb in ((Kb, KbT), (Vb, VbT)):
                def mk(srcb=srcb, dstb=dstb, dt=dt, pp=pp):
                    def t_(t, bank):
                        for cc in range(2):
                            mm = t.transpose(bank[0:64, cc * 128: cc * 128 + 128],
                                             srcb[pp][:, dt * 128 + cc * 64: dt * 128 + cc * 64 + 64],
                                             G["ident"][:])
                        return mm
                    def v_(v, bank):
                        return v.tensor_copy(dstb[pp][:, dt * 256:(dt + 1) * 256],
                                             bank[0:64, 0:256])
                    return J(t=t_, v=v_, deps=prep_ids)
                thunks.append(mk)
        return thunks

    def chain_thunks(q, ja, jb, j_init):
        pp = q % 2
        C = TC[pp]
        seq = [("L2", "Bst", "Lst"), ("B2", "Lst", "Bst"),
               ("L4", "B2", "L2"), ("B4", "L2", "B2"),
               ("L8", "B4", "L4"), ("B8", "L4", "B4"),
               ("L16", "B8", "L8"), ("B16", "L8", "B8"),
               ("L32", "B16", "L16")]
        dep_of = {"Lst": ja, "Bst": jb}
        state = {}
        thunks = []
        for nm, lh, rh in seq:
            def mk(nm=nm, lh=lh, rh=rh):
                def t_(t, bank):
                    for cc in range(2):
                        mm = t.matmul(bank[0:64, cc * 64: cc * 64 + 64],
                                      C[lh][:, cc * 64: cc * 64 + 64],
                                      C[rh][:, cc * 64: cc * 64 + 64],
                                      start=True, stop=True)
                    return mm
                def v_(v, bank):
                    return v.tensor_copy(C[nm][:], bank[0:64, 0:128])
                dep_of[nm] = J(t=t_, v=v_, deps=[dep_of[lh], dep_of[rh]])
                return dep_of[nm]
            thunks.append(mk)
        def mkR0():
            def vR0(v, bank):
                for cc in range(2):
                    ii = v.tensor_sub(C["R"][:, cc * 64: cc * 64 + 64],
                                      G["ident"][0:64, 0:64],
                                      C["Bst"][:, cc * 64: cc * 64 + 64])
                return ii
            state["jr"] = J(v=vR0, deps=[jb, j_init])
            return state["jr"]
        thunks.insert(2, mkR0)
        for lk in ("L2", "L4", "L8", "L16", "L32"):
            def mk(lk=lk):
                def t_(t, bank):
                    for cc in range(2):
                        mm = t.matmul(bank[0:64, cc * 64: cc * 64 + 64],
                                      C[lk][:, cc * 64: cc * 64 + 64],
                                      C["R"][:, cc * 64: cc * 64 + 64],
                                      start=True, stop=True)
                    return mm
                def v_(v, bank):
                    ii = v.tensor_add(C["R2"][:], bank[0:64, 0:128], C["R"][:])
                    p.edge(ii, v, "lcr")
                    return v.tensor_copy(C["R"][:], C["R2"][:])
                state["jr"] = J(t=t_, v=v_, deps=[state["jr"], dep_of[lk]])
                return state["jr"]
            thunks.append(mk)
        def mkT():
            def tT(t, bank):
                for cc in range(2):
                    mm = t.matmul(bank[0:64, cc * 64: cc * 64 + 64],
                                  C["Lst"][:, cc * 64: cc * 64 + 64],
                                  C["R"][:, cc * 64: cc * 64 + 64],
                                  start=True, stop=True)
                return mm
            def vT(v, bank):
                return v.scalar_tensor_tensor(C["TT"][:], bank[0:64, 0:128], -2.0,
                                              C["R"][:], ALU.mult, ALU.add)
            state["jt"] = J(t=tT, v=vT, deps=[state["jr"], ja])
            return state["jt"]
        thunks.append(mkT)
        return thunks, (lambda: state["jt"])

    def wu_jobs(q, jt, trids):
        g = q // 4
        pp, gp = q % 2, (q // 4) % 2
        ci0 = (2 * q) % 8
        wg, ug = Wg[g % 2], Ug[g % 2]
        C = TC[pp]
        ids = []
        for dt in range(NDT):
            for which in (0, 1):
                srcT = (KbT if which == 0 else VbT)
                dst = (wg if which == 0 else ug)
                acc = (Rall if which == 0 else Pall)
                def t_(t, bank, srcT=srcT, dt=dt, pp=pp):
                    for cc in range(2):
                        mm = t.matmul(bank[:, cc * 64: cc * 64 + 64],
                                      srcT[pp][:, dt * 256 + cc * 128: dt * 256 + (cc + 1) * 128],
                                      C["TT"][:, cc * 64: cc * 64 + 64],
                                      start=True, stop=True)
                    return mm
                def v_(v, bank, dst=dst, dt=dt, ci0=ci0, acc=acc, q=q, gp=gp):
                    sl = slice(dt * 512 + ci0 * 64, dt * 512 + ci0 * 64 + 128)
                    v.tensor_copy(dst[:, sl], bank[:, 0:128])
                    for cc in range(2):
                        i = 2 * q + cc
                        ii = v.scalar_tensor_tensor(
                            bank[:, 128 + cc * 64: 128 + cc * 64 + 64],
                            Kg[gp][:, dt * 512 + (ci0 + cc) * 64: dt * 512 + (ci0 + cc) * 64 + 64],
                            1.0, bank[:, cc * 64: cc * 64 + 64], ALU.mult, ALU.mult,
                            accum_out=acc[:, dt * 64 + (i % 64): dt * 64 + (i % 64) + 1])
                    return ii
                vx = []
                if g >= 1:
                    vx.append((ws2[(g - 1) % 2], 16 * 16 * ((g - 1) // 2 + 1)))
                ids.append(J(t=t_, v=v_, deps=[jt] + list(trids), vx=vx))
        return ids

    def aqk_jobs(g, j_init):
        gp = g % 2
        ids = []
        for qq in range(4):
            ci0 = qq * 2
            def vpre(v, bank, ci0=ci0, gp=gp):
                for dt in range(NDT):
                    ii = v.tensor_mul(sqg[:, dt * 128:(dt + 1) * 128],
                                 Qg[:, dt * 512 + ci0 * 64: dt * 512 + ci0 * 64 + 128],
                                 Kg[gp][:, dt * 512 + ci0 * 64: dt * 512 + ci0 * 64 + 128])
                return ii
            jpre = J(v=vpre, vx=[(gl2[gp], gl_val(g))])
            def t_(t, bank):
                for cc in range(2):
                    for dt in range(NDT):
                        mm = t.matmul(bank[:, cc * 64: cc * 64 + 64], G["ones"][:],
                                      sqg[:, dt * 128 + cc * 64: dt * 128 + cc * 64 + 64],
                                      start=(dt == 0), stop=(dt == NDT - 1))
                return mm
            def v_(v, bank, g=g, ci0=ci0):
                for cc in range(2):
                    ii = v.tensor_mul(aqk[:, g * 512 + (ci0 + cc) * 64: g * 512 + (ci0 + cc) * 64 + 64],
                                      bank[:, cc * 64: cc * 64 + 64], G["bmask"][:])
                return ii
            ids.append(J(t=t_, v=v_, deps=[jpre, j_init]))
        return ids

    j_init = init_jobs()
    prep_last = {}
    group_wu_last = {}
    pending = None
    NPAIR = NCK // 2
    for q in range(NPAIR):
        g = q // 4
        if q % 4 == 0:
            bbids = bb_jobs(g)
            aqk_jobs(g, j_init)
        pr = prep_jobs(q, j_init)
        ja, jb = agram_jobs(q, pr, j_init)
        trth = tr_thunks(q, pr)
        chth, get_jt = chain_thunks(q, ja, jb, j_init)
        trids = []
        k1 = k2 = 0
        while k1 < len(trth) or k2 < len(chth):
            if k1 < len(trth):
                trids.append(trth[k1]())
                k1 += 1
            if k2 < len(chth):
                chth[k2]()
                k2 += 1
        if pending is not None:
            qp, jtp, trp = pending
            wids = wu_jobs(qp, jtp, trp)
            group_wu_last[qp // 4] = wids[-1]
        pending = (q, get_jt(), trids)
        prep_last[g] = max(pr)
    qp, jtp, trp = pending
    wids = wu_jobs(qp, jtp, trp)
    group_wu_last[NG - 1] = wids[-1]

    def vscan(v, bank):
        ii = v.tensor_scalar(av[:], Rall[:], -1.0, 1.0, ALU.mult, ALU.add)
        p.edge(ii, v, "lcsc")
        for dt in range(NDT):
            sl = slice(dt * 64, (dt + 1) * 64)
            ii = v.tensor_tensor_scan(Sdall[:, sl], av[:, sl], Pall[:, sl],
                                      0.0, ALU.mult, ALU.add)
        p.edge(ii, v, "lcsc2")
        ii = v.memset(SdPre[:], 0.0)
        p.edge(ii, v, "lcscm")
        for dt in range(NDT):
            ii = v.tensor_copy(SdPre[:, dt * 64 + 1: dt * 64 + 64],
                               Sdall[:, dt * 64: dt * 64 + 63])
        p.edge(ii, v, "lcsc3")
        return v.tensor_scalar_mul(nSdPre[:], SdPre[:], -1.0)
    j_scan = J(v=vscan, deps=[group_wu_last[NG - 1]])

    sub1.close()

    ub = p.sb("lcub", [128, 4096], F32)
    tmp3 = p.sb("lctmp3", [128, 4096], F32)
    ob = p.sb("lcob", [128, 4096], F32)
    onb = p.sb("lconb", [128, 4096], BF16)
    wo_bf = p.sb("lcwo", [128, NDT * 1024], BF16)
    sdq = [p.sb(f"lcsdq{i}", [128, 512], F32) for i in range(2)]
    rno = p.sb("lcrno", [128, 512], F32)
    s2st = [p.sb(f"lcs2{i}", [128, 512], F32) for i in range(2)]

    NLD3 = 24

    def hl_val(g):
        return 16 * NLD3 * (g // 2 + 1)

    s2_jobs = {}
    g3_last = {}
    rms_hist = []
    for g in range(NG):
        gp = g % 2
        def vel(v, bank, g=g, gp=gp):
            for dt in range(NDT):
                for ci in range(8):
                    i = g * 8 + ci
                    sl = slice(dt * 512 + ci * 64, dt * 512 + ci * 64 + 64)
                    v.scalar_tensor_tensor(ub[:, sl], Wg[gp][:, sl],
                                           nSdPre[:, dt * 64 + i % 64: dt * 64 + i % 64 + 1],
                                           Ug[gp][:, sl], ALU.mult, ALU.add)
            ii = v.engine_nop()
            p.edge(ii, v, "lce1")
            for dt in range(NDT):
                for ci in range(8):
                    sl = slice(dt * 512 + ci * 64, dt * 512 + ci * 64 + 64)
                    v.tensor_mul(tmp3[:, sl],
                                 aqk[:, g * 512 + ci * 64: g * 512 + ci * 64 + 64],
                                 ub[:, sl])
            ii = v.engine_nop()
            p.edge(ii, v, "lce2")
            for dt in range(NDT):
                for ci in range(8):
                    i = g * 8 + ci
                    sl = slice(dt * 512 + ci * 64, dt * 512 + ci * 64 + 64)
                    ii = v.scalar_tensor_tensor(ob[:, sl], Qg[:, sl],
                                                SdPre[:, dt * 64 + i % 64: dt * 64 + i % 64 + 1],
                                                tmp3[:, sl], ALU.mult, ALU.add)
            return ii
        j_el = J(v=vel, deps=[j_scan], vx=[(hl2[gp], hl_val(g))])
        last_rms = None
        for h in range(4):
            def vsq_(v, bank, h=h):
                v.tensor_mul(tmp3[:, (2 * h) * 512:(2 * h) * 512 + 512],
                             ob[:, (2 * h) * 512:(2 * h + 1) * 512],
                             ob[:, (2 * h) * 512:(2 * h + 1) * 512])
                return v.tensor_mul(tmp3[:, (2 * h + 1) * 512:(2 * h + 1) * 512 + 512],
                             ob[:, (2 * h + 1) * 512:(2 * h + 2) * 512],
                             ob[:, (2 * h + 1) * 512:(2 * h + 2) * 512])
            j_sq = J(v=vsq_, deps=[j_el])
            rp = len(rms_hist) % 2
            sx = []
            if len(rms_hist) >= 2:
                sx.append((vsm, rms_hist[-2] + 1))
            def t_(t, bank, h=h):
                for u in range(2):
                    mm = t.matmul(bank[:, :], G["ones"][:],
                                  tmp3[:, (2 * h + u) * 512:(2 * h + u) * 512 + 512],
                                  start=(u == 0), stop=(u == 1))
                return mm
            def s_(s_, bank, rp=rp):
                return s_.activation(sdq[rp][:], bank[:, :], AF.Sqrt,
                                     bias=G["eps_rms"][:, 0:1], scale=1.0 / HD)
            def v_(v, bank, h=h, rp=rp):
                rr = v.reciprocal(rno[:], sdq[rp][:])
                p.edge(rr, v, "lcrn")
                for u in range(2):
                    dt = 2 * h + u
                    ii = v.scalar_tensor_tensor(onb[:, dt * 512:(dt + 1) * 512],
                                                ob[:, dt * 512:(dt + 1) * 512],
                                                G["gvec"][:, dt:dt + 1], rno[:],
                                                ALU.mult, ALU.mult)
                return ii
            jr = J(t=t_, s=s_, v=v_, deps=[j_sq], sx=sx)
            rms_hist.append(jr)
            last_rms = jr
        for dto in range(NDT):
            spar = (g * NDT + dto) % 2
            vx = []
            if g * NDT + dto >= 2:
                vx.append((sst2[spar], 16 * ((g * NDT + dto) // 2)))
            def t_(t, bank, dto=dto):
                for dt in range(NDT):
                    mm = t.matmul(bank[:, :],
                                  wo_bf[:, dt * 1024 + dto * 128: dt * 1024 + (dto + 1) * 128],
                                  onb[:, dt * 512:(dt + 1) * 512],
                                  start=(dt == 0), stop=(dt == NDT - 1))
                return mm
            def v_(v, bank, spar=spar):
                return v.tensor_copy(s2st[spar][:], bank[:, :])
            jw = J(t=t_, v=v_, deps=[last_rms], tx=[(wol, 16)], vx=vx)
            s2_jobs[(g, dto)] = jw
        g3_last[g] = len(jobs) - 1

    ntj = {}
    tc = 0
    for jid, jb_ in enumerate(jobs):
        if jb_["t"] is not None:
            tc += 1
        ntj[jid] = tc
    nsj = {}
    sc = 0
    for jid, jb_ in enumerate(jobs):
        if jb_["s"] is not None:
            sc += 1
        nsj[jid] = sc

    RING = [1, 2, 3, 4, 5, 6]
    def bank(jid):
        return p.PS[RING[(ntj[jid] - 1) % len(RING)]]

    @blk.sync
    def _(q):
        q.wait_ge(p.bar_sem, bar0)
        for g in range(NG):
            gp = g % 2
            if g >= 1:
                q.wait_ge(vsm, prep_last[g - 1] + 1)
            if g >= 2:
                q.wait_ge(vsm, group_wu_last[g - 2] + 1)
            for dt in range(NDT):
                q.dma_start(out=Qg[:, dt * 512:(dt + 1) * 512],
                            in_=qnd[dt * 128:(dt + 1) * 128, g * 512:(g + 1) * 512]
                            ).then_inc(gl2[gp], 16)
            for dt in range(NDT):
                q.dma_start(out=Kg[gp][:, dt * 512:(dt + 1) * 512],
                            in_=knd[dt * 128:(dt + 1) * 128, g * 512:(g + 1) * 512]
                            ).then_inc(gl2[gp], 16)
            for dt in range(NDT):
                q.dma_start(out=Vg[:, dt * 512:(dt + 1) * 512],
                            in_=vsd[dt * 128:(dt + 1) * 128, g * 512:(g + 1) * 512]
                            ).then_inc(gl2[gp], 16)
            q.dma_start(out=beg[:], in_=betad[:, g * 512:(g + 1) * 512]
                        ).then_inc(gl2[gp], 16)
        q.wait_ge(vsm, j_scan + 1)
        q.wait_ge(ws2[0], 16 * 16 * 4)
        q.wait_ge(ws2[1], 16 * 16 * 4)
        for g in range(NG):
            gp = g % 2
            if g >= 1:
                q.wait_ge(vsm, g3_last[g - 1] + 1)
            for dt in range(NDT):
                q.dma_start(out=Qg[:, dt * 512:(dt + 1) * 512],
                            in_=qnd[dt * 128:(dt + 1) * 128, g * 512:(g + 1) * 512]
                            ).then_inc(hl2[gp], 16)
            for dt in range(NDT):
                q.dma_start(out=Wg[gp][:, dt * 512:(dt + 1) * 512],
                            in_=wtd[dt * 128:(dt + 1) * 128, g * 512:(g + 1) * 512]
                            ).then_inc(hl2[gp], 16)
            for dt in range(NDT):
                q.dma_start(out=Ug[gp][:, dt * 512:(dt + 1) * 512],
                            in_=utd[dt * 128:(dt + 1) * 128, g * 512:(g + 1) * 512]
                            ).then_inc(hl2[gp], 16)

    @blk.gpsimd
    def _(g_):
        g_.wait_ge(p.bar_sem, bar0)
        g_.iota(ind4[:], [[1, 4], [0, 128]], channel_multiplier=-1,
                allow_small_or_imprecise_dtypes=True).then_inc(i4s, 1)
        for g in range(NG):
            gp = g % 2
            g_.wait_ge(vsm, group_wu_last[g] + 1)
            for dt in range(NDT):
                g_.dma_start(out=wtd[dt * 128:(dt + 1) * 128, g * 512:(g + 1) * 512],
                             in_=Wg[gp][:, dt * 512:(dt + 1) * 512]).then_inc(ws2[gp], 16)
            for dt in range(NDT):
                g_.dma_start(out=utd[dt * 128:(dt + 1) * 128, g * 512:(g + 1) * 512],
                             in_=Ug[gp][:, dt * 512:(dt + 1) * 512]).then_inc(ws2[gp], 16)
        g_.wait_ge(vsm, j_scan + 1)
        g_.dma_start(out=wo_bf[:],
                     in_=dr["w_wo"].rearrange("(dtp pp) dd -> pp dtp dd", pp=128)
                     ).then_inc(wol, 16)
        nst = 0
        for g in range(NG):
            for dto in range(NDT):
                spar = nst % 2
                g_.wait_ge(vsm, s2_jobs[(g, dto)] + 1)
                g_.dma_start(out=s2d[dto * 128:(dto + 1) * 128, g * 512:(g + 1) * 512],
                             in_=s2st[spar][:]).then_inc(sst2[spar], 16)
                nst += 1
        g_.wait_ge(sst2[0], 16 * 32)
        g_.wait_ge(sst2[1], 16 * 32)
        g_.engine_nop().then_inc(p.bar_sem, 1)

    @blk.tensor
    def _(t):
        t.wait_ge(p.bar_sem, bar0)
        for mo, jid in enumerate(tensor_order):
            jb_ = jobs[jid]
            if mo >= len(RING):
                t.wait_ge(vsm, tensor_order[mo - len(RING)] + 1)
            if jb_["deps"]:
                t.wait_ge(vsm, max(jb_["deps"]) + 1)
            for sem, val in jb_["tx"]:
                t.wait_ge(sem, val)
            mm = jb_["t"](t, bank(jid))
            mm.then_inc(tsm, 1)

    @blk.scalar
    def _(s_):
        s_.wait_ge(p.bar_sem, bar0)
        for jid, jb_ in enumerate(jobs):
            if jb_["s"] is None:
                continue
            s_.wait_ge(tsm, ntj[jid])
            for sem, val in jb_["sx"]:
                s_.wait_ge(sem, val)
            ii = jb_["s"](s_, bank(jid))
            ii.then_inc(ssm, 1)

    @blk.vector
    def _(v):
        v.wait_ge(p.bar_sem, bar0)
        v.wait_ge(i4s, 1)
        v.tensor_single_scalar(ind4[:], ind4[:], 0.0, ALU.is_equal).then_inc(i4s, 1)
        for jid, jb_ in enumerate(jobs):
            if jb_["t"] is not None:
                v.wait_ge(tsm, ntj[jid])
            if jb_["s"] is not None:
                v.wait_ge(ssm, nsj[jid])
            if jb_["t"] is None and jb_["deps"]:
                v.wait_ge(vsm, max(jb_["deps"]) + 1)
            for sem, val in jb_["vx"]:
                v.wait_ge(sem, val)
            ii = jb_["v"](v, bank(jid) if jb_["t"] is not None else None)
            ii.then_inc(vsm, 1)
    p.bar += 1


def ph_route(p, dr):
    nc, blk = p.nc, p.block
    G = p.G
    bar0 = p.bar
    affd, cwtd = dr["affd"], dr["cwtd"]
    al2 = [p.sem("rtal_a", dma=True), p.sem("rtal_b", dma=True)]
    vbi = p.sem("rtvbi")
    vcn = p.sem("rtvcn")
    sgs = p.sem("rtsgs")
    ttr = p.sem("rtttr")
    vtr = p.sem("rtvtr")
    st2 = [p.sem("rtst_a", dma=True), p.sem("rtst_b", dma=True)]

    NT = L // 128
    afb = [p.sb(f"rtaf{i}", [128, NE], F32) for i in range(2)]
    bia = p.sb("rtbia", [128, NE], F32)
    cbuf = p.sb("rtcb", [128, NE * NE], F32)
    red = p.sb("rtred", [128, 2048], F32)
    sig = p.sb("rtsig", [128, NE], F32)
    msk = p.sb("rtmsk", [128, NE], F32)
    wsu = p.sb("rtwsu", [128, 1], F32)
    wiv = p.sb("rtwiv", [128, 1], F32)
    cwt = [p.sb(f"rtcw{i}", [128, NE], F32) for i in range(2)]
    cwT = p.sb("rtcwT", [64, 128 * 2], F32)

    @blk.sync
    def _(q):
        q.wait_ge(p.bar_sem, bar0)
        for tt in range(NT):
            A = tt % 2
            if tt >= 2:
                q.wait_ge(sgs, tt - 1)
                q.wait_ge(vcn, tt - 1)
            q.dma_start(out=afb[A][:], in_=affd[tt * 128:(tt + 1) * 128, :]
                        ).then_inc(al2[A], 16)

    @blk.vector
    def _(v):
        v.wait_ge(p.bar_sem, bar0)
        for tt in range(NT):
            A = tt % 2
            v.wait_ge(al2[A], 16 * (tt // 2 + 1))
            ii = v.tensor_add(bia[:], afb[A][:], G["ebias"][:, 0:NE])
            p.edge(ii, v, "rt1")
            for j in range(NE):
                ii = v.tensor_single_scalar(cbuf[:, j * NE:(j + 1) * NE], bia[:],
                                            bia[:, j:j + 1], ALU.is_lt)
            p.edge(ii, v, "rt2")
            width = NE
            srcbuf = cbuf
            while width > 1:
                half = width // 2
                for j in range(half):
                    ii = v.tensor_add(red[:, j * NE:(j + 1) * NE],
                                      srcbuf[:, j * NE:(j + 1) * NE],
                                      srcbuf[:, (j + half) * NE:(j + half + 1) * NE])
                p.edge(ii, v, "rt3")
                srcbuf = red
                width = half
            cnt = red[:, 0:NE]
            ii = v.tensor_single_scalar(msk[:], cnt, float(TOPK), ALU.is_lt)
            p.edge(ii, v, "rt4")
            v.wait_ge(sgs, tt + 1)
            ii = v.tensor_mul(sig[:], sig[:], msk[:])
            p.edge(ii, v, "rt5")
            ii = v.reduce_sum(wsu[:], sig[:], AX.X)
            p.edge(ii, v, "rt6")
            ii = v.tensor_scalar_add(wsu[:], wsu[:], 1e-8)
            p.edge(ii, v, "rt7")
            ii = v.reciprocal(wiv[:], wsu[:])
            p.edge(ii, v, "rt8")
            if tt >= 2:
                v.wait_ge(ttr, tt - 1)
            v.tensor_scalar_mul(cwt[A][:], sig[:], wiv[:, 0:1]).then_inc(vcn, 1)
            v.wait_ge(ttr, tt + 1)
            if tt >= 2:
                v.wait_ge(st2[A], 16 * (tt // 2))
            v.tensor_copy(cwT[:, (A) * 128:(A + 1) * 128],
                          p.PS[4 + A][0:64, 0:128]).then_inc(vtr, 1)

    @blk.scalar
    def _(s_):
        s_.wait_ge(p.bar_sem, bar0)
        for tt in range(NT):
            A = tt % 2
            s_.wait_ge(al2[A], 16 * (tt // 2 + 1))
            if tt >= 1:
                s_.wait_ge(vcn, tt)
            s_.activation(sig[:], afb[A][:], AF.Sigmoid).then_inc(sgs, 1)

    @blk.tensor
    def _(t):
        t.wait_ge(p.bar_sem, bar0)
        for tt in range(NT):
            A = tt % 2
            t.wait_ge(vcn, tt + 1)
            if tt >= 2:
                t.wait_ge(vtr, tt - 1)
            t.transpose(p.PS[4 + A][0:64, 0:128], cwt[A][:, 0:64] if False else cwt[A][:],
                        G["ident"][:]).then_inc(ttr, 1)

    @blk.gpsimd
    def _(g):
        for tt in range(NT):
            A = tt % 2
            g.wait_ge(vtr, tt + 1)
            g.dma_start(out=cwtd[:, tt * 128:(tt + 1) * 128],
                        in_=cwT[0:64, A * 128:(A + 1) * 128]).then_inc(st2[A], 16)
        g.wait_ge(st2[0], 16 * NT // 2)
        g.wait_ge(st2[1], 16 * NT // 2)
        g.engine_nop().then_inc(p.bar_sem, 1)
    p.bar += 1


def ph_moe(p, dr):
    nc, blk = p.nc, p.block
    G = p.G
    bar0 = p.bar
    xn2d, x1d, s2d, outd = dr["xn2d"], dr["x1d"], dr["s2d"], dr["outd"]
    xnl = p.sem("moxnl", dma=True)
    one = p.sem("moone", dma=True)
    wl2 = [p.sem("mowl_a", dma=True), p.sem("mowl_b", dma=True)]
    ie4 = [p.sem(f"moie_{i}") for i in range(4)]
    iv4 = [p.sem(f"moiv_{i}") for i in range(4)]
    tup = p.sem("motup")
    ssl = p.sem("mossl")
    tcb = p.sem("motcb")
    vhs = p.sem("movhs")
    tdn = p.sem("motdn")
    tda = p.sem("motda")
    vev = p.sem("movev")
    xls2 = [p.sem("moxl_a", dma=True), p.sem("moxl_b", dma=True)]
    ssc = p.sem("mossc")
    vcb = p.sem("movcb")
    ost2 = [p.sem("moos_a", dma=True), p.sem("moos_b", dma=True)]

    NHF = 2
    HT = L // 2
    NCH = HT // 256
    GE = 2
    NGRP = NE // GE
    UPH = NCH + NGRP * NCH * GE
    DNH = NCH + NGRP * NCH

    xnh = p.sb("moxn", [128, NDT * HT], BF16)
    acc = p.sb("moacc", [128, NDT * HT], F32)
    wup = [p.sb(f"mowu{i}", [128, GE * 4096], BF16) for i in range(2)]
    wdn = [p.sb(f"mowd{i}", [128, GE * 4096], BF16) for i in range(2)]
    wsu_bf = p.sb("mowsu", [128, NDT * 512], BF16)
    wsd_bf = p.sb("mowsd", [128, 4 * 1024], BF16)
    cwTb = p.sb("mocwT", [64, L], BF16)
    inde = [p.sb(f"moind{i}", [64, 128], BF16) for i in range(4)]
    hb = [p.sb(f"mohb{i}", [128, 4 * 256], BF16) for i in range(2)]
    hsc = [p.sb(f"mohs{i}", [128, 4 * 256], BF16) for i in range(2)]
    xsb = [p.sb(f"moxs{i}", [128, 2 * 256], F32) for i in range(2)]
    slb = [p.sb(f"mosl{i}", [128, 2 * 256], F32) for i in range(2)]
    otb = [p.sb(f"moot{i}", [128, 2 * 256], F32) for i in range(2)]

    def up_bank(ft):
        return p.PS[4 + ft // 2], (ft % 2) * 256
    def dn_bank(dto):
        return p.PS[dto // 2], (dto % 2) * 256
    def upcount_through_group(gi):
        H, grp = divmod(gi, NGRP)
        return UPH * H + NCH + (grp + 1) * NCH * GE
    def dncount_through_group(gi):
        H, grp = divmod(gi, NGRP)
        return DNH * H + NCH + (grp + 1) * NCH

    @blk.gpsimd
    def _(g):
        g.wait_ge(p.bar_sem, bar0)
        g.dma_start(out=wsu_bf[:],
                    in_=dr["w_wsu"].rearrange("(dtp pp) f -> pp dtp f", pp=128)
                    ).then_inc(one, 16)
        g.dma_start(out=wsd_bf[:],
                    in_=dr["w_wsd"].rearrange("(ftp pp) dd -> pp ftp dd", pp=128)
                    ).then_inc(one, 16)
        g.dma_start(out=cwTb[:], in_=dr["cwtd"]).then_inc(one, 16)
        for gi in range(NHF * NGRP):
            sl = gi % 2
            grp = gi % NGRP
            if gi >= 2:
                g.wait_ge(tup, upcount_through_group(gi - 2))
                g.wait_ge(tdn, dncount_through_group(gi - 2))
            for u in range(GE):
                e = grp * GE + u
                g.dma_start(out=wup[sl][:, u * 4096:(u + 1) * 4096],
                            in_=dr["w_wup"][e].rearrange("(dtp pp) f -> pp dtp f", pp=128)
                            ).then_inc(wl2[sl], 16)
                g.dma_start(out=wdn[sl][:, u * 4096:(u + 1) * 4096],
                            in_=dr["w_wdn"][e].rearrange("(ftp pp) dd -> pp ftp dd", pp=128)
                            ).then_inc(wl2[sl], 16)
            for u in range(GE):
                e = grp * GE + u
                ei = gi * GE + u
                par = ei % 4
                if ei >= 4:
                    g.wait_ge(tcb, NCH * GE * (ei // 2 - 1))
                    g.wait_ge(iv4[par], ei // 4)
                g.iota(inde[par][:], [[0, 128]], base=e, channel_multiplier=-1,
                       allow_small_or_imprecise_dtypes=True).then_inc(ie4[par], 1)

    @blk.gpsimd
    def _(g):
        g.wait_ge(ost2[0], 16 * 2 * 32)
        g.wait_ge(ost2[1], 16 * 2 * 32)
        g.engine_nop().then_inc(p.bar_sem, 1)

    @blk.sync
    def _(q):
        q.wait_ge(p.bar_sem, bar0)
        for H in range(NHF):
            if H >= 1:
                q.wait_ge(tup, UPH * H)
            for dt in range(NDT):
                q.dma_start(out=xnh[:, dt * HT:(dt + 1) * HT],
                            in_=xn2d[dt * 128:(dt + 1) * 128, H * HT:(H + 1) * HT]
                            ).then_inc(xnl, 16)
            for st in range(32):
                ns = H * 32 + st
                par = ns % 2
                ch, dq = divmod(st, 4)
                if ns >= 2:
                    q.wait_ge(ost2[par], 16 * 2 * (ns // 2))
                for i in range(2):
                    dt = dq * 2 + i
                    q.dma_start(out=xsb[par][:, i * 256:(i + 1) * 256],
                                in_=x1d[dt * 128:(dt + 1) * 128,
                                        H * HT + ch * 256: H * HT + (ch + 1) * 256]
                                ).then_inc(xls2[par], 16)
                for i in range(2):
                    dt = dq * 2 + i
                    q.dma_start(out=otb[par][:, i * 256:(i + 1) * 256],
                                in_=s2d[dt * 128:(dt + 1) * 128,
                                        H * HT + ch * 256: H * HT + (ch + 1) * 256]
                                ).then_inc(xls2[par], 16)

    @blk.tensor
    def _(t):
        t.wait_ge(p.bar_sem, bar0)
        t.wait_ge(one, 48)
        nup = 0
        ncb = 0
        ndn = 0
        for H in range(NHF):
            t.wait_ge(xnl, 16 * NDT * (H + 1))
            for ch in range(NCH):
                if nup >= 1:
                    t.wait_ge(ssl, nup)
                for ft in range(4):
                    bank, co = up_bank(ft)
                    for dt in range(NDT):
                        mm = t.matmul(bank[:, co:co + 256],
                                      wsu_bf[:, dt * 512 + ft * 128: dt * 512 + (ft + 1) * 128],
                                      xnh[:, dt * HT + ch * 256: dt * HT + (ch + 1) * 256],
                                      start=(dt == 0), stop=(dt == NDT - 1))
                mm.then_inc(tup, 1)
                nup += 1
                t.wait_ge(ssl, nup)
                t.wait_ge(vev, ndn)
                for dto in range(NDT):
                    bank, co = dn_bank(dto)
                    for ft in range(4):
                        mm = t.matmul(bank[:, co:co + 256],
                                      wsd_bf[:, ft * 1024 + dto * 128: ft * 1024 + (dto + 1) * 128],
                                      hb[(nup - 1) % 2][:, ft * 256:(ft + 1) * 256],
                                      start=(ft == 0 and dto % 2 == 0),
                                      stop=(ft == 3 and dto % 2 == 1))
                mm.then_inc(tdn, 1)
                ndn += 1
            pending = None

            def emit_down(t, job):
                nonlocal ndn
                sl_, u_, par_, nhs_ = job
                t.wait_ge(vhs, nhs_)
                if u_ == 0:
                    t.wait_ge(vev, ndn)
                for dto in range(NDT):
                    bank, co = dn_bank(dto)
                    for ft in range(4):
                        mm = t.matmul(bank[:, co:co + 256],
                                      wdn[sl_][:, u_ * 4096 + ft * 1024 + dto * 128:
                                              u_ * 4096 + ft * 1024 + (dto + 1) * 128],
                                      hsc[par_][:, ft * 256:(ft + 1) * 256],
                                      start=(u_ == 0 and ft == 0 and dto % 2 == 0),
                                      stop=(u_ == GE - 1 and ft == 3 and dto % 2 == 1))
                if u_ == GE - 1:
                    mm.then_inc(tdn, 1)
                    ndn += 1
                else:
                    mm.then_inc(tda, 1)

            for grp in range(NGRP):
                gi = H * NGRP + grp
                sl = gi % 2
                t.wait_ge(wl2[sl], 16 * 2 * GE * (gi // 2 + 1))
                for ch in range(NCH):
                    for u in range(GE):
                        e = grp * GE + u
                        ei = H * NE + e
                        if nup >= 1:
                            t.wait_ge(ssl, nup)
                        for ft in range(4):
                            bank, co = up_bank(ft)
                            for dt in range(NDT):
                                mm = t.matmul(bank[:, co:co + 256],
                                              wup[sl][:, u * 4096 + dt * 512 + ft * 128:
                                                      u * 4096 + dt * 512 + (ft + 1) * 128],
                                              xnh[:, dt * HT + ch * 256: dt * HT + (ch + 1) * 256],
                                              start=(dt == 0), stop=(dt == NDT - 1))
                        mm.then_inc(tup, 1)
                        nup += 1
                        if pending is not None:
                            emit_down(t, pending)
                        t.wait_ge(iv4[ei % 4], ei // 4 + 1)
                        t.wait_ge(vhs, ncb)
                        t.matmul(p.PS[6][:, 0:256], inde[ei % 4][:],
                                 cwTb[0:64, H * HT + ch * 256: H * HT + (ch + 1) * 256],
                                 start=True, stop=True).then_inc(tcb, 1)
                        ncb += 1
                        pending = (sl, u, ei % 2, ncb)
                if grp == NGRP - 1:
                    emit_down(t, pending)
                    pending = None

    SW = 3

    @blk.scalar
    def _(s_):
        s_.wait_ge(p.bar_sem, bar0)
        nup = 0
        hb_consumer = [None, None]
        ndn_mirror = 0
        nhs = 0

        def emit_store(ns2):
            par2 = ns2 % 2
            H2, st2_ = divmod(ns2, 32)
            ch2, dq2 = divmod(st2_, 4)
            s_.wait_ge(vcb, ns2 + 1)
            for i2 in range(2):
                dt2 = dq2 * 2 + i2
                s_.dma_start(out=outd[dt2 * 128:(dt2 + 1) * 128,
                                      H2 * HT + ch2 * 256: H2 * HT + (ch2 + 1) * 256],
                             in_=otb[par2][:, i2 * 256:(i2 + 1) * 256]
                             ).then_inc(ost2[par2], 16)

        def comb_step(ns):
            H2, st2 = divmod(ns, 32)
            ch2, dq2 = divmod(st2, 4)
            s_.wait_ge(vev, DNH * H2 + NCH + (NGRP - 1) * NCH + ch2 + 1)
            if ns >= 2:
                s_.wait_ge(vcb, ns - 1)
            ii = None
            for i in range(2):
                dt = dq2 * 2 + i
                ii = s_.activation(slb[ns % 2][:, i * 256:(i + 1) * 256],
                                   acc[:, dt * HT + ch2 * 256: dt * HT + (ch2 + 1) * 256],
                                   AF.Silu)
            ii.then_inc(ssc, 1)
            if ns >= 2:
                emit_store(ns - 2)

        for H in range(NHF):
            for ch in range(NCH):
                par = nup % 2
                s_.wait_ge(tup, nup + 1)
                if hb_consumer[par] is not None:
                    kind, val = hb_consumer[par]
                    s_.wait_ge(tdn if kind == "tdn" else vhs, val)
                ii = None
                for half2 in range(2):
                    ii = s_.activation(hb[par][:, half2 * 512:(half2 + 1) * 512],
                                       p.PS[4 + half2][:, :], AF.Silu)
                ii.then_inc(ssl, 1)
                ndn_mirror += 1
                hb_consumer[par] = ("tdn", ndn_mirror)
                nup += 1
            for grp in range(NGRP):
                for ch in range(NCH):
                    for u in range(GE):
                        par = nup % 2
                        s_.wait_ge(tup, nup + 1)
                        if hb_consumer[par] is not None:
                            kind, val = hb_consumer[par]
                            s_.wait_ge(tdn if kind == "tdn" else vhs, val)
                        ii = None
                        for half2 in range(2):
                            ii = s_.activation(hb[par][:, half2 * 512:(half2 + 1) * 512],
                                               p.PS[4 + half2][:, :], AF.Silu)
                        ii.then_inc(ssl, 1)
                        nhs += 1
                        hb_consumer[par] = ("vhs", nhs)
                        nup += 1
                    ndn_mirror += 1
                    if grp == NGRP - 1 and ch >= SW:
                        hi = 32 if ch == NCH - 1 else 4 * (ch - SW + 1)
                        for st in range(4 * (ch - SW), hi):
                            comb_step(H * 32 + st)
        emit_store(NHF * 32 - 2)
        emit_store(NHF * 32 - 1)

    @blk.vector
    def _(v):
        v.wait_ge(p.bar_sem, bar0)
        nup = 0
        ncb = 0
        nev = 0
        nhs = 0

        def comb_mul(ns):
            par = ns % 2
            v.wait_ge(ssc, ns + 1)
            v.wait_ge(xls2[par], 16 * 4 * (ns // 2 + 1))
            ii = None
            for i in range(2):
                ii = v.tensor_mul(otb[par][:, i * 256:(i + 1) * 256],
                                  otb[par][:, i * 256:(i + 1) * 256],
                                  slb[par][:, i * 256:(i + 1) * 256])
            p.edge(ii, v, "mocb")
            ii = None
            for i in range(2):
                ii = v.tensor_add(otb[par][:, i * 256:(i + 1) * 256],
                                  otb[par][:, i * 256:(i + 1) * 256],
                                  xsb[par][:, i * 256:(i + 1) * 256])
            ii.then_inc(vcb, 1)

        def is_eq_group(gin):
            for u in range(GE):
                ei = gin * GE + u
                par = ei % 4
                if ei >= 4:
                    v.wait_ge(tcb, NCH * GE * (ei // 2 - 1))
                v.wait_ge(ie4[par], ei // 4 + 1)
                v.tensor_single_scalar(inde[par][:], inde[par][:],
                                       0.0, ALU.is_equal).then_inc(iv4[par], 1)

        is_eq_group(0)
        for H in range(NHF):
            for ch in range(NCH):
                nup += 1
                if H >= 1:
                    v.wait_ge(ssc, 32 * (H - 1) + 4 * (ch + 1))
                v.wait_ge(tdn, nev + 1)
                ii = None
                for dto in range(NDT):
                    bank, co = dn_bank(dto)
                    ii = v.tensor_copy(acc[:, dto * HT + ch * 256: dto * HT + (ch + 1) * 256],
                                       bank[:, co:co + 256])
                p.edge(ii, v, "moev")
                v.engine_nop().then_inc(vev, 1)
                nev += 1
            for grp in range(NGRP):
                gi = H * NGRP + grp
                for ch in range(NCH):
                    for u in range(GE):
                        e = grp * GE + u
                        ei = H * NE + e
                        v.wait_ge(ssl, nup + 1)
                        v.wait_ge(tcb, ncb + 1)
                        ii_h = nhs - NE * NCH * H
                        if ii_h >= 2:
                            if ii_h % 2 == 0:
                                v.wait_ge(tda, NE * NCH * H // 2 + ii_h // 2)
                            else:
                                v.wait_ge(tdn, DNH * H + NCH + (ii_h - 1) // 2)
                        par = nup % 2
                        ii = None
                        for ft in range(4):
                            ii = v.tensor_mul(hsc[ei % 2][:, ft * 256:(ft + 1) * 256],
                                              hb[par][:, ft * 256:(ft + 1) * 256],
                                              p.PS[6][:, 0:256])
                        ii.then_inc(vhs, 1)
                        nup += 1
                        ncb += 1
                        nhs += 1
                    v.wait_ge(tdn, nev + 1)
                    ii = None
                    for dto in range(NDT):
                        bank, co = dn_bank(dto)
                        ii = v.tensor_add(acc[:, dto * HT + ch * 256: dto * HT + (ch + 1) * 256],
                                          acc[:, dto * HT + ch * 256: dto * HT + (ch + 1) * 256],
                                          bank[:, co:co + 256])
                    p.edge(ii, v, "moev")
                    v.engine_nop().then_inc(vev, 1)
                    nev += 1
                    if ch == NCH - 2 and gi + 1 < NHF * NGRP:
                        is_eq_group(gi + 1)
                    if grp == NGRP - 1 and ch >= SW:
                        hi = 32 if ch == NCH - 1 else 4 * (ch - SW + 1)
                        for st in range(4 * (ch - SW), hi):
                            comb_mul(H * 32 + st)
    p.bar += 1


def _prep_weights(inp):
    f32 = np.float32
    a = lambda x: np.ascontiguousarray(np.asarray(x, dtype=f32))
    w = {}
    w["w_pq"] = a(inp["L0_attn_proj_q"])
    w["w_pk"] = a(inp["L0_attn_proj_k"])
    w["w_pv"] = a(inp["L0_attn_proj_v"])
    w["w_po"] = a(np.transpose(np.asarray(inp["L0_attn_proj_o"]), (0, 2, 1)))
    w["w_rm"] = a(np.transpose(np.asarray(inp["L0_ffn_rail_maker"]), (0, 1, 3, 2)))
    w["w_rup"] = a(np.transpose(np.asarray(inp["L0_ffn_rail_up_proj"]), (0, 1, 3, 2)))
    w["w_wc"] = a(np.asarray(inp["L0_ffn_w_cproj"]).T)
    w["w_wq"] = a(inp["L1_attn_Wq"])
    w["w_wk"] = a(inp["L1_attn_Wk"])
    w["w_wv"] = a(inp["L1_attn_Wv"])
    w["w_wb"] = a(inp["L1_attn_Wb"])
    w["w_cq"] = a(np.asarray(inp["L1_attn_conv_q"])[:, 0, :])
    w["w_ck"] = a(np.asarray(inp["L1_attn_conv_k"])[:, 0, :])
    w["w_cv"] = a(np.asarray(inp["L1_attn_conv_v"])[:, 0, :])
    g = np.asarray(inp["L1_attn_g"], dtype=f32)
    w["w_g"] = a(np.tile(g, 4).reshape(NDT, 128).T)
    w["w_wo"] = a(inp["L1_attn_Wo"])
    w["w_cent"] = a(np.asarray(inp["L1_ffn_centroids"]).T)
    w["w_eb"] = a(np.asarray(inp["L1_ffn_expert_biases"]).reshape(1, NE))
    w["w_wsu"] = a(inp["L1_ffn_Ws_up"])
    w["w_wsd"] = a(inp["L1_ffn_Ws_down"])
    w["w_wup"] = a(inp["L1_ffn_W_up"])
    w["w_wdn"] = a(inp["L1_ffn_W_down"])
    return w


def _percore_inputs(inp, w):
    x = np.asarray(inp["L0_attn_x"], dtype=np.float32)
    maps = []
    for b in range(B):
        m = dict(w)
        m["x_t"] = np.ascontiguousarray(x[b].T)
        m["bmask"] = (np.arange(CC)[None, :] <= b).astype(np.float32)
        maps.append(m)
    return maps


_TENSORS = {
    "x_t": ((D, L), F32), "xn0d": ((D, L), BF16), "g0d": ((D, L), F32),
    "x1d": ((D, L), F32), "xn2d": ((D, L), BF16), "affd": ((L, NE), F32),
    "qnd": ((D, L), F32), "knd": ((D, L), F32), "vsd": ((D, L), F32),
    "betad": ((NH, L), F32), "s2d": ((D, L), F32), "cwtd": ((NE, L), F32),
    "wtd": ((D, L), F32), "utd": ((D, L), F32),
    "outd": ((D, L), F32),
    "w_pq": ((8, D, 128), F32), "w_pk": ((D, 128), F32), "w_pv": ((D, 128), F32),
    "w_po": ((8, 128, D), F32), "w_rm": ((4, 64, 16, 16), F32),
    "w_rup": ((4, 64, 16, 64), F32), "w_wc": ((16 * D, D), F32),
    "w_wq": ((D, D), F32), "w_wk": ((D, D), F32), "w_wv": ((D, D), F32),
    "w_wb": ((D, NH), F32), "w_cq": ((D, 4), F32), "w_ck": ((D, 4), F32),
    "w_cv": ((D, 4), F32), "w_g": ((128, NDT), F32), "w_wo": ((D, D), F32),
    "w_cent": ((D, NE), F32), "w_eb": ((1, NE), F32),
    "w_wsu": ((D, FF), F32), "w_wsd": ((FF, D), F32),
    "w_wup": ((NE, D, FF), F32), "w_wdn": ((NE, FF, D), F32),
    "bmask": ((1, CC), F32),
}

def _em_init(p, dr):
    ph_init(p, dr)

def _em_ln1(p, dr):
    ph_ln(p, dr["x_t"], dr["xn0d"], name="l1n")

def _em_ln2(p, dr):
    ph_ln(p, dr["x1d"], dr["xn2d"], aff_out=dr["affd"], cent_d=dr["w_cent"], name="l2n")

def _em_ffn0(p, dr):
    ph_ffn0(p, dr)

def _em_attn0(p, dr):
    ph_attn0(p, dr)

def _em_qkv(p, dr):
    ph_qkv(p, dr)

def _em_l1core(p, dr):
    ph_l1core(p, dr)

def _em_route(p, dr):
    ph_route(p, dr)

def _em_moe(p, dr):
    ph_moe(p, dr)

_PHASES = {
    "init": (_em_init, ["w_g", "bmask", "w_eb"], []),
    "ln1": (_em_ln1, ["x_t"], ["xn0d"]),
    "ln2": (_em_ln2, ["x1d", "w_cent"], ["xn2d", "affd"]),
    "ffn0": (_em_ffn0, ["xn0d", "w_rm", "w_rup", "w_wc"], ["g0d"]),
    "attn0": (_em_attn0, ["xn0d", "x_t", "g0d", "w_pq", "w_pk", "w_pv", "w_po"], ["x1d"]),
    "qkv": (_em_qkv, ["xn2d", "w_wq", "w_wk", "w_wv", "w_wb", "w_cq", "w_ck", "w_cv"],
            ["qnd", "knd", "vsd", "betad"]),
    "l1core": (_em_l1core, ["qnd", "knd", "vsd", "betad", "bmask", "w_g", "w_wo"],
               ["s2d", "wtd", "utd"]),
    "route": (_em_route, ["affd", "w_eb"], ["cwtd"]),
    "moe": (_em_moe, ["xn2d", "cwtd", "x1d", "s2d", "w_wsu", "w_wsd", "w_wup", "w_wdn"],
            ["outd"]),
}
_PHASE_ORDER = ["ln1", "ffn0", "attn0", "ln2", "qkv", "l1core", "route", "moe"]


def build(phase_names, ext_out):
    nc = bass.Bass("TRN2", target_bir_lowering=False)
    phase_names = ["init"] + [x for x in phase_names if x != "init"]
    produced = set()
    consumed = set()
    for ph in phase_names:
        _, ins, outs = _PHASES[ph]
        consumed.update(i for i in ins if i not in produced)
        produced.update(outs)
    dr = {}
    for name in sorted(consumed):
        shape, dt = _TENSORS[name]
        dr[name] = nc.dram_tensor(name, list(shape), dt, kind="ExternalInput").ap()
    for name in sorted(produced):
        shape, dt = _TENSORS[name]
        kind = "ExternalOutput" if name in ext_out else "Internal"
        dr[name] = nc.dram_tensor(name, list(shape), dt, kind=kind).ap()
    with ExitStack() as gctx:
        p = Prog(nc, gctx)
        for k, ph in enumerate(phase_names):
            p.phase_sems = []
            with ExitStack() as pctx:
                p.pctx = pctx
                p.block = pctx.enter_context(nc.Block())
                _PHASES[ph][0](p, dr)
                ids = list(p.phase_sems)
                assert not ids or min(ids) >= p.clear_from, (ids, p.clear_from)
            p.pctx = gctx
            if k < len(phase_names) - 1:
                nc.all_engine_barrier()
                nc.gpsimd.dma_reset(range(p.clear_from, 256))
                nc.gpsimd.sem_clear(range(p.clear_from, 256))
                nc.all_engine_barrier()
    return nc, sorted(consumed)


def run_phases(phase_names, ext_out, in_maps, core_ids=None):
    nc, needed = build(phase_names, ext_out)
    maps = [{k: m[k] for k in needed if k in m} for m in in_maps]
    import os as _os
    res = run_bass_kernel_spmd(nc, maps, core_ids=core_ids or list(range(len(in_maps))),
                               trace=bool(_os.environ.get("DEVTRACE")))
    return res.results


_FULL_CHAIN = ["ln1", "ffn0", "attn0", "ln2", "qkv", "l1core", "route", "moe"]


def kernel(**inputs):
    w = _prep_weights(inputs)
    maps = _percore_inputs(inputs, w)
    nc, needed = build(_FULL_CHAIN, {"outd"})
    in_maps = [{k: m[k] for k in needed if k in m} for m in maps]
    res = run_bass_kernel_spmd(nc, in_maps, core_ids=list(range(B)))
    out = np.stack([np.asarray(r["outd"], np.float32).T for r in res.results], axis=0)
    return np.ascontiguousarray(out)


# version 08
# speedup vs baseline: 1.1060x; 1.0073x over previous
import numpy as np
import ml_dtypes
from contextlib import ExitStack

import concourse.bass as bass
import concourse.mybir as mybir
from concourse.bass_utils import run_bass_kernel_spmd

F32 = mybir.dt.float32
BF16 = mybir.dt.bfloat16
AF = mybir.ActivationFunctionType
ALU = mybir.AluOpType
AX = mybir.AxisListType

B, L, D = 8, 4096, 1024
_A0_NQT = [None]
_F0_SKIP = set()
NDT = D // 128
NCHUNK = L // 512
NH, HD = 4, 256
CC = 64
NCK = L // CC
EPS_LN = 1e-5
EPS_RMS = 1e-5
TOPK = 8
NE = 64
FF = 512


class Prog:
    def __init__(self, nc, gctx):
        self.nc = nc
        self.gctx = gctx
        self.pctx = gctx
        self.block = None
        self.bar_sem = gctx.enter_context(nc.semaphore("phase_bar"))
        self.bar = 0
        self._semid = 0
        self.G = {}
        self.PS = [gctx.enter_context(nc.psum_tensor(f"psb{i}", [128, 512], F32))
                   for i in range(7)]
        self.PSBF = gctx.enter_context(nc.psum_tensor("psbf", [128, 1024], BF16))
        self.phase_sems = []
        try:
            nc.sort_free_semaphores()
        except Exception:
            pass
        nc.all_engine_barrier()
        self._ss_sem = {}
        self._ss_cnt = {}
        for k in ("dve", "act", "pool"):
            self._ss_sem[k] = gctx.enter_context(nc.semaphore(f"ss_{k}"))
            self._ss_cnt[k] = 0
        globals_max = max([self.bar_sem.num]
                          + [s.num for s in self._ss_sem.values()])
        self.clear_from = globals_max + 1

    def edge(self, inst, eng, key=None):
        k = ("dve" if eng is self.nc.vector else
             "act" if eng is self.nc.scalar else "pool")
        inst.then_inc(self._ss_sem[k], 1)
        self._ss_cnt[k] += 1
        eng.wait_ge(self._ss_sem[k], self._ss_cnt[k])

    def sb(self, name, shape, dt=F32):
        return self.pctx.enter_context(self.nc.sbuf_tensor(name, shape, dt))

    def sbg(self, name, shape, dt=F32):
        return self.gctx.enter_context(self.nc.sbuf_tensor(name, shape, dt))

    def sem(self, name, dma=False):
        self._semid += 1
        h = self.pctx.enter_context(self.nc.semaphore(f"{name}{self._semid}"))
        self.phase_sems.append(h.num if hasattr(h, "num") else h)
        return h


def ph_init(p, dr):
    nc, blk = p.nc, p.block
    G = p.G
    G["ones"] = p.sbg("g_ones", [128, 128], F32)
    G["ident"] = p.sbg("g_ident", [128, 128], F32)
    G["ident_bf"] = p.sbg("g_identbf", [128, 128], BF16)
    G["iota"] = p.sbg("g_iota", [128, 128], F32)
    G["mlow"] = p.sbg("g_mlow", [64, 64], F32)
    G["mup"] = p.sbg("g_mup", [64, 64], F32)
    G["eps_ln"] = p.sbg("g_epsln", [128, 1], F32)
    G["eps_rms"] = p.sbg("g_epsrms", [128, 1], F32)
    G["gvec"] = p.sbg("g_gvec", [128, NDT], F32)
    G["bmask"] = p.sbg("g_bmask", [128, CC], F32)
    G["ebias"] = p.sbg("g_ebias", [128, NE], F32)
    ldg = p.sem("ini_ld", dma=True)
    lds = p.sem("ini_lds", dma=True)
    gc = p.sem("ini_gc")
    vrd = p.sem("ini_v")

    @blk.gpsimd
    def _(g):
        g.memset(G["ones"][:], 1.0)
        g.memset(G["eps_ln"][:], EPS_LN)
        g.memset(G["eps_rms"][:], EPS_RMS)
        g.iota(G["iota"][:], [[1, 128]], channel_multiplier=-1,
               allow_small_or_imprecise_dtypes=True).then_inc(gc, 1)
        g.dma_start(out=G["gvec"][:], in_=dr["w_g"]).then_inc(ldg, 16)

    @blk.vector
    def _(v):
        v.wait_ge(ldg, 16)
        v.wait_ge(gc, 1)
        v.tensor_single_scalar(G["ident"][:], G["iota"][:], 0.0, ALU.is_equal)
        v.tensor_single_scalar(G["ident_bf"][:], G["iota"][:], 0.0, ALU.is_equal)
        v.tensor_single_scalar(G["mlow"][:], G["iota"][0:64, 0:64], 0.0, ALU.is_lt)
        v.tensor_single_scalar(G["mup"][:], G["iota"][0:64, 0:64], 0.0, ALU.is_gt)
        v.engine_nop().then_inc(vrd, 1)

    bm1 = p.sb("g_bm1", [1, CC], F32)
    eb1 = p.sb("g_eb1", [1, NE], F32)
    ps_b = p.PS[0]
    tdone = p.sem("ini_t")

    @blk.sync
    def _(q):
        q.dma_start(out=bm1[:], in_=dr["bmask"]).then_inc(lds, 16)
        q.dma_start(out=eb1[:], in_=dr["w_eb"]).then_inc(lds, 16)

    @blk.tensor
    def _(t):
        t.wait_ge(ldg, 16)
        t.wait_ge(gc, 1)
        t.wait_ge(lds, 32)
        t.matmul(ps_b[:, 0:CC], G["ones"][0:1, :], bm1[:], start=True, stop=True)
        t.matmul(ps_b[:, CC:CC + NE], G["ones"][0:1, :], eb1[:], start=True, stop=True
                 ).then_inc(tdone, 1)

    @blk.vector
    def _(v):
        v.wait_ge(tdone, 1)
        v.tensor_copy(G["bmask"][:], ps_b[:, 0:CC])
        v.tensor_copy(G["ebias"][:], ps_b[:, CC:CC + NE])
        v.engine_nop().then_inc(p.bar_sem, 1)
    p.bar += 1


def ph_ln(p, src_d, dst_d, aff_out=None, cent_d=None, name="ln"):
    nc, blk = p.nc, p.block
    bar0 = p.bar
    ld2 = [p.sem(name + "ld_a"), p.sem(name + "ld_b")]
    ldc = p.sem(name + "ldc")
    vsq = p.sem(name + "vsq")
    ts1 = p.sem(name + "ts1")
    ts2 = p.sem(name + "ts2")
    vrd = p.sem(name + "vrd")
    ssd = p.sem(name + "ssd")
    vxn = p.sem(name + "vxn")
    st2 = [p.sem(name + "st_a"), p.sem(name + "st_b")]
    sa2 = [p.sem(name + "sa_a"), p.sem(name + "sa_b")]
    taf = p.sem(name + "taf")
    saf = p.sem(name + "saf")
    NC2 = NCHUNK
    do_aff = aff_out is not None

    xch = [p.sb(name + f"x{i}", [128, NDT * 512], F32) for i in range(2)]
    sq = [p.sb(name + f"sq{i}", [128, NDT * 512], F32) for i in range(2)]
    mean = [p.sb(name + f"mean{i}", [128, 512], F32) for i in range(2)]
    var = [p.sb(name + f"var{i}", [128, 512], F32) for i in range(2)]
    var2 = [p.sb(name + f"var2{i}", [128, 512], F32) for i in range(2)]
    rstd = [p.sb(name + f"rstd{i}", [128, 512], F32) for i in range(2)]
    sd = [p.sb(name + f"sd{i}", [128, 512], F32) for i in range(2)]
    xn = [p.sb(name + f"xn{i}", [128, NDT * 512], BF16) for i in range(2)]
    xnf = [p.sb(name + f"xnf{i}", [128, NDT * 512], F32) for i in range(2)] if do_aff else None
    afb = [p.sb(name + f"afb{i}", [128, 4 * 64], F32) for i in range(2)] if do_aff else None
    cent = p.sb(name + "cent", [128, NDT * 64], F32) if do_aff else None
    ps_s1 = [p.PS[0], p.PS[2]]
    ps_s2 = [p.PS[1], p.PS[3]]
    ps_af = [p.PS[4], p.PS[5]] if do_aff else None

    @blk.sync
    def _(q):
        q.wait_ge(p.bar_sem, bar0)
        if do_aff:
            for dt in range(NDT):
                q.dma_start(out=cent[:, dt * 64:(dt + 1) * 64],
                            in_=cent_d[dt * 128:(dt + 1) * 128, :]).then_inc(ldc, 16)
        for c in range(NC2):
            A = c % 2
            if c >= 2:
                q.wait_ge(vxn, c - 1)
                q.wait_ge(ts1, c - 1)
            for dt in range(NDT):
                q.dma_start(out=xch[A][:, dt * 512:(dt + 1) * 512],
                            in_=src_d[dt * 128:(dt + 1) * 128, c * 512:(c + 1) * 512]
                            ).then_inc(ld2[A], 16)

    ones = p.G["ones"]

    def ld_target(c):
        return 16 * NDT * (c // 2 + 1)

    @blk.vector
    def _(v):
        v.wait_ge(p.bar_sem, bar0)
        for c in range(NC2):
            A = c % 2
            v.wait_ge(ld2[A], ld_target(c))
            if c >= 2:
                v.wait_ge(ts2, c - 1)
            for dt in range(NDT):
                sl = slice(dt * 512, (dt + 1) * 512)
                v.tensor_mul(sq[A][:, sl], xch[A][:, sl], xch[A][:, sl])
            v.engine_nop().then_inc(vsq, 1)
            if c >= 2:
                v.wait_ge(ssd, c - 1)
            v.wait_ge(ts2, c + 1)
            p.edge(v.tensor_scalar_mul(mean[A][:], ps_s1[A][:, :], 1.0 / D), v, name + "v1")
            p.edge(v.tensor_mul(var2[A][:], mean[A][:], mean[A][:]), v, name + "v2")
            v.scalar_tensor_tensor(var[A][:], ps_s2[A][:, :], 1.0 / D, var2[A][:],
                                   ALU.mult, ALU.subtract)
            v.engine_nop().then_inc(vrd, 1)
            v.wait_ge(ssd, c + 1)
            p.edge(v.reciprocal(rstd[A][:], sd[A][:]), v, name + "v3")
            per_chunk_st = NDT + (4 if do_aff else 0)
            if c >= 2:
                v.wait_ge(st2[A], 16 * NDT * (c // 2))
                if do_aff:
                    v.wait_ge(taf, c - 1)
            for dt in range(NDT):
                sl = slice(dt * 512, (dt + 1) * 512)
                ii = v.tensor_sub(sq[A][:, sl], xch[A][:, sl], mean[A][:])
            p.edge(ii, v, name + "v4")
            for dt in range(NDT):
                sl = slice(dt * 512, (dt + 1) * 512)
                v.tensor_mul(xn[A][:, sl], sq[A][:, sl], rstd[A][:])
                if do_aff:
                    v.tensor_mul(xnf[A][:, sl], sq[A][:, sl], rstd[A][:])
            v.engine_nop().then_inc(vxn, 1)

    @blk.scalar
    def _(s):
        s.wait_ge(p.bar_sem, bar0)
        for c in range(NC2):
            A = c % 2
            s.wait_ge(vrd, c + 1)
            s.activation(sd[A][:], var[A][:], AF.Sqrt, bias=p.G["eps_ln"][:, 0:1], scale=1.0
                         ).then_inc(ssd, 1)
            if do_aff:
                s.wait_ge(taf, c + 1)
                if c >= 2:
                    s.wait_ge(sa2[A], 16 * 4 * (c // 2))
                s.copy(afb[A][:], ps_af[A][:, 0:256]).then_inc(saf, 1)

    @blk.tensor
    def _(t):
        t.wait_ge(p.bar_sem, bar0)
        for c in range(NC2):
            A = c % 2
            t.wait_ge(ld2[A], ld_target(c))
            if do_aff and c == 0:
                t.wait_ge(ldc, 16 * NDT)
            if c >= 2:
                t.wait_ge(vrd, c - 1)
            for dt in range(NDT):
                mm = t.matmul(ps_s1[A][:, :], ones[:], xch[A][:, dt * 512:(dt + 1) * 512],
                              start=(dt == 0), stop=(dt == NDT - 1))
            mm.then_inc(ts1, 1)
            t.wait_ge(vsq, c + 1)
            for dt in range(NDT):
                mm = t.matmul(ps_s2[A][:, :], ones[:], sq[A][:, dt * 512:(dt + 1) * 512],
                              start=(dt == 0), stop=(dt == NDT - 1))
            mm.then_inc(ts2, 1)
            if do_aff:
                t.wait_ge(vxn, c + 1)
                if c >= 2:
                    t.wait_ge(saf, c - 1)
                for tt in range(4):
                    for dt in range(NDT):
                        mm = t.matmul(ps_af[A][:, tt * 64:tt * 64 + 64],
                                      xnf[A][:, dt * 512 + tt * 128: dt * 512 + tt * 128 + 128],
                                      cent[:, dt * 64:(dt + 1) * 64],
                                      start=(dt == 0), stop=(dt == NDT - 1))
                mm.then_inc(taf, 1)

    @blk.gpsimd
    def _(g):
        for c in range(NC2):
            A = c % 2
            g.wait_ge(vxn, c + 1)
            for dt in range(NDT):
                g.dma_start(out=dst_d[dt * 128:(dt + 1) * 128, c * 512:(c + 1) * 512],
                            in_=xn[A][:, dt * 512:(dt + 1) * 512]).then_inc(st2[A], 16)
            if do_aff:
                g.wait_ge(saf, c + 1)
                for tt in range(4):
                    g.dma_start(out=aff_out[(c * 4 + tt) * 128:(c * 4 + tt + 1) * 128, :],
                                in_=afb[A][:, tt * 64:(tt + 1) * 64]).then_inc(sa2[A], 16)
        g.wait_ge(st2[0], 16 * NDT * (NC2 // 2))
        g.wait_ge(st2[1], 16 * NDT * (NC2 // 2))
        if do_aff:
            g.wait_ge(sa2[0], 16 * 4 * (NC2 // 2))
            g.wait_ge(sa2[1], 16 * 4 * (NC2 // 2))
        g.engine_nop().then_inc(p.bar_sem, 1)
    p.bar += 1


def ph_ffn0(p, dr):
    nc, blk = p.nc, p.block
    bar0 = p.bar
    xn0d, g0d = dr["xn0d"], dr["g0d"]
    ldm = p.sem("f0ldm", dma=True)
    tmc = p.sem("f0tmc")
    vmc = p.sem("f0vmc")
    gbd = p.sem("f0gbd")
    bdq = p.sem("f0bdq", dma=True)
    wc4 = [p.sem(f"f0wcq{j}") for j in range(4)]
    xl2 = [p.sem("f0xld_a", dma=True), p.sem("f0xld_b", dma=True)]
    tz = p.sem("f0tz")
    sh = p.sem("f0sh")
    tco = p.sem("f0tco")
    vgo = p.sem("f0vgo")
    gs2 = [p.sem("f0gst_a", dma=True), p.sem("f0gst_b", dma=True)]
    sl2 = [p.sem("f0slq_a", dma=True), p.sem("f0slq_b", dma=True)]
    svn = p.sem("f0svn")
    ss2 = [p.sem("f0ssq_a", dma=True), p.sem("f0ssq_b", dma=True)]
    NQ = 4
    NSTEP = NQ * NCHUNK
    RE = "p (h q8 pgx s2 k) -> p h q8 pgx s2 k"

    BD = p.sb("f0bd", [128, 16384], BF16)

    with ExitStack() as sub:
        rm_sb = sub.enter_context(nc.sbuf_tensor("f0rm", [16, 4096], F32))
        rup_sb = sub.enter_context(nc.sbuf_tensor("f0rup", [16, 16384], F32))
        m_all = sub.enter_context(nc.sbuf_tensor("f0mall", [16, 16384], BF16))

        @blk.sync
        def _(q):
            q.wait_ge(p.bar_sem, bar0)
            q.dma_start(out=rm_sb[:],
                        in_=dr["w_rm"].rearrange("h r j i -> j (h r) i")
                        ).then_inc(ldm, 16)
            q.dma_start(out=rup_sb[:],
                        in_=dr["w_rup"].rearrange("h r j k -> j (h r) k")
                        ).then_inc(ldm, 16)

        @blk.tensor
        def _(t):
            t.wait_ge(p.bar_sem, bar0)
            t.wait_ge(ldm, 32)
            for grp in range(32):
                if grp >= 2:
                    t.wait_ge(vmc, grp - 1)
                for u in range(8):
                    hr = grp * 8 + u
                    mm = t.matmul(p.PS[grp % 2][0:16, u * 64:(u + 1) * 64],
                                  rm_sb[:, hr * 16:(hr + 1) * 16],
                                  rup_sb[:, hr * 64:(hr + 1) * 64],
                                  start=True, stop=True)
                mm.then_inc(tmc, 1)

        @blk.vector
        def _(v):
            v.wait_ge(p.bar_sem, bar0)
            for grp in range(32):
                v.wait_ge(tmc, grp + 1)
                v.tensor_copy(m_all[:, grp * 512:(grp + 1) * 512],
                              p.PS[grp % 2][0:16, :]).then_inc(vmc, 1)

        @blk.gpsimd
        def _(g):
            g.wait_ge(p.bar_sem, bar0)
            g.memset(BD[:], 0.0).then_inc(gbd, 1)
            g.wait_ge(gbd, 1)
            g.wait_ge(vmc, 32)
            for pg in range(4):
                for s in range(2):
                    base = 32 * pg + 16 * s
                    dst = BD[base:base + 16, :].rearrange(
                        RE, h=4, q8=8, pgx=4, s2=2, k=64)[:, :, :, pg, s, :]
                    srcv = m_all[0:16, :].rearrange(
                        RE, h=4, q8=8, pgx=4, s2=2, k=64)[:, :, :, pg, s, :]
                    g.dma_start(out=dst, in_=srcv).then_inc(bdq, 16)
            g.wait_ge(bdq, 16 * 8)
            g.engine_nop().then_inc(gbd, 1)
        sub.close()

    wcb = p.sb("f0wcb", [128, 32 * 1024], BF16)
    hb = [p.sb(f"f0h{i}", [128, 16384], BF16) for i in range(2)]
    xnc = [p.sb(f"f0x{i}", [128, NDT * 512], BF16) for i in range(2)]
    gtmp = p.sb("f0gt", [128, 4096], F32)

    @blk.sync
    def _(q):
        q.wait_ge(gbd, 2)
        for s in range(NSTEP):
            qq, c = divmod(s, NCHUNK)
            A = s % 2
            if s >= 2:
                q.wait_ge(tz, (s - 2) * 32 + 32)
            for dt in range(NDT):
                q.dma_start(out=xnc[A][:, dt * 512:(dt + 1) * 512],
                            in_=xn0d[dt * 128:(dt + 1) * 128, c * 512:(c + 1) * 512]
                            ).then_inc(xl2[A], 16)

    @blk.gpsimd
    def _(g):
        for qq in range(NQ):
            if qq >= 1:
                g.wait_ge(vgo, 16 * qq)
            for piece in range(4):
                g.dma_start(
                    out=wcb[:, piece * 8 * 1024:(piece + 1) * 8 * 1024],
                    in_=dr["w_wc"].rearrange("(ftg pp) d -> pp ftg d", pp=128)
                        [:, qq * 32 + piece * 8: qq * 32 + (piece + 1) * 8, :]
                    ).then_inc(wc4[piece], 16)
            for c in range(NCHUNK):
                s = qq * NCHUNK + c
                for hf in range(2):
                    g.wait_ge(vgo, 2 * s + hf + 1)
                    for i in range(4):
                        dt = hf * 4 + i
                        g.dma_start(
                            out=g0d[dt * 128:(dt + 1) * 128, c * 512:(c + 1) * 512],
                            in_=gtmp[:, hf * 2048 + i * 512: hf * 2048 + (i + 1) * 512],
                            accum_op=(ALU.add if (qq > 0 and 'acc' not in _F0_SKIP) else ALU.bypass)
                            ).then_inc(gs2[hf], 16)

    @blk.tensor
    def _(t):
        t.wait_ge(gbd, 2)
        for s in range(NSTEP):
            qq, c = divmod(s, NCHUNK)
            A = s % 2
            t.wait_ge(xl2[s % 2], 16 * NDT * (s // 2 + 1))
            t.wait_ge(vgo, 2 * s)
            for ft in range(32):
                gz = s * 32 + ft
                ftg = qq * 32 + ft
                dt_in = ft // 4
                if gz >= 2:
                    t.wait_ge(sh, gz - 1)
                t.matmul(p.PS[4 + gz % 2][:, :], BD[:, ftg * 128:(ftg + 1) * 128],
                         xnc[A][:, dt_in * 512:(dt_in + 1) * 512],
                         start=True, stop=True).then_inc(tz, 1)
                if ft >= 1:
                    fprev = ft - 1
                    t.wait_ge(sh, s * 32 + fprev + 1)
                    if fprev % 8 == 0:
                        t.wait_ge(wc4[fprev // 8], 16 * (qq + 1))
                    for dto in range(4):
                        mm = t.matmul(p.PS[dto][:, :],
                                      wcb[:, fprev * 1024 + dto * 128: fprev * 1024 + (dto + 1) * 128],
                                      hb[A][:, fprev * 512:(fprev + 1) * 512],
                                      start=(fprev == 0), stop=(fprev == 31))
            t.wait_ge(sh, s * 32 + 32)
            for dto in range(4):
                mm = t.matmul(p.PS[dto][:, :],
                              wcb[:, 31 * 1024 + dto * 128: 31 * 1024 + (dto + 1) * 128],
                              hb[A][:, 31 * 512: 32 * 512],
                              start=False, stop=True)
            mm.then_inc(tco, 1)
            t.wait_ge(vgo, 2 * s + 1)
            for ft in range(32):
                for dto in range(4):
                    mm = t.matmul(p.PS[dto][:, :],
                                  wcb[:, ft * 1024 + (4 + dto) * 128: ft * 1024 + (5 + dto) * 128],
                                  hb[A][:, ft * 512:(ft + 1) * 512],
                                  start=(ft == 0), stop=(ft == 31))
            mm.then_inc(tco, 1)

    @blk.scalar
    def _(s_):
        s_.wait_ge(p.bar_sem, bar0)
        for s in range(NSTEP):
            A = s % 2
            if s >= 2:
                s_.wait_ge(tco, 2 * (s - 1))
            for ft in range(32):
                gz = s * 32 + ft
                s_.wait_ge(tz, gz + 1)
                s_.activation(hb[A][:, ft * 512:(ft + 1) * 512],
                              p.PS[4 + gz % 2][:, :], AF.Gelu).then_inc(sh, 1)

    @blk.vector
    def _(v):
        for s in range(NSTEP):
            for hf in range(2):
                v.wait_ge(tco, 2 * s + hf + 1)
                if s >= 1:
                    v.wait_ge(gs2[hf], 16 * 4 * s)
                for i in range(4):
                    v.tensor_copy(gtmp[:, hf * 2048 + i * 512: hf * 2048 + (i + 1) * 512],
                                  p.PS[i][:, :])
                v.engine_nop().then_inc(vgo, 1)

    GSTOT = 16 * 4 * NSTEP
    if "silu" in _F0_SKIP:
        @blk.gpsimd
        def _(g):
            g.wait_ge(gs2[0], GSTOT)
            g.wait_ge(gs2[1], GSTOT)
            g.engine_nop().then_inc(p.bar_sem, 1)
        p.bar += 1
        return

    @blk.sync
    def _(q):
        q.wait_ge(gs2[0], GSTOT)
        q.wait_ge(gs2[1], GSTOT)
        for u in range(16):
            c, hf = divmod(u, 2)
            if u >= 2:
                q.wait_ge(ss2[hf], 16 * 4 * (u // 2))
            for i in range(4):
                dt = hf * 4 + i
                q.dma_start(out=gtmp[:, hf * 2048 + i * 512: hf * 2048 + (i + 1) * 512],
                            in_=g0d[dt * 128:(dt + 1) * 128, c * 512:(c + 1) * 512]
                            ).then_inc(sl2[hf], 16)

    @blk.scalar
    def _(s_):
        for u in range(16):
            c, hf = divmod(u, 2)
            s_.wait_ge(sl2[hf], 16 * 4 * (u // 2 + 1))
            s_.activation(gtmp[:, hf * 2048: hf * 2048 + 2048],
                          gtmp[:, hf * 2048: hf * 2048 + 2048], AF.Silu).then_inc(svn, 1)

    @blk.gpsimd
    def _(g):
        for u in range(16):
            c, hf = divmod(u, 2)
            g.wait_ge(svn, u + 1)
            for i in range(4):
                dt = hf * 4 + i
                g.dma_start(out=g0d[dt * 128:(dt + 1) * 128, c * 512:(c + 1) * 512],
                            in_=gtmp[:, hf * 2048 + i * 512: hf * 2048 + (i + 1) * 512]
                            ).then_inc(ss2[hf], 16)
        g.wait_ge(ss2[0], 16 * 4 * 8)
        g.wait_ge(ss2[1], 16 * 4 * 8)
        g.engine_nop().then_inc(p.bar_sem, 1)
    p.bar += 1


def ph_attn0(p, dr):
    nc, blk = p.nc, p.block
    G = p.G
    bar0 = p.bar
    xn0d, x_t, g0d, x1d = dr["xn0d"], dr["x_t"], dr["g0d"], dr["x1d"]
    pld = p.sem("a0pld", dma=True)
    xl2 = [p.sem("a0xl_a", dma=True), p.sem("a0xl_b", dma=True)]
    tpj = p.sem("a0tpj")
    vpj = p.sem("a0vpj")
    tvt = p.sem("a0tvt")
    vvt = p.sem("a0vvt")
    tL = p.sem("a0tL")
    sE = p.sem("a0sE")
    tP = p.sem("a0tP")
    vS = p.sem("a0vS")
    vR = p.sem("a0vR")
    vV = p.sem("a0vV")
    tT = p.sem("a0tT")
    vO = p.sem("a0vO")
    tY = p.sem("a0tY")
    vC = p.sem("a0vC")
    xg2 = [p.sem("a0xg_a", dma=True), p.sem("a0xg_b", dma=True)]
    so2 = [p.sem("a0so_a", dma=True), p.sem("a0so_b", dma=True)]

    NQT = (L // 128) if _A0_NQT[0] is None else _A0_NQT[0]
    pq_sb = p.sb("a0pq", [128, 8 * NDT * 128], BF16)
    pk_sb = p.sb("a0pk", [128, NDT * 128], BF16)
    pv_sb = p.sb("a0pv", [128, NDT * 128], BF16)
    po_sb = p.sb("a0po", [128, 8 * 1024], BF16)
    xnc = [p.sb(f"a0x{i}", [128, NDT * 512], BF16) for i in range(2)]
    Qs = p.sb("a0Q", [128, 8 * L], BF16)
    Ks = p.sb("a0K", [128, L], BF16)
    Vs = p.sb("a0V", [128, L], BF16)
    vtok = p.sb("a0vt", [128, 32 * 132], BF16)
    PT = [p.sb(f"a0pt{i}", [128, 32 * 128], BF16) for i in range(2)]
    rs = p.sb("a0rs", [128, 1], F32)
    onb = p.sb("a0onb", [128, 128], BF16)
    otb = [p.sb(f"a0ot{i}", [128, 128], BF16) for i in range(2)]
    xg = [p.sb(f"a0xg{i}", [128, 2 * NDT * 128], F32) for i in range(2)]
    outb = [p.sb(f"a0ob{i}", [128, NDT * 128], F32) for i in range(2)]

    @blk.gpsimd
    def _(g):
        g.wait_ge(p.bar_sem, bar0)
        g.dma_start(out=pq_sb[:],
                    in_=dr["w_pq"].rearrange("h (dtp pp) k -> pp (h dtp) k", pp=128)
                    ).then_inc(pld, 16)
        g.dma_start(out=pk_sb[:],
                    in_=dr["w_pk"].rearrange("(dtp pp) k -> pp dtp k", pp=128)
                    ).then_inc(pld, 16)
        g.dma_start(out=pv_sb[:],
                    in_=dr["w_pv"].rearrange("(dtp pp) k -> pp dtp k", pp=128)
                    ).then_inc(pld, 16)
        g.dma_start(out=po_sb[:],
                    in_=dr["w_po"].rearrange("h v dd -> v h dd")).then_inc(pld, 16)

    @blk.sync
    def _(q):
        q.wait_ge(p.bar_sem, bar0)
        for c in range(NCHUNK):
            A = c % 2
            if c >= 2:
                q.wait_ge(tpj, 10 * (c - 1))
            for dt in range(NDT):
                q.dma_start(out=xnc[A][:, dt * 512:(dt + 1) * 512],
                            in_=xn0d[dt * 128:(dt + 1) * 128, c * 512:(c + 1) * 512]
                            ).then_inc(xl2[A], 16)

    @blk.tensor
    def _(t):
        t.wait_ge(p.bar_sem, bar0)
        t.wait_ge(pld, 64)
        ng = 0
        for c in range(NCHUNK):
            A = c % 2
            t.wait_ge(xl2[A], 16 * NDT * (c // 2 + 1))
            for grp in range(10):
                if ng >= 2:
                    t.wait_ge(vpj, ng - 1)
                for dt in range(NDT):
                    if grp == 0:
                        lhs = pk_sb[:, dt * 128:(dt + 1) * 128]
                    elif grp == 1:
                        lhs = pv_sb[:, dt * 128:(dt + 1) * 128]
                    else:
                        h = grp - 2
                        lhs = pq_sb[:, (h * NDT + dt) * 128:(h * NDT + dt + 1) * 128]
                    mm = t.matmul(p.PS[ng % 2][:, :], lhs,
                                  xnc[A][:, dt * 512:(dt + 1) * 512],
                                  start=(dt == 0), stop=(dt == NDT - 1))
                mm.then_inc(tpj, 1)
                ng += 1

    @blk.vector
    def _(v):
        v.wait_ge(p.bar_sem, bar0)
        ng = 0
        for c in range(NCHUNK):
            sl = slice(c * 512, (c + 1) * 512)
            for grp in range(10):
                v.wait_ge(tpj, ng + 1)
                if grp == 0:
                    v.tensor_copy(Ks[:, sl], p.PS[ng % 2][:, :])
                elif grp == 1:
                    v.tensor_copy(Vs[:, sl], p.PS[ng % 2][:, :])
                else:
                    h = grp - 2
                    v.tensor_copy(Qs[:, h * L + c * 512: h * L + (c + 1) * 512],
                                  p.PS[ng % 2][:, :])
                v.engine_nop().then_inc(vpj, 1)
                ng += 1

    @blk.vector
    def _(v):
        v.memset(vtok[:], 1.0)
        v.engine_nop().then_inc(vvt, 1)

    @blk.tensor
    def _(t):
        t.wait_ge(vpj, 80)
        t.wait_ge(vvt, 1)
        for mt in range(32):
            if mt >= 1:
                t.wait_ge(vvt, mt + 1)
            t.transpose(p.PSBF[:, (mt % 2) * 128:(mt % 2) * 128 + 128],
                        Vs[:, mt * 128:(mt + 1) * 128],
                        G["ident_bf"][:]).then_inc(tvt, 1)

    @blk.vector
    def _(v):
        for mt in range(32):
            v.wait_ge(tvt, mt + 1)
            v.tensor_copy(vtok[:, mt * 132: mt * 132 + 128],
                          p.PSBF[:, (mt % 2) * 128:(mt % 2) * 128 + 128])
            v.engine_nop().then_inc(vvt, 1)

    @blk.sync
    def _(q):
        for qt in range(NQT):
            A = qt % 2
            if qt >= 2:
                q.wait_ge(vC, qt - 1)
            for dt in range(NDT):
                q.dma_start(out=xg[A][:, dt * 128:(dt + 1) * 128],
                            in_=x_t[dt * 128:(dt + 1) * 128, qt * 128:(qt + 1) * 128]
                            ).then_inc(xg2[A], 16)
            for dt in range(NDT):
                q.dma_start(out=xg[A][:, (NDT + dt) * 128:(NDT + dt + 1) * 128],
                            in_=g0d[dt * 128:(dt + 1) * 128, qt * 128:(qt + 1) * 128]
                            ).then_inc(xg2[A], 16)

    nL = [0]; nE = [0]; nP = [0]; nS = [0]; nT = [0]; nO = [0]; nY = [0]

    @blk.tensor
    def _(t):
        t.wait_ge(vvt, 33)
        pend = [None]

        def emit_tr(t, idx):
            t.wait_ge(vS, idx + 1)
            if idx >= 2:
                t.wait_ge(vO, idx - 1)
            if idx % 2 == 0:
                t.transpose(p.PSBF[:, 256:384], onb[:],
                            G["ident_bf"][:]).then_inc(tT, 1)
            else:
                t.matmul(p.PS[3][:, 0:128], onb[:], G["ident_bf"][:],
                         start=True, stop=True).then_inc(tT, 1)

        def emit_y(t, idx):
            qtp, hq = divmod(idx, 8)
            hpp = hq % 2
            t.wait_ge(vO, idx + 1)
            if hq == 0 and qtp >= 1:
                t.wait_ge(vC, qtp)
            for dt in range(NDT):
                bank = 5 + dt // 4
                col = (dt % 4) * 128
                mm = t.matmul(p.PS[bank][:, col:col + 128],
                              po_sb[:, hq * 1024 + dt * 128: hq * 1024 + (dt + 1) * 128],
                              otb[hpp][:],
                              start=(hq == 0 and dt % 4 == 0),
                              stop=(hq == 7 and dt % 4 == 3))
            mm.then_inc(tY, 1)

        for qt in range(NQT):
            for h in range(8):
                hp = h % 2
                idx = qt * 8 + h
                for mt in range(32):
                    slot = nL[0] % 12
                    bank = slot // 4
                    col = (slot % 4) * 128
                    if slot % 4 == 0 and nL[0] >= 9:
                        t.wait_ge(sE, (nL[0] + 3 - 12) // 4 + 1)
                    t.matmul(p.PS[bank][:, col:col + 128],
                             Ks[:, mt * 128:(mt + 1) * 128],
                             Qs[:, h * L + qt * 128: h * L + (qt + 1) * 128],
                             start=True, stop=True).then_inc(tL, 1)
                    nL[0] += 1
                if pend[0] is not None:
                    emit_tr(t, pend[0])
                t.wait_ge(vS, idx)
                for mt in range(32):
                    if mt % 4 == 0:
                        t.wait_ge(sE, nE[0] + mt // 4 + 1)
                    mm = t.matmul(p.PS[4][:, 0:129],
                                  PT[hp][:, mt * 128:(mt + 1) * 128],
                                  vtok[:, mt * 132: mt * 132 + 129],
                                  start=(mt == 0), stop=(mt == 31))
                mm.then_inc(tP, 1)
                nE[0] += 8
                if pend[0] is not None:
                    emit_y(t, pend[0])
                pend[0] = idx
        emit_tr(t, pend[0])
        emit_y(t, pend[0])

    @blk.scalar
    def _(s_):
        cnt = 0
        for qt in range(NQT):
            for h in range(8):
                hp = h % 2
                if cnt >= 2:
                    s_.wait_ge(tP, cnt - 1)
                for b4 in range(8):
                    s_.wait_ge(tL, cnt * 32 + (b4 + 1) * 4)
                    bank = (cnt * 8 + b4) % 3
                    s_.activation(PT[hp][:, b4 * 512:(b4 + 1) * 512],
                                  p.PS[bank][:, :], AF.Exp).then_inc(sE, 1)
                cnt += 1

    @blk.vector
    def _(v):
        for qt in range(NQT):
            A = qt % 2
            for h in range(8):
                hp = h % 2
                v.wait_ge(tP, qt * 8 + h + 1)
                v.reciprocal(rs[:], p.PS[4][:, 128:129]).then_inc(vR, 1)
                v.wait_ge(vR, qt * 8 + h + 1)
                v.tensor_scalar_mul(onb[:], p.PS[4][:, 0:128], rs[:, 0:1])
                v.engine_nop().then_inc(vS, 1)
                v.wait_ge(tT, qt * 8 + h + 1)
                idx = qt * 8 + h
                if idx % 2 == 0:
                    v.tensor_copy(otb[hp][:], p.PSBF[:, 256:384])
                else:
                    v.tensor_copy(otb[hp][:], p.PS[3][:, 0:128])
                v.engine_nop().then_inc(vO, 1)
            v.wait_ge(tY, (qt + 1) * 8)
            v.wait_ge(xg2[A], 16 * 2 * NDT * (qt // 2 + 1))
            if qt >= 2:
                v.wait_ge(so2[A], 16 * NDT * (qt // 2))
            for dt in range(NDT):
                bank = 5 + dt // 4
                col = (dt % 4) * 128
                v.scalar_tensor_tensor(outb[A][:, dt * 128:(dt + 1) * 128],
                                       p.PS[bank][:, col:col + 128], 1.0,
                                       xg[A][:, (NDT + dt) * 128:(NDT + dt + 1) * 128],
                                       ALU.mult, ALU.mult)
            v.engine_nop().then_inc(vV, 1)
            v.wait_ge(vV, qt + 1)
            for dt in range(NDT):
                v.tensor_add(outb[A][:, dt * 128:(dt + 1) * 128],
                             outb[A][:, dt * 128:(dt + 1) * 128],
                             xg[A][:, dt * 128:(dt + 1) * 128])
            v.engine_nop().then_inc(vC, 1)

    @blk.gpsimd
    def _(g):
        for qt in range(NQT):
            A = qt % 2
            g.wait_ge(vC, qt + 1)
            for dt in range(NDT):
                g.dma_start(out=x1d[dt * 128:(dt + 1) * 128, qt * 128:(qt + 1) * 128],
                            in_=outb[A][:, dt * 128:(dt + 1) * 128]).then_inc(so2[A], 16)
        g.wait_ge(so2[0], 16 * NDT * (NQT // 2))
        g.wait_ge(so2[1], 16 * NDT * (NQT // 2))
        g.engine_nop().then_inc(p.bar_sem, 1)
    p.bar += 1


def ph_qkv(p, dr):
    nc, blk = p.nc, p.block
    G = p.G
    bar0 = p.bar
    xn2d = dr["xn2d"]
    outs = {"q": dr["qnd"], "k": dr["knd"], "v": dr["vsd"]}
    wsrc = {"q": dr["w_wq"], "k": dr["w_wk"], "v": dr["w_wv"]}
    csrc = {"q": dr["w_cq"], "k": dr["w_ck"], "v": dr["w_cv"]}
    xlq = p.sem("qkxl", dma=True)
    wl = p.sem("qkwl", dma=True)
    wlb = p.sem("qkwlb", dma=True)
    cl = p.sem("qkcl", dma=True)
    tpr = p.sem("qktpr")
    vev = p.sem("qkvev")
    vcv = p.sem("qkvcv")
    vcb2 = p.sem("qkvcb2")
    ssl = p.sem("qkssl")
    vsq = p.sem("qkvsq")
    tsq = p.sem("qktsq")
    ssr = p.sem("qkssr")
    vsc = p.sem("qkvsc")
    st2 = [p.sem("qkst_a", dma=True), p.sem("qkst_b", dma=True)]
    tbe = p.sem("qktbe")
    sbe = p.sem("qksbe")

    xns = p.sb("qkxn", [128, NDT * L], BF16)
    wcur = p.sb("qkw", [128, NDT * 1024], BF16)
    cbuf = [p.sb(f"qkcb{i}", [128, 4099], F32) for i in range(2)]
    cnv = [p.sb(f"qkcv{i}", [128, L], F32) for i in range(2)]
    qh = p.sb("qkqh", [128, 2 * L], F32)
    cvw = p.sb("qkcvw", [128, 3 * NDT * 4], F32)
    wbsb = p.sb("qkwb", [128, NDT * 4], BF16)
    betab = p.sb("qkbeta", [4, L], F32)
    sdq = p.sb("qksdq", [128, 512], F32)
    rno = p.sb("qkrno", [128, 512], F32)

    PLIST = ("q", "k", "v")
    SLOTS = [("q", pr, 2) for pr in range(4)] + [("k", pr, 2) for pr in range(4)] \
            + [("v", dto, 1) for dto in range(NDT)]

    def st_total(upto, par):
        return 16 * sum(nd for j, (_, _, nd) in enumerate(SLOTS)
                        if j <= upto and j % 2 == par)

    def slot_of(pi, dto):
        P = PLIST[pi]
        if P == "v":
            return 8 + dto
        return pi * 4 + dto // 2

    @blk.sync
    def _(q):
        q.wait_ge(p.bar_sem, bar0)
        for dt in range(NDT):
            q.dma_start(out=xns[:, dt * L:(dt + 1) * L],
                        in_=xn2d[dt * 128:(dt + 1) * 128, :]).then_inc(xlq, 16)
        for pi, P in enumerate(PLIST):
            q.dma_start(out=cvw[:, pi * 32:(pi + 1) * 32],
                        in_=csrc[P].rearrange("(dtp pp) k4 -> pp dtp k4", pp=128)
                        ).then_inc(cl, 16)

    @blk.gpsimd
    def _(g):
        g.wait_ge(p.bar_sem, bar0)
        g.memset(cbuf[0][:, 0:3], 0.0)
        g.memset(cbuf[1][:, 0:3], 0.0)
        g.engine_nop().then_inc(vcv, 1)
        g.dma_start(out=wbsb[:], in_=dr["w_wb"].rearrange("(dtp pp) e -> pp dtp e", pp=128)
                    ).then_inc(wlb, 16)
        g.dma_start(out=wcur[:],
                    in_=wsrc["q"].rearrange("(dtp pp) dd -> pp dtp dd", pp=128)
                    ).then_inc(wl, 16)

    @blk.tensor
    def _(t):
        t.wait_ge(p.bar_sem, bar0)
        t.wait_ge(xlq, 16 * NDT)
        t.wait_ge(wlb, 16)
        for c in range(8):
            if c >= 1:
                t.wait_ge(sbe, c)
            for dt in range(NDT):
                mm = t.matmul(p.PS[2][0:4, :], wbsb[:, dt * 4:(dt + 1) * 4],
                              xns[:, dt * L + c * 512: dt * L + (c + 1) * 512],
                              start=(dt == 0), stop=(dt == NDT - 1))
            mm.then_inc(tbe, 1)
        ng = 0
        npair = 0
        for pi, P in enumerate(PLIST):
            t.wait_ge(wl, 16 * (pi + 1))
            for dto in range(NDT):
                for c in range(8):
                    if ng >= 2:
                        t.wait_ge(vev, ng - 1)
                    for dt in range(NDT):
                        mm = t.matmul(p.PS[ng % 2][:, :],
                                      wcur[:, dt * 1024 + dto * 128: dt * 1024 + (dto + 1) * 128],
                                      xns[:, dt * L + c * 512: dt * L + (c + 1) * 512],
                                      start=(dt == 0), stop=(dt == NDT - 1))
                    mm.then_inc(tpr, 1)
                    ng += 1
                if P != "v" and dto % 2 == 1:
                    t.wait_ge(vsq, npair + 1)
                    for c in range(8):
                        if npair * 8 + c >= 1:
                            t.wait_ge(ssr, npair * 8 + c)
                        for u in range(2):
                            mm = t.matmul(p.PS[3][:, :], G["ones"][:],
                                          cnv[u][:, c * 512:(c + 1) * 512],
                                          start=(u == 0), stop=(u == 1))
                        mm.then_inc(tsq, 1)
                    npair += 1

    @blk.vector
    def _(v):
        v.wait_ge(p.bar_sem, bar0)
        v.wait_ge(cl, 48)
        ng = 0
        npair = 0
        for pi, P in enumerate(PLIST):
            for dto in range(NDT):
                j = pi * NDT + dto
                A = dto % 2
                if ng == 0:
                    v.wait_ge(vcv, 1)
                for c in range(8):
                    v.wait_ge(tpr, ng + 1)
                    ii = v.tensor_copy(cbuf[A][:, 3 + c * 512: 3 + (c + 1) * 512],
                                       p.PS[ng % 2][:, :])
                    v.engine_nop().then_inc(vev, 1)
                    ng += 1
                p.edge(ii, v, "qkcb")
                if j >= 1:
                    v.wait_ge(ssl, j)
                if npair >= 1:
                    v.wait_ge(tsq, 8 * npair)
                wt = lambda k: cvw[:, (pi * 8 + dto) * 4 + k: (pi * 8 + dto) * 4 + k + 1]
                other, me = cnv[1 - A], cnv[A]
                p.edge(v.tensor_scalar_mul(other[:], cbuf[A][:, 0:L], wt(0)), v, "qkc0")
                p.edge(v.scalar_tensor_tensor(me[:], cbuf[A][:, 1:1 + L], wt(1), other[:],
                                              ALU.mult, ALU.add), v, "qkc1")
                p.edge(v.scalar_tensor_tensor(other[:], cbuf[A][:, 2:2 + L], wt(2), me[:],
                                              ALU.mult, ALU.add), v, "qkc2")
                v.scalar_tensor_tensor(me[:], cbuf[A][:, 3:3 + L], wt(3), other[:],
                                       ALU.mult, ALU.add).then_inc(vcv, 1)
                if P != "v" and dto % 2 == 1:
                    v.wait_ge(ssl, j + 1)
                    v.tensor_mul(cnv[0][:], qh[:, 0:L], qh[:, 0:L])
                    v.tensor_mul(cnv[1][:], qh[:, L:2 * L], qh[:, L:2 * L]
                                 ).then_inc(vsq, 1)
                    for c in range(8):
                        v.wait_ge(ssr, npair * 8 + c + 1)
                        p.edge(v.reciprocal(rno[:], sdq[:]), v, "qkrn")
                        v.tensor_mul(qh[:, c * 512:(c + 1) * 512],
                                     qh[:, c * 512:(c + 1) * 512], rno[:])
                        v.tensor_mul(qh[:, L + c * 512: L + (c + 1) * 512],
                                     qh[:, L + c * 512: L + (c + 1) * 512], rno[:]
                                     ).then_inc(vsc, 1)
                    npair += 1

    @blk.scalar
    def _(s_):
        s_.wait_ge(p.bar_sem, bar0)
        for c in range(8):
            s_.wait_ge(tbe, c + 1)
            s_.activation(betab[:, c * 512:(c + 1) * 512], p.PS[2][0:4, :],
                          AF.Sigmoid).then_inc(sbe, 1)
        npair = 0
        for pi, P in enumerate(PLIST):
            for dto in range(NDT):
                j = pi * NDT + dto
                A = dto % 2
                s_.wait_ge(vcv, 1 + j + 1)
                sl = slot_of(pi, dto)
                prev = sl - 1
                if prev >= 0 and (P == "v" or dto % 2 == 0):
                    s_.wait_ge(st2[prev % 2], st_total(prev, prev % 2))
                s_.activation(qh[:, A * L:(A + 1) * L], cnv[A][:], AF.Silu
                              ).then_inc(ssl, 1)
                if P != "v" and dto % 2 == 1:
                    for c in range(8):
                        s_.wait_ge(tsq, npair * 8 + c + 1)
                        if npair * 8 + c >= 1:
                            s_.wait_ge(vsc, npair * 8 + c)
                        s_.activation(sdq[:], p.PS[3][:, :], AF.Sqrt
                                      ).then_inc(ssr, 1)
                    npair += 1

    @blk.gpsimd
    def _(g):
        npair = 0
        for sl, (P, idx, nd) in enumerate(SLOTS):
            if sl == 4:
                g.wait_ge(tpr, 64)
                g.dma_start(out=wcur[:],
                            in_=wsrc["k"].rearrange("(dtp pp) dd -> pp dtp dd", pp=128)
                            ).then_inc(wl, 16)
            if sl == 8:
                g.wait_ge(tpr, 128)
                g.dma_start(out=wcur[:],
                            in_=wsrc["v"].rearrange("(dtp pp) dd -> pp dtp dd", pp=128)
                            ).then_inc(wl, 16)
            if P == "v":
                dto = idx
                g.wait_ge(ssl, 16 + dto + 1)
                g.dma_start(out=outs["v"][dto * 128:(dto + 1) * 128, :],
                            in_=qh[:, (dto % 2) * L:(dto % 2 + 1) * L]
                            ).then_inc(st2[sl % 2], 16)
            else:
                g.wait_ge(vsc, npair * 8 + 8)
                for u in range(2):
                    dto = idx * 2 + u
                    g.dma_start(out=outs[P][dto * 128:(dto + 1) * 128, :],
                                in_=qh[:, u * L:(u + 1) * L]
                                ).then_inc(st2[sl % 2], 16)
                npair += 1
        g.wait_ge(sbe, 8)
        g.dma_start(out=dr["betad"], in_=betab[:]).then_inc(st2[0], 16)
        g.wait_ge(st2[0], st_total(15, 0) + 16)
        g.wait_ge(st2[1], st_total(15, 1))
        g.engine_nop().then_inc(p.bar_sem, 1)
    p.bar += 1


def ph_l1core(p, dr):
    nc, blk = p.nc, p.block
    G = p.G
    bar0 = p.bar
    tsm = p.sem("lctsm")
    ssm = p.sem("lcssm")
    vsm = p.sem("lcvsm")
    gl2 = [p.sem("lcgl_a", dma=True), p.sem("lcgl_b", dma=True)]
    ws2 = [p.sem("lcws_a", dma=True), p.sem("lcws_b", dma=True)]
    hl2 = [p.sem("lchl_a", dma=True), p.sem("lchl_b", dma=True)]
    sst2 = [p.sem("lcss_a", dma=True), p.sem("lcss_b", dma=True)]
    wol = p.sem("lcwol", dma=True)

    NG = NCK // 8
    qnd, knd, vsd, betad = dr["qnd"], dr["knd"], dr["vsd"], dr["betad"]
    wtd, utd, s2d = dr["wtd"], dr["utd"], dr["s2d"]

    Qg = p.sb("lcQ", [128, 4096], F32)
    Wg_ = p.sb("lcWg", [128, 4096], F32)
    Ug_ = p.sb("lcUg", [128, 4096], F32)
    Wg = [Wg_, Wg_]
    Ug = [Ug_, Ug_]
    Rall = p.sb("lcRall", [128, 512], F32)
    Pall = p.sb("lcPall", [128, 512], F32)
    aqk = p.sb("lcaqk", [128, L], F32)
    tmpg = p.sb("lctmpg", [128, 512], F32)
    Sdall = p.sb("lcSd", [128, 512], F32)
    SdPre = p.sb("lcSdP", [128, 512], F32)
    nSdPre = p.sb("lcnSdP", [128, 512], F32)
    av = p.sb("lcav", [128, 512], F32)

    jobs = []
    tensor_order = []

    def J(t=None, s=None, v=None, deps=(), tx=(), vx=(), sx=()):
        jid = len(jobs)
        jobs.append(dict(t=t, s=s, v=v, deps=tuple(deps), tx=list(tx),
                         vx=list(vx), sx=list(sx)))
        if t is not None:
            tensor_order.append(jid)
        return jid

    sub1 = ExitStack()
    Kg = [sub1.enter_context(nc.sbuf_tensor(f"lcK{i}", [128, 4096], F32)) for i in range(2)]
    Vg = sub1.enter_context(nc.sbuf_tensor("lcV", [128, 4096], F32))
    bb = sub1.enter_context(nc.sbuf_tensor("lcbb", [128, 4 * 512], F32))
    beg = sub1.enter_context(nc.sbuf_tensor("lcbeg", [4, 512], F32))
    Kb = [sub1.enter_context(nc.sbuf_tensor(f"lcKb{i}", [128, 1024], F32)) for i in range(2)]
    Vb = [sub1.enter_context(nc.sbuf_tensor(f"lcVb{i}", [128, 1024], F32)) for i in range(2)]
    KbT = [sub1.enter_context(nc.sbuf_tensor(f"lcKbT{i}", [64, 2048], F32)) for i in range(2)]
    VbT = [sub1.enter_context(nc.sbuf_tensor(f"lcVbT{i}", [64, 2048], F32)) for i in range(2)]
    sqg = sub1.enter_context(nc.sbuf_tensor("lcsqg", [128, 1024], F32))
    ind4 = sub1.enter_context(nc.sbuf_tensor("lcind4", [4, 512], F32))
    i4s = p.sem("lci4")
    TC = [{n: sub1.enter_context(nc.sbuf_tensor(f"lc{n}{i}", [64, 128], F32)) for n in
           ("Lst", "Bst", "L2", "B2", "L4", "B4", "L8", "B8", "L16", "B16",
            "L32", "R", "R2", "TT")} for i in range(2)]


    NLD1 = 25

    def gl_val(g):
        return 16 * NLD1 * (g // 2 + 1)

    def init_jobs():
        def v_(v, bank):
            return v.engine_nop()
        return J(v=v_)

    def bb_jobs(g):
        gp = g % 2
        ids = []
        for h in range(4):
            def t_(t, bank, h=h):
                return t.matmul(bank[:, :], ind4[:, h * 128:(h + 1) * 128], beg[0:4, :],
                                start=True, stop=True)
            def v_(v, bank, h=h):
                return v.tensor_copy(bb[:, h * 512:(h + 1) * 512], bank[:, :])
            ids.append(J(t=t_, v=v_, tx=[(gl2[gp], gl_val(g)), (i4s, 2)]))
        return ids

    def prep_jobs(q, j_init):
        pp, gp = q % 2, (q // 4) % 2
        ci0 = (2 * q) % 8
        ids = []
        for dt in range(NDT):
            def v_(v, bank, dt=dt, pp=pp, gp=gp, ci0=ci0):
                v.tensor_mul(Kb[pp][:, dt * 128:(dt + 1) * 128],
                             Kg[gp][:, dt * 512 + ci0 * 64: dt * 512 + ci0 * 64 + 128],
                             bb[:, (dt // 2) * 512 + ci0 * 64: (dt // 2) * 512 + ci0 * 64 + 128])
                return v.tensor_mul(Vb[pp][:, dt * 128:(dt + 1) * 128],
                             Vg[:, dt * 512 + ci0 * 64: dt * 512 + ci0 * 64 + 128],
                             bb[:, (dt // 2) * 512 + ci0 * 64: (dt // 2) * 512 + ci0 * 64 + 128])
            ids.append(J(v=v_, vx=[(gl2[gp], gl_val(q // 4))], deps=[j_init]))
        return ids

    def agram_jobs(q, prep_ids, j_init):
        pp, gp = q % 2, (q // 4) % 2
        ci0 = (2 * q) % 8
        def tA(t, bank):
            for cc in range(2):
                for dt in range(NDT):
                    mm = t.matmul(bank[0:64, cc * 64: cc * 64 + 64],
                                  Kb[pp][:, dt * 128 + cc * 64: dt * 128 + cc * 64 + 64],
                                  Kg[gp][:, dt * 512 + (ci0 + cc) * 64: dt * 512 + (ci0 + cc) * 64 + 64],
                                  start=(dt == 0), stop=(dt == NDT - 1))
            return mm
        def vA(v, bank):
            for cc in range(2):
                ii = v.tensor_mul(TC[pp]["Lst"][:, cc * 64: cc * 64 + 64],
                                  bank[0:64, cc * 64: cc * 64 + 64], G["mlow"][:])
            return ii
        def tB(t, bank):
            for cc in range(2):
                for dt in range(NDT):
                    mm = t.matmul(bank[0:64, cc * 64: cc * 64 + 64],
                                  Kg[gp][:, dt * 512 + (ci0 + cc) * 64: dt * 512 + (ci0 + cc) * 64 + 64],
                                  Kb[pp][:, dt * 128 + cc * 64: dt * 128 + cc * 64 + 64],
                                  start=(dt == 0), stop=(dt == NDT - 1))
            return mm
        def vB(v, bank):
            for cc in range(2):
                ii = v.tensor_mul(TC[pp]["Bst"][:, cc * 64: cc * 64 + 64],
                                  bank[0:64, cc * 64: cc * 64 + 64], G["mup"][:])
            return ii
        ja = J(t=tA, v=vA, deps=list(prep_ids) + [j_init])
        jb = J(t=tB, v=vB, deps=list(prep_ids) + [j_init])
        return ja, jb

    def tr_thunks(q, prep_ids):
        pp = q % 2
        thunks = []
        for dt in range(NDT):
            for srcb, dstb in ((Kb, KbT), (Vb, VbT)):
                def mk(srcb=srcb, dstb=dstb, dt=dt, pp=pp):
                    def t_(t, bank):
                        for cc in range(2):
                            mm = t.transpose(bank[0:64, cc * 128: cc * 128 + 128],
                                             srcb[pp][:, dt * 128 + cc * 64: dt * 128 + cc * 64 + 64],
                                             G["ident"][:])
                        return mm
                    def v_(v, bank):
                        return v.tensor_copy(dstb[pp][:, dt * 256:(dt + 1) * 256],
                                             bank[0:64, 0:256])
                    return J(t=t_, v=v_, deps=prep_ids)
                thunks.append(mk)
        return thunks

    def chain_thunks(q, ja, jb, j_init):
        pp = q % 2
        C = TC[pp]
        seq = [("L2", "Bst", "Lst"), ("B2", "Lst", "Bst"),
               ("L4", "B2", "L2"), ("B4", "L2", "B2"),
               ("L8", "B4", "L4"), ("B8", "L4", "B4"),
               ("L16", "B8", "L8"), ("B16", "L8", "B8"),
               ("L32", "B16", "L16")]
        dep_of = {"Lst": ja, "Bst": jb}
        state = {}
        thunks = []
        for nm, lh, rh in seq:
            def mk(nm=nm, lh=lh, rh=rh):
                def t_(t, bank):
                    for cc in range(2):
                        mm = t.matmul(bank[0:64, cc * 64: cc * 64 + 64],
                                      C[lh][:, cc * 64: cc * 64 + 64],
                                      C[rh][:, cc * 64: cc * 64 + 64],
                                      start=True, stop=True)
                    return mm
                def v_(v, bank):
                    return v.tensor_copy(C[nm][:], bank[0:64, 0:128])
                dep_of[nm] = J(t=t_, v=v_, deps=[dep_of[lh], dep_of[rh]])
                return dep_of[nm]
            thunks.append(mk)
        def mkR0():
            def vR0(v, bank):
                for cc in range(2):
                    ii = v.tensor_sub(C["R"][:, cc * 64: cc * 64 + 64],
                                      G["ident"][0:64, 0:64],
                                      C["Bst"][:, cc * 64: cc * 64 + 64])
                return ii
            state["jr"] = J(v=vR0, deps=[jb, j_init])
            return state["jr"]
        thunks.insert(2, mkR0)
        for lk in ("L2", "L4", "L8", "L16", "L32"):
            def mk(lk=lk):
                def t_(t, bank):
                    for cc in range(2):
                        mm = t.matmul(bank[0:64, cc * 64: cc * 64 + 64],
                                      C[lk][:, cc * 64: cc * 64 + 64],
                                      C["R"][:, cc * 64: cc * 64 + 64],
                                      start=True, stop=True)
                    return mm
                def v_(v, bank):
                    ii = v.tensor_add(C["R2"][:], bank[0:64, 0:128], C["R"][:])
                    p.edge(ii, v, "lcr")
                    return v.tensor_copy(C["R"][:], C["R2"][:])
                state["jr"] = J(t=t_, v=v_, deps=[state["jr"], dep_of[lk]])
                return state["jr"]
            thunks.append(mk)
        def mkT():
            def tT(t, bank):
                for cc in range(2):
                    mm = t.matmul(bank[0:64, cc * 64: cc * 64 + 64],
                                  C["Lst"][:, cc * 64: cc * 64 + 64],
                                  C["R"][:, cc * 64: cc * 64 + 64],
                                  start=True, stop=True)
                return mm
            def vT(v, bank):
                return v.scalar_tensor_tensor(C["TT"][:], bank[0:64, 0:128], -2.0,
                                              C["R"][:], ALU.mult, ALU.add)
            state["jt"] = J(t=tT, v=vT, deps=[state["jr"], ja])
            return state["jt"]
        thunks.append(mkT)
        return thunks, (lambda: state["jt"])

    def wu_jobs(q, jt, trids):
        g = q // 4
        pp, gp = q % 2, (q // 4) % 2
        ci0 = (2 * q) % 8
        wg, ug = Wg[g % 2], Ug[g % 2]
        C = TC[pp]
        ids = []
        for dt in range(NDT):
            for which in (0, 1):
                srcT = (KbT if which == 0 else VbT)
                dst = (wg if which == 0 else ug)
                acc = (Rall if which == 0 else Pall)
                def t_(t, bank, srcT=srcT, dt=dt, pp=pp):
                    for cc in range(2):
                        mm = t.matmul(bank[:, cc * 64: cc * 64 + 64],
                                      srcT[pp][:, dt * 256 + cc * 128: dt * 256 + (cc + 1) * 128],
                                      C["TT"][:, cc * 64: cc * 64 + 64],
                                      start=True, stop=True)
                    return mm
                def v_(v, bank, dst=dst, dt=dt, ci0=ci0, acc=acc, q=q, gp=gp):
                    sl = slice(dt * 512 + ci0 * 64, dt * 512 + ci0 * 64 + 128)
                    v.tensor_copy(dst[:, sl], bank[:, 0:128])
                    for cc in range(2):
                        i = 2 * q + cc
                        ii = v.scalar_tensor_tensor(
                            bank[:, 128 + cc * 64: 128 + cc * 64 + 64],
                            Kg[gp][:, dt * 512 + (ci0 + cc) * 64: dt * 512 + (ci0 + cc) * 64 + 64],
                            1.0, bank[:, cc * 64: cc * 64 + 64], ALU.mult, ALU.mult,
                            accum_out=acc[:, dt * 64 + (i % 64): dt * 64 + (i % 64) + 1])
                    return ii
                vx = []
                if g >= 1:
                    vx.append((ws2[(g - 1) % 2], 16 * 16 * ((g - 1) // 2 + 1)))
                ids.append(J(t=t_, v=v_, deps=[jt] + list(trids), vx=vx))
        return ids

    def aqk_jobs(g, j_init):
        gp = g % 2
        ids = []
        for qq in range(4):
            ci0 = qq * 2
            def vpre(v, bank, ci0=ci0, gp=gp):
                for dt in range(NDT):
                    ii = v.tensor_mul(sqg[:, dt * 128:(dt + 1) * 128],
                                 Qg[:, dt * 512 + ci0 * 64: dt * 512 + ci0 * 64 + 128],
                                 Kg[gp][:, dt * 512 + ci0 * 64: dt * 512 + ci0 * 64 + 128])
                return ii
            jpre = J(v=vpre, vx=[(gl2[gp], gl_val(g))])
            def t_(t, bank):
                for cc in range(2):
                    for dt in range(NDT):
                        mm = t.matmul(bank[:, cc * 64: cc * 64 + 64], G["ones"][:],
                                      sqg[:, dt * 128 + cc * 64: dt * 128 + cc * 64 + 64],
                                      start=(dt == 0), stop=(dt == NDT - 1))
                return mm
            def v_(v, bank, g=g, ci0=ci0):
                for cc in range(2):
                    ii = v.tensor_mul(aqk[:, g * 512 + (ci0 + cc) * 64: g * 512 + (ci0 + cc) * 64 + 64],
                                      bank[:, cc * 64: cc * 64 + 64], G["bmask"][:])
                return ii
            ids.append(J(t=t_, v=v_, deps=[jpre, j_init]))
        return ids

    j_init = init_jobs()
    prep_last = {}
    group_wu_last = {}
    pending = None
    NPAIR = NCK // 2
    for q in range(NPAIR):
        g = q // 4
        if q % 4 == 0:
            bbids = bb_jobs(g)
            aqk_jobs(g, j_init)
        pr = prep_jobs(q, j_init)
        ja, jb = agram_jobs(q, pr, j_init)
        trth = tr_thunks(q, pr)
        chth, get_jt = chain_thunks(q, ja, jb, j_init)
        trids = []
        k1 = k2 = 0
        while k1 < len(trth) or k2 < len(chth):
            if k1 < len(trth):
                trids.append(trth[k1]())
                k1 += 1
            if k2 < len(chth):
                chth[k2]()
                k2 += 1
        if pending is not None:
            qp, jtp, trp = pending
            wids = wu_jobs(qp, jtp, trp)
            group_wu_last[qp // 4] = wids[-1]
        pending = (q, get_jt(), trids)
        prep_last[g] = max(pr)
    qp, jtp, trp = pending
    wids = wu_jobs(qp, jtp, trp)
    group_wu_last[NG - 1] = wids[-1]

    def vscan(v, bank):
        ii = v.tensor_scalar(av[:], Rall[:], -1.0, 1.0, ALU.mult, ALU.add)
        p.edge(ii, v, "lcsc")
        for dt in range(NDT):
            sl = slice(dt * 64, (dt + 1) * 64)
            ii = v.tensor_tensor_scan(Sdall[:, sl], av[:, sl], Pall[:, sl],
                                      0.0, ALU.mult, ALU.add)
        p.edge(ii, v, "lcsc2")
        ii = v.memset(SdPre[:], 0.0)
        p.edge(ii, v, "lcscm")
        for dt in range(NDT):
            ii = v.tensor_copy(SdPre[:, dt * 64 + 1: dt * 64 + 64],
                               Sdall[:, dt * 64: dt * 64 + 63])
        p.edge(ii, v, "lcsc3")
        return v.tensor_scalar_mul(nSdPre[:], SdPre[:], -1.0)
    j_scan = J(v=vscan, deps=[group_wu_last[NG - 1]])

    sub1.close()

    ub = p.sb("lcub", [128, 4096], F32)
    tmp3 = p.sb("lctmp3", [128, 4096], F32)
    ob = p.sb("lcob", [128, 4096], F32)
    onb = p.sb("lconb", [128, 4096], BF16)
    wo_bf = p.sb("lcwo", [128, NDT * 1024], BF16)
    sdq = [p.sb(f"lcsdq{i}", [128, 512], F32) for i in range(2)]
    rno = p.sb("lcrno", [128, 512], F32)
    s2st = [p.sb(f"lcs2{i}", [128, 512], F32) for i in range(2)]

    NLD3 = 24

    def hl_val(g):
        return 16 * NLD3 * (g // 2 + 1)

    s2_jobs = {}
    g3_last = {}
    rms_hist = []
    for g in range(NG):
        gp = g % 2
        def vel(v, bank, g=g, gp=gp):
            for dt in range(NDT):
                for ci in range(8):
                    i = g * 8 + ci
                    sl = slice(dt * 512 + ci * 64, dt * 512 + ci * 64 + 64)
                    v.scalar_tensor_tensor(ub[:, sl], Wg[gp][:, sl],
                                           nSdPre[:, dt * 64 + i % 64: dt * 64 + i % 64 + 1],
                                           Ug[gp][:, sl], ALU.mult, ALU.add)
            ii = v.engine_nop()
            p.edge(ii, v, "lce1")
            for dt in range(NDT):
                for ci in range(8):
                    sl = slice(dt * 512 + ci * 64, dt * 512 + ci * 64 + 64)
                    v.tensor_mul(tmp3[:, sl],
                                 aqk[:, g * 512 + ci * 64: g * 512 + ci * 64 + 64],
                                 ub[:, sl])
            ii = v.engine_nop()
            p.edge(ii, v, "lce2")
            for dt in range(NDT):
                for ci in range(8):
                    i = g * 8 + ci
                    sl = slice(dt * 512 + ci * 64, dt * 512 + ci * 64 + 64)
                    ii = v.scalar_tensor_tensor(ob[:, sl], Qg[:, sl],
                                                SdPre[:, dt * 64 + i % 64: dt * 64 + i % 64 + 1],
                                                tmp3[:, sl], ALU.mult, ALU.add)
            return ii
        j_el = J(v=vel, deps=[j_scan], vx=[(hl2[gp], hl_val(g))])
        last_rms = None
        for h in range(4):
            def vsq_(v, bank, h=h):
                v.tensor_mul(tmp3[:, (2 * h) * 512:(2 * h) * 512 + 512],
                             ob[:, (2 * h) * 512:(2 * h + 1) * 512],
                             ob[:, (2 * h) * 512:(2 * h + 1) * 512])
                return v.tensor_mul(tmp3[:, (2 * h + 1) * 512:(2 * h + 1) * 512 + 512],
                             ob[:, (2 * h + 1) * 512:(2 * h + 2) * 512],
                             ob[:, (2 * h + 1) * 512:(2 * h + 2) * 512])
            j_sq = J(v=vsq_, deps=[j_el])
            rp = len(rms_hist) % 2
            sx = []
            if len(rms_hist) >= 2:
                sx.append((vsm, rms_hist[-2] + 1))
            def t_(t, bank, h=h):
                for u in range(2):
                    mm = t.matmul(bank[:, :], G["ones"][:],
                                  tmp3[:, (2 * h + u) * 512:(2 * h + u) * 512 + 512],
                                  start=(u == 0), stop=(u == 1))
                return mm
            def s_(s_, bank, rp=rp):
                return s_.activation(sdq[rp][:], bank[:, :], AF.Sqrt,
                                     bias=G["eps_rms"][:, 0:1], scale=1.0 / HD)
            def v_(v, bank, h=h, rp=rp):
                rr = v.reciprocal(rno[:], sdq[rp][:])
                p.edge(rr, v, "lcrn")
                for u in range(2):
                    dt = 2 * h + u
                    ii = v.scalar_tensor_tensor(onb[:, dt * 512:(dt + 1) * 512],
                                                ob[:, dt * 512:(dt + 1) * 512],
                                                G["gvec"][:, dt:dt + 1], rno[:],
                                                ALU.mult, ALU.mult)
                return ii
            jr = J(t=t_, s=s_, v=v_, deps=[j_sq], sx=sx)
            rms_hist.append(jr)
            last_rms = jr
        for dto in range(NDT):
            spar = (g * NDT + dto) % 2
            vx = []
            if g * NDT + dto >= 2:
                vx.append((sst2[spar], 16 * ((g * NDT + dto) // 2)))
            def t_(t, bank, dto=dto):
                for dt in range(NDT):
                    mm = t.matmul(bank[:, :],
                                  wo_bf[:, dt * 1024 + dto * 128: dt * 1024 + (dto + 1) * 128],
                                  onb[:, dt * 512:(dt + 1) * 512],
                                  start=(dt == 0), stop=(dt == NDT - 1))
                return mm
            def v_(v, bank, spar=spar):
                return v.tensor_copy(s2st[spar][:], bank[:, :])
            jw = J(t=t_, v=v_, deps=[last_rms], tx=[(wol, 16)], vx=vx)
            s2_jobs[(g, dto)] = jw
        g3_last[g] = len(jobs) - 1

    ntj = {}
    tc = 0
    for jid, jb_ in enumerate(jobs):
        if jb_["t"] is not None:
            tc += 1
        ntj[jid] = tc
    nsj = {}
    sc = 0
    for jid, jb_ in enumerate(jobs):
        if jb_["s"] is not None:
            sc += 1
        nsj[jid] = sc

    RING = [1, 2, 3, 4, 5, 6]
    def bank(jid):
        return p.PS[RING[(ntj[jid] - 1) % len(RING)]]

    @blk.sync
    def _(q):
        q.wait_ge(p.bar_sem, bar0)
        for g in range(NG):
            gp = g % 2
            if g >= 1:
                q.wait_ge(vsm, prep_last[g - 1] + 1)
            if g >= 2:
                q.wait_ge(vsm, group_wu_last[g - 2] + 1)
            for dt in range(NDT):
                q.dma_start(out=Qg[:, dt * 512:(dt + 1) * 512],
                            in_=qnd[dt * 128:(dt + 1) * 128, g * 512:(g + 1) * 512]
                            ).then_inc(gl2[gp], 16)
            for dt in range(NDT):
                q.dma_start(out=Kg[gp][:, dt * 512:(dt + 1) * 512],
                            in_=knd[dt * 128:(dt + 1) * 128, g * 512:(g + 1) * 512]
                            ).then_inc(gl2[gp], 16)
            for dt in range(NDT):
                q.dma_start(out=Vg[:, dt * 512:(dt + 1) * 512],
                            in_=vsd[dt * 128:(dt + 1) * 128, g * 512:(g + 1) * 512]
                            ).then_inc(gl2[gp], 16)
            q.dma_start(out=beg[:], in_=betad[:, g * 512:(g + 1) * 512]
                        ).then_inc(gl2[gp], 16)
        q.wait_ge(vsm, j_scan + 1)
        q.wait_ge(ws2[0], 16 * 16 * 4)
        q.wait_ge(ws2[1], 16 * 16 * 4)
        for g in range(NG):
            gp = g % 2
            if g >= 1:
                q.wait_ge(vsm, g3_last[g - 1] + 1)
            for dt in range(NDT):
                q.dma_start(out=Qg[:, dt * 512:(dt + 1) * 512],
                            in_=qnd[dt * 128:(dt + 1) * 128, g * 512:(g + 1) * 512]
                            ).then_inc(hl2[gp], 16)
            for dt in range(NDT):
                q.dma_start(out=Wg[gp][:, dt * 512:(dt + 1) * 512],
                            in_=wtd[dt * 128:(dt + 1) * 128, g * 512:(g + 1) * 512]
                            ).then_inc(hl2[gp], 16)
            for dt in range(NDT):
                q.dma_start(out=Ug[gp][:, dt * 512:(dt + 1) * 512],
                            in_=utd[dt * 128:(dt + 1) * 128, g * 512:(g + 1) * 512]
                            ).then_inc(hl2[gp], 16)

    @blk.gpsimd
    def _(g_):
        g_.wait_ge(p.bar_sem, bar0)
        g_.iota(ind4[:], [[1, 4], [0, 128]], channel_multiplier=-1,
                allow_small_or_imprecise_dtypes=True).then_inc(i4s, 1)
        for g in range(NG):
            gp = g % 2
            g_.wait_ge(vsm, group_wu_last[g] + 1)
            for dt in range(NDT):
                g_.dma_start(out=wtd[dt * 128:(dt + 1) * 128, g * 512:(g + 1) * 512],
                             in_=Wg[gp][:, dt * 512:(dt + 1) * 512]).then_inc(ws2[gp], 16)
            for dt in range(NDT):
                g_.dma_start(out=utd[dt * 128:(dt + 1) * 128, g * 512:(g + 1) * 512],
                             in_=Ug[gp][:, dt * 512:(dt + 1) * 512]).then_inc(ws2[gp], 16)
        g_.wait_ge(vsm, j_scan + 1)
        g_.dma_start(out=wo_bf[:],
                     in_=dr["w_wo"].rearrange("(dtp pp) dd -> pp dtp dd", pp=128)
                     ).then_inc(wol, 16)
        nst = 0
        for g in range(NG):
            for dto in range(NDT):
                spar = nst % 2
                g_.wait_ge(vsm, s2_jobs[(g, dto)] + 1)
                g_.dma_start(out=s2d[dto * 128:(dto + 1) * 128, g * 512:(g + 1) * 512],
                             in_=s2st[spar][:]).then_inc(sst2[spar], 16)
                nst += 1
        g_.wait_ge(sst2[0], 16 * 32)
        g_.wait_ge(sst2[1], 16 * 32)
        g_.engine_nop().then_inc(p.bar_sem, 1)

    @blk.tensor
    def _(t):
        t.wait_ge(p.bar_sem, bar0)
        for mo, jid in enumerate(tensor_order):
            jb_ = jobs[jid]
            if mo >= len(RING):
                t.wait_ge(vsm, tensor_order[mo - len(RING)] + 1)
            if jb_["deps"]:
                t.wait_ge(vsm, max(jb_["deps"]) + 1)
            for sem, val in jb_["tx"]:
                t.wait_ge(sem, val)
            mm = jb_["t"](t, bank(jid))
            mm.then_inc(tsm, 1)

    @blk.scalar
    def _(s_):
        s_.wait_ge(p.bar_sem, bar0)
        for jid, jb_ in enumerate(jobs):
            if jb_["s"] is None:
                continue
            s_.wait_ge(tsm, ntj[jid])
            for sem, val in jb_["sx"]:
                s_.wait_ge(sem, val)
            ii = jb_["s"](s_, bank(jid))
            ii.then_inc(ssm, 1)

    @blk.vector
    def _(v):
        v.wait_ge(p.bar_sem, bar0)
        v.wait_ge(i4s, 1)
        v.tensor_single_scalar(ind4[:], ind4[:], 0.0, ALU.is_equal).then_inc(i4s, 1)
        for jid, jb_ in enumerate(jobs):
            if jb_["t"] is not None:
                v.wait_ge(tsm, ntj[jid])
            if jb_["s"] is not None:
                v.wait_ge(ssm, nsj[jid])
            if jb_["t"] is None and jb_["deps"]:
                v.wait_ge(vsm, max(jb_["deps"]) + 1)
            for sem, val in jb_["vx"]:
                v.wait_ge(sem, val)
            ii = jb_["v"](v, bank(jid) if jb_["t"] is not None else None)
            ii.then_inc(vsm, 1)
    p.bar += 1


def ph_route(p, dr):
    nc, blk = p.nc, p.block
    G = p.G
    bar0 = p.bar
    affd, cwtd = dr["affd"], dr["cwtd"]
    al2 = [p.sem("rtal_a", dma=True), p.sem("rtal_b", dma=True)]
    vbi = p.sem("rtvbi")
    vcn = p.sem("rtvcn")
    sgs = p.sem("rtsgs")
    ttr = p.sem("rtttr")
    vtr = p.sem("rtvtr")
    st2 = [p.sem("rtst_a", dma=True), p.sem("rtst_b", dma=True)]

    NT = L // 128
    afb = [p.sb(f"rtaf{i}", [128, NE], F32) for i in range(2)]
    bia = p.sb("rtbia", [128, NE], F32)
    cbuf = p.sb("rtcb", [128, NE * NE], F32)
    red = p.sb("rtred", [128, 2048], F32)
    sig = p.sb("rtsig", [128, NE], F32)
    msk = p.sb("rtmsk", [128, NE], F32)
    wsu = p.sb("rtwsu", [128, 1], F32)
    wiv = p.sb("rtwiv", [128, 1], F32)
    cwt = [p.sb(f"rtcw{i}", [128, NE], F32) for i in range(2)]
    cwT = p.sb("rtcwT", [64, 128 * 2], F32)

    @blk.sync
    def _(q):
        q.wait_ge(p.bar_sem, bar0)
        for tt in range(NT):
            A = tt % 2
            if tt >= 2:
                q.wait_ge(sgs, tt - 1)
                q.wait_ge(vcn, tt - 1)
            q.dma_start(out=afb[A][:], in_=affd[tt * 128:(tt + 1) * 128, :]
                        ).then_inc(al2[A], 16)

    @blk.vector
    def _(v):
        v.wait_ge(p.bar_sem, bar0)
        for tt in range(NT):
            A = tt % 2
            v.wait_ge(al2[A], 16 * (tt // 2 + 1))
            ii = v.tensor_add(bia[:], afb[A][:], G["ebias"][:, 0:NE])
            p.edge(ii, v, "rt1")
            for e in range(NE):
                ii = v.scalar_tensor_tensor(cbuf[:, e * NE:(e + 1) * NE], bia[:],
                                            bia[:, e:e + 1], G["ones"][:, 0:NE],
                                            ALU.is_gt, ALU.mult,
                                            accum_out=red[:, e:e + 1])
            p.edge(ii, v, "rt2")
            cnt = red[:, 0:NE]
            ii = v.tensor_single_scalar(msk[:], cnt, float(TOPK), ALU.is_lt)
            p.edge(ii, v, "rt4")
            v.wait_ge(sgs, tt + 1)
            ii = v.tensor_mul(sig[:], sig[:], msk[:])
            p.edge(ii, v, "rt5")
            ii = v.reduce_sum(wsu[:], sig[:], AX.X)
            p.edge(ii, v, "rt6")
            ii = v.tensor_scalar_add(wsu[:], wsu[:], 1e-8)
            p.edge(ii, v, "rt7")
            ii = v.reciprocal(wiv[:], wsu[:])
            p.edge(ii, v, "rt8")
            if tt >= 2:
                v.wait_ge(ttr, tt - 1)
            v.tensor_scalar_mul(cwt[A][:], sig[:], wiv[:, 0:1]).then_inc(vcn, 1)
            v.wait_ge(ttr, tt + 1)
            if tt >= 2:
                v.wait_ge(st2[A], 16 * (tt // 2))
            v.tensor_copy(cwT[:, (A) * 128:(A + 1) * 128],
                          p.PS[4 + A][0:64, 0:128]).then_inc(vtr, 1)

    @blk.scalar
    def _(s_):
        s_.wait_ge(p.bar_sem, bar0)
        for tt in range(NT):
            A = tt % 2
            s_.wait_ge(al2[A], 16 * (tt // 2 + 1))
            if tt >= 1:
                s_.wait_ge(vcn, tt)
            s_.activation(sig[:], afb[A][:], AF.Sigmoid).then_inc(sgs, 1)

    @blk.tensor
    def _(t):
        t.wait_ge(p.bar_sem, bar0)
        for tt in range(NT):
            A = tt % 2
            t.wait_ge(vcn, tt + 1)
            if tt >= 2:
                t.wait_ge(vtr, tt - 1)
            t.transpose(p.PS[4 + A][0:64, 0:128], cwt[A][:, 0:64] if False else cwt[A][:],
                        G["ident"][:]).then_inc(ttr, 1)

    @blk.gpsimd
    def _(g):
        for tt in range(NT):
            A = tt % 2
            g.wait_ge(vtr, tt + 1)
            g.dma_start(out=cwtd[:, tt * 128:(tt + 1) * 128],
                        in_=cwT[0:64, A * 128:(A + 1) * 128]).then_inc(st2[A], 16)
        g.wait_ge(st2[0], 16 * NT // 2)
        g.wait_ge(st2[1], 16 * NT // 2)
        g.engine_nop().then_inc(p.bar_sem, 1)
    p.bar += 1


def ph_moe(p, dr):
    nc, blk = p.nc, p.block
    G = p.G
    bar0 = p.bar
    xn2d, x1d, s2d, outd = dr["xn2d"], dr["x1d"], dr["s2d"], dr["outd"]
    xnl = p.sem("moxnl", dma=True)
    one = p.sem("moone", dma=True)
    wl2 = [p.sem("mowl_a", dma=True), p.sem("mowl_b", dma=True)]
    ie4 = [p.sem(f"moie_{i}") for i in range(4)]
    iv4 = [p.sem(f"moiv_{i}") for i in range(4)]
    tup = p.sem("motup")
    ssl = p.sem("mossl")
    tcb = p.sem("motcb")
    vhs = p.sem("movhs")
    tdn = p.sem("motdn")
    tda = p.sem("motda")
    vev = p.sem("movev")
    xls2 = [p.sem("moxl_a", dma=True), p.sem("moxl_b", dma=True)]
    ssc = p.sem("mossc")
    vcb = p.sem("movcb")
    ost2 = [p.sem("moos_a", dma=True), p.sem("moos_b", dma=True)]

    NHF = 2
    HT = L // 2
    NCH = HT // 256
    GE = 2
    NGRP = NE // GE
    UPH = NCH + NGRP * NCH * GE
    DNH = NCH + NGRP * NCH

    xnh = p.sb("moxn", [128, NDT * HT], BF16)
    acc = p.sb("moacc", [128, NDT * HT], F32)
    wup = [p.sb(f"mowu{i}", [128, GE * 4096], BF16) for i in range(2)]
    wdn = [p.sb(f"mowd{i}", [128, GE * 4096], BF16) for i in range(2)]
    wsu_bf = p.sb("mowsu", [128, NDT * 512], BF16)
    wsd_bf = p.sb("mowsd", [128, 4 * 1024], BF16)
    cwTb = p.sb("mocwT", [64, L], BF16)
    inde = [p.sb(f"moind{i}", [64, 128], BF16) for i in range(4)]
    hb = [p.sb(f"mohb{i}", [128, 4 * 256], BF16) for i in range(2)]
    hsc = [p.sb(f"mohs{i}", [128, 4 * 256], BF16) for i in range(2)]
    xsb = [p.sb(f"moxs{i}", [128, 2 * 256], F32) for i in range(2)]
    slb = [p.sb(f"mosl{i}", [128, 2 * 256], F32) for i in range(2)]
    otb = [p.sb(f"moot{i}", [128, 2 * 256], F32) for i in range(2)]

    def up_bank(ft):
        return p.PS[4 + ft // 2], (ft % 2) * 256
    def dn_bank(dto):
        return p.PS[dto // 2], (dto % 2) * 256
    def upcount_through_group(gi):
        H, grp = divmod(gi, NGRP)
        return UPH * H + NCH + (grp + 1) * NCH * GE
    def dncount_through_group(gi):
        H, grp = divmod(gi, NGRP)
        return DNH * H + NCH + (grp + 1) * NCH

    @blk.gpsimd
    def _(g):
        g.wait_ge(p.bar_sem, bar0)
        g.dma_start(out=wsu_bf[:],
                    in_=dr["w_wsu"].rearrange("(dtp pp) f -> pp dtp f", pp=128)
                    ).then_inc(one, 16)
        g.dma_start(out=wsd_bf[:],
                    in_=dr["w_wsd"].rearrange("(ftp pp) dd -> pp ftp dd", pp=128)
                    ).then_inc(one, 16)
        g.dma_start(out=cwTb[:], in_=dr["cwtd"]).then_inc(one, 16)
        for gi in range(NHF * NGRP):
            sl = gi % 2
            grp = gi % NGRP
            if gi >= 2:
                g.wait_ge(tup, upcount_through_group(gi - 2))
                g.wait_ge(tdn, dncount_through_group(gi - 2))
            for u in range(GE):
                e = grp * GE + u
                g.dma_start(out=wup[sl][:, u * 4096:(u + 1) * 4096],
                            in_=dr["w_wup"][e].rearrange("(dtp pp) f -> pp dtp f", pp=128)
                            ).then_inc(wl2[sl], 16)
                g.dma_start(out=wdn[sl][:, u * 4096:(u + 1) * 4096],
                            in_=dr["w_wdn"][e].rearrange("(ftp pp) dd -> pp ftp dd", pp=128)
                            ).then_inc(wl2[sl], 16)
            for u in range(GE):
                e = grp * GE + u
                ei = gi * GE + u
                par = ei % 4
                if ei >= 4:
                    g.wait_ge(tcb, NCH * GE * (ei // 2 - 1))
                    g.wait_ge(iv4[par], ei // 4)
                g.iota(inde[par][:], [[0, 128]], base=e, channel_multiplier=-1,
                       allow_small_or_imprecise_dtypes=True).then_inc(ie4[par], 1)

    @blk.gpsimd
    def _(g):
        g.wait_ge(ost2[0], 16 * 2 * 32)
        g.wait_ge(ost2[1], 16 * 2 * 32)
        g.engine_nop().then_inc(p.bar_sem, 1)

    @blk.sync
    def _(q):
        q.wait_ge(p.bar_sem, bar0)
        for H in range(NHF):
            if H >= 1:
                q.wait_ge(tup, UPH * H)
            for dt in range(NDT):
                q.dma_start(out=xnh[:, dt * HT:(dt + 1) * HT],
                            in_=xn2d[dt * 128:(dt + 1) * 128, H * HT:(H + 1) * HT]
                            ).then_inc(xnl, 16)
            for st in range(32):
                ns = H * 32 + st
                par = ns % 2
                ch, dq = divmod(st, 4)
                if ns >= 2:
                    q.wait_ge(ost2[par], 16 * 2 * (ns // 2))
                for i in range(2):
                    dt = dq * 2 + i
                    q.dma_start(out=xsb[par][:, i * 256:(i + 1) * 256],
                                in_=x1d[dt * 128:(dt + 1) * 128,
                                        H * HT + ch * 256: H * HT + (ch + 1) * 256]
                                ).then_inc(xls2[par], 16)
                for i in range(2):
                    dt = dq * 2 + i
                    q.dma_start(out=otb[par][:, i * 256:(i + 1) * 256],
                                in_=s2d[dt * 128:(dt + 1) * 128,
                                        H * HT + ch * 256: H * HT + (ch + 1) * 256]
                                ).then_inc(xls2[par], 16)

    @blk.tensor
    def _(t):
        t.wait_ge(p.bar_sem, bar0)
        t.wait_ge(one, 48)
        nup = 0
        ncb = 0
        ndn = 0
        for H in range(NHF):
            t.wait_ge(xnl, 16 * NDT * (H + 1))
            for ch in range(NCH):
                if nup >= 1:
                    t.wait_ge(ssl, nup)
                for ft in range(4):
                    bank, co = up_bank(ft)
                    for dt in range(NDT):
                        mm = t.matmul(bank[:, co:co + 256],
                                      wsu_bf[:, dt * 512 + ft * 128: dt * 512 + (ft + 1) * 128],
                                      xnh[:, dt * HT + ch * 256: dt * HT + (ch + 1) * 256],
                                      start=(dt == 0), stop=(dt == NDT - 1))
                mm.then_inc(tup, 1)
                nup += 1
                t.wait_ge(ssl, nup)
                t.wait_ge(vev, ndn)
                for dto in range(NDT):
                    bank, co = dn_bank(dto)
                    for ft in range(4):
                        mm = t.matmul(bank[:, co:co + 256],
                                      wsd_bf[:, ft * 1024 + dto * 128: ft * 1024 + (dto + 1) * 128],
                                      hb[(nup - 1) % 2][:, ft * 256:(ft + 1) * 256],
                                      start=(ft == 0 and dto % 2 == 0),
                                      stop=(ft == 3 and dto % 2 == 1))
                mm.then_inc(tdn, 1)
                ndn += 1
            pending = None

            def emit_down(t, job):
                nonlocal ndn
                sl_, u_, par_, nhs_ = job
                t.wait_ge(vhs, nhs_)
                if u_ == 0:
                    t.wait_ge(vev, ndn)
                for dto in range(NDT):
                    bank, co = dn_bank(dto)
                    for ft in range(4):
                        mm = t.matmul(bank[:, co:co + 256],
                                      wdn[sl_][:, u_ * 4096 + ft * 1024 + dto * 128:
                                              u_ * 4096 + ft * 1024 + (dto + 1) * 128],
                                      hsc[par_][:, ft * 256:(ft + 1) * 256],
                                      start=(u_ == 0 and ft == 0 and dto % 2 == 0),
                                      stop=(u_ == GE - 1 and ft == 3 and dto % 2 == 1))
                if u_ == GE - 1:
                    mm.then_inc(tdn, 1)
                    ndn += 1
                else:
                    mm.then_inc(tda, 1)

            for grp in range(NGRP):
                gi = H * NGRP + grp
                sl = gi % 2
                t.wait_ge(wl2[sl], 16 * 2 * GE * (gi // 2 + 1))
                for ch in range(NCH):
                    for u in range(GE):
                        e = grp * GE + u
                        ei = H * NE + e
                        if nup >= 1:
                            t.wait_ge(ssl, nup)
                        for ft in range(4):
                            bank, co = up_bank(ft)
                            for dt in range(NDT):
                                mm = t.matmul(bank[:, co:co + 256],
                                              wup[sl][:, u * 4096 + dt * 512 + ft * 128:
                                                      u * 4096 + dt * 512 + (ft + 1) * 128],
                                              xnh[:, dt * HT + ch * 256: dt * HT + (ch + 1) * 256],
                                              start=(dt == 0), stop=(dt == NDT - 1))
                        mm.then_inc(tup, 1)
                        nup += 1
                        if pending is not None:
                            emit_down(t, pending)
                        if ch == 0:
                            t.wait_ge(iv4[ei % 4], ei // 4 + 1)
                        t.wait_ge(vhs, ncb)
                        t.matmul(p.PS[6][:, 0:256], inde[ei % 4][:],
                                 cwTb[0:64, H * HT + ch * 256: H * HT + (ch + 1) * 256],
                                 start=True, stop=True).then_inc(tcb, 1)
                        ncb += 1
                        pending = (sl, u, ei % 2, ncb)
                if grp == NGRP - 1:
                    emit_down(t, pending)
                    pending = None

    SW = 3

    @blk.scalar
    def _(s_):
        s_.wait_ge(p.bar_sem, bar0)
        nup = 0
        hb_consumer = [None, None]
        ndn_mirror = 0
        nhs = 0

        def emit_store(ns2):
            par2 = ns2 % 2
            H2, st2_ = divmod(ns2, 32)
            ch2, dq2 = divmod(st2_, 4)
            s_.wait_ge(vcb, ns2 + 1)
            for i2 in range(2):
                dt2 = dq2 * 2 + i2
                s_.dma_start(out=outd[dt2 * 128:(dt2 + 1) * 128,
                                      H2 * HT + ch2 * 256: H2 * HT + (ch2 + 1) * 256],
                             in_=otb[par2][:, i2 * 256:(i2 + 1) * 256]
                             ).then_inc(ost2[par2], 16)

        def comb_step(ns):
            H2, st2 = divmod(ns, 32)
            ch2, dq2 = divmod(st2, 4)
            s_.wait_ge(vev, DNH * H2 + NCH + (NGRP - 1) * NCH + ch2 + 1)
            if ns >= 2:
                s_.wait_ge(vcb, ns - 1)
            ii = None
            for i in range(2):
                dt = dq2 * 2 + i
                ii = s_.activation(slb[ns % 2][:, i * 256:(i + 1) * 256],
                                   acc[:, dt * HT + ch2 * 256: dt * HT + (ch2 + 1) * 256],
                                   AF.Silu)
            ii.then_inc(ssc, 1)
            if ns >= 2:
                emit_store(ns - 2)

        for H in range(NHF):
            for ch in range(NCH):
                par = nup % 2
                s_.wait_ge(tup, nup + 1)
                if hb_consumer[par] is not None:
                    kind, val = hb_consumer[par]
                    s_.wait_ge(tdn if kind == "tdn" else vhs, val)
                ii = None
                for half2 in range(2):
                    ii = s_.activation(hb[par][:, half2 * 512:(half2 + 1) * 512],
                                       p.PS[4 + half2][:, :], AF.Silu)
                ii.then_inc(ssl, 1)
                ndn_mirror += 1
                hb_consumer[par] = ("tdn", ndn_mirror)
                nup += 1
            for grp in range(NGRP):
                for ch in range(NCH):
                    for u in range(GE):
                        par = nup % 2
                        s_.wait_ge(tup, nup + 1)
                        if hb_consumer[par] is not None:
                            kind, val = hb_consumer[par]
                            s_.wait_ge(tdn if kind == "tdn" else vhs, val)
                        ii = None
                        for half2 in range(2):
                            ii = s_.activation(hb[par][:, half2 * 512:(half2 + 1) * 512],
                                               p.PS[4 + half2][:, :], AF.Silu)
                        ii.then_inc(ssl, 1)
                        nhs += 1
                        hb_consumer[par] = ("vhs", nhs)
                        nup += 1
                    ndn_mirror += 1
                    if grp == NGRP - 1 and ch >= SW:
                        hi = 32 if ch == NCH - 1 else 4 * (ch - SW + 1)
                        for st in range(4 * (ch - SW), hi):
                            comb_step(H * 32 + st)
        emit_store(NHF * 32 - 2)
        emit_store(NHF * 32 - 1)

    @blk.vector
    def _(v):
        v.wait_ge(p.bar_sem, bar0)
        nup = 0
        ncb = 0
        nev = 0
        nhs = 0

        def comb_mul(ns):
            par = ns % 2
            v.wait_ge(ssc, ns + 1)
            v.wait_ge(xls2[par], 16 * 4 * (ns // 2 + 1))
            ii = None
            for i in range(2):
                ii = v.tensor_mul(otb[par][:, i * 256:(i + 1) * 256],
                                  otb[par][:, i * 256:(i + 1) * 256],
                                  slb[par][:, i * 256:(i + 1) * 256])
            p.edge(ii, v, "mocb")
            ii = None
            for i in range(2):
                ii = v.tensor_add(otb[par][:, i * 256:(i + 1) * 256],
                                  otb[par][:, i * 256:(i + 1) * 256],
                                  xsb[par][:, i * 256:(i + 1) * 256])
            ii.then_inc(vcb, 1)

        def is_eq_group(gin):
            for u in range(GE):
                ei = gin * GE + u
                par = ei % 4
                if ei >= 4:
                    v.wait_ge(tcb, NCH * GE * (ei // 2 - 1))
                v.wait_ge(ie4[par], ei // 4 + 1)
                v.tensor_single_scalar(inde[par][:], inde[par][:],
                                       0.0, ALU.is_equal).then_inc(iv4[par], 1)

        is_eq_group(0)
        for H in range(NHF):
            for ch in range(NCH):
                nup += 1
                if H >= 1:
                    v.wait_ge(ssc, 32 * (H - 1) + 4 * (ch + 1))
                v.wait_ge(tdn, nev + 1)
                ii = None
                for dto in range(NDT):
                    bank, co = dn_bank(dto)
                    ii = v.tensor_copy(acc[:, dto * HT + ch * 256: dto * HT + (ch + 1) * 256],
                                       bank[:, co:co + 256])
                p.edge(ii, v, "moev")
                v.engine_nop().then_inc(vev, 1)
                nev += 1
            for grp in range(NGRP):
                gi = H * NGRP + grp
                for ch in range(NCH):
                    for u in range(GE):
                        e = grp * GE + u
                        ei = H * NE + e
                        v.wait_ge(ssl, nup + 1)
                        v.wait_ge(tcb, ncb + 1)
                        ii_h = nhs - NE * NCH * H
                        if ii_h >= 2:
                            if ii_h % 2 == 0:
                                v.wait_ge(tda, NE * NCH * H // 2 + ii_h // 2)
                            else:
                                v.wait_ge(tdn, DNH * H + NCH + (ii_h - 1) // 2)
                        par = nup % 2
                        ii = None
                        for ft in range(4):
                            ii = v.tensor_mul(hsc[ei % 2][:, ft * 256:(ft + 1) * 256],
                                              hb[par][:, ft * 256:(ft + 1) * 256],
                                              p.PS[6][:, 0:256])
                        ii.then_inc(vhs, 1)
                        nup += 1
                        ncb += 1
                        nhs += 1
                    v.wait_ge(tdn, nev + 1)
                    ii = None
                    for dto in range(NDT):
                        bank, co = dn_bank(dto)
                        ii = v.tensor_add(acc[:, dto * HT + ch * 256: dto * HT + (ch + 1) * 256],
                                          acc[:, dto * HT + ch * 256: dto * HT + (ch + 1) * 256],
                                          bank[:, co:co + 256])
                    p.edge(ii, v, "moev")
                    v.engine_nop().then_inc(vev, 1)
                    nev += 1
                    if ch == NCH - 2 and gi + 1 < NHF * NGRP:
                        is_eq_group(gi + 1)
                    if grp == NGRP - 1 and ch >= SW:
                        hi = 32 if ch == NCH - 1 else 4 * (ch - SW + 1)
                        for st in range(4 * (ch - SW), hi):
                            comb_mul(H * 32 + st)
    p.bar += 1


def _prep_weights(inp):
    f32 = np.float32
    a = lambda x: np.ascontiguousarray(np.asarray(x, dtype=f32))
    w = {}
    w["w_pq"] = a(inp["L0_attn_proj_q"])
    w["w_pk"] = a(inp["L0_attn_proj_k"])
    w["w_pv"] = a(inp["L0_attn_proj_v"])
    w["w_po"] = a(np.transpose(np.asarray(inp["L0_attn_proj_o"]), (0, 2, 1)))
    w["w_rm"] = a(np.transpose(np.asarray(inp["L0_ffn_rail_maker"]), (0, 1, 3, 2)))
    w["w_rup"] = a(np.transpose(np.asarray(inp["L0_ffn_rail_up_proj"]), (0, 1, 3, 2)))
    w["w_wc"] = a(np.asarray(inp["L0_ffn_w_cproj"]).T)
    w["w_wq"] = a(inp["L1_attn_Wq"])
    w["w_wk"] = a(inp["L1_attn_Wk"])
    w["w_wv"] = a(inp["L1_attn_Wv"])
    w["w_wb"] = a(inp["L1_attn_Wb"])
    w["w_cq"] = a(np.asarray(inp["L1_attn_conv_q"])[:, 0, :])
    w["w_ck"] = a(np.asarray(inp["L1_attn_conv_k"])[:, 0, :])
    w["w_cv"] = a(np.asarray(inp["L1_attn_conv_v"])[:, 0, :])
    g = np.asarray(inp["L1_attn_g"], dtype=f32)
    w["w_g"] = a(np.tile(g, 4).reshape(NDT, 128).T)
    w["w_wo"] = a(inp["L1_attn_Wo"])
    w["w_cent"] = a(np.asarray(inp["L1_ffn_centroids"]).T)
    w["w_eb"] = a(np.asarray(inp["L1_ffn_expert_biases"]).reshape(1, NE))
    w["w_wsu"] = a(inp["L1_ffn_Ws_up"])
    w["w_wsd"] = a(inp["L1_ffn_Ws_down"])
    w["w_wup"] = a(inp["L1_ffn_W_up"])
    w["w_wdn"] = a(inp["L1_ffn_W_down"])
    return w


def _percore_inputs(inp, w):
    x = np.asarray(inp["L0_attn_x"], dtype=np.float32)
    maps = []
    for b in range(B):
        m = dict(w)
        m["x_t"] = np.ascontiguousarray(x[b].T)
        m["bmask"] = (np.arange(CC)[None, :] <= b).astype(np.float32)
        maps.append(m)
    return maps


_TENSORS = {
    "x_t": ((D, L), F32), "xn0d": ((D, L), BF16), "g0d": ((D, L), F32),
    "x1d": ((D, L), F32), "xn2d": ((D, L), BF16), "affd": ((L, NE), F32),
    "qnd": ((D, L), F32), "knd": ((D, L), F32), "vsd": ((D, L), F32),
    "betad": ((NH, L), F32), "s2d": ((D, L), F32), "cwtd": ((NE, L), F32),
    "wtd": ((D, L), F32), "utd": ((D, L), F32),
    "outd": ((D, L), F32),
    "w_pq": ((8, D, 128), F32), "w_pk": ((D, 128), F32), "w_pv": ((D, 128), F32),
    "w_po": ((8, 128, D), F32), "w_rm": ((4, 64, 16, 16), F32),
    "w_rup": ((4, 64, 16, 64), F32), "w_wc": ((16 * D, D), F32),
    "w_wq": ((D, D), F32), "w_wk": ((D, D), F32), "w_wv": ((D, D), F32),
    "w_wb": ((D, NH), F32), "w_cq": ((D, 4), F32), "w_ck": ((D, 4), F32),
    "w_cv": ((D, 4), F32), "w_g": ((128, NDT), F32), "w_wo": ((D, D), F32),
    "w_cent": ((D, NE), F32), "w_eb": ((1, NE), F32),
    "w_wsu": ((D, FF), F32), "w_wsd": ((FF, D), F32),
    "w_wup": ((NE, D, FF), F32), "w_wdn": ((NE, FF, D), F32),
    "bmask": ((1, CC), F32),
}

def _em_init(p, dr):
    ph_init(p, dr)

def _em_ln1(p, dr):
    ph_ln(p, dr["x_t"], dr["xn0d"], name="l1n")

def _em_ln2(p, dr):
    ph_ln(p, dr["x1d"], dr["xn2d"], aff_out=dr["affd"], cent_d=dr["w_cent"], name="l2n")

def _em_ffn0(p, dr):
    ph_ffn0(p, dr)

def _em_attn0(p, dr):
    ph_attn0(p, dr)

def _em_qkv(p, dr):
    ph_qkv(p, dr)

def _em_l1core(p, dr):
    ph_l1core(p, dr)

def _em_route(p, dr):
    ph_route(p, dr)

def _em_moe(p, dr):
    ph_moe(p, dr)

_PHASES = {
    "init": (_em_init, ["w_g", "bmask", "w_eb"], []),
    "ln1": (_em_ln1, ["x_t"], ["xn0d"]),
    "ln2": (_em_ln2, ["x1d", "w_cent"], ["xn2d", "affd"]),
    "ffn0": (_em_ffn0, ["xn0d", "w_rm", "w_rup", "w_wc"], ["g0d"]),
    "attn0": (_em_attn0, ["xn0d", "x_t", "g0d", "w_pq", "w_pk", "w_pv", "w_po"], ["x1d"]),
    "qkv": (_em_qkv, ["xn2d", "w_wq", "w_wk", "w_wv", "w_wb", "w_cq", "w_ck", "w_cv"],
            ["qnd", "knd", "vsd", "betad"]),
    "l1core": (_em_l1core, ["qnd", "knd", "vsd", "betad", "bmask", "w_g", "w_wo"],
               ["s2d", "wtd", "utd"]),
    "route": (_em_route, ["affd", "w_eb"], ["cwtd"]),
    "moe": (_em_moe, ["xn2d", "cwtd", "x1d", "s2d", "w_wsu", "w_wsd", "w_wup", "w_wdn"],
            ["outd"]),
}
_PHASE_ORDER = ["ln1", "ffn0", "attn0", "ln2", "qkv", "l1core", "route", "moe"]


def build(phase_names, ext_out):
    nc = bass.Bass("TRN2", target_bir_lowering=False)
    phase_names = ["init"] + [x for x in phase_names if x != "init"]
    produced = set()
    consumed = set()
    for ph in phase_names:
        _, ins, outs = _PHASES[ph]
        consumed.update(i for i in ins if i not in produced)
        produced.update(outs)
    dr = {}
    for name in sorted(consumed):
        shape, dt = _TENSORS[name]
        dr[name] = nc.dram_tensor(name, list(shape), dt, kind="ExternalInput").ap()
    for name in sorted(produced):
        shape, dt = _TENSORS[name]
        kind = "ExternalOutput" if name in ext_out else "Internal"
        dr[name] = nc.dram_tensor(name, list(shape), dt, kind=kind).ap()
    with ExitStack() as gctx:
        p = Prog(nc, gctx)
        for k, ph in enumerate(phase_names):
            p.phase_sems = []
            with ExitStack() as pctx:
                p.pctx = pctx
                p.block = pctx.enter_context(nc.Block())
                _PHASES[ph][0](p, dr)
                ids = list(p.phase_sems)
                assert not ids or min(ids) >= p.clear_from, (ids, p.clear_from)
            p.pctx = gctx
            if k < len(phase_names) - 1:
                nc.all_engine_barrier()
                nc.gpsimd.dma_reset(range(p.clear_from, 256))
                nc.gpsimd.sem_clear(range(p.clear_from, 256))
                nc.all_engine_barrier()
    return nc, sorted(consumed)


def run_phases(phase_names, ext_out, in_maps, core_ids=None):
    nc, needed = build(phase_names, ext_out)
    maps = [{k: m[k] for k in needed if k in m} for m in in_maps]
    import os as _os
    res = run_bass_kernel_spmd(nc, maps, core_ids=core_ids or list(range(len(in_maps))),
                               trace=bool(_os.environ.get("DEVTRACE")))
    return res.results


_FULL_CHAIN = ["ln1", "ffn0", "attn0", "ln2", "qkv", "l1core", "route", "moe"]


def kernel(**inputs):
    w = _prep_weights(inputs)
    maps = _percore_inputs(inputs, w)
    nc, needed = build(_FULL_CHAIN, {"outd"})
    in_maps = [{k: m[k] for k in needed if k in m} for m in maps]
    res = run_bass_kernel_spmd(nc, in_maps, core_ids=list(range(B)))
    out = np.stack([np.asarray(r["outd"], np.float32).T for r in res.results], axis=0)
    return np.ascontiguousarray(out)
